# version 02
# baseline (speedup 1.0000x reference)
import dataclasses
import numpy as np
import ml_dtypes
import concourse.bass as bass
import concourse.mybir as mybir
from concourse.bass_utils import run_bass_kernel_spmd

F32 = mybir.dt.float32
BF16 = mybir.dt.bfloat16
AF = mybir.ActivationFunctionType
ALU = mybir.AluOpType
AX = mybir.AxisListType
BF = ml_dtypes.bfloat16

NCORES = 8
SEQ = 16384
D = 2048
TC = SEQ // NCORES
PAD = 128
TF = TC + PAD
NKD = D // 128
INTER = 4096
HEADS = 32
HD = 128
STATE = 128
CONVK = 4
CONV_DIM = INTER + 2 * STATE
NPROJ = INTER + CONV_DIM + HEADS
NO_HBC = CONV_DIM // 128
NO_GATE = INTER // 128
CHUNK = 256
NCH = TC // CHUNK
EPS = 1e-5
E = 8
I_FF = 1024
NKI = I_FF // 128
RG = [[0, 1, 2, 3, 4, 5, 6, 7]]


def bcast_part(ap, n):
    new = [[0, n]] + [list(x) for x in ap.ap][1:]
    return dataclasses.replace(ap, ap=type(ap.ap)(new))


def bcast_prepend(ap, n):
    new = [[0, n]] + [list(x) for x in ap.ap]
    return dataclasses.replace(ap, ap=type(ap.ap)(new))


def rep_free(ap, n):
    aps = [list(x) for x in ap.ap]
    new = [aps[0]] + [[0, n]] + aps[1:]
    return dataclasses.replace(ap, ap=type(ap.ap)(new))


class EM:
    COMPUTE = ("v", "s", "t", "gp")
    NDMA = 6

    def __init__(self, nc):
        self.nc = nc
        self.q = {k: [] for k in ("sync", "gp", "v", "s", "t")}
        self.sem = {k: nc.alloc_semaphore(f"sem_{k}") for k in self.COMPUTE}
        self.cnt = {k: 0 for k in self.COMPUTE}
        self.dma_pools = {q: [nc.alloc_semaphore(f"dma_{q}{i}") for i in range(self.NDMA)]
                          for q in ("sync", "gp")}
        self.dma_uses = {q: [0] * self.NDMA for q in ("sync", "gp")}
        self.dma_rr = {q: 0 for q in ("sync", "gp")}
        self.waited = {k: {} for k in self.q}
        self.writer = {}
        self.readers = {}
        self.pending_t = []
        self._segs = []

    def _wait(self, queue, rec):
        if rec is None:
            return
        sem, val, name = rec
        if val <= 0:
            return
        w = self.waited[queue]
        if w.get(name, -1) >= val:
            return
        w[name] = val
        self.q[queue].append(lambda e, s=sem, v=val: e.wait_ge(s, v))

    def _note_read(self, key, rec):
        d = self.readers.setdefault(key, {})
        old = d.get(rec[2])
        if old is None or old[1] < rec[1]:
            d[rec[2]] = rec

    def _pre(self, queue, reads, writes, pe_acc_key=None):
        for k in reads:
            self._wait(queue, self.writer.get(k))
        for k in writes:
            wrec = self.writer.get(k)
            if not (k == pe_acc_key and wrec is not None and wrec[2] == "c_t"):
                self._wait(queue, wrec)
            for rec in self.readers.get(k, {}).values():
                self._wait(queue, rec)

    def _finish(self, reads, writes, rec):
        for k in reads:
            self._note_read(k, rec)
        for k in writes:
            self.writer[k] = rec
            self.readers[k] = {}

    def op(self, queue, fn, reads=(), writes=()):
        assert queue in self.COMPUTE
        reads, writes = list(reads), list(writes)
        self._pre(queue, reads, writes)
        self.cnt[queue] += 1
        sem, val = self.sem[queue], self.cnt[queue]
        self.q[queue].append(lambda e, f=fn, s=sem, v=1: f(e).then_inc(s, v))
        self._finish(reads, writes, (sem, val, f"c_{queue}"))

    def mm(self, fn, out_key, in_keys, last=True):
        q = "t"
        in_keys = [k for k in in_keys if k is not None]
        self._pre(q, in_keys, [out_key], pe_acc_key=out_key)
        if last:
            self.cnt[q] += 1
            sem, val = self.sem[q], self.cnt[q]
            self.q[q].append(lambda e, f=fn, s=sem: f(e).then_inc(s, 1))
            rec = (sem, val, "c_t")
            self.pending_t.append((in_keys, [out_key]))
            for rds, wrs in self.pending_t:
                self._finish(rds, wrs, rec)
            self.pending_t = []
        else:
            self.q[q].append(fn)
            self.pending_t.append((in_keys, [out_key]))
            self.writer[out_key] = (self.sem[q], self.cnt[q] + 1, "c_t")

    def dma(self, queue, dst_ap, src_ap, wkey, rkeys=(), **kw):
        pool, uses = self.dma_pools[queue], self.dma_uses[queue]
        i = self.dma_rr[queue]
        self.dma_rr[queue] = (i + 1) % self.NDMA
        sem = pool[i]
        name = f"d_{queue}{i}"
        self._wait(queue, (sem, uses[i] * 16, name))
        rkeys = list(rkeys)
        self._pre(queue, rkeys, [wkey] if wkey is not None else [])
        uses[i] += 1
        val = uses[i] * 16
        self.q[queue].append(
            lambda e, d=dst_ap, s=src_ap, sm=sem, k=dict(kw): e.dma_start(d, s, **k).then_inc(sm, 16))
        self._finish(rkeys, [wkey] if wkey is not None else [], (sem, val, name))

    def collective(self, fn, rkeys, wkey):
        q = "gp"
        self._pre(q, list(rkeys), [wkey])
        self.cnt[q] += 1
        sem, val = self.sem[q], self.cnt[q]
        self.q[q].append(lambda e, f=fn, s=sem: f(e).then_inc(s, 1))
        self._finish(list(rkeys), [wkey], (sem, val, f"c_{q}"))

    def barrier(self):
        for q in ("sync", "gp"):
            for i, sem in enumerate(self.dma_pools[q]):
                u = self.dma_uses[q][i]
                if u:
                    self._wait("sync", (sem, u * 16, f"d_{q}{i}"))
        self._segs.append(self.q)
        self.q = {k: [] for k in self.q}
        for qq in self.waited:
            w = self.waited[qq]
            for c in self.COMPUTE:
                w[f"c_{c}"] = max(w.get(f"c_{c}", 0), self.cnt[c])
            for q in ("sync", "gp"):
                for i in range(self.NDMA):
                    w[f"d_{q}{i}"] = max(w.get(f"d_{q}{i}", 0), self.dma_uses[q][i] * 16)

    def finalize(self):
        self.barrier()
        for seg in self._segs:
            if all(len(v) == 0 for v in seg.values()):
                continue
            with self.nc.Block() as b:
                for qname, reg in (("sync", b.sync), ("gp", b.gpsimd), ("v", b.vector),
                                   ("s", b.scalar), ("t", b.tensor)):
                    if seg[qname]:
                        reg(lambda e, ops=tuple(seg[qname]): [f(e) for f in ops])


def build_nc(upto="full", dbg=()):
    nc = bass.Bass("TRN2", target_bir_lowering=False, debug=False)
    em = EM(nc)

    def dram(name, shape, dtype=F32):
        kind = "ExternalOutput" if name in dbg else "Internal"
        return nc.dram_tensor(name, shape, dtype, kind=kind).ap()

    def inp(name, shape, dtype=F32):
        return nc.dram_tensor(name, shape, dtype, kind="ExternalInput").ap()

    def sb(name, shape, dtype=F32):
        return nc.alloc_sbuf_tensor(name, shape, dtype)

    scopes = []

    def push_scope():
        scopes.append([])

    def sbp(name, shape, dtype=F32):
        g = nc.sbuf_tensor(name, shape, dtype)
        t = g.__enter__()
        scopes[-1].append(g)
        return t

    def pop_scope():
        em.barrier()
        for g in reversed(scopes.pop()):
            g.__exit__(None, None, None)

    x_in = inp("x", [TF, D])
    ipWT = inp("ipWT", [D, NPROJ])
    convdiag = inp("convdiag", [NO_HBC * CONVK * 128, 128])
    convb = inp("convb", [NO_HBC, 128])
    alog = inp("alog", [32, 1])
    dtbias = inp("dtbias", [32, 1])
    drep = inp("drep", [1, INTER])
    normcols = inp("normcols", [128, 32])
    opWT = inp("opWT", [INTER, D])
    gateWT = inp("gateWT", [D, E])
    gatebB = inp("gatebB", [1, E])
    idxdB = inp("idxdB", [1, E])
    w1t = inp("w1t", [E * D, I_FF])
    w3t = inp("w3t", [E * D, I_FF])
    w2t = inp("w2t", [E * I_FF, D])
    ws1t = inp("ws1t", [D, I_FF])
    ws3t = inp("ws3t", [D, I_FF])
    ws2t = inp("ws2t", [I_FF, D])
    mask01 = inp("mask01", [2 * 128, CHUNK])
    cmaskB = inp("cmaskB", [32, E])
    negB = inp("negB", [32, E])
    onehotB = inp("onehotB", [32, E])
    eye_in = inp("eye", [128, 128])

    out_d = nc.dram_tensor("out", [TC, D], F32, kind="ExternalOutput").ap()
    xs_d = dram("xs_d", [INTER, TC], BF16)
    u_d = dram("u_d", [128, NKD * TF], BF16)
    ypart_d = dram("ypart_d", [TC, INTER])
    acum_d = dram("acum_d", [32, TC])
    x1_d = dram("x1_d", [TC, D])
    gate_s_d = dram("gate_s_d", [TC, INTER], BF16)
    opWTb_d = dram("opWTb_d", [INTER, D], BF16)
    y_d = dram("y_d", [TC, D])
    sstates_d = dram("sstates_d", [NCH * 128, HEADS * HD])
    dvec_d = dram("dvec_d", [NCH, 32])
    cc_in = dram("cc_in", [129, HEADS * HD])
    cc_out = dram("cc_out", [NCORES * 129, HEADS * HD])
    wflat_d = dram("wflat_d", [1, NCORES * 32])
    dbg_d = dram("dbg_d", [512, max(TC, INTER)]) if "dbg_d" in dbg else None

    P = [nc.alloc_psum_tensor(f"P{i}", [128, 512], F32) for i in range(6)]
    PT = [nc.alloc_psum_tensor(f"PT{i}", [128, 1024], BF16) for i in range(2)]
    pk = [f"P{i}" for i in range(6)]
    ptk = [f"PT{i}" for i in range(2)]
    prr = [0]

    def next_p():
        i = prr[0] % 6
        prr[0] += 1
        return P[i], pk[i]

    ptrr = [0]

    def next_pt():
        i = ptrr[0] % 2
        ptrr[0] += 1
        return PT[i], ptk[i]

    eye_f = sb("eye_f", [128, 128], F32)
    eye_b = sb("eye_b", [128, 128], BF16)
    ones_f = sb("ones_f", [1, 128], F32)
    em.dma("sync", eye_f[:], eye_in, "eye_f")
    em.op("v", lambda e: e.tensor_copy(eye_b[:], eye_f[:]), reads=["eye_f"], writes=["eye_b"])
    em.op("v", lambda e: e.memset(ones_f[:], 1.0), writes=["ones_f"])
    normc_sb = sb("normc_sb", [128, 32])
    em.dma("sync", normc_sb[:], normcols, "normc_sb")
    push_scope()
    C_fm = sbp("C_fm", [128, TC], BF16)
    push_scope()
    B_fm = sbp("B_fm", [128, TC], BF16)
    dtv = sbp("dtv", [32, TC])
    ad = sbp("ad", [32, TC])
    alog_s = sbp("alog_s", [32, 1])
    dtb_s = sbp("dtb_s", [32, 1])
    acol = sbp("acol", [32, 1])
    ltot = sbp("ltot", [32, 1])

    push_scope()
    u_fm = sbp("u_fm", [128, NKD * TF], BF16)
    NT = TF // 128
    dtraw = sbp("dtraw", [32, TC])
    spt = sbp("spt", [32, TC])
    xt = [sbp(f"xt{i}", [128, D]) for i in range(2)]
    bnb = [sbp(f"bnb{i}", [128, 24]) for i in range(2)]
    mv = [sbp(f"mv{i}", [128, 2]) for i in range(2)]
    rstd = [sbp(f"rstd{i}", [128, 1]) for i in range(2)]
    negmr = [sbp(f"negmr{i}", [128, 1]) for i in range(2)]
    ubf = [sbp(f"ubf{i}", [128, D], BF16) for i in range(2)]

    for i in range(NT):
        j = i % 2
        kx, kb, km, kr, kn, ku = (f"xt{j}", f"bnb{j}", f"mv{j}", f"rstd{j}", f"negmr{j}", f"ubf{j}")
        em.dma("sync", xt[j][:], x_in[i * 128:(i + 1) * 128, :], kx)
        for s4 in range(4):
            em.op("v", lambda e, j=j, s4=s4: e.bn_stats(bnb[j][:, s4 * 6:(s4 + 1) * 6],
                                                        xt[j][:, s4 * 512:(s4 + 1) * 512]),
                  reads=[kx], writes=[kb])
        em.op("v", lambda e, j=j: e.bn_aggr(mv[j][:], bnb[j][:]), reads=[kb], writes=[km])
        em.op("v", lambda e, j=j: e.tensor_scalar(rstd[j][:], mv[j][:, 1:2], EPS, None, ALU.add),
              reads=[km], writes=[kr])
        em.op("s", lambda e, j=j: e.activation(rstd[j][:], rstd[j][:], AF.Ln, bias=0.0, scale=1.0),
              reads=[kr], writes=[kr])
        em.op("s", lambda e, j=j: e.activation(rstd[j][:], rstd[j][:], AF.Exp, bias=0.0, scale=-0.5),
              reads=[kr], writes=[kr])
        em.op("v", lambda e, j=j: e.tensor_scalar(negmr[j][:], rstd[j][:], mv[j][:, 0:1], -1.0,
                                                  ALU.mult, ALU.mult),
              reads=[kr, km], writes=[kn])
        em.op("v", lambda e, j=j: e.tensor_scalar(ubf[j][:], xt[j][:], rstd[j][:], negmr[j][:],
                                                  ALU.mult, ALU.add),
              reads=[kx, kr, kn], writes=[ku])
        for half in range(2):
            pt, ptkey = next_pt()
            for kk in range(8):
                k = half * 8 + kk
                em.mm(lambda e, j=j, k=k, kk=kk, pt=pt: e.transpose(
                    pt[:, kk * 128:(kk + 1) * 128], ubf[j][:, k * 128:(k + 1) * 128], eye_b[:]),
                    ptkey, [ku, "eye_b"], last=(kk == 7))
            em.op("v", lambda e, i=i, half=half, pt=pt: e.tensor_copy(
                u_fm.ap().rearrange("p (k t) -> p k t", k=NKD)[:, half * 8:(half + 1) * 8,
                                                              i * 128:(i + 1) * 128],
                pt.ap().rearrange("p (k t) -> p k t", k=8)),
                reads=[ptkey], writes=["u_fm"])

    for f in range(32):
        jj = f % 2
        em.dma("sync", xt[jj][:], opWT[f * 128:(f + 1) * 128, :], f"xt{jj}")
        em.op("v", lambda e, jj=jj, f=f: e.tensor_scalar(ubf[jj][:], xt[jj][:],
              normc_sb[:, f:f + 1], None, ALU.mult),
              reads=[f"xt{jj}", "normc_sb"], writes=[f"ubf{jj}"])
        em.dma("sync", opWTb_d[f * 128:(f + 1) * 128, :], ubf[jj][:], f"opwtb_{f}", [f"ubf{jj}"])

    if upto == "A1ln":
        em.dma("sync", u_d, u_fm[:], "u_d", ["u_fm"])
        _final_wait(nc, em)
        return nc
    wsb = [sbp(f"wsb{i}", [128, NKD * 128], BF16) for i in range(2)]
    projbf = [sbp(f"projbf{i}", [128, TF], BF16) for i in range(2)]
    cdiag = [sbp(f"cdiag{i}", [128, CONVK * 128], BF16) for i in range(2)]
    cbcol = [sbp(f"cbcol{i}", [128, 1]) for i in range(2)]
    hbco = [sbp(f"hbco{i}", [128, TC], BF16) for i in range(2)]
    PIECES = [(0, 512), (512, 512), (1024, 512), (1536, 512), (2048, 128)]

    ipWT_r = ipWT.rearrange("(k p) c -> p k c", p=128)

    for oj in range(NO_HBC + 1):
        jj = oj % 2
        ocol = NO_GATE * 128 + oj * 128
        wk = f"wsb{jj}"
        ncols = 128 if oj < NO_HBC else 32
        em.dma("gp", wsb[jj].ap().rearrange("p (k c) -> p k c", k=NKD)[:, :, 0:ncols],
               ipWT_r[:, :, ocol:ocol + ncols], wk)
        pj = f"projbf{jj}"
        if oj < NO_HBC:
            for (poff, plen) in PIECES:
                pp, ppk = next_p()
                for k in range(NKD):
                    em.mm(lambda e, pp=pp, plen=plen, k=k, jj=jj, poff=poff: e.matmul(
                        pp[:, 0:plen], wsb[jj][:, k * 128:(k + 1) * 128],
                        u_fm[:, k * TF + poff: k * TF + poff + plen],
                        start=(k == 0), stop=(k == NKD - 1)),
                        ppk, [wk, "u_fm"], last=(k == NKD - 1))
                em.op("v", lambda e, jj=jj, poff=poff, plen=plen, pp=pp: e.tensor_copy(
                    projbf[jj][:, poff:poff + plen], pp[:, 0:plen]),
                    reads=[ppk], writes=[pj])
            ck = f"cdiag{jj}"
            cbk = f"cbcol{jj}"
            em.dma("gp", cdiag[jj][:], convdiag[oj * CONVK * 128:(oj + 1) * CONVK * 128, :]
                   .rearrange("(k p) c -> p k c", p=128), ck)
            em.dma("sync", cbcol[jj][:], convb[oj:oj + 1, :].rearrange("x c -> c x"), cbk)
            hk = f"hbco{jj}"
            for piece in range(4):
                pp, ppk = next_p()
                for k in range(CONVK):
                    em.mm(lambda e, pp=pp, k=k, jj=jj, piece=piece: e.matmul(
                        pp[:], cdiag[jj][:, k * 128:(k + 1) * 128],
                        projbf[jj][:, 125 + k + piece * 512: 125 + k + piece * 512 + 512],
                        start=(k == 0), stop=(k == CONVK - 1)),
                        ppk, [ck, pj], last=(k == CONVK - 1))
                if oj < NO_GATE:
                    em.op("s", lambda e, jj=jj, piece=piece, pp=pp: e.activation(
                        hbco[jj][:, piece * 512:(piece + 1) * 512], pp[:], AF.Silu,
                        bias=cbcol[jj][:], scale=1.0), reads=[ppk, cbk], writes=[hk])
                else:
                    dstt = B_fm if oj == NO_GATE else C_fm
                    dk = "B_fm" if oj == NO_GATE else "C_fm"
                    em.op("s", lambda e, dstt=dstt, piece=piece, pp=pp, jj=jj: e.activation(
                        dstt[:, piece * 512:(piece + 1) * 512], pp[:], AF.Silu,
                        bias=cbcol[jj][:], scale=1.0), reads=[ppk, cbk], writes=[dk])
            if oj < NO_GATE:
                em.dma("sync", xs_d[oj * 128:(oj + 1) * 128, :], hbco[jj][:], f"xs_d_{oj}", [hk])
        else:
            for piece in range(4):
                pp, ppk = next_p()
                for k in range(NKD):
                    em.mm(lambda e, pp=pp, k=k, jj=jj, piece=piece: e.matmul(
                        pp[0:32, :], wsb[jj][:, k * 128: k * 128 + 32],
                        u_fm[:, k * TF + 128 + piece * 512: k * TF + 128 + piece * 512 + 512],
                        start=(k == 0), stop=(k == NKD - 1)),
                        ppk, [wk, "u_fm"], last=(k == NKD - 1))
                em.op("v", lambda e, piece=piece, pp=pp: e.tensor_copy(
                    dtraw[:, piece * 512:(piece + 1) * 512], pp[0:32, :]),
                    reads=[ppk], writes=["dtraw"])


    em.dma("sync", alog_s[:], alog, "alog_s")
    em.dma("sync", dtb_s[:], dtbias, "dtb_s")
    em.op("s", lambda e: e.activation(acol[:], alog_s[:], AF.Exp, bias=0.0, scale=1.0),
          reads=["alog_s"], writes=["acol"])
    em.op("v", lambda e: e.tensor_scalar(acol[:], acol[:], -1.0, None, ALU.mult),
          reads=["acol"], writes=["acol"])
    em.op("v", lambda e: e.tensor_scalar(dtv[:], dtraw[:], dtb_s[:], None, ALU.add),
          reads=["dtraw", "dtb_s"], writes=["dtv"])
    em.op("s", lambda e: e.activation(spt[:], dtv[:], AF.Abs, bias=0.0, scale=1.0),
          reads=["dtv"], writes=["spt"])
    em.op("s", lambda e: e.activation(spt[:], spt[:], AF.Exp, bias=0.0, scale=-1.0),
          reads=["spt"], writes=["spt"])
    em.op("v", lambda e: e.tensor_scalar(spt[:], spt[:], 1.0, None, ALU.add),
          reads=["spt"], writes=["spt"])
    em.op("s", lambda e: e.activation(spt[:], spt[:], AF.Ln, bias=0.0, scale=1.0),
          reads=["spt"], writes=["spt"])
    em.op("v", lambda e: e.tensor_scalar(dtv[:], dtv[:], 0.0, None, ALU.max),
          reads=["dtv"], writes=["dtv"])
    em.op("v", lambda e: e.tensor_tensor(dtv[:], dtv[:], spt[:], ALU.add),
          reads=["dtv", "spt"], writes=["dtv"])
    em.op("v", lambda e: e.tensor_scalar(dtv[:], dtv[:], 100.0, None, ALU.min),
          reads=["dtv"], writes=["dtv"])
    em.op("v", lambda e: e.tensor_scalar(ad[:], dtv[:], acol[:], None, ALU.mult),
          reads=["dtv", "acol"], writes=["ad"])

    if upto == "A1":
        em.dma("sync", dbg_d[0:32, 0:TC], dtv[:], "dbg_d", ["dtv"])
        em.dma("sync", dbg_d[32:64, 0:TC], ad[:], "dbg_d", ["ad"])
        em.dma("gp", dbg_d[64:192, 0:TC], B_fm[:], "dbg_d", ["B_fm"])
        em.dma("gp", dbg_d[192:320, 0:TC], C_fm[:], "dbg_d", ["C_fm"])
        _final_wait(nc, em)
        return nc

    wgbuf = [sbp(f"wgbuf{i}", [128, NKD * 512], BF16) for i in range(2)]
    gstg = [sbp(f"gstg{i}", [128, 512], BF16) for i in range(2)]
    for piece in range(8):
        jj = piece % 2
        wk = f"wgbuf{jj}"
        em.dma("gp", wgbuf[jj].ap().rearrange("p (k c) -> p k c", k=NKD),
               ipWT_r[:, :, piece * 512:(piece + 1) * 512], wk)
        for tt in range(16):
            pp, ppk = next_p()
            for k in range(NKD):
                em.mm(lambda e, pp=pp, k=k, jj=jj, tt=tt: e.matmul(
                    pp[:], u_fm[:, k * TF + 128 + tt * 128: k * TF + 128 + (tt + 1) * 128],
                    wgbuf[jj][:, k * 512:(k + 1) * 512],
                    start=(k == 0), stop=(k == NKD - 1)),
                    ppk, [wk, "u_fm"], last=(k == NKD - 1))
            sj = tt % 2
            em.op("s", lambda e, sj=sj, pp=pp: e.activation(gstg[sj][:], pp[:], AF.Silu,
                  bias=0.0, scale=1.0), reads=[ppk], writes=[f"gstg{sj}"])
            em.dma("sync", gate_s_d[tt * 128:(tt + 1) * 128, piece * 512:(piece + 1) * 512],
                   gstg[sj][:], f"gates_{tt}_{piece}", [f"gstg{sj}"])

    pop_scope()

    push_scope()
    mask_sb = sbp("mask_sb", [128, 2 * CHUNK])
    em.dma("sync", mask_sb.ap().rearrange("p (s c) -> p s c", s=2),
           mask01.rearrange("(s p) c -> p s c", p=128), "mask_sb")
    drep_sb = sbp("drep_sb", [128, INTER], BF16)
    em.dma("gp", drep_sb[:], bcast_part(drep, 128), "drep_sb")
    acA = [sbp(f"acA{i}", [32, 384]) for i in range(2)]
    acB = [sbp(f"acB{i}", [32, 384]) for i in range(2)]
    for i in range(2):
        em.op("v", lambda e, i=i: e.memset(acA[i][:], 0.0), writes=[f"acA{i}"])
        em.op("v", lambda e, i=i: e.memset(acB[i][:], 0.0), writes=[f"acB{i}"])
    em.op("v", lambda e: e.memset(ltot[:], 0.0), writes=["ltot"])
    acumC = [sbp(f"acumC{i}", [32, CHUNK]) for i in range(2)]
    dvecC = [sbp(f"dvecC{i}", [32, 1]) for i in range(2)]
    dsrow = [sbp(f"dsrow{i}", [32, CHUNK]) for i in range(2)]
    ddrow = [sbp(f"ddrow{i}", [32, CHUNK]) for i in range(2)]
    acumT = [[sbp(f"acumT{i}_{p}", [128, 32]) for i in range(2)] for p in range(2)]
    dtvT = [[sbp(f"dtvT{i}_{p}", [128, 32]) for i in range(2)] for p in range(2)]
    ddT = [[sbp(f"ddT{i}_{p}", [128, 32]) for i in range(2)] for p in range(2)]
    dcols = [sbp(f"dcols{i}", [128, 32]) for i in range(2)]
    rbg = [sbp(f"rbg{i}", [128, 8 * CHUNK]) for i in range(2)]
    dbg8 = [sbp(f"dbg8{i}", [128, 8 * CHUNK], BF16) for i in range(2)]
    lbg = [sbp(f"lbg{i}", [128, 8 * CHUNK], BF16) for i in range(2)]
    mbufS = [sbp(f"mbuf{i}", [128, HEADS * CHUNK], BF16) for i in range(2)]
    mbuf = [mbufS, mbufS]
    gm = [[sbp(f"gm{i}_{p}", [128, CHUNK]) for i in range(2)] for p in range(2)]
    xsgS = sbp("xsg", [128, 8 * CHUNK], BF16)
    xsg = [xsgS, xsgS]
    xsTS = [sbp(f"xsT{i}", [128, HEADS * 128], BF16) for i in range(2)]
    xsT = [xsTS, xsTS]
    xdXS = [sbp(f"xdX{i}", [128, HEADS * 128], BF16) for i in range(2)]
    xdX = [xdXS, xdXS]
    btT = [sbp(f"btT{i}", [128, 2 * 128], BF16) for i in range(2)]
    state_run = sbp("state_run", [128, HEADS * 128])
    em.op("v", lambda e, t=state_run: e.memset(t[:], 0.0), writes=["state_run"])
    scsb = [sbp(f"scsb{i}", [128, 512]) for i in range(2)]
    ysbS = [sbp(f"ysb{i}", [128, INTER]) for i in range(2)]
    ysb = [ysbS, ysbS]

    def a2_chunk(c):
        c0 = c * CHUNK
        par = c % 2
        acum, dvec = acumC[par], dvecC[par]
        acA_, acB_ = acA[par], acB[par]
        dsr, ddr = dsrow[par], ddrow[par]
        acT, dtT, ddTT = acumT[par], dtvT[par], ddT[par]
        dcl, rbg_, dbg_, lbg_ = dcols[par], rbg[par], dbg8[par], lbg[par]
        mb, gmp, xsgp, xsTp, xdXp = mbuf[par], gm[par], xsg[par], xsT[par], xdX[par]
        btTp, scsbp, ysbp = btT[par], scsb[par], ysb[par]
        kac, kdv = f"acumC{par}", f"dvecC{par}"
        kaA, kaB = f"acA{par}", f"acB{par}"
        kdsr, kddr = f"dsrow{par}", f"ddrow{par}"
        kdcl, krbg, kdbg, klbg = f"dcols{par}", f"rbg{par}", f"dbg8{par}", f"lbg{par}"
        kxsg, kbtT, kscsb = "xsg", f"btT{par}", f"scsb{par}"
        em.op("v", lambda e, c0=c0: e.tensor_copy(acA_[:, 128:384], ad[:, c0:c0 + CHUNK]),
              reads=["ad"], writes=[kaA])
        src_, dst_, sk, dk = acA_, acB_, kaA, kaB
        for s in (1, 2, 4, 8, 16, 32, 64, 128):
            em.op("v", lambda e, src_=src_, dst_=dst_, s=s: e.tensor_tensor(
                dst_[:, 128:384], src_[:, 128:384], src_[:, 128 - s:384 - s], ALU.add),
                reads=[sk], writes=[dk])
            src_, dst_, sk, dk = dst_, src_, dk, sk
        em.op("v", lambda e, src_=src_, acum=acum: e.tensor_copy(acum[:], src_[:, 128:384]),
              reads=[sk], writes=[kac])
        em.op("v", lambda e, acum=acum: e.tensor_tensor(ltot[:], ltot[:], acum[:, 255:256], ALU.add),
              reads=["ltot", kac], writes=["ltot"])
        em.op("s", lambda e, acum=acum, dvec=dvec: e.activation(dvec[:], acum[:, 255:256], AF.Exp, bias=0.0, scale=1.0),
              reads=[kac], writes=[kdv])
        em.dma("sync", dvec_d[c:c + 1, :].rearrange("x h -> h x"), dvec[:], f"dvec_d_{c}", [kdv])
        em.dma("sync", acum_d[:, c0:c0 + CHUNK], acum[:], f"acum_d_{c}", [kac])
        em.op("v", lambda e, acum=acum, dsr=dsr: e.tensor_scalar(dsr[:], acum[:], acum[:, 255:256], -1.0,
              ALU.subtract, ALU.mult), reads=[kac], writes=[kdsr])
        em.op("s", lambda e, dsr=dsr: e.activation(dsr[:], dsr[:], AF.Exp, bias=0.0, scale=1.0),
              reads=[kdsr], writes=[kdsr])
        em.op("v", lambda e, c0=c0, ddr=ddr, dsr=dsr: e.tensor_tensor(ddr[:], dtv[:, c0:c0 + CHUNK], dsr[:], ALU.mult),
              reads=["dtv", kdsr], writes=[kddr])
        for st in range(2):
            for (srcap, srck, dstt, dstk) in (
                    (acum[:, st * 128:(st + 1) * 128], kac, acT[st], f"acumT{st}_{par}"),
                    (dtv[:, c0 + st * 128:c0 + (st + 1) * 128], "dtv", dtT[st], f"dtvT{st}_{par}"),
                    (ddr[:, st * 128:(st + 1) * 128], kddr, ddTT[st], f"ddT{st}_{par}")):
                pp, ppk = next_p()
                em.mm(lambda e, pp=pp, srcap=srcap: e.transpose(pp[0:128, 0:32], srcap,
                      eye_f[0:32, 0:32]), ppk, [srck, "eye_f"], last=True)
                em.op("v", lambda e, pp=pp, dstt=dstt: e.tensor_copy(dstt[:], pp[0:128, 0:32]),
                      reads=[ppk], writes=[dstk])
        em.dma("sync", dcl[:], bcast_part(dvec_d[c:c + 1, :], 128), kdcl, [f"dvec_d_{c}"])
        for st in range(2):
            pp, ppk = next_p()
            em.mm(lambda e, st=st, pp=pp, c0=c0: e.matmul(pp[:, 0:CHUNK],
                  B_fm[:, c0 + st * 128:c0 + (st + 1) * 128], C_fm[:, c0:c0 + CHUNK],
                  start=True, stop=True), ppk, ["B_fm", "C_fm"], last=True)
            em.op("v", lambda e, st=st, pp=pp, gmp=gmp: e.tensor_tensor(gmp[st][:], pp[:, 0:CHUNK],
                  mask_sb[:, st * CHUNK:(st + 1) * CHUNK], ALU.mult),
                  reads=[ppk, "mask_sb"], writes=[f"gm{st}_{par}"])
        for g in range(4):
            em.dma("sync", rbg_.ap().rearrange("p (h l) -> p h l", h=8),
                   bcast_prepend(acum_d[8 * g:8 * (g + 1), c0:c0 + CHUNK], 128), krbg, [f"acum_d_{c}"])
            for st in range(2):
                for hh in range(8):
                    em.op("gp", lambda e, st=st, hh=hh, g=g, dbgt=dbg_, rbgt=rbg_, acT=acT: e.tensor_scalar(
                        dbgt[:, hh * CHUNK:(hh + 1) * CHUNK], rbgt[:, hh * CHUNK:(hh + 1) * CHUNK],
                        acT[st][:, 8 * g + hh:8 * g + hh + 1], 0.0, ALU.subtract, ALU.min),
                        reads=[krbg, f"acumT{st}_{par}"], writes=[kdbg])
                em.op("s", lambda e, lbgt=lbg_, dbgt=dbg_: e.activation(lbgt[:], dbgt[:], AF.Exp, bias=0.0, scale=1.0),
                      reads=[kdbg], writes=[klbg])
                em.op("v", lambda e, st=st, g=g, mb=mb, lbgt=lbg_, gmp=gmp: e.tensor_tensor(
                    mb[st].ap().rearrange("p (h l) -> p h l", h=HEADS)[:, 8 * g:8 * (g + 1), :],
                    lbgt.ap().rearrange("p (h l) -> p h l", h=8),
                    rep_free(gmp[st][:], 8), ALU.mult),
                    reads=[klbg, f"gm{st}_{par}"], writes=[f"mbuf{st}"])
        for g in range(4):
            em.dma("sync", xsgp.ap().rearrange("p (h t) -> p h t", h=8),
                   xs_d.rearrange("(h p) t -> p h t", p=128)[:, 8 * g:8 * (g + 1), c0:c0 + CHUNK],
                   kxsg, [f"xs_d_{o}" for o in range(8 * g, 8 * (g + 1))])
            for st in range(2):
                pt, ptkey = next_pt()
                for hh in range(8):
                    em.mm(lambda e, pt=pt, hh=hh, st=st, xsgp=xsgp: e.transpose(
                        pt[:, hh * 128:(hh + 1) * 128],
                        xsgp[:, hh * CHUNK + st * 128: hh * CHUNK + (st + 1) * 128], eye_b[:]),
                        ptkey, [kxsg, "eye_b"], last=(hh == 7))
                em.op("v", lambda e, st=st, g=g, pt=pt, xsTp=xsTp: e.tensor_copy(
                    xsTp[st][:, g * 1024:(g + 1) * 1024], pt[:]),
                    reads=[ptkey], writes=[f"xsT{st}"])
        for st in range(2):
            for h in range(HEADS):
                em.op("v", lambda e, st=st, h=h, xdXp=xdXp, xsTp=xsTp, ddTT=ddTT: e.tensor_scalar(
                    xdXp[st][:, h * 128:(h + 1) * 128], xsTp[st][:, h * 128:(h + 1) * 128],
                    ddTT[st][:, h:h + 1], None, ALU.mult),
                    reads=[f"xsT{st}", f"ddT{st}_{par}"], writes=[f"xdX{st}"])
        pt, ptkey = next_pt()
        for st in range(2):
            em.mm(lambda e, pt=pt, st=st, c0=c0: e.transpose(pt[:, st * 128:(st + 1) * 128],
                  B_fm[:, c0 + st * 128:c0 + (st + 1) * 128], eye_b[:]),
                  ptkey, ["B_fm", "eye_b"], last=(st == 1))
        em.op("v", lambda e, pt=pt, btTp=btTp: e.tensor_copy(btTp[:], pt[:, 0:256]),
              reads=[ptkey], writes=[kbtT])
        for g4 in range(8):
            pp, ppk = next_p()
            for q in range(4):
                h = g4 * 4 + q
                for st in range(2):
                    em.mm(lambda e, pp=pp, q=q, h=h, st=st, btTp=btTp, xdXp=xdXp: e.matmul(
                        pp[:, q * 128:(q + 1) * 128], btTp[:, st * 128:(st + 1) * 128],
                        xdXp[st][:, h * 128:(h + 1) * 128],
                        start=(q == 0 and st == 0), stop=(q == 3 and st == 1)),
                        ppk, [kbtT, f"xdX{st}"], last=(q == 3 and st == 1))
            em.op("v", lambda e, pp=pp, scsbp=scsbp: e.tensor_copy(scsbp[:], pp[:]),
                  reads=[ppk], writes=[kscsb])
            em.dma("sync", sstates_d[c * 128:(c + 1) * 128, g4 * 512:(g4 + 1) * 512],
                   scsbp[:], f"sst_{c}_{g4}", [kscsb])
            for q in range(4):
                h = g4 * 4 + q
                em.op("v", lambda e, h=h, q=q, t=state_run, dcl=dcl, scsbp=scsbp: e.scalar_tensor_tensor(
                    t[:, h * 128:(h + 1) * 128], t[:, h * 128:(h + 1) * 128],
                    dcl[:, h:h + 1], scsbp[:, q * 128:(q + 1) * 128], ALU.mult, ALU.add),
                    reads=["state_run", kdcl, kscsb], writes=["state_run"])
        for st in range(2):
            for h in range(HEADS):
                em.op("v", lambda e, st=st, h=h, xdXp=xdXp, xsTp=xsTp, dtT=dtT: e.tensor_scalar(
                    xdXp[st][:, h * 128:(h + 1) * 128], xsTp[st][:, h * 128:(h + 1) * 128],
                    dtT[st][:, h:h + 1], None, ALU.mult),
                    reads=[f"xsT{st}", f"dtvT{st}_{par}"], writes=[f"xdX{st}"])
        for lt in range(2):
            em.op("v", lambda e, lt=lt, ysbp=ysbp, xsTp=xsTp: e.tensor_tensor(ysbp[lt][:], xsTp[lt][:], drep_sb[:], ALU.mult),
                  reads=[f"xsT{lt}", "drep_sb"], writes=[f"ysb{lt}"])
            for g4 in range(8):
                pp, ppk = next_p()
                for q in range(4):
                    h = g4 * 4 + q
                    for st in range(2):
                        em.mm(lambda e, pp=pp, q=q, h=h, st=st, lt=lt, mb=mb, xdXp=xdXp: e.matmul(
                            pp[:, q * 128:(q + 1) * 128],
                            mb[st][:, h * CHUNK + lt * 128: h * CHUNK + lt * 128 + 128],
                            xdXp[st][:, h * 128:(h + 1) * 128],
                            start=(q == 0 and st == 0), stop=(q == 3 and st == 1)),
                            ppk, [f"mbuf{st}", f"xdX{st}"], last=(q == 3 and st == 1))
                em.op("v", lambda e, pp=pp, lt=lt, g4=g4, ysbp=ysbp: e.tensor_tensor(
                    ysbp[lt][:, g4 * 512:(g4 + 1) * 512], ysbp[lt][:, g4 * 512:(g4 + 1) * 512],
                    pp[:], ALU.add), reads=[ppk, f"ysb{lt}"], writes=[f"ysb{lt}"])
            em.dma("sync", ypart_d[c0 + lt * 128:c0 + (lt + 1) * 128, :], ysbp[lt][:],
                   f"ypart_{c}_{lt}", [f"ysb{lt}"])

    for c in range(NCH):
        a2_chunk(c)

    em.dma("sync", cc_in[0:128, :], state_run[:], "cc_in", ["state_run"])
    em.dma("sync", cc_in[128:129, 0:32].rearrange("x h -> h x"), ltot[:], "ltot_cc", ["ltot"])

    if upto == "A2":
        _final_wait(nc, em)
        return nc
    pop_scope()
    pop_scope()

    em.collective(lambda e: e.collective_compute(
        "AllGather", ALU.bypass, replica_groups=RG,
        ins=[cc_in.opt()], outs=[cc_out.opt()]), ["cc_in", "ltot_cc"], "cc_out")

    state_b = sbp("state_run_b", [128, HEADS * HD])
    push_scope()
    ltg = sbp("ltg", [32, 16])
    ltg2 = sbp("ltg2", [32, 16])
    cmk_sb = sbp("cmk_sb", [32, E])
    ngk_sb = sbp("ngk_sb", [32, E])
    ohk_sb = sbp("ohk_sb", [32, E])
    t8 = sbp("t8", [32, E])
    cum1 = sbp("cum1", [32, 1])
    wmat = sbp("wmat", [32, 8])
    wcols = sbp("wcols", [128, 256])
    Sg = [sbp(f"Sg{i}", [128, HEADS * HD]) for i in range(2)]
    Inp = [sbp(f"Inp{i}", [128, HEADS * HD]) for i in range(2)]
    em.dma("sync", cmk_sb[:], cmaskB, "cmk_sb")
    em.dma("sync", ngk_sb[:], negB, "ngk_sb")
    em.dma("sync", ohk_sb[:], onehotB, "ohk_sb")
    em.op("v", lambda e: e.memset(ltg[:], 0.0), writes=["ltg"])
    em.op("v", lambda e: e.memset(ltg2[:], 0.0), writes=["ltg2"])
    em.dma("sync", ltg[:, 8:16].rearrange("h (x j) -> h x j", x=1),
           cc_out.rearrange("(j r) n -> n r j", r=129)[0:32, 128:129, :],
           "ltg", ["cc_out"], allow_slow_non_contiguous=True)
    src_, dst_, sk, dk = ltg, ltg2, "ltg", "ltg2"
    for s in (1, 2, 4):
        em.op("v", lambda e, src_=src_, dst_=dst_, s=s: e.tensor_tensor(
            dst_[:, 8:16], src_[:, 8:16], src_[:, 8 - s:16 - s], ALU.add),
            reads=[sk], writes=[dk])
        src_, dst_, sk, dk = dst_, src_, dk, sk
    cum, cumk = src_, sk
    em.op("v", lambda e, cum=cum: e.tensor_tensor(t8[:], cum[:, 8:16], ohk_sb[:], ALU.mult),
          reads=[cumk, "ohk_sb"], writes=["t8"])
    em.op("v", lambda e: e.tensor_reduce(cum1[:], t8[:], AX.X, ALU.add),
          reads=["t8"], writes=["cum1"])
    em.op("v", lambda e, cum=cum: e.tensor_scalar(wmat[:], cum[:, 8:16], cum1[:], -1.0,
          ALU.subtract, ALU.mult), reads=[cumk, "cum1"], writes=["wmat"])
    em.op("v", lambda e: e.tensor_tensor(wmat[:], wmat[:], cmk_sb[:], ALU.mult),
          reads=["wmat", "cmk_sb"], writes=["wmat"])
    em.op("v", lambda e: e.tensor_tensor(wmat[:], wmat[:], ngk_sb[:], ALU.add),
          reads=["wmat", "ngk_sb"], writes=["wmat"])
    em.op("s", lambda e: e.activation(wmat[:], wmat[:], AF.Exp, bias=0.0, scale=1.0),
          reads=["wmat"], writes=["wmat"])
    em.dma("sync", wflat_d.rearrange("x (j h) -> h x j", h=32), wmat.ap().rearrange("h (x j) -> h x j", x=1),
           "wflat_d", ["wmat"], allow_slow_non_contiguous=True)
    em.dma("sync", wcols[:], bcast_part(wflat_d, 128), "wcols", ["wflat_d"])
    for j in range(NCORES):
        sj = j % 2
        em.dma("sync", Sg[sj][:], cc_out[j * 129:j * 129 + 128, :], f"Sg{sj}", ["cc_out"])
        cur, prv = f"Inp{j % 2}", f"Inp{(j + 1) % 2}"
        curt, prvt = Inp[j % 2], Inp[(j + 1) % 2]
        for h in range(HEADS):
            if j == 0:
                em.op("v", lambda e, h=h, curt=curt, sj=sj: e.tensor_scalar(
                    curt[:, h * 128:(h + 1) * 128], Sg[sj][:, h * 128:(h + 1) * 128],
                    wcols[:, h:h + 1], None, ALU.mult),
                    reads=[f"Sg{sj}", "wcols"], writes=[cur])
            else:
                em.op("v", lambda e, h=h, j=j, curt=curt, prvt=prvt, sj=sj: e.scalar_tensor_tensor(
                    curt[:, h * 128:(h + 1) * 128], Sg[sj][:, h * 128:(h + 1) * 128],
                    wcols[:, j * 32 + h:j * 32 + h + 1], prvt[:, h * 128:(h + 1) * 128],
                    ALU.mult, ALU.add),
                    reads=[f"Sg{sj}", "wcols", prv], writes=[cur])
    em.op("v", lambda e, curt=Inp[(NCORES - 1) % 2]: e.tensor_copy(state_b[:], curt[:]),
          reads=[f"Inp{(NCORES - 1) % 2}"], writes=["state_run_b"])
    pop_scope()

    if upto == "IN":
        em.dma("sync", dbg_d[0:128, 0:HEADS * HD], state_b[:], "dbg_d", ["state_run_b"])
        _final_wait(nc, em)
        return nc

    push_scope()
    rbB = sbp("rbB", [128, 8 * CHUNK])
    ebg = sbp("ebg", [128, 8 * CHUNK], BF16)
    cpb = sbp("cpb", [128, HEADS * CHUNK], BF16)
    scast = sbp("scast", [128, HEADS * HD], BF16)
    dcolsB = sbp("dcolsB", [128, 32])
    scld = sbp("scld", [128, HEADS * HD])
    yp = [sbp(f"yp{i}", [128, INTER]) for i in range(2)]
    gload = [sbp(f"gload{i}", [128, INTER], BF16) for i in range(2)]
    sumsq = sbp("sumsq", [128, 1])
    rstdB = sbp("rstdB", [128, 4])
    yfmg = sbp("yfmg", [128, 32 * 512], BF16)
    attn_sb = sbp("attn_sb", [128, 4 * D])
    wop = [sbp(f"wop{i}", [128, 512], BF16) for i in range(6)]
    bnb2 = sbp("bnb2", [128, 24])
    mv2 = sbp("mv2", [128, 2])
    c1 = sbp("c1", [128, 1])
    c2 = sbp("c2", [128, 1])
    c3 = sbp("c3", [128, 1])
    xres = sbp("xres", [128, D])
    x1o = sbp("x1o", [128, D])
    ybf = scast

    def b_chunk(c):
        c0 = c * CHUNK
        ltg4 = (c % 2) * 2
        for g in range(4):
            em.dma("sync", rbB.ap().rearrange("p (h l) -> p h l", h=8),
                   bcast_prepend(acum_d[8 * g:8 * (g + 1), c0:c0 + CHUNK], 128), "rbB", [f"acum_d_{c}"])
            em.op("s", lambda e: e.activation(ebg[:], rbB[:], AF.Exp, bias=0.0, scale=1.0),
                  reads=["rbB"], writes=["ebg"])
            em.op("v", lambda e, g=g, c0=c0: e.tensor_tensor(
                cpb.ap().rearrange("p (h l) -> p h l", h=HEADS)[:, 8 * g:8 * (g + 1), :],
                rep_free(C_fm[:, c0:c0 + CHUNK], 8),
                ebg.ap().rearrange("p (h l) -> p h l", h=8), ALU.mult),
                reads=["ebg", "C_fm"], writes=["cpb"])
        em.op("v", lambda e: e.tensor_copy(scast[:], state_b[:]),
              reads=["state_run_b"], writes=["scast"])
        em.dma("sync", dcolsB[:], bcast_part(dvec_d[c:c + 1, :], 128), "dcolsB", [f"dvec_d_{c}"])
        em.dma("sync", scld[:], sstates_d[c * 128:(c + 1) * 128, :], "scld", [f"sst_{c}_{g4}" for g4 in range(8)])
        for h in range(HEADS):
            em.op("v", lambda e, h=h: e.scalar_tensor_tensor(
                state_b[:, h * 128:(h + 1) * 128], state_b[:, h * 128:(h + 1) * 128],
                dcolsB[:, h:h + 1], scld[:, h * 128:(h + 1) * 128], ALU.mult, ALU.add),
                reads=["state_run_b", "dcolsB", "scld"], writes=["state_run_b"])
        for lt in range(2):
            ypt, kyp = yp[lt], f"yp{lt}"
            glt, kgl = gload[lt], f"gload{lt}"
            em.dma("sync", ypt[:], ypart_d[c0 + lt * 128:c0 + (lt + 1) * 128, :], kyp, [f"ypart_{c}_{lt}"])
            for g4 in range(8):
                pp, ppk = next_p()
                for q in range(4):
                    h = g4 * 4 + q
                    em.mm(lambda e, pp=pp, q=q, h=h, lt=lt: e.matmul(
                        pp[:, q * 128:(q + 1) * 128],
                        cpb[:, h * CHUNK + lt * 128: h * CHUNK + lt * 128 + 128],
                        scast[:, h * 128:(h + 1) * 128],
                        start=(q == 0), stop=(q == 3)),
                        ppk, ["cpb", "scast"], last=(q == 3))
                em.op("v", lambda e, pp=pp, g4=g4, ypt=ypt: e.tensor_tensor(
                    ypt[:, g4 * 512:(g4 + 1) * 512], ypt[:, g4 * 512:(g4 + 1) * 512], pp[:],
                    ALU.add), reads=[ppk, kyp], writes=[kyp])
            em.dma("sync", glt[:], gate_s_d[c0 + lt * 128:c0 + (lt + 1) * 128, :], kgl,
                   [f"gates_{c * 2 + lt}_{p}" for p in range(8)])
            em.op("v", lambda e, ypt=ypt, glt=glt: e.tensor_tensor(ypt[:], ypt[:], glt[:], ALU.mult),
                  reads=[kyp, kgl], writes=[kyp])
            em.op("s", lambda e, ypt=ypt: e.activation(scld[:, 0:2048], ypt[:, 0:2048], AF.Square,
                  bias=0.0, scale=1.0, accum_out=sumsq[:]),
                  reads=[kyp], writes=["scld", "sumsq"])
            em.op("s", lambda e, ypt=ypt: e.activation(scld[:, 0:2048], ypt[:, 2048:4096], AF.Square,
                  bias=0.0, scale=1.0, accum_out=c1[:]),
                  reads=[kyp, "scld"], writes=["scld", "c1"])
            em.op("v", lambda e: e.tensor_tensor(sumsq[:], sumsq[:], c1[:], ALU.add),
                  reads=["sumsq", "c1"], writes=["sumsq"])
            em.op("v", lambda e: e.tensor_scalar(sumsq[:], sumsq[:], 1.0 / INTER, EPS,
                  ALU.mult, ALU.add), reads=["sumsq"], writes=["sumsq"])
            em.op("s", lambda e: e.activation(sumsq[:], sumsq[:], AF.Ln, bias=0.0, scale=1.0),
                  reads=["sumsq"], writes=["sumsq"])
            em.op("s", lambda e, lt=lt, ltg4=ltg4: e.activation(rstdB[:, ltg4 + lt:ltg4 + lt + 1],
                  sumsq[:], AF.Exp, bias=0.0, scale=-0.5),
                  reads=["sumsq"], writes=["rstdB"])
            em.op("v", lambda e, ypt=ypt: e.tensor_copy(ybf[:], ypt[:]), reads=[kyp, "scast"], writes=["scast"])
            for fh in range(4):
                pt, ptkey = next_pt()
                for ff in range(8):
                    f = fh * 8 + ff
                    em.mm(lambda e, pt=pt, ff=ff, f=f: e.transpose(
                        pt[:, ff * 128:(ff + 1) * 128], ybf[:, f * 128:(f + 1) * 128], eye_b[:]),
                        ptkey, ["scast", "eye_b"], last=(ff == 7))
                em.op("v", lambda e, pt=pt, fh=fh, lt=lt, ltg4=ltg4: e.tensor_copy(
                    yfmg.ap().rearrange("p (f l) -> p f l", l=512)[:, fh * 8:(fh + 1) * 8,
                                                                  (ltg4 + lt) * 128:(ltg4 + lt + 1) * 128],
                    pt.ap().rearrange("p (f l) -> p f l", f=8)),
                    reads=[ptkey], writes=["yfmg"])
        if c % 2 == 1:
            gtok0 = (c - 1) * CHUNK
            for dpiece in range(4):
                for f in range(32):
                    jw = f % 6
                    em.dma("sync", wop[jw][:], opWTb_d[f * 128:(f + 1) * 128,
                           dpiece * 512:(dpiece + 1) * 512], f"wop{jw}", [f"opwtb_{f}"])
                    for lt4 in range(4):
                        em.mm(lambda e, lt4=lt4, f=f, jw=jw, dpiece=dpiece: e.matmul(
                            P[lt4][:], yfmg[:, f * 512 + lt4 * 128: f * 512 + (lt4 + 1) * 128],
                            wop[jw][:], start=(f == 0), stop=(f == 31)),
                            pk[lt4], ["yfmg", f"wop{jw}"], last=(lt4 == 3))
                for lt4 in range(4):
                    em.op("v", lambda e, lt4=lt4, dpiece=dpiece: e.tensor_copy(
                        attn_sb[:, lt4 * D + dpiece * 512: lt4 * D + (dpiece + 1) * 512],
                        P[lt4][:]), reads=[pk[lt4]], writes=["attn_sb"])
            for lt4 in range(4):
                for s4 in range(4):
                    em.op("v", lambda e, lt4=lt4, s4=s4: e.bn_stats(
                        bnb2[:, s4 * 6:(s4 + 1) * 6],
                        attn_sb[:, lt4 * D + s4 * 512: lt4 * D + (s4 + 1) * 512]),
                        reads=["attn_sb"], writes=["bnb2"])
                em.op("v", lambda e: e.bn_aggr(mv2[:], bnb2[:]), reads=["bnb2"], writes=["mv2"])
                em.op("v", lambda e, lt4=lt4: e.tensor_tensor(c1[:], rstdB[:, lt4:lt4 + 1],
                      rstdB[:, lt4:lt4 + 1], ALU.mult), reads=["rstdB"], writes=["c1"])
                em.op("v", lambda e: e.tensor_scalar(c1[:], c1[:], mv2[:, 1:2], EPS,
                      ALU.mult, ALU.add), reads=["c1", "mv2"], writes=["c1"])
                em.op("s", lambda e: e.activation(c1[:], c1[:], AF.Ln, bias=0.0, scale=1.0),
                      reads=["c1"], writes=["c1"])
                em.op("s", lambda e: e.activation(c1[:], c1[:], AF.Exp, bias=0.0, scale=-0.5),
                      reads=["c1"], writes=["c1"])
                em.op("v", lambda e, lt4=lt4: e.tensor_tensor(c2[:], c1[:], rstdB[:, lt4:lt4 + 1],
                      ALU.mult), reads=["c1", "rstdB"], writes=["c2"])
                em.op("v", lambda e: e.tensor_scalar(c3[:], mv2[:, 0:1], c2[:], -1.0,
                      ALU.mult, ALU.mult), reads=["mv2", "c2"], writes=["c3"])
                tok0 = gtok0 + lt4 * 128
                em.dma("sync", xres[:], x_in[PAD + tok0:PAD + tok0 + 128, :], "xres")
                em.op("v", lambda e: e.tensor_scalar(xres[:], xres[:], c3[:], None, ALU.add),
                      reads=["xres", "c3"], writes=["xres"])
                em.op("v", lambda e, lt4=lt4: e.scalar_tensor_tensor(x1o[:],
                      attn_sb[:, lt4 * D:(lt4 + 1) * D], c2[:], xres[:], ALU.mult, ALU.add),
                      reads=["attn_sb", "c2", "xres"], writes=["x1o"])
                em.dma("sync", x1_d[tok0:tok0 + 128, :], x1o[:], f"x1_{tok0 // 128}", ["x1o"])
    for c in range(NCH):
        b_chunk(c)

    pop_scope()

    if upto == "B":
        _final_wait(nc, em)
        return nc
    pop_scope()

    push_scope()
    xt_fm = sbp("xt_fm", [128, NKD * TC], BF16)
    h_sb = sbp("h_sb", [128, NKI * TC], BF16)
    w1sb = [sbp(f"w1sb{i}", [128, NKD * 128], BF16) for i in range(2)]
    w3sb = [sbp(f"w3sb{i}", [128, NKD * 128], BF16) for i in range(2)]
    w2sb = [sbp(f"w2sb{i}", [128, NKI * 512], BF16) for i in range(2)]
    x1t = [sbp(f"x1t{i}", [128, D]) for i in range(2)]
    bnb3 = [sbp(f"bnb3{i}", [128, 24]) for i in range(2)]
    mv3 = [sbp(f"mv3{i}", [128, 2]) for i in range(2)]
    r3 = [sbp(f"r3{i}", [128, 1]) for i in range(2)]
    n3 = [sbp(f"n3{i}", [128, 1]) for i in range(2)]
    xth = [sbp(f"xth{i}", [128, D], BF16) for i in range(2)]
    xtl = [sbp(f"xtl{i}", [128, D], BF16) for i in range(2)]
    lofm = sbp("lofm", [128, NKD * 128], BF16)
    gw_sb = sbp("gw_sb", [128, NKD * E], BF16)
    gb_bc = sbp("gb_bc", [128, E])
    id_bc = sbp("id_bc", [128, E])
    lg = sbp("lg", [128, E])
    rcol = sbp("rcol", [128, 1])
    r2col = sbp("r2col", [128, 1])
    e8a = sbp("e8a", [128, E])
    e8b = sbp("e8b", [128, E])
    dw_sb = sbp("dw_sb", [128, 16 * E])
    stg = [sbp(f"stg{i}", [128, 512]) for i in range(4)]
    stgb = [sbp(f"stgb{i}", [128, 512]) for i in range(4)]
    outx = [sbp(f"outx{i}", [128, D]) for i in range(2)]

    em.dma("gp", gw_sb.ap().rearrange("p (k c) -> p k c", k=NKD),
           gateWT.rearrange("(k p) c -> p k c", p=128), "gw_sb")
    em.dma("sync", gb_bc[:], bcast_part(gatebB, 128), "gb_bc")
    em.dma("sync", id_bc[:], bcast_part(idxdB, 128), "id_bc")

    for tt in range(16):
        j = tt % 2
        kx, kb, km, kr, kn = f"x1t{j}", f"bnb3{j}", f"mv3{j}", f"r3{j}", f"n3{j}"
        em.dma("sync", x1t[j][:], x1_d[tt * 128:(tt + 1) * 128, :], kx, [f"x1_{tt}"])
        for s4 in range(4):
            em.op("v", lambda e, j=j, s4=s4: e.bn_stats(bnb3[j][:, s4 * 6:(s4 + 1) * 6],
                  x1t[j][:, s4 * 512:(s4 + 1) * 512]), reads=[kx], writes=[kb])
        em.op("v", lambda e, j=j: e.bn_aggr(mv3[j][:], bnb3[j][:]), reads=[kb], writes=[km])
        em.op("v", lambda e, j=j: e.tensor_scalar(r3[j][:], mv3[j][:, 1:2], EPS, None, ALU.add),
              reads=[km], writes=[kr])
        em.op("s", lambda e, j=j: e.activation(r3[j][:], r3[j][:], AF.Ln, bias=0.0, scale=1.0),
              reads=[kr], writes=[kr])
        em.op("s", lambda e, j=j: e.activation(r3[j][:], r3[j][:], AF.Exp, bias=0.0, scale=-0.5),
              reads=[kr], writes=[kr])
        em.op("v", lambda e, j=j: e.tensor_scalar(n3[j][:], r3[j][:], mv3[j][:, 0:1], -1.0,
              ALU.mult, ALU.mult), reads=[kr, km], writes=[kn])
        em.op("v", lambda e, j=j: e.tensor_scalar(xth[j][:], x1t[j][:], r3[j][:], n3[j][:],
              ALU.mult, ALU.add), reads=[kx, kr, kn], writes=[f"xth{j}"])
        em.op("v", lambda e, j=j: e.tensor_scalar(x1t[j][:], x1t[j][:], r3[j][:], n3[j][:],
              ALU.mult, ALU.add), reads=[kx, kr, kn], writes=[kx])
        em.op("v", lambda e, j=j: e.tensor_tensor(xtl[j][:], x1t[j][:], xth[j][:], ALU.subtract),
              reads=[kx, f"xth{j}"], writes=[f"xtl{j}"])
        for half in range(2):
            pt, ptkey = next_pt()
            for kk in range(8):
                k = half * 8 + kk
                em.mm(lambda e, j=j, k=k, kk=kk, pt=pt: e.transpose(
                    pt[:, kk * 128:(kk + 1) * 128], xth[j][:, k * 128:(k + 1) * 128], eye_b[:]),
                    ptkey, [f"xth{j}", "eye_b"], last=(kk == 7))
            em.op("v", lambda e, tt=tt, half=half, pt=pt: e.tensor_copy(
                xt_fm.ap().rearrange("p (k t) -> p k t", k=NKD)[:, half * 8:(half + 1) * 8,
                                                               tt * 128:(tt + 1) * 128],
                pt.ap().rearrange("p (k t) -> p k t", k=8)),
                reads=[ptkey], writes=["xt_fm"])
        for half in range(2):
            pt, ptkey = next_pt()
            for kk in range(8):
                k = half * 8 + kk
                em.mm(lambda e, j=j, k=k, kk=kk, pt=pt: e.transpose(
                    pt[:, kk * 128:(kk + 1) * 128], xtl[j][:, k * 128:(k + 1) * 128], eye_b[:]),
                    ptkey, [f"xtl{j}", "eye_b"], last=(kk == 7))
            em.op("v", lambda e, half=half, pt=pt: e.tensor_copy(
                lofm.ap().rearrange("p (k t) -> p k t", k=NKD)[:, half * 8:(half + 1) * 8, :],
                pt.ap().rearrange("p (k t) -> p k t", k=8)),
                reads=[ptkey], writes=["lofm"])
        pp, ppk = next_p()
        for k in range(NKD):
            em.mm(lambda e, pp=pp, k=k, tt=tt: e.matmul(pp[:, 0:E],
                  xt_fm[:, k * TC + tt * 128: k * TC + (tt + 1) * 128],
                  gw_sb[:, k * E:(k + 1) * E], start=(k == 0), stop=False),
                  ppk, ["xt_fm", "gw_sb"], last=False)
        for k in range(NKD):
            em.mm(lambda e, pp=pp, k=k: e.matmul(pp[:, 0:E],
                  lofm[:, k * 128:(k + 1) * 128],
                  gw_sb[:, k * E:(k + 1) * E], start=False, stop=(k == NKD - 1)),
                  ppk, ["lofm", "gw_sb"], last=(k == NKD - 1))
        em.op("v", lambda e, pp=pp: e.tensor_copy(lg[:], pp[:, 0:E]), reads=[ppk], writes=["lg"])
        em.op("v", lambda e: e.tensor_reduce(rcol[:], lg[:], AX.X, ALU.max),
              reads=["lg"], writes=["rcol"])
        em.op("v", lambda e: e.tensor_scalar(rcol[:], rcol[:], -1.0, None, ALU.mult),
              reads=["rcol"], writes=["rcol"])
        em.op("s", lambda e: e.activation(lg[:], lg[:], AF.Exp, bias=rcol[:], scale=1.0),
              reads=["lg", "rcol"], writes=["lg"])
        em.op("v", lambda e: e.tensor_reduce(rcol[:], lg[:], AX.X, ALU.add),
              reads=["lg"], writes=["rcol"])
        em.op("v", lambda e: e.reciprocal(rcol[:], rcol[:]), reads=["rcol"], writes=["rcol"])
        em.op("v", lambda e: e.tensor_scalar(lg[:], lg[:], rcol[:], None, ALU.mult),
              reads=["lg", "rcol"], writes=["lg"])
        em.op("v", lambda e: e.tensor_tensor(e8a[:], lg[:], gb_bc[:], ALU.add),
              reads=["lg", "gb_bc"], writes=["e8a"])
        em.op("v", lambda e: e.tensor_reduce(rcol[:], e8a[:], AX.X, ALU.max),
              reads=["e8a"], writes=["rcol"])
        em.op("v", lambda e: e.tensor_scalar(e8b[:], e8a[:], rcol[:], None, ALU.is_equal),
              reads=["e8a", "rcol"], writes=["e8b"])
        em.op("v", lambda e: e.tensor_tensor(e8b[:], e8b[:], id_bc[:], ALU.mult),
              reads=["e8b", "id_bc"], writes=["e8b"])
        em.op("v", lambda e: e.tensor_reduce(r2col[:], e8b[:], AX.X, ALU.max),
              reads=["e8b"], writes=["r2col"])
        em.op("v", lambda e: e.tensor_scalar(e8b[:], e8b[:], r2col[:], None, ALU.is_equal),
              reads=["e8b", "r2col"], writes=["e8b"])
        em.op("v", lambda e: e.scalar_tensor_tensor(e8a[:], e8b[:], -1e9, e8a[:],
              ALU.mult, ALU.add), reads=["e8b", "e8a"], writes=["e8a"])
        em.op("v", lambda e: e.tensor_reduce(rcol[:], e8a[:], AX.X, ALU.max),
              reads=["e8a"], writes=["rcol"])
        em.op("v", lambda e: e.tensor_scalar(e8a[:], e8a[:], rcol[:], None, ALU.is_equal),
              reads=["e8a", "rcol"], writes=["e8a"])
        em.op("v", lambda e: e.tensor_tensor(e8a[:], e8a[:], id_bc[:], ALU.mult),
              reads=["e8a", "id_bc"], writes=["e8a"])
        em.op("v", lambda e: e.tensor_reduce(r2col[:], e8a[:], AX.X, ALU.max),
              reads=["e8a"], writes=["r2col"])
        em.op("v", lambda e: e.tensor_scalar(e8a[:], e8a[:], r2col[:], None, ALU.is_equal),
              reads=["e8a", "r2col"], writes=["e8a"])
        em.op("v", lambda e: e.tensor_tensor(e8a[:], e8a[:], e8b[:], ALU.add),
              reads=["e8a", "e8b"], writes=["e8a"])
        em.op("v", lambda e, tt=tt: e.tensor_tensor(dw_sb[:, tt * E:(tt + 1) * E], lg[:],
              e8a[:], ALU.mult), reads=["lg", "e8a"], writes=["dw_sb"])

    for ei in range(E + 1):
        shared = (ei == 0)
        e_idx = ei - 1
        for ik in range(NKI):
            jj = ik % 2
            if shared:
                w1src = ws1t[:, ik * 128:(ik + 1) * 128]
                w3src = ws3t[:, ik * 128:(ik + 1) * 128]
            else:
                w1src = w1t[e_idx * D:(e_idx + 1) * D, ik * 128:(ik + 1) * 128]
                w3src = w3t[e_idx * D:(e_idx + 1) * D, ik * 128:(ik + 1) * 128]
            em.dma("gp", w1sb[jj].ap().rearrange("p (k c) -> p k c", k=NKD),
                   w1src.rearrange("(k p) c -> p k c", p=128), f"w1sb{jj}")
            em.dma("gp", w3sb[jj].ap().rearrange("p (k c) -> p k c", k=NKD),
                   w3src.rearrange("(k p) c -> p k c", p=128), f"w3sb{jj}")
            for piece in range(4):
                pa, pak = next_p()
                pb, pbk = next_p()
                for k in range(NKD):
                    em.mm(lambda e, pa=pa, k=k, jj=jj, piece=piece: e.matmul(
                        pa[:], w1sb[jj][:, k * 128:(k + 1) * 128],
                        xt_fm[:, k * TC + piece * 512: k * TC + (piece + 1) * 512],
                        start=(k == 0), stop=(k == NKD - 1)),
                        pak, [f"w1sb{jj}", "xt_fm"], last=(k == NKD - 1))
                for k in range(NKD):
                    em.mm(lambda e, pb=pb, k=k, jj=jj, piece=piece: e.matmul(
                        pb[:], w3sb[jj][:, k * 128:(k + 1) * 128],
                        xt_fm[:, k * TC + piece * 512: k * TC + (piece + 1) * 512],
                        start=(k == 0), stop=(k == NKD - 1)),
                        pbk, [f"w3sb{jj}", "xt_fm"], last=(k == NKD - 1))
                sj = piece % 4
                em.op("s", lambda e, sj=sj, pa=pa: e.activation(stg[sj][:], pa[:], AF.Silu,
                      bias=0.0, scale=1.0), reads=[pak], writes=[f"stg{sj}"])
                em.op("v", lambda e, sj=sj, pb=pb, ik=ik, piece=piece: e.tensor_tensor(
                    h_sb[:, ik * TC + piece * 512: ik * TC + (piece + 1) * 512],
                    stg[sj][:], pb[:], ALU.mult),
                    reads=[f"stg{sj}", pbk], writes=["h_sb"])
        for dpiece in range(4):
            jj = dpiece % 2
            w2src = (ws2t if shared else w2t[e_idx * I_FF:(e_idx + 1) * I_FF, :])
            em.dma("gp", w2sb[jj].ap().rearrange("p (k c) -> p k c", k=NKI),
                   w2src[:, dpiece * 512:(dpiece + 1) * 512].rearrange("(k p) c -> p k c", p=128),
                   f"w2sb{jj}")
            for tt in range(16):
                pp, ppk = next_p()
                for ik in range(NKI):
                    em.mm(lambda e, pp=pp, ik=ik, jj=jj, tt=tt: e.matmul(
                        pp[:], h_sb[:, ik * TC + tt * 128: ik * TC + (tt + 1) * 128],
                        w2sb[jj][:, ik * 512:(ik + 1) * 512],
                        start=(ik == 0), stop=(ik == NKI - 1)),
                        ppk, ["h_sb", f"w2sb{jj}"], last=(ik == NKI - 1))
                sj = tt % 4
                if shared:
                    em.op("v", lambda e, sj=sj, pp=pp: e.tensor_copy(stgb[sj][:], pp[:]),
                          reads=[ppk], writes=[f"stgb{sj}"])
                    em.dma("sync", y_d[tt * 128:(tt + 1) * 128, dpiece * 512:(dpiece + 1) * 512],
                           stgb[sj][:], f"y_d_{tt}_{dpiece}", [f"stgb{sj}"])
                else:
                    em.op("v", lambda e, sj=sj, pp=pp, e_idx=e_idx, tt=tt: e.tensor_scalar(
                        stgb[sj][:], pp[:], dw_sb[:, tt * E + e_idx: tt * E + e_idx + 1], None,
                        ALU.mult), reads=[ppk, "dw_sb"], writes=[f"stgb{sj}"])
                    em.dma("gp", y_d[tt * 128:(tt + 1) * 128, dpiece * 512:(dpiece + 1) * 512],
                           stgb[sj][:], f"y_d_{tt}_{dpiece}", [f"stgb{sj}"], accum_op=ALU.add)

    for tt in range(16):
        j = tt % 2
        kx, kb, km, kr, kn = f"x1t{j}", f"bnb3{j}", f"mv3{j}", f"r3{j}", f"n3{j}"
        em.dma("sync", outx[j][:], y_d[tt * 128:(tt + 1) * 128, :], f"outx{j}", [f"y_d_{tt}_{dp}" for dp in range(4)])
        for s4 in range(4):
            em.op("v", lambda e, j=j, s4=s4: e.bn_stats(bnb3[j][:, s4 * 6:(s4 + 1) * 6],
                  outx[j][:, s4 * 512:(s4 + 1) * 512]), reads=[f"outx{j}"], writes=[kb])
        em.op("v", lambda e, j=j: e.bn_aggr(mv3[j][:], bnb3[j][:]), reads=[kb], writes=[km])
        em.op("v", lambda e, j=j: e.tensor_scalar(r3[j][:], mv3[j][:, 1:2], EPS, None, ALU.add),
              reads=[km], writes=[kr])
        em.op("s", lambda e, j=j: e.activation(r3[j][:], r3[j][:], AF.Ln, bias=0.0, scale=1.0),
              reads=[kr], writes=[kr])
        em.op("s", lambda e, j=j: e.activation(r3[j][:], r3[j][:], AF.Exp, bias=0.0, scale=-0.5),
              reads=[kr], writes=[kr])
        em.op("v", lambda e, j=j: e.tensor_scalar(n3[j][:], r3[j][:], mv3[j][:, 0:1], -1.0,
              ALU.mult, ALU.mult), reads=[kr, km], writes=[kn])
        em.dma("sync", x1t[j][:], x1_d[tt * 128:(tt + 1) * 128, :], kx, [f"x1_{tt}"])
        em.op("v", lambda e, j=j: e.tensor_scalar(x1t[j][:], x1t[j][:], n3[j][:], None, ALU.add),
              reads=[kx, kn], writes=[kx])
        em.op("v", lambda e, j=j: e.scalar_tensor_tensor(outx[j][:], outx[j][:], r3[j][:],
              x1t[j][:], ALU.mult, ALU.add), reads=[f"outx{j}", kr, kx], writes=[f"outx{j}"])
        em.dma("sync", out_d[tt * 128:(tt + 1) * 128, :], outx[j][:], f"out_{tt}", [f"outx{j}"])

    pop_scope()
    _final_wait(nc, em)
    return nc


def _final_wait(nc, em):
    for q in ("sync", "gp"):
        for i, sem in enumerate(em.dma_pools[q]):
            u = em.dma_uses[q][i]
            if u:
                em._wait("sync", (sem, u * 16, f"d_{q}{i}"))
    for c in em.COMPUTE:
        if em.cnt[c]:
            em._wait("sync", (em.sem[c], em.cnt[c], f"c_{c}"))
    em.finalize()


def host_inputs(inputs):
    x = np.asarray(inputs["L0_attn_hidden_states"])[0]
    ipW = np.asarray(inputs["L0_attn_in_proj_w"])
    conv_w = np.asarray(inputs["L0_attn_conv_w"])
    conv_b = np.asarray(inputs["L0_attn_conv_b"])
    A_log = np.asarray(inputs["L0_attn_A_log"])
    dt_bias = np.asarray(inputs["L0_attn_dt_bias"])
    Dp = np.asarray(inputs["L0_attn_D"])
    norm_w = np.asarray(inputs["L0_attn_norm_w"])
    out_w = np.asarray(inputs["L0_attn_out_proj_w"])
    gate_w = np.asarray(inputs["L0_ffn_gate_w"])
    gate_b = np.asarray(inputs["L0_ffn_gate_b"])
    W1 = np.asarray(inputs["L0_ffn_W1"])
    W2 = np.asarray(inputs["L0_ffn_W2"])
    W3 = np.asarray(inputs["L0_ffn_W3"])
    Ws1 = np.asarray(inputs["L0_ffn_Ws1"])
    Ws2 = np.asarray(inputs["L0_ffn_Ws2"])
    Ws3 = np.asarray(inputs["L0_ffn_Ws3"])

    shared = {}
    shared["ipWT"] = np.ascontiguousarray(ipW.T)
    cd = np.zeros((NO_HBC, CONVK, 128, 128), np.float32)
    cw = conv_w[:, 0, :]
    ar = np.arange(128)
    for j in range(NO_HBC):
        for k in range(CONVK):
            cd[j, k, ar, ar] = cw[j * 128:(j + 1) * 128, k]
    shared["convdiag"] = cd.reshape(NO_HBC * CONVK * 128, 128)
    shared["convb"] = np.ascontiguousarray(conv_b.reshape(NO_HBC, 128))
    shared["alog"] = np.ascontiguousarray(A_log.reshape(32, 1))
    shared["dtbias"] = np.ascontiguousarray(dt_bias.reshape(32, 1))
    shared["drep"] = np.ascontiguousarray(np.repeat(Dp, HD).reshape(1, INTER))
    shared["normcols"] = np.ascontiguousarray(norm_w.reshape(32, 128).T)
    shared["opWT"] = np.ascontiguousarray(out_w.T)
    shared["gateWT"] = np.ascontiguousarray(gate_w.T)
    shared["gatebB"] = np.ascontiguousarray(gate_b.reshape(1, E))
    shared["idxdB"] = np.ascontiguousarray((E - np.arange(E, dtype=np.float32)).reshape(1, E))
    shared["w1t"] = np.ascontiguousarray(W1.transpose(0, 2, 1)).reshape(E * D, I_FF)
    shared["w3t"] = np.ascontiguousarray(W3.transpose(0, 2, 1)).reshape(E * D, I_FF)
    shared["w2t"] = np.ascontiguousarray(W2.transpose(0, 2, 1)).reshape(E * I_FF, D)
    shared["ws1t"] = np.ascontiguousarray(Ws1.T)
    shared["ws3t"] = np.ascontiguousarray(Ws3.T)
    shared["ws2t"] = np.ascontiguousarray(Ws2.T)
    m01 = np.zeros((2, 128, CHUNK), np.float32)
    for st in range(2):
        for s in range(128):
            m01[st, s, st * 128 + s:] = 1.0
    shared["mask01"] = m01.reshape(2 * 128, CHUNK)
    shared["eye"] = np.eye(128, dtype=np.float32)

    in_maps = []
    for c in range(NCORES):
        m = dict(shared)
        halo = np.zeros((3, D), np.float32) if c == 0 else x[c * TC - 3:c * TC]
        m["x"] = np.ascontiguousarray(
            np.concatenate([np.zeros((125, D), np.float32), halo, x[c * TC:(c + 1) * TC]], 0))
        cm = np.zeros((32, E), np.float32)
        nb = np.full((32, E), -30000.0, np.float32)
        oh = np.zeros((32, E), np.float32)
        cm[:, :c] = 1.0
        nb[:, :c] = 0.0
        if c > 0:
            oh[:, c - 1] = 1.0
        m["cmaskB"], m["negB"], m["onehotB"] = cm, nb, oh
        in_maps.append(m)
    return in_maps


def kernel(**inputs):
    in_maps = host_inputs(inputs)
    nc = build_nc("full")
    res = run_bass_kernel_spmd(nc, in_maps, core_ids=list(range(NCORES)))
    out = np.concatenate([r["out"] for r in res.results], axis=0)
    return out.reshape(1, SEQ, D).astype(np.float32)


# version 03
# speedup vs baseline: 1.1710x; 1.1710x over previous
import dataclasses
import numpy as np
import ml_dtypes
import concourse.bass as bass
import concourse.mybir as mybir
from concourse.bass_utils import run_bass_kernel_spmd

F32 = mybir.dt.float32
BF16 = mybir.dt.bfloat16
AF = mybir.ActivationFunctionType
ALU = mybir.AluOpType
AX = mybir.AxisListType
BF = ml_dtypes.bfloat16

NCORES = 8
SEQ = 16384
D = 2048
TC = SEQ // NCORES
PAD = 128
TF = TC + PAD
NKD = D // 128
INTER = 4096
HEADS = 32
HD = 128
STATE = 128
CONVK = 4
CONV_DIM = INTER + 2 * STATE
NPROJ = INTER + CONV_DIM + HEADS
NO_HBC = CONV_DIM // 128
NO_GATE = INTER // 128
CHUNK = 256
NCH = TC // CHUNK
EPS = 1e-5
E = 8
I_FF = 1024
NKI = I_FF // 128
RG = [[0, 1, 2, 3, 4, 5, 6, 7]]


def bcast_part(ap, n):
    new = [[0, n]] + [list(x) for x in ap.ap][1:]
    return dataclasses.replace(ap, ap=type(ap.ap)(new))


def bcast_prepend(ap, n):
    new = [[0, n]] + [list(x) for x in ap.ap]
    return dataclasses.replace(ap, ap=type(ap.ap)(new))


def rep_free(ap, n):
    aps = [list(x) for x in ap.ap]
    new = [aps[0]] + [[0, n]] + aps[1:]
    return dataclasses.replace(ap, ap=type(ap.ap)(new))


class EM:
    COMPUTE = ("v", "s", "t", "gp")
    NDMA = 6

    def __init__(self, nc):
        self.nc = nc
        self.q = {k: [] for k in ("sync", "gp", "v", "s", "t")}
        self.sem = {k: nc.alloc_semaphore(f"sem_{k}") for k in self.COMPUTE}
        self.cnt = {k: 0 for k in self.COMPUTE}
        self.dma_pools = {q: [nc.alloc_semaphore(f"dma_{q}{i}") for i in range(self.NDMA)]
                          for q in ("sync", "gp")}
        self.dma_uses = {q: [0] * self.NDMA for q in ("sync", "gp")}
        self.dma_rr = {q: 0 for q in ("sync", "gp")}
        self.waited = {k: {} for k in self.q}
        self.writer = {}
        self.readers = {}
        self.pending_t = []
        self._segs = []

    def _wait(self, queue, rec):
        if rec is None:
            return
        sem, val, name = rec
        if val <= 0:
            return
        w = self.waited[queue]
        if w.get(name, -1) >= val:
            return
        w[name] = val
        self.q[queue].append(lambda e, s=sem, v=val: e.wait_ge(s, v))

    def _note_read(self, key, rec):
        d = self.readers.setdefault(key, {})
        old = d.get(rec[2])
        if old is None or old[1] < rec[1]:
            d[rec[2]] = rec

    def _pre(self, queue, reads, writes, pe_acc_key=None):
        for k in reads:
            self._wait(queue, self.writer.get(k))
        for k in writes:
            wrec = self.writer.get(k)
            if not (k == pe_acc_key and wrec is not None and wrec[2] == "c_t"):
                self._wait(queue, wrec)
            for rec in self.readers.get(k, {}).values():
                self._wait(queue, rec)

    def _finish(self, reads, writes, rec):
        for k in reads:
            self._note_read(k, rec)
        for k in writes:
            self.writer[k] = rec
            self.readers[k] = {}

    def op(self, queue, fn, reads=(), writes=()):
        assert queue in self.COMPUTE
        reads, writes = list(reads), list(writes)
        self._pre(queue, reads, writes)
        self.cnt[queue] += 1
        sem, val = self.sem[queue], self.cnt[queue]
        self.q[queue].append(lambda e, f=fn, s=sem, v=1: f(e).then_inc(s, v))
        self._finish(reads, writes, (sem, val, f"c_{queue}"))

    def mm(self, fn, out_key, in_keys, last=True):
        q = "t"
        in_keys = [k for k in in_keys if k is not None]
        self._pre(q, in_keys, [out_key], pe_acc_key=out_key)
        if last:
            self.cnt[q] += 1
            sem, val = self.sem[q], self.cnt[q]
            self.q[q].append(lambda e, f=fn, s=sem: f(e).then_inc(s, 1))
            rec = (sem, val, "c_t")
            self.pending_t.append((in_keys, [out_key]))
            for rds, wrs in self.pending_t:
                self._finish(rds, wrs, rec)
            self.pending_t = []
        else:
            self.q[q].append(fn)
            self.pending_t.append((in_keys, [out_key]))
            self.writer[out_key] = (self.sem[q], self.cnt[q] + 1, "c_t")

    def dma(self, queue, dst_ap, src_ap, wkey, rkeys=(), **kw):
        pool, uses = self.dma_pools[queue], self.dma_uses[queue]
        i = self.dma_rr[queue]
        self.dma_rr[queue] = (i + 1) % self.NDMA
        sem = pool[i]
        name = f"d_{queue}{i}"
        self._wait(queue, (sem, uses[i] * 16, name))
        rkeys = list(rkeys)
        self._pre(queue, rkeys, [wkey] if wkey is not None else [])
        uses[i] += 1
        val = uses[i] * 16
        self.q[queue].append(
            lambda e, d=dst_ap, s=src_ap, sm=sem, k=dict(kw): e.dma_start(d, s, **k).then_inc(sm, 16))
        self._finish(rkeys, [wkey] if wkey is not None else [], (sem, val, name))

    def collective(self, fn, rkeys, wkey):
        q = "gp"
        self._pre(q, list(rkeys), [wkey])
        self.cnt[q] += 1
        sem, val = self.sem[q], self.cnt[q]
        self.q[q].append(lambda e, f=fn, s=sem: f(e).then_inc(s, 1))
        self._finish(list(rkeys), [wkey], (sem, val, f"c_{q}"))

    def barrier(self):
        for q in ("sync", "gp"):
            for i, sem in enumerate(self.dma_pools[q]):
                u = self.dma_uses[q][i]
                if u:
                    self._wait("sync", (sem, u * 16, f"d_{q}{i}"))
        self._segs.append(self.q)
        self.q = {k: [] for k in self.q}
        for qq in self.waited:
            w = self.waited[qq]
            for c in self.COMPUTE:
                w[f"c_{c}"] = max(w.get(f"c_{c}", 0), self.cnt[c])
            for q in ("sync", "gp"):
                for i in range(self.NDMA):
                    w[f"d_{q}{i}"] = max(w.get(f"d_{q}{i}", 0), self.dma_uses[q][i] * 16)

    def finalize(self):
        self.barrier()
        for seg in self._segs:
            if all(len(v) == 0 for v in seg.values()):
                continue
            with self.nc.Block() as b:
                for qname, reg in (("sync", b.sync), ("gp", b.gpsimd), ("v", b.vector),
                                   ("s", b.scalar), ("t", b.tensor)):
                    if seg[qname]:
                        reg(lambda e, ops=tuple(seg[qname]): [f(e) for f in ops])


def build_nc(upto="full", dbg=()):
    nc = bass.Bass("TRN2", target_bir_lowering=False, debug=False)
    em = EM(nc)

    def dram(name, shape, dtype=F32):
        kind = "ExternalOutput" if name in dbg else "Internal"
        return nc.dram_tensor(name, shape, dtype, kind=kind).ap()

    def inp(name, shape, dtype=F32):
        return nc.dram_tensor(name, shape, dtype, kind="ExternalInput").ap()

    def sb(name, shape, dtype=F32):
        return nc.alloc_sbuf_tensor(name, shape, dtype)

    scopes = []

    def push_scope():
        scopes.append([])

    def sbp(name, shape, dtype=F32):
        g = nc.sbuf_tensor(name, shape, dtype)
        t = g.__enter__()
        scopes[-1].append(g)
        return t

    def pop_scope():
        em.barrier()
        for g in reversed(scopes.pop()):
            g.__exit__(None, None, None)

    x_in = inp("x", [TF, D])
    ipWT = inp("ipWT", [D, NPROJ])
    convdiag = inp("convdiag", [NO_HBC * CONVK * 128, 128])
    convb = inp("convb", [NO_HBC, 128])
    alog = inp("alog", [32, 1])
    dtbias = inp("dtbias", [32, 1])
    drep = inp("drep", [1, INTER])
    normcols = inp("normcols", [128, 32])
    opWT = inp("opWT", [INTER, D])
    gateWT = inp("gateWT", [D, E])
    gatebB = inp("gatebB", [1, E])
    idxdB = inp("idxdB", [1, E])
    w1t = inp("w1t", [E * D, I_FF])
    w3t = inp("w3t", [E * D, I_FF])
    w2t = inp("w2t", [E * I_FF, D])
    ws1t = inp("ws1t", [D, I_FF])
    ws3t = inp("ws3t", [D, I_FF])
    ws2t = inp("ws2t", [I_FF, D])
    mask01 = inp("mask01", [2 * 128, CHUNK])
    cmaskB = inp("cmaskB", [32, E])
    negB = inp("negB", [32, E])
    onehotB = inp("onehotB", [32, E])
    eye_in = inp("eye", [128, 128])

    out_d = nc.dram_tensor("out", [TC, D], F32, kind="ExternalOutput").ap()
    xs_d = dram("xs_d", [INTER, TC], BF16)
    u_d = dram("u_d", [128, NKD * TF], BF16)
    ypart_d = dram("ypart_d", [TC, INTER])
    acum_d = dram("acum_d", [32, TC])
    x1_d = dram("x1_d", [TC, D])
    gate_s_d = dram("gate_s_d", [TC, INTER], BF16)
    opWTb_d = dram("opWTb_d", [INTER, D], BF16)
    y_d = dram("y_d", [TC, D])
    sstates_d = dram("sstates_d", [NCH * 128, HEADS * HD])
    dvec_d = dram("dvec_d", [NCH, 32])
    cc_in = dram("cc_in", [129, HEADS * HD])
    cc_out = dram("cc_out", [NCORES * 129, HEADS * HD])
    wflat_d = dram("wflat_d", [1, NCORES * 32])
    dbg_d = dram("dbg_d", [512, max(TC, INTER)]) if "dbg_d" in dbg else None

    P = [nc.alloc_psum_tensor(f"P{i}", [128, 512], F32) for i in range(6)]
    PT = [nc.alloc_psum_tensor(f"PT{i}", [128, 1024], BF16) for i in range(2)]
    pk = [f"P{i}" for i in range(6)]
    ptk = [f"PT{i}" for i in range(2)]
    prr = [0]

    def next_p():
        i = prr[0] % 6
        prr[0] += 1
        return P[i], pk[i]

    ptrr = [0]

    def next_pt():
        i = ptrr[0] % 2
        ptrr[0] += 1
        return PT[i], ptk[i]

    eye_f = sb("eye_f", [128, 128], F32)
    eye_b = sb("eye_b", [128, 128], BF16)
    ones_f = sb("ones_f", [1, 128], F32)
    em.dma("sync", eye_f[:], eye_in, "eye_f")
    em.op("v", lambda e: e.tensor_copy(eye_b[:], eye_f[:]), reads=["eye_f"], writes=["eye_b"])
    em.op("v", lambda e: e.memset(ones_f[:], 1.0), writes=["ones_f"])
    normc_sb = sb("normc_sb", [128, 32])
    em.dma("sync", normc_sb[:], normcols, "normc_sb")
    push_scope()
    C_fm = sbp("C_fm", [128, TC], BF16)
    push_scope()
    B_fm = sbp("B_fm", [128, TC], BF16)
    dtv = sbp("dtv", [32, TC])
    ad = sbp("ad", [32, TC])
    alog_s = sbp("alog_s", [32, 1])
    dtb_s = sbp("dtb_s", [32, 1])
    acol = sbp("acol", [32, 1])
    ltot = sbp("ltot", [32, 1])

    push_scope()
    u_fm = sbp("u_fm", [128, NKD * TF], BF16)
    NT = TF // 128
    dtraw = sbp("dtraw", [32, TC])
    spt = sbp("spt", [32, TC])
    xt = [sbp(f"xt{i}", [128, D]) for i in range(2)]
    bnb = [sbp(f"bnb{i}", [128, 24]) for i in range(2)]
    mv = [sbp(f"mv{i}", [128, 2]) for i in range(2)]
    rstd = [sbp(f"rstd{i}", [128, 1]) for i in range(2)]
    negmr = [sbp(f"negmr{i}", [128, 1]) for i in range(2)]
    ubf = [sbp(f"ubf{i}", [128, D], BF16) for i in range(2)]

    for i in range(NT):
        j = i % 2
        kx, kb, km, kr, kn, ku = (f"xt{j}", f"bnb{j}", f"mv{j}", f"rstd{j}", f"negmr{j}", f"ubf{j}")
        em.dma("sync", xt[j][:], x_in[i * 128:(i + 1) * 128, :], kx)
        for s4 in range(4):
            em.op("v", lambda e, j=j, s4=s4: e.bn_stats(bnb[j][:, s4 * 6:(s4 + 1) * 6],
                                                        xt[j][:, s4 * 512:(s4 + 1) * 512]),
                  reads=[kx], writes=[kb])
        em.op("v", lambda e, j=j: e.bn_aggr(mv[j][:], bnb[j][:]), reads=[kb], writes=[km])
        em.op("v", lambda e, j=j: e.tensor_scalar(rstd[j][:], mv[j][:, 1:2], EPS, None, ALU.add),
              reads=[km], writes=[kr])
        em.op("s", lambda e, j=j: e.activation(rstd[j][:], rstd[j][:], AF.Ln, bias=0.0, scale=1.0),
              reads=[kr], writes=[kr])
        em.op("s", lambda e, j=j: e.activation(rstd[j][:], rstd[j][:], AF.Exp, bias=0.0, scale=-0.5),
              reads=[kr], writes=[kr])
        em.op("v", lambda e, j=j: e.tensor_scalar(negmr[j][:], rstd[j][:], mv[j][:, 0:1], -1.0,
                                                  ALU.mult, ALU.mult),
              reads=[kr, km], writes=[kn])
        em.op("v", lambda e, j=j: e.tensor_scalar(ubf[j][:], xt[j][:], rstd[j][:], negmr[j][:],
                                                  ALU.mult, ALU.add),
              reads=[kx, kr, kn], writes=[ku])
        for half in range(2):
            pt, ptkey = next_pt()
            for kk in range(8):
                k = half * 8 + kk
                em.mm(lambda e, j=j, k=k, kk=kk, pt=pt: e.transpose(
                    pt[:, kk * 128:(kk + 1) * 128], ubf[j][:, k * 128:(k + 1) * 128], eye_b[:]),
                    ptkey, [ku, "eye_b"], last=(kk == 7))
            em.op("v", lambda e, i=i, half=half, pt=pt: e.tensor_copy(
                u_fm.ap().rearrange("p (k t) -> p k t", k=NKD)[:, half * 8:(half + 1) * 8,
                                                              i * 128:(i + 1) * 128],
                pt.ap().rearrange("p (k t) -> p k t", k=8)),
                reads=[ptkey], writes=["u_fm"])

    for f in range(32):
        jj = f % 2
        em.dma("sync", xt[jj][:], opWT[f * 128:(f + 1) * 128, :], f"xt{jj}")
        em.op("v", lambda e, jj=jj, f=f: e.tensor_scalar(ubf[jj][:], xt[jj][:],
              normc_sb[:, f:f + 1], None, ALU.mult),
              reads=[f"xt{jj}", "normc_sb"], writes=[f"ubf{jj}"])
        em.dma("sync", opWTb_d[f * 128:(f + 1) * 128, :], ubf[jj][:], f"opwtb_{f}", [f"ubf{jj}"])

    if upto == "A1ln":
        em.dma("sync", u_d, u_fm[:], "u_d", ["u_fm"])
        _final_wait(nc, em)
        return nc
    wsb = [sbp(f"wsb{i}", [128, NKD * 128], BF16) for i in range(2)]
    projbf = [sbp(f"projbf{i}", [128, TF], BF16) for i in range(2)]
    cdiag = [sbp(f"cdiag{i}", [128, CONVK * 128], BF16) for i in range(2)]
    cbcol = [sbp(f"cbcol{i}", [128, 1]) for i in range(2)]
    hbco = [sbp(f"hbco{i}", [128, TC], BF16) for i in range(2)]
    PIECES = [(0, 512), (512, 512), (1024, 512), (1536, 512), (2048, 128)]

    ipWT_r = ipWT.rearrange("(k p) c -> p k c", p=128)

    for oj in range(NO_HBC + 1):
        jj = oj % 2
        ocol = NO_GATE * 128 + oj * 128
        wk = f"wsb{jj}"
        ncols = 128 if oj < NO_HBC else 32
        em.dma("gp", wsb[jj].ap().rearrange("p (k c) -> p k c", k=NKD)[:, :, 0:ncols],
               ipWT_r[:, :, ocol:ocol + ncols], wk)
        pj = f"projbf{jj}"
        if oj < NO_HBC:
            for (poff, plen) in PIECES:
                pp, ppk = next_p()
                for k in range(NKD):
                    em.mm(lambda e, pp=pp, plen=plen, k=k, jj=jj, poff=poff: e.matmul(
                        pp[:, 0:plen], wsb[jj][:, k * 128:(k + 1) * 128],
                        u_fm[:, k * TF + poff: k * TF + poff + plen],
                        start=(k == 0), stop=(k == NKD - 1)),
                        ppk, [wk, "u_fm"], last=(k == NKD - 1))
                em.op("v", lambda e, jj=jj, poff=poff, plen=plen, pp=pp: e.tensor_copy(
                    projbf[jj][:, poff:poff + plen], pp[:, 0:plen]),
                    reads=[ppk], writes=[pj])
            ck = f"cdiag{jj}"
            cbk = f"cbcol{jj}"
            em.dma("gp", cdiag[jj][:], convdiag[oj * CONVK * 128:(oj + 1) * CONVK * 128, :]
                   .rearrange("(k p) c -> p k c", p=128), ck)
            em.dma("sync", cbcol[jj][:], convb[oj:oj + 1, :].rearrange("x c -> c x"), cbk)
            hk = f"hbco{jj}"
            for piece in range(4):
                pp, ppk = next_p()
                for k in range(CONVK):
                    em.mm(lambda e, pp=pp, k=k, jj=jj, piece=piece: e.matmul(
                        pp[:], cdiag[jj][:, k * 128:(k + 1) * 128],
                        projbf[jj][:, 125 + k + piece * 512: 125 + k + piece * 512 + 512],
                        start=(k == 0), stop=(k == CONVK - 1)),
                        ppk, [ck, pj], last=(k == CONVK - 1))
                if oj < NO_GATE:
                    em.op("s", lambda e, jj=jj, piece=piece, pp=pp: e.activation(
                        hbco[jj][:, piece * 512:(piece + 1) * 512], pp[:], AF.Silu,
                        bias=cbcol[jj][:], scale=1.0), reads=[ppk, cbk], writes=[hk])
                else:
                    dstt = B_fm if oj == NO_GATE else C_fm
                    dk = "B_fm" if oj == NO_GATE else "C_fm"
                    em.op("s", lambda e, dstt=dstt, piece=piece, pp=pp, jj=jj: e.activation(
                        dstt[:, piece * 512:(piece + 1) * 512], pp[:], AF.Silu,
                        bias=cbcol[jj][:], scale=1.0), reads=[ppk, cbk], writes=[dk])
            if oj < NO_GATE:
                em.dma("sync", xs_d[oj * 128:(oj + 1) * 128, :], hbco[jj][:], f"xs_d_{oj}", [hk])
        else:
            for piece in range(4):
                pp, ppk = next_p()
                for k in range(NKD):
                    em.mm(lambda e, pp=pp, k=k, jj=jj, piece=piece: e.matmul(
                        pp[0:32, :], wsb[jj][:, k * 128: k * 128 + 32],
                        u_fm[:, k * TF + 128 + piece * 512: k * TF + 128 + piece * 512 + 512],
                        start=(k == 0), stop=(k == NKD - 1)),
                        ppk, [wk, "u_fm"], last=(k == NKD - 1))
                em.op("v", lambda e, piece=piece, pp=pp: e.tensor_copy(
                    dtraw[:, piece * 512:(piece + 1) * 512], pp[0:32, :]),
                    reads=[ppk], writes=["dtraw"])


    em.dma("sync", alog_s[:], alog, "alog_s")
    em.dma("sync", dtb_s[:], dtbias, "dtb_s")
    em.op("s", lambda e: e.activation(acol[:], alog_s[:], AF.Exp, bias=0.0, scale=1.0),
          reads=["alog_s"], writes=["acol"])
    em.op("v", lambda e: e.tensor_scalar(acol[:], acol[:], -1.0, None, ALU.mult),
          reads=["acol"], writes=["acol"])
    em.op("v", lambda e: e.tensor_scalar(dtv[:], dtraw[:], dtb_s[:], None, ALU.add),
          reads=["dtraw", "dtb_s"], writes=["dtv"])
    em.op("s", lambda e: e.activation(spt[:], dtv[:], AF.Abs, bias=0.0, scale=1.0),
          reads=["dtv"], writes=["spt"])
    em.op("s", lambda e: e.activation(spt[:], spt[:], AF.Exp, bias=0.0, scale=-1.0),
          reads=["spt"], writes=["spt"])
    em.op("v", lambda e: e.tensor_scalar(spt[:], spt[:], 1.0, None, ALU.add),
          reads=["spt"], writes=["spt"])
    em.op("s", lambda e: e.activation(spt[:], spt[:], AF.Ln, bias=0.0, scale=1.0),
          reads=["spt"], writes=["spt"])
    em.op("v", lambda e: e.tensor_scalar(dtv[:], dtv[:], 0.0, None, ALU.max),
          reads=["dtv"], writes=["dtv"])
    em.op("v", lambda e: e.tensor_tensor(dtv[:], dtv[:], spt[:], ALU.add),
          reads=["dtv", "spt"], writes=["dtv"])
    em.op("v", lambda e: e.tensor_scalar(dtv[:], dtv[:], 100.0, None, ALU.min),
          reads=["dtv"], writes=["dtv"])
    em.op("v", lambda e: e.tensor_scalar(ad[:], dtv[:], acol[:], None, ALU.mult),
          reads=["dtv", "acol"], writes=["ad"])

    if upto == "A1":
        em.dma("sync", dbg_d[0:32, 0:TC], dtv[:], "dbg_d", ["dtv"])
        em.dma("sync", dbg_d[32:64, 0:TC], ad[:], "dbg_d", ["ad"])
        em.dma("gp", dbg_d[64:192, 0:TC], B_fm[:], "dbg_d", ["B_fm"])
        em.dma("gp", dbg_d[192:320, 0:TC], C_fm[:], "dbg_d", ["C_fm"])
        _final_wait(nc, em)
        return nc

    wgbuf = [sbp(f"wgbuf{i}", [128, NKD * 512], BF16) for i in range(2)]
    gstg = [sbp(f"gstg{i}", [128, 512], BF16) for i in range(2)]
    for piece in range(8):
        jj = piece % 2
        wk = f"wgbuf{jj}"
        em.dma("gp", wgbuf[jj].ap().rearrange("p (k c) -> p k c", k=NKD),
               ipWT_r[:, :, piece * 512:(piece + 1) * 512], wk)
        for tt in range(16):
            pp, ppk = next_p()
            for k in range(NKD):
                em.mm(lambda e, pp=pp, k=k, jj=jj, tt=tt: e.matmul(
                    pp[:], u_fm[:, k * TF + 128 + tt * 128: k * TF + 128 + (tt + 1) * 128],
                    wgbuf[jj][:, k * 512:(k + 1) * 512],
                    start=(k == 0), stop=(k == NKD - 1)),
                    ppk, [wk, "u_fm"], last=(k == NKD - 1))
            sj = tt % 2
            em.op("s", lambda e, sj=sj, pp=pp: e.activation(gstg[sj][:], pp[:], AF.Silu,
                  bias=0.0, scale=1.0), reads=[ppk], writes=[f"gstg{sj}"])
            em.dma("sync", gate_s_d[tt * 128:(tt + 1) * 128, piece * 512:(piece + 1) * 512],
                   gstg[sj][:], f"gates_{tt}_{piece}", [f"gstg{sj}"])

    pop_scope()

    push_scope()
    mask_sb = sbp("mask_sb", [128, 2 * CHUNK])
    em.dma("sync", mask_sb.ap().rearrange("p (s c) -> p s c", s=2),
           mask01.rearrange("(s p) c -> p s c", p=128), "mask_sb")
    drep_sb = sbp("drep_sb", [128, INTER], BF16)
    em.dma("gp", drep_sb[:], bcast_part(drep, 128), "drep_sb")
    acA = [sbp(f"acA{i}", [32, 384]) for i in range(2)]
    acB = [sbp(f"acB{i}", [32, 384]) for i in range(2)]
    for i in range(2):
        em.op("v", lambda e, i=i: e.memset(acA[i][:], 0.0), writes=[f"acA{i}"])
        em.op("v", lambda e, i=i: e.memset(acB[i][:], 0.0), writes=[f"acB{i}"])
    em.op("v", lambda e: e.memset(ltot[:], 0.0), writes=["ltot"])
    acumC = [sbp(f"acumC{i}", [32, CHUNK]) for i in range(2)]
    dvecC = [sbp(f"dvecC{i}", [32, 1]) for i in range(2)]
    dsrow = [sbp(f"dsrow{i}", [32, CHUNK]) for i in range(2)]
    ddrow = [sbp(f"ddrow{i}", [32, CHUNK]) for i in range(2)]
    acumT = [[sbp(f"acumT{i}_{p}", [128, 32]) for i in range(2)] for p in range(2)]
    dtvT = [[sbp(f"dtvT{i}_{p}", [128, 32]) for i in range(2)] for p in range(2)]
    ddT = [[sbp(f"ddT{i}_{p}", [128, 32]) for i in range(2)] for p in range(2)]
    dcols = [sbp(f"dcols{i}", [128, 32]) for i in range(2)]
    rbg = [sbp(f"rbg{i}", [128, 8 * CHUNK]) for i in range(2)]
    dbg8 = [sbp(f"dbg8{i}", [128, 8 * CHUNK], BF16) for i in range(2)]
    lbg = [sbp(f"lbg{i}", [128, 8 * CHUNK], BF16) for i in range(2)]
    mbufS = [sbp(f"mbuf{i}", [128, HEADS * CHUNK], BF16) for i in range(2)]
    mbuf = [mbufS, mbufS]
    gm = [[sbp(f"gm{i}_{p}", [128, CHUNK]) for i in range(2)] for p in range(2)]
    xsgS = sbp("xsg", [128, 8 * CHUNK], BF16)
    xsg = [xsgS, xsgS]
    xsTS = [sbp(f"xsT{i}", [128, HEADS * 128], BF16) for i in range(2)]
    xsT = [xsTS, xsTS]
    xdXS = [sbp(f"xdX{i}", [128, HEADS * 128], BF16) for i in range(2)]
    xdX = [xdXS, xdXS]
    btT = [sbp(f"btT{i}", [128, 2 * 128], BF16) for i in range(2)]
    state_run = sbp("state_run", [128, HEADS * 128])
    em.op("v", lambda e, t=state_run: e.memset(t[:], 0.0), writes=["state_run"])
    scsb = [sbp(f"scsb{i}", [128, 512]) for i in range(2)]
    ysbS = [sbp(f"ysb{i}", [128, INTER]) for i in range(2)]
    ysb = [ysbS, ysbS]

    def a2_chunk(c):
        c0 = c * CHUNK
        par = c % 2
        acum, dvec = acumC[par], dvecC[par]
        acA_, acB_ = acA[par], acB[par]
        dsr, ddr = dsrow[par], ddrow[par]
        acT, dtT, ddTT = acumT[par], dtvT[par], ddT[par]
        dcl, rbg_, dbg_, lbg_ = dcols[par], rbg[par], dbg8[par], lbg[par]
        mb, gmp, xsgp, xsTp, xdXp = mbuf[par], gm[par], xsg[par], xsT[par], xdX[par]
        btTp, scsbp, ysbp = btT[par], scsb[par], ysb[par]
        kac, kdv = f"acumC{par}", f"dvecC{par}"
        kaA, kaB = f"acA{par}", f"acB{par}"
        kdsr, kddr = f"dsrow{par}", f"ddrow{par}"
        kdcl, krbg, kdbg, klbg = f"dcols{par}", f"rbg{par}", f"dbg8{par}", f"lbg{par}"
        kxsg, kbtT, kscsb = "xsg", f"btT{par}", f"scsb{par}"
        em.op("v", lambda e, c0=c0: e.tensor_copy(acA_[:, 128:384], ad[:, c0:c0 + CHUNK]),
              reads=["ad"], writes=[kaA])
        src_, dst_, sk, dk = acA_, acB_, kaA, kaB
        for s in (1, 2, 4, 8, 16, 32, 64, 128):
            em.op("v", lambda e, src_=src_, dst_=dst_, s=s: e.tensor_tensor(
                dst_[:, 128:384], src_[:, 128:384], src_[:, 128 - s:384 - s], ALU.add),
                reads=[sk], writes=[dk])
            src_, dst_, sk, dk = dst_, src_, dk, sk
        em.op("v", lambda e, src_=src_, acum=acum: e.tensor_copy(acum[:], src_[:, 128:384]),
              reads=[sk], writes=[kac])
        em.op("v", lambda e, acum=acum: e.tensor_tensor(ltot[:], ltot[:], acum[:, 255:256], ALU.add),
              reads=["ltot", kac], writes=["ltot"])
        em.op("s", lambda e, acum=acum, dvec=dvec: e.activation(dvec[:], acum[:, 255:256], AF.Exp, bias=0.0, scale=1.0),
              reads=[kac], writes=[kdv])
        em.dma("sync", dvec_d[c:c + 1, :].rearrange("x h -> h x"), dvec[:], f"dvec_d_{c}", [kdv])
        em.dma("sync", acum_d[:, c0:c0 + CHUNK], acum[:], f"acum_d_{c}", [kac])
        em.op("v", lambda e, acum=acum, dsr=dsr: e.tensor_scalar(dsr[:], acum[:], acum[:, 255:256], -1.0,
              ALU.subtract, ALU.mult), reads=[kac], writes=[kdsr])
        em.op("s", lambda e, dsr=dsr: e.activation(dsr[:], dsr[:], AF.Exp, bias=0.0, scale=1.0),
              reads=[kdsr], writes=[kdsr])
        em.op("v", lambda e, c0=c0, ddr=ddr, dsr=dsr: e.tensor_tensor(ddr[:], dtv[:, c0:c0 + CHUNK], dsr[:], ALU.mult),
              reads=["dtv", kdsr], writes=[kddr])
        for st in range(2):
            for (srcap, srck, dstt, dstk) in (
                    (acum[:, st * 128:(st + 1) * 128], kac, acT[st], f"acumT{st}_{par}"),
                    (dtv[:, c0 + st * 128:c0 + (st + 1) * 128], "dtv", dtT[st], f"dtvT{st}_{par}"),
                    (ddr[:, st * 128:(st + 1) * 128], kddr, ddTT[st], f"ddT{st}_{par}")):
                pp, ppk = next_p()
                em.mm(lambda e, pp=pp, srcap=srcap: e.transpose(pp[0:128, 0:32], srcap,
                      eye_f[0:32, 0:32]), ppk, [srck, "eye_f"], last=True)
                em.op("v", lambda e, pp=pp, dstt=dstt: e.tensor_copy(dstt[:], pp[0:128, 0:32]),
                      reads=[ppk], writes=[dstk])
        em.dma("sync", dcl[:], bcast_part(dvec_d[c:c + 1, :], 128), kdcl, [f"dvec_d_{c}"])
        for st in range(2):
            pp, ppk = next_p()
            em.mm(lambda e, st=st, pp=pp, c0=c0: e.matmul(pp[:, 0:CHUNK],
                  B_fm[:, c0 + st * 128:c0 + (st + 1) * 128], C_fm[:, c0:c0 + CHUNK],
                  start=True, stop=True), ppk, ["B_fm", "C_fm"], last=True)
            em.op("v", lambda e, st=st, pp=pp, gmp=gmp: e.tensor_tensor(gmp[st][:], pp[:, 0:CHUNK],
                  mask_sb[:, st * CHUNK:(st + 1) * CHUNK], ALU.mult),
                  reads=[ppk, "mask_sb"], writes=[f"gm{st}_{par}"])
        for g in range(4):
            em.dma("sync", rbg_.ap().rearrange("p (h l) -> p h l", h=8),
                   bcast_prepend(acum_d[8 * g:8 * (g + 1), c0:c0 + CHUNK], 128), krbg, [f"acum_d_{c}"])
            for st in range(2):
                for hh in range(8):
                    em.op("v", lambda e, st=st, hh=hh, g=g, dbgt=dbg_, rbgt=rbg_, acT=acT: e.tensor_scalar(
                        dbgt[:, hh * CHUNK:(hh + 1) * CHUNK], rbgt[:, hh * CHUNK:(hh + 1) * CHUNK],
                        acT[st][:, 8 * g + hh:8 * g + hh + 1], 0.0, ALU.subtract, ALU.min),
                        reads=[krbg, f"acumT{st}_{par}"], writes=[kdbg])
                em.op("s", lambda e, lbgt=lbg_, dbgt=dbg_: e.activation(lbgt[:], dbgt[:], AF.Exp, bias=0.0, scale=1.0),
                      reads=[kdbg], writes=[klbg])
                em.op("v", lambda e, st=st, g=g, mb=mb, lbgt=lbg_, gmp=gmp: e.tensor_tensor(
                    mb[st].ap().rearrange("p (h l) -> p h l", h=HEADS)[:, 8 * g:8 * (g + 1), :],
                    lbgt.ap().rearrange("p (h l) -> p h l", h=8),
                    rep_free(gmp[st][:], 8), ALU.mult),
                    reads=[klbg, f"gm{st}_{par}"], writes=[f"mbuf{st}"])
        for g in range(4):
            em.dma("sync", xsgp.ap().rearrange("p (h t) -> p h t", h=8),
                   xs_d.rearrange("(h p) t -> p h t", p=128)[:, 8 * g:8 * (g + 1), c0:c0 + CHUNK],
                   kxsg, [f"xs_d_{o}" for o in range(8 * g, 8 * (g + 1))])
            for st in range(2):
                pt, ptkey = next_pt()
                for hh in range(8):
                    em.mm(lambda e, pt=pt, hh=hh, st=st, xsgp=xsgp: e.transpose(
                        pt[:, hh * 128:(hh + 1) * 128],
                        xsgp[:, hh * CHUNK + st * 128: hh * CHUNK + (st + 1) * 128], eye_b[:]),
                        ptkey, [kxsg, "eye_b"], last=(hh == 7))
                em.op("v", lambda e, st=st, g=g, pt=pt, xsTp=xsTp: e.tensor_copy(
                    xsTp[st][:, g * 1024:(g + 1) * 1024], pt[:]),
                    reads=[ptkey], writes=[f"xsT{st}"])
        for st in range(2):
            for h in range(HEADS):
                em.op("s", lambda e, st=st, h=h, xdXp=xdXp, xsTp=xsTp, ddTT=ddTT: e.activation(
                    xdXp[st][:, h * 128:(h + 1) * 128], xsTp[st][:, h * 128:(h + 1) * 128],
                    AF.Copy, bias=0.0, scale=ddTT[st][:, h:h + 1]),
                    reads=[f"xsT{st}", f"ddT{st}_{par}"], writes=[f"xdX{st}"])
        pt, ptkey = next_pt()
        for st in range(2):
            em.mm(lambda e, pt=pt, st=st, c0=c0: e.transpose(pt[:, st * 128:(st + 1) * 128],
                  B_fm[:, c0 + st * 128:c0 + (st + 1) * 128], eye_b[:]),
                  ptkey, ["B_fm", "eye_b"], last=(st == 1))
        em.op("v", lambda e, pt=pt, btTp=btTp: e.tensor_copy(btTp[:], pt[:, 0:256]),
              reads=[ptkey], writes=[kbtT])
        for g4 in range(8):
            pp, ppk = next_p()
            for q in range(4):
                h = g4 * 4 + q
                for st in range(2):
                    em.mm(lambda e, pp=pp, q=q, h=h, st=st, btTp=btTp, xdXp=xdXp: e.matmul(
                        pp[:, q * 128:(q + 1) * 128], btTp[:, st * 128:(st + 1) * 128],
                        xdXp[st][:, h * 128:(h + 1) * 128],
                        start=(q == 0 and st == 0), stop=(q == 3 and st == 1)),
                        ppk, [kbtT, f"xdX{st}"], last=(q == 3 and st == 1))
            em.op("v", lambda e, pp=pp, scsbp=scsbp: e.tensor_copy(scsbp[:], pp[:]),
                  reads=[ppk], writes=[kscsb])
            em.dma("sync", sstates_d[c * 128:(c + 1) * 128, g4 * 512:(g4 + 1) * 512],
                   scsbp[:], f"sst_{c}_{g4}", [kscsb])
            for q in range(4):
                h = g4 * 4 + q
                em.op("v", lambda e, h=h, q=q, t=state_run, dcl=dcl, scsbp=scsbp: e.scalar_tensor_tensor(
                    t[:, h * 128:(h + 1) * 128], t[:, h * 128:(h + 1) * 128],
                    dcl[:, h:h + 1], scsbp[:, q * 128:(q + 1) * 128], ALU.mult, ALU.add),
                    reads=["state_run", kdcl, kscsb], writes=["state_run"])
        for st in range(2):
            for h in range(HEADS):
                em.op("v", lambda e, st=st, h=h, xdXp=xdXp, xsTp=xsTp, dtT=dtT: e.tensor_scalar(
                    xdXp[st][:, h * 128:(h + 1) * 128], xsTp[st][:, h * 128:(h + 1) * 128],
                    dtT[st][:, h:h + 1], None, ALU.mult),
                    reads=[f"xsT{st}", f"dtvT{st}_{par}"], writes=[f"xdX{st}"])
        for lt in range(2):
            em.op("v", lambda e, lt=lt, ysbp=ysbp, xsTp=xsTp: e.tensor_tensor(ysbp[lt][:], xsTp[lt][:], drep_sb[:], ALU.mult),
                  reads=[f"xsT{lt}", "drep_sb"], writes=[f"ysb{lt}"])
            for g4 in range(8):
                pp, ppk = next_p()
                for q in range(4):
                    h = g4 * 4 + q
                    for st in range(2):
                        em.mm(lambda e, pp=pp, q=q, h=h, st=st, lt=lt, mb=mb, xdXp=xdXp: e.matmul(
                            pp[:, q * 128:(q + 1) * 128],
                            mb[st][:, h * CHUNK + lt * 128: h * CHUNK + lt * 128 + 128],
                            xdXp[st][:, h * 128:(h + 1) * 128],
                            start=(q == 0 and st == 0), stop=(q == 3 and st == 1)),
                            ppk, [f"mbuf{st}", f"xdX{st}"], last=(q == 3 and st == 1))
                em.op("v", lambda e, pp=pp, lt=lt, g4=g4, ysbp=ysbp: e.tensor_tensor(
                    ysbp[lt][:, g4 * 512:(g4 + 1) * 512], ysbp[lt][:, g4 * 512:(g4 + 1) * 512],
                    pp[:], ALU.add), reads=[ppk, f"ysb{lt}"], writes=[f"ysb{lt}"])
            em.dma("sync", ypart_d[c0 + lt * 128:c0 + (lt + 1) * 128, :], ysbp[lt][:],
                   f"ypart_{c}_{lt}", [f"ysb{lt}"])

    for c in range(NCH):
        a2_chunk(c)

    em.dma("sync", cc_in[0:128, :], state_run[:], "cc_in", ["state_run"])
    em.dma("sync", cc_in[128:129, 0:32].rearrange("x h -> h x"), ltot[:], "ltot_cc", ["ltot"])

    if upto == "A2":
        _final_wait(nc, em)
        return nc
    pop_scope()
    pop_scope()

    em.collective(lambda e: e.collective_compute(
        "AllGather", ALU.bypass, replica_groups=RG,
        ins=[cc_in.opt()], outs=[cc_out.opt()]), ["cc_in", "ltot_cc"], "cc_out")

    state_b = sbp("state_run_b", [128, HEADS * HD])
    push_scope()
    ltg = sbp("ltg", [32, 16])
    ltg2 = sbp("ltg2", [32, 16])
    cmk_sb = sbp("cmk_sb", [32, E])
    ngk_sb = sbp("ngk_sb", [32, E])
    ohk_sb = sbp("ohk_sb", [32, E])
    t8 = sbp("t8", [32, E])
    cum1 = sbp("cum1", [32, 1])
    wmat = sbp("wmat", [32, 8])
    wcols = sbp("wcols", [128, 256])
    Sg = [sbp(f"Sg{i}", [128, HEADS * HD]) for i in range(2)]
    Inp = [sbp(f"Inp{i}", [128, HEADS * HD]) for i in range(2)]
    em.dma("sync", cmk_sb[:], cmaskB, "cmk_sb")
    em.dma("sync", ngk_sb[:], negB, "ngk_sb")
    em.dma("sync", ohk_sb[:], onehotB, "ohk_sb")
    em.op("v", lambda e: e.memset(ltg[:], 0.0), writes=["ltg"])
    em.op("v", lambda e: e.memset(ltg2[:], 0.0), writes=["ltg2"])
    em.dma("sync", ltg[:, 8:16].rearrange("h (x j) -> h x j", x=1),
           cc_out.rearrange("(j r) n -> n r j", r=129)[0:32, 128:129, :],
           "ltg", ["cc_out"], allow_slow_non_contiguous=True)
    src_, dst_, sk, dk = ltg, ltg2, "ltg", "ltg2"
    for s in (1, 2, 4):
        em.op("v", lambda e, src_=src_, dst_=dst_, s=s: e.tensor_tensor(
            dst_[:, 8:16], src_[:, 8:16], src_[:, 8 - s:16 - s], ALU.add),
            reads=[sk], writes=[dk])
        src_, dst_, sk, dk = dst_, src_, dk, sk
    cum, cumk = src_, sk
    em.op("v", lambda e, cum=cum: e.tensor_tensor(t8[:], cum[:, 8:16], ohk_sb[:], ALU.mult),
          reads=[cumk, "ohk_sb"], writes=["t8"])
    em.op("v", lambda e: e.tensor_reduce(cum1[:], t8[:], AX.X, ALU.add),
          reads=["t8"], writes=["cum1"])
    em.op("v", lambda e, cum=cum: e.tensor_scalar(wmat[:], cum[:, 8:16], cum1[:], -1.0,
          ALU.subtract, ALU.mult), reads=[cumk, "cum1"], writes=["wmat"])
    em.op("v", lambda e: e.tensor_tensor(wmat[:], wmat[:], cmk_sb[:], ALU.mult),
          reads=["wmat", "cmk_sb"], writes=["wmat"])
    em.op("v", lambda e: e.tensor_tensor(wmat[:], wmat[:], ngk_sb[:], ALU.add),
          reads=["wmat", "ngk_sb"], writes=["wmat"])
    em.op("s", lambda e: e.activation(wmat[:], wmat[:], AF.Exp, bias=0.0, scale=1.0),
          reads=["wmat"], writes=["wmat"])
    em.dma("sync", wflat_d.rearrange("x (j h) -> h x j", h=32), wmat.ap().rearrange("h (x j) -> h x j", x=1),
           "wflat_d", ["wmat"], allow_slow_non_contiguous=True)
    em.dma("sync", wcols[:], bcast_part(wflat_d, 128), "wcols", ["wflat_d"])
    for j in range(NCORES):
        sj = j % 2
        em.dma("sync", Sg[sj][:], cc_out[j * 129:j * 129 + 128, :], f"Sg{sj}", ["cc_out"])
        cur, prv = f"Inp{j % 2}", f"Inp{(j + 1) % 2}"
        curt, prvt = Inp[j % 2], Inp[(j + 1) % 2]
        for h in range(HEADS):
            if j == 0:
                em.op("v", lambda e, h=h, curt=curt, sj=sj: e.tensor_scalar(
                    curt[:, h * 128:(h + 1) * 128], Sg[sj][:, h * 128:(h + 1) * 128],
                    wcols[:, h:h + 1], None, ALU.mult),
                    reads=[f"Sg{sj}", "wcols"], writes=[cur])
            else:
                em.op("v", lambda e, h=h, j=j, curt=curt, prvt=prvt, sj=sj: e.scalar_tensor_tensor(
                    curt[:, h * 128:(h + 1) * 128], Sg[sj][:, h * 128:(h + 1) * 128],
                    wcols[:, j * 32 + h:j * 32 + h + 1], prvt[:, h * 128:(h + 1) * 128],
                    ALU.mult, ALU.add),
                    reads=[f"Sg{sj}", "wcols", prv], writes=[cur])
    em.op("v", lambda e, curt=Inp[(NCORES - 1) % 2]: e.tensor_copy(state_b[:], curt[:]),
          reads=[f"Inp{(NCORES - 1) % 2}"], writes=["state_run_b"])
    pop_scope()

    if upto == "IN":
        em.dma("sync", dbg_d[0:128, 0:HEADS * HD], state_b[:], "dbg_d", ["state_run_b"])
        _final_wait(nc, em)
        return nc

    push_scope()
    rbB = sbp("rbB", [128, 8 * CHUNK])
    ebg = sbp("ebg", [128, 8 * CHUNK], BF16)
    cpb = sbp("cpb", [128, HEADS * CHUNK], BF16)
    scast = sbp("scast", [128, HEADS * HD], BF16)
    dcolsB = sbp("dcolsB", [128, 32])
    scld = sbp("scld", [128, HEADS * HD])
    yp = [sbp(f"yp{i}", [128, INTER]) for i in range(2)]
    gload = [sbp(f"gload{i}", [128, INTER], BF16) for i in range(2)]
    sumsq = sbp("sumsq", [128, 1])
    rstdB = sbp("rstdB", [128, 4])
    yfmg = sbp("yfmg", [128, 32 * 512], BF16)
    attn_sb = sbp("attn_sb", [128, 4 * D])
    wop = [sbp(f"wop{i}", [128, 512], BF16) for i in range(6)]
    bnb2 = sbp("bnb2", [128, 24])
    mv2 = sbp("mv2", [128, 2])
    c1 = sbp("c1", [128, 1])
    c2 = sbp("c2", [128, 1])
    c3 = sbp("c3", [128, 1])
    xres = sbp("xres", [128, D])
    x1o = sbp("x1o", [128, D])
    ybf = scast

    def b_chunk(c):
        c0 = c * CHUNK
        ltg4 = (c % 2) * 2
        for g in range(4):
            em.dma("sync", rbB.ap().rearrange("p (h l) -> p h l", h=8),
                   bcast_prepend(acum_d[8 * g:8 * (g + 1), c0:c0 + CHUNK], 128), "rbB", [f"acum_d_{c}"])
            em.op("s", lambda e: e.activation(ebg[:], rbB[:], AF.Exp, bias=0.0, scale=1.0),
                  reads=["rbB"], writes=["ebg"])
            em.op("v", lambda e, g=g, c0=c0: e.tensor_tensor(
                cpb.ap().rearrange("p (h l) -> p h l", h=HEADS)[:, 8 * g:8 * (g + 1), :],
                rep_free(C_fm[:, c0:c0 + CHUNK], 8),
                ebg.ap().rearrange("p (h l) -> p h l", h=8), ALU.mult),
                reads=["ebg", "C_fm"], writes=["cpb"])
        em.op("v", lambda e: e.tensor_copy(scast[:], state_b[:]),
              reads=["state_run_b"], writes=["scast"])
        em.dma("sync", dcolsB[:], bcast_part(dvec_d[c:c + 1, :], 128), "dcolsB", [f"dvec_d_{c}"])
        em.dma("sync", scld[:], sstates_d[c * 128:(c + 1) * 128, :], "scld", [f"sst_{c}_{g4}" for g4 in range(8)])
        for h in range(HEADS):
            em.op("v", lambda e, h=h: e.scalar_tensor_tensor(
                state_b[:, h * 128:(h + 1) * 128], state_b[:, h * 128:(h + 1) * 128],
                dcolsB[:, h:h + 1], scld[:, h * 128:(h + 1) * 128], ALU.mult, ALU.add),
                reads=["state_run_b", "dcolsB", "scld"], writes=["state_run_b"])
        for lt in range(2):
            ypt, kyp = yp[lt], f"yp{lt}"
            glt, kgl = gload[lt], f"gload{lt}"
            em.dma("sync", ypt[:], ypart_d[c0 + lt * 128:c0 + (lt + 1) * 128, :], kyp, [f"ypart_{c}_{lt}"])
            for g4 in range(8):
                pp, ppk = next_p()
                for q in range(4):
                    h = g4 * 4 + q
                    em.mm(lambda e, pp=pp, q=q, h=h, lt=lt: e.matmul(
                        pp[:, q * 128:(q + 1) * 128],
                        cpb[:, h * CHUNK + lt * 128: h * CHUNK + lt * 128 + 128],
                        scast[:, h * 128:(h + 1) * 128],
                        start=(q == 0), stop=(q == 3)),
                        ppk, ["cpb", "scast"], last=(q == 3))
                em.op("v", lambda e, pp=pp, g4=g4, ypt=ypt: e.tensor_tensor(
                    ypt[:, g4 * 512:(g4 + 1) * 512], ypt[:, g4 * 512:(g4 + 1) * 512], pp[:],
                    ALU.add), reads=[ppk, kyp], writes=[kyp])
            em.dma("sync", glt[:], gate_s_d[c0 + lt * 128:c0 + (lt + 1) * 128, :], kgl,
                   [f"gates_{c * 2 + lt}_{p}" for p in range(8)])
            em.op("v", lambda e, ypt=ypt, glt=glt: e.tensor_tensor(ypt[:], ypt[:], glt[:], ALU.mult),
                  reads=[kyp, kgl], writes=[kyp])
            em.op("s", lambda e, ypt=ypt: e.activation(scld[:, 0:2048], ypt[:, 0:2048], AF.Square,
                  bias=0.0, scale=1.0, accum_out=sumsq[:]),
                  reads=[kyp], writes=["scld", "sumsq"])
            em.op("s", lambda e, ypt=ypt: e.activation(scld[:, 0:2048], ypt[:, 2048:4096], AF.Square,
                  bias=0.0, scale=1.0, accum_out=c1[:]),
                  reads=[kyp, "scld"], writes=["scld", "c1"])
            em.op("v", lambda e: e.tensor_tensor(sumsq[:], sumsq[:], c1[:], ALU.add),
                  reads=["sumsq", "c1"], writes=["sumsq"])
            em.op("v", lambda e: e.tensor_scalar(sumsq[:], sumsq[:], 1.0 / INTER, EPS,
                  ALU.mult, ALU.add), reads=["sumsq"], writes=["sumsq"])
            em.op("s", lambda e: e.activation(sumsq[:], sumsq[:], AF.Ln, bias=0.0, scale=1.0),
                  reads=["sumsq"], writes=["sumsq"])
            em.op("s", lambda e, lt=lt, ltg4=ltg4: e.activation(rstdB[:, ltg4 + lt:ltg4 + lt + 1],
                  sumsq[:], AF.Exp, bias=0.0, scale=-0.5),
                  reads=["sumsq"], writes=["rstdB"])
            em.op("v", lambda e, ypt=ypt: e.tensor_copy(ybf[:], ypt[:]), reads=[kyp, "scast"], writes=["scast"])
            for fh in range(4):
                pt, ptkey = next_pt()
                for ff in range(8):
                    f = fh * 8 + ff
                    em.mm(lambda e, pt=pt, ff=ff, f=f: e.transpose(
                        pt[:, ff * 128:(ff + 1) * 128], ybf[:, f * 128:(f + 1) * 128], eye_b[:]),
                        ptkey, ["scast", "eye_b"], last=(ff == 7))
                em.op("v", lambda e, pt=pt, fh=fh, lt=lt, ltg4=ltg4: e.tensor_copy(
                    yfmg.ap().rearrange("p (f l) -> p f l", l=512)[:, fh * 8:(fh + 1) * 8,
                                                                  (ltg4 + lt) * 128:(ltg4 + lt + 1) * 128],
                    pt.ap().rearrange("p (f l) -> p f l", f=8)),
                    reads=[ptkey], writes=["yfmg"])
        if c % 2 == 1:
            gtok0 = (c - 1) * CHUNK
            for dpiece in range(4):
                for f in range(32):
                    jw = f % 6
                    em.dma("sync", wop[jw][:], opWTb_d[f * 128:(f + 1) * 128,
                           dpiece * 512:(dpiece + 1) * 512], f"wop{jw}", [f"opwtb_{f}"])
                    for lt4 in range(4):
                        em.mm(lambda e, lt4=lt4, f=f, jw=jw, dpiece=dpiece: e.matmul(
                            P[lt4][:], yfmg[:, f * 512 + lt4 * 128: f * 512 + (lt4 + 1) * 128],
                            wop[jw][:], start=(f == 0), stop=(f == 31)),
                            pk[lt4], ["yfmg", f"wop{jw}"], last=(lt4 == 3))
                for lt4 in range(4):
                    em.op("v", lambda e, lt4=lt4, dpiece=dpiece: e.tensor_copy(
                        attn_sb[:, lt4 * D + dpiece * 512: lt4 * D + (dpiece + 1) * 512],
                        P[lt4][:]), reads=[pk[lt4]], writes=["attn_sb"])
            for lt4 in range(4):
                for s4 in range(4):
                    em.op("v", lambda e, lt4=lt4, s4=s4: e.bn_stats(
                        bnb2[:, s4 * 6:(s4 + 1) * 6],
                        attn_sb[:, lt4 * D + s4 * 512: lt4 * D + (s4 + 1) * 512]),
                        reads=["attn_sb"], writes=["bnb2"])
                em.op("v", lambda e: e.bn_aggr(mv2[:], bnb2[:]), reads=["bnb2"], writes=["mv2"])
                em.op("v", lambda e, lt4=lt4: e.tensor_tensor(c1[:], rstdB[:, lt4:lt4 + 1],
                      rstdB[:, lt4:lt4 + 1], ALU.mult), reads=["rstdB"], writes=["c1"])
                em.op("v", lambda e: e.tensor_scalar(c1[:], c1[:], mv2[:, 1:2], EPS,
                      ALU.mult, ALU.add), reads=["c1", "mv2"], writes=["c1"])
                em.op("s", lambda e: e.activation(c1[:], c1[:], AF.Ln, bias=0.0, scale=1.0),
                      reads=["c1"], writes=["c1"])
                em.op("s", lambda e: e.activation(c1[:], c1[:], AF.Exp, bias=0.0, scale=-0.5),
                      reads=["c1"], writes=["c1"])
                em.op("v", lambda e, lt4=lt4: e.tensor_tensor(c2[:], c1[:], rstdB[:, lt4:lt4 + 1],
                      ALU.mult), reads=["c1", "rstdB"], writes=["c2"])
                em.op("v", lambda e: e.tensor_scalar(c3[:], mv2[:, 0:1], c2[:], -1.0,
                      ALU.mult, ALU.mult), reads=["mv2", "c2"], writes=["c3"])
                tok0 = gtok0 + lt4 * 128
                em.dma("sync", xres[:], x_in[PAD + tok0:PAD + tok0 + 128, :], "xres")
                em.op("v", lambda e: e.tensor_scalar(xres[:], xres[:], c3[:], None, ALU.add),
                      reads=["xres", "c3"], writes=["xres"])
                em.op("v", lambda e, lt4=lt4: e.scalar_tensor_tensor(x1o[:],
                      attn_sb[:, lt4 * D:(lt4 + 1) * D], c2[:], xres[:], ALU.mult, ALU.add),
                      reads=["attn_sb", "c2", "xres"], writes=["x1o"])
                em.dma("sync", x1_d[tok0:tok0 + 128, :], x1o[:], f"x1_{tok0 // 128}", ["x1o"])
    for c in range(NCH):
        b_chunk(c)

    pop_scope()

    if upto == "B":
        _final_wait(nc, em)
        return nc
    pop_scope()

    push_scope()
    xt_fm = sbp("xt_fm", [128, NKD * TC], BF16)
    h_sb = sbp("h_sb", [128, NKI * TC], BF16)
    w1sb = [sbp(f"w1sb{i}", [128, NKD * 128], BF16) for i in range(2)]
    w3sb = [sbp(f"w3sb{i}", [128, NKD * 128], BF16) for i in range(2)]
    w2sb = [sbp(f"w2sb{i}", [128, NKI * 512], BF16) for i in range(2)]
    x1t = [sbp(f"x1t{i}", [128, D]) for i in range(2)]
    bnb3 = [sbp(f"bnb3{i}", [128, 24]) for i in range(2)]
    mv3 = [sbp(f"mv3{i}", [128, 2]) for i in range(2)]
    r3 = [sbp(f"r3{i}", [128, 1]) for i in range(2)]
    n3 = [sbp(f"n3{i}", [128, 1]) for i in range(2)]
    xth = [sbp(f"xth{i}", [128, D], BF16) for i in range(2)]
    xtl = [sbp(f"xtl{i}", [128, D], BF16) for i in range(2)]
    lofm = sbp("lofm", [128, NKD * 128], BF16)
    gw_sb = sbp("gw_sb", [128, NKD * E], BF16)
    gb_bc = sbp("gb_bc", [128, E])
    id_bc = sbp("id_bc", [128, E])
    lg = sbp("lg", [128, E])
    rcol = sbp("rcol", [128, 1])
    r2col = sbp("r2col", [128, 1])
    e8a = sbp("e8a", [128, E])
    e8b = sbp("e8b", [128, E])
    dw_sb = sbp("dw_sb", [128, 16 * E])
    stg = [sbp(f"stg{i}", [128, 512]) for i in range(4)]
    stgb = [sbp(f"stgb{i}", [128, 512]) for i in range(4)]
    outx = [sbp(f"outx{i}", [128, D]) for i in range(2)]

    em.dma("gp", gw_sb.ap().rearrange("p (k c) -> p k c", k=NKD),
           gateWT.rearrange("(k p) c -> p k c", p=128), "gw_sb")
    em.dma("sync", gb_bc[:], bcast_part(gatebB, 128), "gb_bc")
    em.dma("sync", id_bc[:], bcast_part(idxdB, 128), "id_bc")

    for tt in range(16):
        j = tt % 2
        kx, kb, km, kr, kn = f"x1t{j}", f"bnb3{j}", f"mv3{j}", f"r3{j}", f"n3{j}"
        em.dma("sync", x1t[j][:], x1_d[tt * 128:(tt + 1) * 128, :], kx, [f"x1_{tt}"])
        for s4 in range(4):
            em.op("v", lambda e, j=j, s4=s4: e.bn_stats(bnb3[j][:, s4 * 6:(s4 + 1) * 6],
                  x1t[j][:, s4 * 512:(s4 + 1) * 512]), reads=[kx], writes=[kb])
        em.op("v", lambda e, j=j: e.bn_aggr(mv3[j][:], bnb3[j][:]), reads=[kb], writes=[km])
        em.op("v", lambda e, j=j: e.tensor_scalar(r3[j][:], mv3[j][:, 1:2], EPS, None, ALU.add),
              reads=[km], writes=[kr])
        em.op("s", lambda e, j=j: e.activation(r3[j][:], r3[j][:], AF.Ln, bias=0.0, scale=1.0),
              reads=[kr], writes=[kr])
        em.op("s", lambda e, j=j: e.activation(r3[j][:], r3[j][:], AF.Exp, bias=0.0, scale=-0.5),
              reads=[kr], writes=[kr])
        em.op("v", lambda e, j=j: e.tensor_scalar(n3[j][:], r3[j][:], mv3[j][:, 0:1], -1.0,
              ALU.mult, ALU.mult), reads=[kr, km], writes=[kn])
        em.op("v", lambda e, j=j: e.tensor_scalar(xth[j][:], x1t[j][:], r3[j][:], n3[j][:],
              ALU.mult, ALU.add), reads=[kx, kr, kn], writes=[f"xth{j}"])
        em.op("v", lambda e, j=j: e.tensor_scalar(x1t[j][:], x1t[j][:], r3[j][:], n3[j][:],
              ALU.mult, ALU.add), reads=[kx, kr, kn], writes=[kx])
        em.op("v", lambda e, j=j: e.tensor_tensor(xtl[j][:], x1t[j][:], xth[j][:], ALU.subtract),
              reads=[kx, f"xth{j}"], writes=[f"xtl{j}"])
        for half in range(2):
            pt, ptkey = next_pt()
            for kk in range(8):
                k = half * 8 + kk
                em.mm(lambda e, j=j, k=k, kk=kk, pt=pt: e.transpose(
                    pt[:, kk * 128:(kk + 1) * 128], xth[j][:, k * 128:(k + 1) * 128], eye_b[:]),
                    ptkey, [f"xth{j}", "eye_b"], last=(kk == 7))
            em.op("v", lambda e, tt=tt, half=half, pt=pt: e.tensor_copy(
                xt_fm.ap().rearrange("p (k t) -> p k t", k=NKD)[:, half * 8:(half + 1) * 8,
                                                               tt * 128:(tt + 1) * 128],
                pt.ap().rearrange("p (k t) -> p k t", k=8)),
                reads=[ptkey], writes=["xt_fm"])
        for half in range(2):
            pt, ptkey = next_pt()
            for kk in range(8):
                k = half * 8 + kk
                em.mm(lambda e, j=j, k=k, kk=kk, pt=pt: e.transpose(
                    pt[:, kk * 128:(kk + 1) * 128], xtl[j][:, k * 128:(k + 1) * 128], eye_b[:]),
                    ptkey, [f"xtl{j}", "eye_b"], last=(kk == 7))
            em.op("v", lambda e, half=half, pt=pt: e.tensor_copy(
                lofm.ap().rearrange("p (k t) -> p k t", k=NKD)[:, half * 8:(half + 1) * 8, :],
                pt.ap().rearrange("p (k t) -> p k t", k=8)),
                reads=[ptkey], writes=["lofm"])
        pp, ppk = next_p()
        for k in range(NKD):
            em.mm(lambda e, pp=pp, k=k, tt=tt: e.matmul(pp[:, 0:E],
                  xt_fm[:, k * TC + tt * 128: k * TC + (tt + 1) * 128],
                  gw_sb[:, k * E:(k + 1) * E], start=(k == 0), stop=False),
                  ppk, ["xt_fm", "gw_sb"], last=False)
        for k in range(NKD):
            em.mm(lambda e, pp=pp, k=k: e.matmul(pp[:, 0:E],
                  lofm[:, k * 128:(k + 1) * 128],
                  gw_sb[:, k * E:(k + 1) * E], start=False, stop=(k == NKD - 1)),
                  ppk, ["lofm", "gw_sb"], last=(k == NKD - 1))
        em.op("v", lambda e, pp=pp: e.tensor_copy(lg[:], pp[:, 0:E]), reads=[ppk], writes=["lg"])
        em.op("v", lambda e: e.tensor_reduce(rcol[:], lg[:], AX.X, ALU.max),
              reads=["lg"], writes=["rcol"])
        em.op("v", lambda e: e.tensor_scalar(rcol[:], rcol[:], -1.0, None, ALU.mult),
              reads=["rcol"], writes=["rcol"])
        em.op("s", lambda e: e.activation(lg[:], lg[:], AF.Exp, bias=rcol[:], scale=1.0),
              reads=["lg", "rcol"], writes=["lg"])
        em.op("v", lambda e: e.tensor_reduce(rcol[:], lg[:], AX.X, ALU.add),
              reads=["lg"], writes=["rcol"])
        em.op("v", lambda e: e.reciprocal(rcol[:], rcol[:]), reads=["rcol"], writes=["rcol"])
        em.op("v", lambda e: e.tensor_scalar(lg[:], lg[:], rcol[:], None, ALU.mult),
              reads=["lg", "rcol"], writes=["lg"])
        em.op("v", lambda e: e.tensor_tensor(e8a[:], lg[:], gb_bc[:], ALU.add),
              reads=["lg", "gb_bc"], writes=["e8a"])
        em.op("v", lambda e: e.tensor_reduce(rcol[:], e8a[:], AX.X, ALU.max),
              reads=["e8a"], writes=["rcol"])
        em.op("v", lambda e: e.tensor_scalar(e8b[:], e8a[:], rcol[:], None, ALU.is_equal),
              reads=["e8a", "rcol"], writes=["e8b"])
        em.op("v", lambda e: e.tensor_tensor(e8b[:], e8b[:], id_bc[:], ALU.mult),
              reads=["e8b", "id_bc"], writes=["e8b"])
        em.op("v", lambda e: e.tensor_reduce(r2col[:], e8b[:], AX.X, ALU.max),
              reads=["e8b"], writes=["r2col"])
        em.op("v", lambda e: e.tensor_scalar(e8b[:], e8b[:], r2col[:], None, ALU.is_equal),
              reads=["e8b", "r2col"], writes=["e8b"])
        em.op("v", lambda e: e.scalar_tensor_tensor(e8a[:], e8b[:], -1e9, e8a[:],
              ALU.mult, ALU.add), reads=["e8b", "e8a"], writes=["e8a"])
        em.op("v", lambda e: e.tensor_reduce(rcol[:], e8a[:], AX.X, ALU.max),
              reads=["e8a"], writes=["rcol"])
        em.op("v", lambda e: e.tensor_scalar(e8a[:], e8a[:], rcol[:], None, ALU.is_equal),
              reads=["e8a", "rcol"], writes=["e8a"])
        em.op("v", lambda e: e.tensor_tensor(e8a[:], e8a[:], id_bc[:], ALU.mult),
              reads=["e8a", "id_bc"], writes=["e8a"])
        em.op("v", lambda e: e.tensor_reduce(r2col[:], e8a[:], AX.X, ALU.max),
              reads=["e8a"], writes=["r2col"])
        em.op("v", lambda e: e.tensor_scalar(e8a[:], e8a[:], r2col[:], None, ALU.is_equal),
              reads=["e8a", "r2col"], writes=["e8a"])
        em.op("v", lambda e: e.tensor_tensor(e8a[:], e8a[:], e8b[:], ALU.add),
              reads=["e8a", "e8b"], writes=["e8a"])
        em.op("v", lambda e, tt=tt: e.tensor_tensor(dw_sb[:, tt * E:(tt + 1) * E], lg[:],
              e8a[:], ALU.mult), reads=["lg", "e8a"], writes=["dw_sb"])

    for ei in range(E + 1):
        shared = (ei == 0)
        e_idx = ei - 1
        for ik in range(NKI):
            jj = ik % 2
            if shared:
                w1src = ws1t[:, ik * 128:(ik + 1) * 128]
                w3src = ws3t[:, ik * 128:(ik + 1) * 128]
            else:
                w1src = w1t[e_idx * D:(e_idx + 1) * D, ik * 128:(ik + 1) * 128]
                w3src = w3t[e_idx * D:(e_idx + 1) * D, ik * 128:(ik + 1) * 128]
            em.dma("gp", w1sb[jj].ap().rearrange("p (k c) -> p k c", k=NKD),
                   w1src.rearrange("(k p) c -> p k c", p=128), f"w1sb{jj}")
            em.dma("gp", w3sb[jj].ap().rearrange("p (k c) -> p k c", k=NKD),
                   w3src.rearrange("(k p) c -> p k c", p=128), f"w3sb{jj}")
            for piece in range(4):
                pa, pak = next_p()
                pb, pbk = next_p()
                for k in range(NKD):
                    em.mm(lambda e, pa=pa, k=k, jj=jj, piece=piece: e.matmul(
                        pa[:], w1sb[jj][:, k * 128:(k + 1) * 128],
                        xt_fm[:, k * TC + piece * 512: k * TC + (piece + 1) * 512],
                        start=(k == 0), stop=(k == NKD - 1)),
                        pak, [f"w1sb{jj}", "xt_fm"], last=(k == NKD - 1))
                for k in range(NKD):
                    em.mm(lambda e, pb=pb, k=k, jj=jj, piece=piece: e.matmul(
                        pb[:], w3sb[jj][:, k * 128:(k + 1) * 128],
                        xt_fm[:, k * TC + piece * 512: k * TC + (piece + 1) * 512],
                        start=(k == 0), stop=(k == NKD - 1)),
                        pbk, [f"w3sb{jj}", "xt_fm"], last=(k == NKD - 1))
                sj = piece % 4
                em.op("s", lambda e, sj=sj, pa=pa: e.activation(stg[sj][:], pa[:], AF.Silu,
                      bias=0.0, scale=1.0), reads=[pak], writes=[f"stg{sj}"])
                em.op("v", lambda e, sj=sj, pb=pb, ik=ik, piece=piece: e.tensor_tensor(
                    h_sb[:, ik * TC + piece * 512: ik * TC + (piece + 1) * 512],
                    stg[sj][:], pb[:], ALU.mult),
                    reads=[f"stg{sj}", pbk], writes=["h_sb"])
        for dpiece in range(4):
            jj = dpiece % 2
            w2src = (ws2t if shared else w2t[e_idx * I_FF:(e_idx + 1) * I_FF, :])
            em.dma("gp", w2sb[jj].ap().rearrange("p (k c) -> p k c", k=NKI),
                   w2src[:, dpiece * 512:(dpiece + 1) * 512].rearrange("(k p) c -> p k c", p=128),
                   f"w2sb{jj}")
            for tt in range(16):
                pp, ppk = next_p()
                for ik in range(NKI):
                    em.mm(lambda e, pp=pp, ik=ik, jj=jj, tt=tt: e.matmul(
                        pp[:], h_sb[:, ik * TC + tt * 128: ik * TC + (tt + 1) * 128],
                        w2sb[jj][:, ik * 512:(ik + 1) * 512],
                        start=(ik == 0), stop=(ik == NKI - 1)),
                        ppk, ["h_sb", f"w2sb{jj}"], last=(ik == NKI - 1))
                sj = tt % 4
                if shared:
                    em.op("v", lambda e, sj=sj, pp=pp: e.tensor_copy(stgb[sj][:], pp[:]),
                          reads=[ppk], writes=[f"stgb{sj}"])
                    em.dma("sync", y_d[tt * 128:(tt + 1) * 128, dpiece * 512:(dpiece + 1) * 512],
                           stgb[sj][:], f"y_d_{tt}_{dpiece}", [f"stgb{sj}"])
                else:
                    em.op("v", lambda e, sj=sj, pp=pp, e_idx=e_idx, tt=tt: e.tensor_scalar(
                        stgb[sj][:], pp[:], dw_sb[:, tt * E + e_idx: tt * E + e_idx + 1], None,
                        ALU.mult), reads=[ppk, "dw_sb"], writes=[f"stgb{sj}"])
                    em.dma("gp", y_d[tt * 128:(tt + 1) * 128, dpiece * 512:(dpiece + 1) * 512],
                           stgb[sj][:], f"y_d_{tt}_{dpiece}", [f"stgb{sj}"], accum_op=ALU.add)

    for tt in range(16):
        j = tt % 2
        kx, kb, km, kr, kn = f"x1t{j}", f"bnb3{j}", f"mv3{j}", f"r3{j}", f"n3{j}"
        em.dma("sync", outx[j][:], y_d[tt * 128:(tt + 1) * 128, :], f"outx{j}", [f"y_d_{tt}_{dp}" for dp in range(4)])
        for s4 in range(4):
            em.op("v", lambda e, j=j, s4=s4: e.bn_stats(bnb3[j][:, s4 * 6:(s4 + 1) * 6],
                  outx[j][:, s4 * 512:(s4 + 1) * 512]), reads=[f"outx{j}"], writes=[kb])
        em.op("v", lambda e, j=j: e.bn_aggr(mv3[j][:], bnb3[j][:]), reads=[kb], writes=[km])
        em.op("v", lambda e, j=j: e.tensor_scalar(r3[j][:], mv3[j][:, 1:2], EPS, None, ALU.add),
              reads=[km], writes=[kr])
        em.op("s", lambda e, j=j: e.activation(r3[j][:], r3[j][:], AF.Ln, bias=0.0, scale=1.0),
              reads=[kr], writes=[kr])
        em.op("s", lambda e, j=j: e.activation(r3[j][:], r3[j][:], AF.Exp, bias=0.0, scale=-0.5),
              reads=[kr], writes=[kr])
        em.op("v", lambda e, j=j: e.tensor_scalar(n3[j][:], r3[j][:], mv3[j][:, 0:1], -1.0,
              ALU.mult, ALU.mult), reads=[kr, km], writes=[kn])
        em.dma("sync", x1t[j][:], x1_d[tt * 128:(tt + 1) * 128, :], kx, [f"x1_{tt}"])
        em.op("v", lambda e, j=j: e.tensor_scalar(x1t[j][:], x1t[j][:], n3[j][:], None, ALU.add),
              reads=[kx, kn], writes=[kx])
        em.op("v", lambda e, j=j: e.scalar_tensor_tensor(outx[j][:], outx[j][:], r3[j][:],
              x1t[j][:], ALU.mult, ALU.add), reads=[f"outx{j}", kr, kx], writes=[f"outx{j}"])
        em.dma("sync", out_d[tt * 128:(tt + 1) * 128, :], outx[j][:], f"out_{tt}", [f"outx{j}"])

    pop_scope()
    _final_wait(nc, em)
    return nc


def _final_wait(nc, em):
    for q in ("sync", "gp"):
        for i, sem in enumerate(em.dma_pools[q]):
            u = em.dma_uses[q][i]
            if u:
                em._wait("sync", (sem, u * 16, f"d_{q}{i}"))
    for c in em.COMPUTE:
        if em.cnt[c]:
            em._wait("sync", (em.sem[c], em.cnt[c], f"c_{c}"))
    em.finalize()


def host_inputs(inputs):
    x = np.asarray(inputs["L0_attn_hidden_states"])[0]
    ipW = np.asarray(inputs["L0_attn_in_proj_w"])
    conv_w = np.asarray(inputs["L0_attn_conv_w"])
    conv_b = np.asarray(inputs["L0_attn_conv_b"])
    A_log = np.asarray(inputs["L0_attn_A_log"])
    dt_bias = np.asarray(inputs["L0_attn_dt_bias"])
    Dp = np.asarray(inputs["L0_attn_D"])
    norm_w = np.asarray(inputs["L0_attn_norm_w"])
    out_w = np.asarray(inputs["L0_attn_out_proj_w"])
    gate_w = np.asarray(inputs["L0_ffn_gate_w"])
    gate_b = np.asarray(inputs["L0_ffn_gate_b"])
    W1 = np.asarray(inputs["L0_ffn_W1"])
    W2 = np.asarray(inputs["L0_ffn_W2"])
    W3 = np.asarray(inputs["L0_ffn_W3"])
    Ws1 = np.asarray(inputs["L0_ffn_Ws1"])
    Ws2 = np.asarray(inputs["L0_ffn_Ws2"])
    Ws3 = np.asarray(inputs["L0_ffn_Ws3"])

    shared = {}
    shared["ipWT"] = np.ascontiguousarray(ipW.T)
    cd = np.zeros((NO_HBC, CONVK, 128, 128), np.float32)
    cw = conv_w[:, 0, :]
    ar = np.arange(128)
    for j in range(NO_HBC):
        for k in range(CONVK):
            cd[j, k, ar, ar] = cw[j * 128:(j + 1) * 128, k]
    shared["convdiag"] = cd.reshape(NO_HBC * CONVK * 128, 128)
    shared["convb"] = np.ascontiguousarray(conv_b.reshape(NO_HBC, 128))
    shared["alog"] = np.ascontiguousarray(A_log.reshape(32, 1))
    shared["dtbias"] = np.ascontiguousarray(dt_bias.reshape(32, 1))
    shared["drep"] = np.ascontiguousarray(np.repeat(Dp, HD).reshape(1, INTER))
    shared["normcols"] = np.ascontiguousarray(norm_w.reshape(32, 128).T)
    shared["opWT"] = np.ascontiguousarray(out_w.T)
    shared["gateWT"] = np.ascontiguousarray(gate_w.T)
    shared["gatebB"] = np.ascontiguousarray(gate_b.reshape(1, E))
    shared["idxdB"] = np.ascontiguousarray((E - np.arange(E, dtype=np.float32)).reshape(1, E))
    shared["w1t"] = np.ascontiguousarray(W1.transpose(0, 2, 1)).reshape(E * D, I_FF)
    shared["w3t"] = np.ascontiguousarray(W3.transpose(0, 2, 1)).reshape(E * D, I_FF)
    shared["w2t"] = np.ascontiguousarray(W2.transpose(0, 2, 1)).reshape(E * I_FF, D)
    shared["ws1t"] = np.ascontiguousarray(Ws1.T)
    shared["ws3t"] = np.ascontiguousarray(Ws3.T)
    shared["ws2t"] = np.ascontiguousarray(Ws2.T)
    m01 = np.zeros((2, 128, CHUNK), np.float32)
    for st in range(2):
        for s in range(128):
            m01[st, s, st * 128 + s:] = 1.0
    shared["mask01"] = m01.reshape(2 * 128, CHUNK)
    shared["eye"] = np.eye(128, dtype=np.float32)

    in_maps = []
    for c in range(NCORES):
        m = dict(shared)
        halo = np.zeros((3, D), np.float32) if c == 0 else x[c * TC - 3:c * TC]
        m["x"] = np.ascontiguousarray(
            np.concatenate([np.zeros((125, D), np.float32), halo, x[c * TC:(c + 1) * TC]], 0))
        cm = np.zeros((32, E), np.float32)
        nb = np.full((32, E), -30000.0, np.float32)
        oh = np.zeros((32, E), np.float32)
        cm[:, :c] = 1.0
        nb[:, :c] = 0.0
        if c > 0:
            oh[:, c - 1] = 1.0
        m["cmaskB"], m["negB"], m["onehotB"] = cm, nb, oh
        in_maps.append(m)
    return in_maps


def kernel(**inputs):
    in_maps = host_inputs(inputs)
    nc = build_nc("full")
    res = run_bass_kernel_spmd(nc, in_maps, core_ids=list(range(NCORES)))
    out = np.concatenate([r["out"] for r in res.results], axis=0)
    return out.reshape(1, SEQ, D).astype(np.float32)


# version 05
# speedup vs baseline: 1.1856x; 1.0125x over previous
import dataclasses
import numpy as np
import ml_dtypes
import concourse.bass as bass
import concourse.mybir as mybir
from concourse.bass_utils import run_bass_kernel_spmd

F32 = mybir.dt.float32
BF16 = mybir.dt.bfloat16
AF = mybir.ActivationFunctionType
ALU = mybir.AluOpType
AX = mybir.AxisListType
BF = ml_dtypes.bfloat16

NCORES = 8
SEQ = 16384
D = 2048
TC = SEQ // NCORES
PAD = 128
TF = TC + PAD
NKD = D // 128
INTER = 4096
HEADS = 32
HD = 128
STATE = 128
CONVK = 4
CONV_DIM = INTER + 2 * STATE
NPROJ = INTER + CONV_DIM + HEADS
NO_HBC = CONV_DIM // 128
NO_GATE = INTER // 128
CHUNK = 256
NCH = TC // CHUNK
EPS = 1e-5
E = 8
I_FF = 1024
NKI = I_FF // 128
RG = [[0, 1, 2, 3, 4, 5, 6, 7]]


def bcast_part(ap, n):
    new = [[0, n]] + [list(x) for x in ap.ap][1:]
    return dataclasses.replace(ap, ap=type(ap.ap)(new))


def bcast_prepend(ap, n):
    new = [[0, n]] + [list(x) for x in ap.ap]
    return dataclasses.replace(ap, ap=type(ap.ap)(new))


def rep_inner(ap, n):
    aps = [list(x) for x in ap.ap]
    new = aps + [[0, n]]
    return dataclasses.replace(ap, ap=type(ap.ap)(new))


def rep_free(ap, n):
    aps = [list(x) for x in ap.ap]
    new = [aps[0]] + [[0, n]] + aps[1:]
    return dataclasses.replace(ap, ap=type(ap.ap)(new))


class EM:
    COMPUTE = ("v", "s", "t", "gp")
    NDMA = 6

    def __init__(self, nc):
        self.nc = nc
        self.q = {k: [] for k in ("sync", "gp", "v", "s", "t")}
        self.sem = {k: nc.alloc_semaphore(f"sem_{k}") for k in self.COMPUTE}
        self.cnt = {k: 0 for k in self.COMPUTE}
        self.dma_pools = {q: [nc.alloc_semaphore(f"dma_{q}{i}") for i in range(self.NDMA)]
                          for q in ("sync", "gp")}
        self.dma_uses = {q: [0] * self.NDMA for q in ("sync", "gp")}
        self.dma_rr = {q: 0 for q in ("sync", "gp")}
        self.waited = {k: {} for k in self.q}
        self.writer = {}
        self.readers = {}
        self.pending_t = []
        self._segs = []

    def _wait(self, queue, rec):
        if rec is None:
            return
        sem, val, name = rec
        if val <= 0:
            return
        w = self.waited[queue]
        if w.get(name, -1) >= val:
            return
        w[name] = val
        self.q[queue].append(lambda e, s=sem, v=val: e.wait_ge(s, v))

    def _note_read(self, key, rec):
        d = self.readers.setdefault(key, {})
        old = d.get(rec[2])
        if old is None or old[1] < rec[1]:
            d[rec[2]] = rec

    def _pre(self, queue, reads, writes, pe_acc_key=None):
        for k in reads:
            self._wait(queue, self.writer.get(k))
        for k in writes:
            wrec = self.writer.get(k)
            if not (k == pe_acc_key and wrec is not None and wrec[2] == "c_t"):
                self._wait(queue, wrec)
            for rec in self.readers.get(k, {}).values():
                self._wait(queue, rec)

    def _finish(self, reads, writes, rec):
        for k in reads:
            self._note_read(k, rec)
        for k in writes:
            self.writer[k] = rec
            self.readers[k] = {}

    def op(self, queue, fn, reads=(), writes=()):
        assert queue in self.COMPUTE
        reads, writes = list(reads), list(writes)
        self._pre(queue, reads, writes)
        self.cnt[queue] += 1
        sem, val = self.sem[queue], self.cnt[queue]
        self.q[queue].append(lambda e, f=fn, s=sem, v=1: f(e).then_inc(s, v))
        self._finish(reads, writes, (sem, val, f"c_{queue}"))

    def mm(self, fn, out_key, in_keys, last=True):
        q = "t"
        in_keys = [k for k in in_keys if k is not None]
        self._pre(q, in_keys, [out_key], pe_acc_key=out_key)
        if last:
            self.cnt[q] += 1
            sem, val = self.sem[q], self.cnt[q]
            self.q[q].append(lambda e, f=fn, s=sem: f(e).then_inc(s, 1))
            rec = (sem, val, "c_t")
            self.pending_t.append((in_keys, [out_key]))
            for rds, wrs in self.pending_t:
                self._finish(rds, wrs, rec)
            self.pending_t = []
        else:
            self.q[q].append(fn)
            self.pending_t.append((in_keys, [out_key]))
            self.writer[out_key] = (self.sem[q], self.cnt[q] + 1, "c_t")

    def dma(self, queue, dst_ap, src_ap, wkey, rkeys=(), **kw):
        pool, uses = self.dma_pools[queue], self.dma_uses[queue]
        i = self.dma_rr[queue]
        self.dma_rr[queue] = (i + 1) % self.NDMA
        sem = pool[i]
        name = f"d_{queue}{i}"
        self._wait(queue, (sem, uses[i] * 16, name))
        rkeys = list(rkeys)
        self._pre(queue, rkeys, [wkey] if wkey is not None else [])
        uses[i] += 1
        val = uses[i] * 16
        self.q[queue].append(
            lambda e, d=dst_ap, s=src_ap, sm=sem, k=dict(kw): e.dma_start(d, s, **k).then_inc(sm, 16))
        self._finish(rkeys, [wkey] if wkey is not None else [], (sem, val, name))

    def collective(self, fn, rkeys, wkey):
        q = "gp"
        self._pre(q, list(rkeys), [wkey])
        self.cnt[q] += 1
        sem, val = self.sem[q], self.cnt[q]
        self.q[q].append(lambda e, f=fn, s=sem: f(e).then_inc(s, 1))
        self._finish(list(rkeys), [wkey], (sem, val, f"c_{q}"))

    def barrier(self):
        for q in ("sync", "gp"):
            for i, sem in enumerate(self.dma_pools[q]):
                u = self.dma_uses[q][i]
                if u:
                    self._wait("sync", (sem, u * 16, f"d_{q}{i}"))
        self._segs.append(self.q)
        self.q = {k: [] for k in self.q}
        for qq in self.waited:
            w = self.waited[qq]
            for c in self.COMPUTE:
                w[f"c_{c}"] = max(w.get(f"c_{c}", 0), self.cnt[c])
            for q in ("sync", "gp"):
                for i in range(self.NDMA):
                    w[f"d_{q}{i}"] = max(w.get(f"d_{q}{i}", 0), self.dma_uses[q][i] * 16)

    def finalize(self):
        self.barrier()
        for seg in self._segs:
            if all(len(v) == 0 for v in seg.values()):
                continue
            with self.nc.Block() as b:
                for qname, reg in (("sync", b.sync), ("gp", b.gpsimd), ("v", b.vector),
                                   ("s", b.scalar), ("t", b.tensor)):
                    if seg[qname]:
                        reg(lambda e, ops=tuple(seg[qname]): [f(e) for f in ops])


def build_nc(upto="full", dbg=()):
    nc = bass.Bass("TRN2", target_bir_lowering=False, debug=False)
    em = EM(nc)

    def dram(name, shape, dtype=F32):
        kind = "ExternalOutput" if name in dbg else "Internal"
        return nc.dram_tensor(name, shape, dtype, kind=kind).ap()

    def inp(name, shape, dtype=F32):
        return nc.dram_tensor(name, shape, dtype, kind="ExternalInput").ap()

    def sb(name, shape, dtype=F32):
        return nc.alloc_sbuf_tensor(name, shape, dtype)

    scopes = []

    def push_scope():
        scopes.append([])

    def sbp(name, shape, dtype=F32):
        g = nc.sbuf_tensor(name, shape, dtype)
        t = g.__enter__()
        scopes[-1].append(g)
        return t

    def pop_scope():
        em.barrier()
        for g in reversed(scopes.pop()):
            g.__exit__(None, None, None)

    x_in = inp("x", [TF, D])
    ipWT = inp("ipWT", [D, NPROJ])
    convdiag = inp("convdiag", [NO_HBC * CONVK * 128, 128])
    convb = inp("convb", [NO_HBC, 128])
    alog = inp("alog", [32, 1])
    dtbias = inp("dtbias", [32, 1])
    ddiag = inp("ddiag", [HEADS * 128, 128])
    normcols = inp("normcols", [128, 32])
    opWT = inp("opWT", [INTER, D])
    gateWT = inp("gateWT", [D, E])
    gatebB = inp("gatebB", [1, E])
    idxdB = inp("idxdB", [1, E])
    w1t = inp("w1t", [E * D, I_FF])
    w3t = inp("w3t", [E * D, I_FF])
    w2t = inp("w2t", [E * I_FF, D])
    ws1t = inp("ws1t", [D, I_FF])
    ws3t = inp("ws3t", [D, I_FF])
    ws2t = inp("ws2t", [I_FF, D])
    mask01 = inp("mask01", [2 * 128, CHUNK])
    cmaskB = inp("cmaskB", [32, E])
    negB = inp("negB", [32, E])
    onehotB = inp("onehotB", [32, E])
    eye_in = inp("eye", [128, 128])
    trilM_in = inp("trilM_in", [128, 128])

    out_d = nc.dram_tensor("out", [TC, D], F32, kind="ExternalOutput").ap()
    xs_d = dram("xs_d", [INTER, TC], BF16)
    u_d = dram("u_d", [128, NKD * TF], BF16)
    ypart_d = dram("ypart_d", [TC, INTER])
    acum_d = dram("acum_d", [32, TC])
    x1_d = dram("x1_d", [TC, D])
    gate_s_d = dram("gate_s_d", [TC, INTER], BF16)
    opWTb_d = dram("opWTb_d", [INTER, D], BF16)
    y_d = dram("y_d", [TC, D])
    sstates_d = dram("sstates_d", [NCH * 128, HEADS * HD])
    dvec_d = dram("dvec_d", [NCH, 32])
    cc_in = dram("cc_in", [129, HEADS * HD])
    cc_out = dram("cc_out", [NCORES * 129, HEADS * HD])
    wflat_d = dram("wflat_d", [1, NCORES * 32])
    dbg_d = dram("dbg_d", [512, max(TC, INTER)]) if "dbg_d" in dbg else None

    P = [nc.alloc_psum_tensor(f"P{i}", [128, 512], F32) for i in range(6)]
    PT = [nc.alloc_psum_tensor(f"PT{i}", [128, 1024], BF16) for i in range(2)]
    pk = [f"P{i}" for i in range(6)]
    ptk = [f"PT{i}" for i in range(2)]
    prr = [0]

    def next_p():
        i = prr[0] % 6
        prr[0] += 1
        return P[i], pk[i]

    ptrr = [0]

    def next_pt():
        i = ptrr[0] % 2
        ptrr[0] += 1
        return PT[i], ptk[i]

    eye_f = sb("eye_f", [128, 128], F32)
    trilM = sb("trilM", [128, 128], F32)
    onesM = sb("onesM", [128, 128], F32)
    eye_b = sb("eye_b", [128, 128], BF16)
    ones_f = sb("ones_f", [1, 128], F32)
    em.dma("sync", eye_f[:], eye_in, "eye_f")
    em.dma("sync", trilM[:], trilM_in, "trilM")
    em.op("v", lambda e: e.tensor_copy(eye_b[:], eye_f[:]), reads=["eye_f"], writes=["eye_b"])
    em.op("v", lambda e: e.memset(ones_f[:], 1.0), writes=["ones_f"])
    em.op("v", lambda e: e.memset(onesM[:], 1.0), writes=["onesM"])
    normc_sb = sb("normc_sb", [128, 32])
    em.dma("sync", normc_sb[:], normcols, "normc_sb")
    push_scope()
    C_fm = sbp("C_fm", [128, TC], BF16)
    push_scope()
    B_fm = sbp("B_fm", [128, TC], BF16)
    dtv = sbp("dtv", [32, TC])
    ad = sbp("ad", [32, TC])
    alog_s = sbp("alog_s", [32, 1])
    dtb_s = sbp("dtb_s", [32, 1])
    acol = sbp("acol", [32, 1])
    ltot = sbp("ltot", [32, 1])

    push_scope()
    u_fm = sbp("u_fm", [128, NKD * TF], BF16)
    NT = TF // 128
    dtraw = sbp("dtraw", [32, TC])
    spt = sbp("spt", [32, TC])
    xt = [sbp(f"xt{i}", [128, D]) for i in range(2)]
    bnb = [sbp(f"bnb{i}", [128, 24]) for i in range(2)]
    mv = [sbp(f"mv{i}", [128, 2]) for i in range(2)]
    rstd = [sbp(f"rstd{i}", [128, 1]) for i in range(2)]
    negmr = [sbp(f"negmr{i}", [128, 1]) for i in range(2)]
    ubf = [sbp(f"ubf{i}", [128, D], BF16) for i in range(2)]

    for i in range(NT):
        j = i % 2
        kx, kb, km, kr, kn, ku = (f"xt{j}", f"bnb{j}", f"mv{j}", f"rstd{j}", f"negmr{j}", f"ubf{j}")
        em.dma("sync", xt[j][:], x_in[i * 128:(i + 1) * 128, :], kx)
        for s4 in range(4):
            em.op("v", lambda e, j=j, s4=s4: e.bn_stats(bnb[j][:, s4 * 6:(s4 + 1) * 6],
                                                        xt[j][:, s4 * 512:(s4 + 1) * 512]),
                  reads=[kx], writes=[kb])
        em.op("v", lambda e, j=j: e.bn_aggr(mv[j][:], bnb[j][:]), reads=[kb], writes=[km])
        em.op("v", lambda e, j=j: e.tensor_scalar(rstd[j][:], mv[j][:, 1:2], EPS, None, ALU.add),
              reads=[km], writes=[kr])
        em.op("s", lambda e, j=j: e.activation(rstd[j][:], rstd[j][:], AF.Ln, bias=0.0, scale=1.0),
              reads=[kr], writes=[kr])
        em.op("s", lambda e, j=j: e.activation(rstd[j][:], rstd[j][:], AF.Exp, bias=0.0, scale=-0.5),
              reads=[kr], writes=[kr])
        em.op("v", lambda e, j=j: e.tensor_scalar(negmr[j][:], rstd[j][:], mv[j][:, 0:1], -1.0,
                                                  ALU.mult, ALU.mult),
              reads=[kr, km], writes=[kn])
        em.op("v", lambda e, j=j: e.tensor_scalar(ubf[j][:], xt[j][:], rstd[j][:], negmr[j][:],
                                                  ALU.mult, ALU.add),
              reads=[kx, kr, kn], writes=[ku])
        for half in range(2):
            pt, ptkey = next_pt()
            for kk in range(8):
                k = half * 8 + kk
                em.mm(lambda e, j=j, k=k, kk=kk, pt=pt: e.transpose(
                    pt[:, kk * 128:(kk + 1) * 128], ubf[j][:, k * 128:(k + 1) * 128], eye_b[:]),
                    ptkey, [ku, "eye_b"], last=(kk == 7))
            em.op("v", lambda e, i=i, half=half, pt=pt: e.tensor_copy(
                u_fm.ap().rearrange("p (k t) -> p k t", k=NKD)[:, half * 8:(half + 1) * 8,
                                                              i * 128:(i + 1) * 128],
                pt.ap().rearrange("p (k t) -> p k t", k=8)),
                reads=[ptkey], writes=["u_fm"])

    for f in range(32):
        jj = f % 2
        em.dma("sync", xt[jj][:], opWT[f * 128:(f + 1) * 128, :], f"xt{jj}")
        em.op("v", lambda e, jj=jj, f=f: e.tensor_scalar(ubf[jj][:], xt[jj][:],
              normc_sb[:, f:f + 1], None, ALU.mult),
              reads=[f"xt{jj}", "normc_sb"], writes=[f"ubf{jj}"])
        em.dma("sync", opWTb_d[f * 128:(f + 1) * 128, :], ubf[jj][:], f"opwtb_{f}", [f"ubf{jj}"])

    if upto == "A1ln":
        em.dma("sync", u_d, u_fm[:], "u_d", ["u_fm"])
        _final_wait(nc, em)
        return nc
    wsb = [sbp(f"wsb{i}", [128, NKD * 128], BF16) for i in range(2)]
    projbf = [sbp(f"projbf{i}", [128, TF], BF16) for i in range(2)]
    cdiag = [sbp(f"cdiag{i}", [128, CONVK * 128], BF16) for i in range(2)]
    cbcol = [sbp(f"cbcol{i}", [128, 1]) for i in range(2)]
    hbco = [sbp(f"hbco{i}", [128, TC], BF16) for i in range(2)]
    PIECES = [(0, 512), (512, 512), (1024, 512), (1536, 512), (2048, 128)]

    ipWT_r = ipWT.rearrange("(k p) c -> p k c", p=128)

    for oj in range(NO_HBC + 1):
        jj = oj % 2
        ocol = NO_GATE * 128 + oj * 128
        wk = f"wsb{jj}"
        ncols = 128 if oj < NO_HBC else 32
        em.dma("gp", wsb[jj].ap().rearrange("p (k c) -> p k c", k=NKD)[:, :, 0:ncols],
               ipWT_r[:, :, ocol:ocol + ncols], wk)
        pj = f"projbf{jj}"
        if oj < NO_HBC:
            pps = [next_p() for _ in range(5)]
            for k in range(NKD):
                for pi, (poff, plen) in enumerate(PIECES):
                    pp, ppk = pps[pi]
                    em.mm(lambda e, pp=pp, plen=plen, k=k, jj=jj, poff=poff: e.matmul(
                        pp[:, 0:plen], wsb[jj][:, k * 128:(k + 1) * 128],
                        u_fm[:, k * TF + poff: k * TF + poff + plen],
                        start=(k == 0), stop=(k == NKD - 1)),
                        ppk, [wk, "u_fm"], last=(k == NKD - 1 and pi == 4))
            for pi, (poff, plen) in enumerate(PIECES):
                pp, ppk = pps[pi]
                eng = "v" if pi % 2 == 0 else "s"
                if eng == "v":
                    em.op("v", lambda e, jj=jj, poff=poff, plen=plen, pp=pp: e.tensor_copy(
                        projbf[jj][:, poff:poff + plen], pp[:, 0:plen]),
                        reads=[ppk], writes=[pj])
                else:
                    em.op("s", lambda e, jj=jj, poff=poff, plen=plen, pp=pp: e.activation(
                        projbf[jj][:, poff:poff + plen], pp[:, 0:plen], AF.Copy, bias=0.0, scale=1.0),
                        reads=[ppk], writes=[pj])
            ck = f"cdiag{jj}"
            cbk = f"cbcol{jj}"
            em.dma("gp", cdiag[jj][:], convdiag[oj * CONVK * 128:(oj + 1) * CONVK * 128, :]
                   .rearrange("(k p) c -> p k c", p=128), ck)
            em.dma("sync", cbcol[jj][:], convb[oj:oj + 1, :].rearrange("x c -> c x"), cbk)
            hk = f"hbco{jj}"
            for piece in range(4):
                pp, ppk = next_p()
                for k in range(CONVK):
                    em.mm(lambda e, pp=pp, k=k, jj=jj, piece=piece: e.matmul(
                        pp[:], cdiag[jj][:, k * 128:(k + 1) * 128],
                        projbf[jj][:, 125 + k + piece * 512: 125 + k + piece * 512 + 512],
                        start=(k == 0), stop=(k == CONVK - 1)),
                        ppk, [ck, pj], last=(k == CONVK - 1))
                if oj < NO_GATE:
                    em.op("s", lambda e, jj=jj, piece=piece, pp=pp: e.activation(
                        hbco[jj][:, piece * 512:(piece + 1) * 512], pp[:], AF.Silu,
                        bias=cbcol[jj][:], scale=1.0), reads=[ppk, cbk], writes=[hk])
                else:
                    dstt = B_fm if oj == NO_GATE else C_fm
                    dk = "B_fm" if oj == NO_GATE else "C_fm"
                    em.op("s", lambda e, dstt=dstt, piece=piece, pp=pp, jj=jj: e.activation(
                        dstt[:, piece * 512:(piece + 1) * 512], pp[:], AF.Silu,
                        bias=cbcol[jj][:], scale=1.0), reads=[ppk, cbk], writes=[dk])
            if oj < NO_GATE:
                em.dma("sync", xs_d[oj * 128:(oj + 1) * 128, :], hbco[jj][:], f"xs_d_{oj}", [hk])
        else:
            for piece in range(4):
                pp, ppk = next_p()
                for k in range(NKD):
                    em.mm(lambda e, pp=pp, k=k, jj=jj, piece=piece: e.matmul(
                        pp[0:32, :], wsb[jj][:, k * 128: k * 128 + 32],
                        u_fm[:, k * TF + 128 + piece * 512: k * TF + 128 + piece * 512 + 512],
                        start=(k == 0), stop=(k == NKD - 1)),
                        ppk, [wk, "u_fm"], last=(k == NKD - 1))
                em.op("v", lambda e, piece=piece, pp=pp: e.tensor_copy(
                    dtraw[:, piece * 512:(piece + 1) * 512], pp[0:32, :]),
                    reads=[ppk], writes=["dtraw"])


    em.dma("sync", alog_s[:], alog, "alog_s")
    em.dma("sync", dtb_s[:], dtbias, "dtb_s")
    em.op("s", lambda e: e.activation(acol[:], alog_s[:], AF.Exp, bias=0.0, scale=1.0),
          reads=["alog_s"], writes=["acol"])
    em.op("v", lambda e: e.tensor_scalar(acol[:], acol[:], -1.0, None, ALU.mult),
          reads=["acol"], writes=["acol"])
    em.op("v", lambda e: e.tensor_scalar(dtv[:], dtraw[:], dtb_s[:], None, ALU.add),
          reads=["dtraw", "dtb_s"], writes=["dtv"])
    em.op("s", lambda e: e.activation(spt[:], dtv[:], AF.Abs, bias=0.0, scale=1.0),
          reads=["dtv"], writes=["spt"])
    em.op("s", lambda e: e.activation(spt[:], spt[:], AF.Exp, bias=0.0, scale=-1.0),
          reads=["spt"], writes=["spt"])
    em.op("v", lambda e: e.tensor_scalar(spt[:], spt[:], 1.0, None, ALU.add),
          reads=["spt"], writes=["spt"])
    em.op("s", lambda e: e.activation(spt[:], spt[:], AF.Ln, bias=0.0, scale=1.0),
          reads=["spt"], writes=["spt"])
    em.op("v", lambda e: e.tensor_scalar(dtv[:], dtv[:], 0.0, None, ALU.max),
          reads=["dtv"], writes=["dtv"])
    em.op("v", lambda e: e.tensor_tensor(dtv[:], dtv[:], spt[:], ALU.add),
          reads=["dtv", "spt"], writes=["dtv"])
    em.op("v", lambda e: e.tensor_scalar(dtv[:], dtv[:], 100.0, None, ALU.min),
          reads=["dtv"], writes=["dtv"])
    em.op("v", lambda e: e.tensor_scalar(ad[:], dtv[:], acol[:], None, ALU.mult),
          reads=["dtv", "acol"], writes=["ad"])

    if upto == "A1":
        em.dma("sync", dbg_d[0:32, 0:TC], dtv[:], "dbg_d", ["dtv"])
        em.dma("sync", dbg_d[32:64, 0:TC], ad[:], "dbg_d", ["ad"])
        em.dma("gp", dbg_d[64:192, 0:TC], B_fm[:], "dbg_d", ["B_fm"])
        em.dma("gp", dbg_d[192:320, 0:TC], C_fm[:], "dbg_d", ["C_fm"])
        _final_wait(nc, em)
        return nc

    wgbuf = [sbp(f"wgbuf{i}", [128, NKD * 512], BF16) for i in range(2)]
    gstg = [sbp(f"gstg{i}", [128, 512], BF16) for i in range(2)]
    for piece in range(8):
        jj = piece % 2
        wk = f"wgbuf{jj}"
        em.dma("gp", wgbuf[jj].ap().rearrange("p (k c) -> p k c", k=NKD),
               ipWT_r[:, :, piece * 512:(piece + 1) * 512], wk)
        for tt in range(16):
            pp, ppk = next_p()
            for k in range(NKD):
                em.mm(lambda e, pp=pp, k=k, jj=jj, tt=tt: e.matmul(
                    pp[:], u_fm[:, k * TF + 128 + tt * 128: k * TF + 128 + (tt + 1) * 128],
                    wgbuf[jj][:, k * 512:(k + 1) * 512],
                    start=(k == 0), stop=(k == NKD - 1)),
                    ppk, [wk, "u_fm"], last=(k == NKD - 1))
            sj = tt % 2
            em.op("s", lambda e, sj=sj, pp=pp: e.activation(gstg[sj][:], pp[:], AF.Silu,
                  bias=0.0, scale=1.0), reads=[ppk], writes=[f"gstg{sj}"])
            em.dma("sync", gate_s_d[tt * 128:(tt + 1) * 128, piece * 512:(piece + 1) * 512],
                   gstg[sj][:], f"gates_{tt}_{piece}", [f"gstg{sj}"])

    pop_scope()

    push_scope()
    mask_sb = sbp("mask_sb", [128, 2 * CHUNK])
    em.dma("sync", mask_sb.ap().rearrange("p (s c) -> p s c", s=2),
           mask01.rearrange("(s p) c -> p s c", p=128), "mask_sb")
    ddiag_sb = sbp("ddiag_sb", [128, HEADS * 128], BF16)
    em.dma("gp", ddiag_sb.ap().rearrange("p (h c) -> p h c", h=HEADS),
           ddiag.rearrange("(h p) c -> p h c", p=128), "ddiag_sb")
    adTb = [sbp(f"adT{i}", [128, 64]) for i in range(2)]
    em.op("v", lambda e: e.memset(ltot[:], 0.0), writes=["ltot"])
    acumC = [sbp(f"acumC{i}", [32, CHUNK]) for i in range(2)]
    dvecC = [sbp(f"dvecC{i}", [32, 1]) for i in range(2)]
    dsrow = [sbp(f"dsrow{i}", [32, CHUNK]) for i in range(2)]
    ddrow = [sbp(f"ddrow{i}", [32, CHUNK]) for i in range(2)]
    acumT = [[sbp(f"acumT{i}_{p}", [128, 32]) for i in range(2)] for p in range(2)]
    dtvT = [[sbp(f"dtvT{i}_{p}", [128, 32]) for i in range(2)] for p in range(2)]
    ddT = [[sbp(f"ddT{i}_{p}", [128, 32]) for i in range(2)] for p in range(2)]
    dcols = [sbp(f"dcols{i}", [128, 32]) for i in range(2)]
    rbg = [sbp(f"rbg{i}", [128, 8 * CHUNK]) for i in range(2)]
    dbg8 = [sbp(f"dbg8{i}", [128, 8 * CHUNK], BF16) for i in range(2)]
    lbg = [sbp(f"lbg{i}", [128, 8 * CHUNK], BF16) for i in range(2)]
    mbufS = [sbp(f"mbuf{i}", [128, HEADS * CHUNK], BF16) for i in range(2)]
    mbuf = [mbufS, mbufS]
    gm = [[sbp(f"gm{i}_{p}", [128, CHUNK], BF16) for i in range(2)] for p in range(2)]
    xsgS = sbp("xsg", [128, 8 * CHUNK], BF16)
    xsg = [xsgS, xsgS]
    xsTS = [sbp(f"xsT{i}", [128, HEADS * 128], BF16) for i in range(2)]
    xsT = [xsTS, xsTS]
    xdXS = [sbp(f"xdX{i}", [128, HEADS * 128], BF16) for i in range(2)]
    xdX = [xdXS, xdXS]
    btT = [sbp(f"btT{i}", [128, 2 * 128], BF16) for i in range(2)]
    state_run = sbp("state_run", [128, HEADS * 128])
    em.op("v", lambda e, t=state_run: e.memset(t[:], 0.0), writes=["state_run"])
    scsb = [sbp(f"scsb{i}", [128, 512]) for i in range(2)]
    ysbS = [sbp(f"ysb{i}", [128, INTER]) for i in range(2)]
    ysb = [ysbS, ysbS]

    def a2_chunk(c):
        c0 = c * CHUNK
        par = c % 2
        acum, dvec = acumC[par], dvecC[par]
        adT, kadT = adTb[par], f"adT{par}"
        dsr, ddr = dsrow[par], ddrow[par]
        acT, dtT, ddTT = acumT[par], dtvT[par], ddT[par]
        dcl, rbg_, dbg_, lbg_ = dcols[par], rbg[par], dbg8[par], lbg[par]
        mb, gmp, xsgp, xsTp, xdXp = mbuf[par], gm[par], xsg[par], xsT[par], xdX[par]
        btTp, scsbp, ysbp = btT[par], scsb[par], ysb[par]
        kac, kdv = f"acumC{par}", f"dvecC{par}"
        kaA, kaB = f"acA{par}", f"acB{par}"
        kdsr, kddr = f"dsrow{par}", f"ddrow{par}"
        kdcl, krbg, kdbg, klbg = f"dcols{par}", f"rbg{par}", f"dbg8{par}", f"lbg{par}"
        kxsg, kbtT, kscsb = "xsg", f"btT{par}", f"scsb{par}"
        adTp, adTk = next_p()
        for st in range(2):
            em.mm(lambda e, st=st, c0=c0, adTp=adTp: e.transpose(
                adTp[0:128, st * 32:(st + 1) * 32], ad[:, c0 + st * 128:c0 + (st + 1) * 128],
                eye_f[0:32, 0:32]), adTk, ["ad", "eye_f"], last=(st == 1))
        em.op("v", lambda e, adTp=adTp, adT=adT: e.tensor_copy(adT[:], adTp[0:128, 0:64]),
              reads=[adTk], writes=[kadT])
        acTp, acTk = next_p()
        em.mm(lambda e, acTp=acTp, adT=adT: e.matmul(acTp[:, 0:32], trilM[:], adT[:, 0:32],
              start=True, stop=True), acTk, [kadT, "trilM"], last=False)
        em.mm(lambda e, acTp=acTp, adT=adT: e.matmul(acTp[:, 32:64], onesM[:], adT[:, 0:32],
              start=True, stop=False), acTk, [kadT, "onesM"], last=False)
        em.mm(lambda e, acTp=acTp, adT=adT: e.matmul(acTp[:, 32:64], trilM[:], adT[:, 32:64],
              start=False, stop=True), acTk, [kadT, "trilM"], last=True)
        for st in range(2):
            em.op("v", lambda e, st=st, acTp=acTp, acT=acT: e.tensor_copy(
                acT[st][:], acTp[:, st * 32:(st + 1) * 32]), reads=[acTk], writes=[f"acumT{st}_{par}"])
        acRp, acRk = next_p()
        for st in range(2):
            em.mm(lambda e, st=st, acT=acT, acRp=acRp: e.transpose(
                acRp[0:32, st * 128:(st + 1) * 128], acT[st][:],
                eye_f[0:128, 0:128]), acRk, [f"acumT{st}_{par}", "eye_f"], last=(st == 1))
        em.op("v", lambda e, acRp=acRp, acum=acum: e.tensor_copy(acum[:], acRp[0:32, 0:256]),
              reads=[acRk], writes=[kac])
        em.op("v", lambda e, acum=acum: e.tensor_tensor(ltot[:], ltot[:], acum[:, 255:256], ALU.add),
              reads=["ltot", kac], writes=["ltot"])
        em.op("s", lambda e, acum=acum, dvec=dvec: e.activation(dvec[:], acum[:, 255:256], AF.Exp, bias=0.0, scale=1.0),
              reads=[kac], writes=[kdv])
        em.dma("sync", dvec_d[c:c + 1, :].rearrange("x h -> h x"), dvec[:], f"dvec_d_{c}", [kdv])
        em.dma("sync", acum_d[:, c0:c0 + CHUNK], acum[:], f"acum_d_{c}", [kac])
        em.op("v", lambda e, acum=acum, dsr=dsr: e.tensor_scalar(dsr[:], acum[:], acum[:, 255:256], -1.0,
              ALU.subtract, ALU.mult), reads=[kac], writes=[kdsr])
        em.op("s", lambda e, dsr=dsr: e.activation(dsr[:], dsr[:], AF.Exp, bias=0.0, scale=1.0),
              reads=[kdsr], writes=[kdsr])
        em.op("v", lambda e, c0=c0, ddr=ddr, dsr=dsr: e.tensor_tensor(ddr[:], dtv[:, c0:c0 + CHUNK], dsr[:], ALU.mult),
              reads=["dtv", kdsr], writes=[kddr])
        for st in range(2):
            for (srcap, srck, dstt, dstk) in (
                    (dtv[:, c0 + st * 128:c0 + (st + 1) * 128], "dtv", dtT[st], f"dtvT{st}_{par}"),
                    (ddr[:, st * 128:(st + 1) * 128], kddr, ddTT[st], f"ddT{st}_{par}")):
                pp, ppk = next_p()
                em.mm(lambda e, pp=pp, srcap=srcap: e.transpose(pp[0:128, 0:32], srcap,
                      eye_f[0:32, 0:32]), ppk, [srck, "eye_f"], last=True)
                em.op("v", lambda e, pp=pp, dstt=dstt: e.tensor_copy(dstt[:], pp[0:128, 0:32]),
                      reads=[ppk], writes=[dstk])
        em.dma("sync", dcl[:], bcast_part(dvec_d[c:c + 1, :], 128), kdcl, [f"dvec_d_{c}"])
        for st in range(2):
            pp, ppk = next_p()
            em.mm(lambda e, st=st, pp=pp, c0=c0: e.matmul(pp[:, 0:CHUNK],
                  B_fm[:, c0 + st * 128:c0 + (st + 1) * 128], C_fm[:, c0:c0 + CHUNK],
                  start=True, stop=True), ppk, ["B_fm", "C_fm"], last=True)
            em.op("v", lambda e, st=st, pp=pp, gmp=gmp: e.tensor_tensor(gmp[st][:], pp[:, 0:CHUNK],
                  mask_sb[:, st * CHUNK:(st + 1) * CHUNK], ALU.mult),
                  reads=[ppk, "mask_sb"], writes=[f"gm{st}_{par}"])
        for g in range(4):
            em.dma("sync", rbg_.ap().rearrange("p (h l) -> p h l", h=8),
                   bcast_prepend(acum_d[8 * g:8 * (g + 1), c0:c0 + CHUNK], 128), krbg, [f"acum_d_{c}"])
            for st in range(2):
                em.op("v", lambda e, st=st, g=g, dbgt=dbg_, rbgt=rbg_, acT=acT: e.tensor_tensor(
                    dbgt.ap().rearrange("p (h l) -> p h l", h=8),
                    rbgt.ap().rearrange("p (h l) -> p h l", h=8),
                    rep_inner(acT[st][:, 8 * g:8 * (g + 1)], CHUNK), ALU.subtract),
                    reads=[krbg, f"acumT{st}_{par}"], writes=[kdbg])
                em.op("s", lambda e, lbgt=lbg_, dbgt=dbg_: e.activation(lbgt[:], dbgt[:], AF.Exp, bias=0.0, scale=1.0),
                      reads=[kdbg], writes=[klbg])
                em.op("v", lambda e, lbgt=lbg_: e.tensor_scalar(lbgt[:], lbgt[:], 1.0, None, ALU.min),
                      reads=[klbg], writes=[klbg])
                em.op("v", lambda e, st=st, g=g, mb=mb, lbgt=lbg_, gmp=gmp: e.tensor_tensor(
                    mb[st].ap().rearrange("p (h l) -> p h l", h=HEADS)[:, 8 * g:8 * (g + 1), :],
                    lbgt.ap().rearrange("p (h l) -> p h l", h=8),
                    rep_free(gmp[st][:], 8), ALU.mult),
                    reads=[klbg, f"gm{st}_{par}"], writes=[f"mbuf{st}"])
        for g in range(4):
            em.dma("sync", xsgp.ap().rearrange("p (h t) -> p h t", h=8),
                   xs_d.rearrange("(h p) t -> p h t", p=128)[:, 8 * g:8 * (g + 1), c0:c0 + CHUNK],
                   kxsg, [f"xs_d_{o}" for o in range(8 * g, 8 * (g + 1))])
            for st in range(2):
                pt, ptkey = next_pt()
                for hh in range(8):
                    em.mm(lambda e, pt=pt, hh=hh, st=st, xsgp=xsgp: e.transpose(
                        pt[:, hh * 128:(hh + 1) * 128],
                        xsgp[:, hh * CHUNK + st * 128: hh * CHUNK + (st + 1) * 128], eye_b[:]),
                        ptkey, [kxsg, "eye_b"], last=(hh == 7))
                em.op("v", lambda e, st=st, g=g, pt=pt, xsTp=xsTp: e.tensor_copy(
                    xsTp[st][:, g * 1024:(g + 1) * 1024], pt[:]),
                    reads=[ptkey], writes=[f"xsT{st}"])
        for st in range(2):
            for g in range(4):
                em.op("v", lambda e, st=st, g=g, xdXp=xdXp, xsTp=xsTp, ddTT=ddTT: e.tensor_tensor(
                    xdXp[st].ap().rearrange("p (h q) -> p h q", q=128)[:, 8 * g:8 * (g + 1), :],
                    xsTp[st].ap().rearrange("p (h q) -> p h q", q=128)[:, 8 * g:8 * (g + 1), :],
                    rep_inner(ddTT[st][:, 8 * g:8 * (g + 1)], 128), ALU.mult),
                    reads=[f"xsT{st}", f"ddT{st}_{par}"], writes=[f"xdX{st}"])
        pt, ptkey = next_pt()
        for st in range(2):
            em.mm(lambda e, pt=pt, st=st, c0=c0: e.transpose(pt[:, st * 128:(st + 1) * 128],
                  B_fm[:, c0 + st * 128:c0 + (st + 1) * 128], eye_b[:]),
                  ptkey, ["B_fm", "eye_b"], last=(st == 1))
        em.op("v", lambda e, pt=pt, btTp=btTp: e.tensor_copy(btTp[:], pt[:, 0:256]),
              reads=[ptkey], writes=[kbtT])
        for g4 in range(8):
            pp, ppk = next_p()
            for q in range(4):
                h = g4 * 4 + q
                for st in range(2):
                    em.mm(lambda e, pp=pp, q=q, h=h, st=st, btTp=btTp, xdXp=xdXp: e.matmul(
                        pp[:, q * 128:(q + 1) * 128], btTp[:, st * 128:(st + 1) * 128],
                        xdXp[st][:, h * 128:(h + 1) * 128],
                        start=(q == 0 and st == 0), stop=(q == 3 and st == 1)),
                        ppk, [kbtT, f"xdX{st}"], last=(q == 3 and st == 1))
            em.op("v", lambda e, pp=pp, scsbp=scsbp: e.tensor_copy(scsbp[:], pp[:]),
                  reads=[ppk], writes=[kscsb])
            em.dma("sync", sstates_d[c * 128:(c + 1) * 128, g4 * 512:(g4 + 1) * 512],
                   scsbp[:], f"sst_{c}_{g4}", [kscsb])
            for q in range(4):
                h = g4 * 4 + q
                em.op("v", lambda e, h=h, q=q, t=state_run, dcl=dcl, scsbp=scsbp: e.scalar_tensor_tensor(
                    t[:, h * 128:(h + 1) * 128], t[:, h * 128:(h + 1) * 128],
                    dcl[:, h:h + 1], scsbp[:, q * 128:(q + 1) * 128], ALU.mult, ALU.add),
                    reads=["state_run", kdcl, kscsb], writes=["state_run"])
        for st in range(2):
            for g in range(4):
                em.op("v", lambda e, st=st, g=g, xdXp=xdXp, xsTp=xsTp, dtT=dtT: e.tensor_tensor(
                    xdXp[st].ap().rearrange("p (h q) -> p h q", q=128)[:, 8 * g:8 * (g + 1), :],
                    xsTp[st].ap().rearrange("p (h q) -> p h q", q=128)[:, 8 * g:8 * (g + 1), :],
                    rep_inner(dtT[st][:, 8 * g:8 * (g + 1)], 128), ALU.mult),
                    reads=[f"xsT{st}", f"dtvT{st}_{par}"], writes=[f"xdX{st}"])
        for lt in range(2):
            for g4 in range(8):
                pp, ppk = next_p()
                for q in range(4):
                    h = g4 * 4 + q
                    for st in range(2):
                        em.mm(lambda e, pp=pp, q=q, h=h, st=st, lt=lt, mb=mb, xdXp=xdXp: e.matmul(
                            pp[:, q * 128:(q + 1) * 128],
                            mb[st][:, h * CHUNK + lt * 128: h * CHUNK + lt * 128 + 128],
                            xdXp[st][:, h * 128:(h + 1) * 128],
                            start=(q == 0 and st == 0), stop=False),
                            ppk, [f"mbuf{st}", f"xdX{st}"], last=False)
                    em.mm(lambda e, pp=pp, q=q, h=h, lt=lt, xsTp=xsTp: e.matmul(
                        pp[:, q * 128:(q + 1) * 128], ddiag_sb[:, h * 128:(h + 1) * 128],
                        xsTp[lt][:, h * 128:(h + 1) * 128],
                        start=False, stop=(q == 3)),
                        ppk, ["ddiag_sb", f"xsT{lt}"], last=(q == 3))
                eng = "v" if g4 % 2 == 0 else "s"
                if eng == "v":
                    em.op("v", lambda e, pp=pp, lt=lt, g4=g4, ysbp=ysbp: e.tensor_copy(
                        ysbp[lt][:, g4 * 512:(g4 + 1) * 512], pp[:]),
                        reads=[ppk], writes=[f"ysb{lt}"])
                else:
                    em.op("s", lambda e, pp=pp, lt=lt, g4=g4, ysbp=ysbp: e.activation(
                        ysbp[lt][:, g4 * 512:(g4 + 1) * 512], pp[:], AF.Copy, bias=0.0, scale=1.0),
                        reads=[ppk], writes=[f"ysb{lt}"])
            em.dma("sync", ypart_d[c0 + lt * 128:c0 + (lt + 1) * 128, :], ysbp[lt][:],
                   f"ypart_{c}_{lt}", [f"ysb{lt}"])

    for c in range(NCH):
        a2_chunk(c)

    em.dma("sync", cc_in[0:128, :], state_run[:], "cc_in", ["state_run"])
    em.dma("sync", cc_in[128:129, 0:32].rearrange("x h -> h x"), ltot[:], "ltot_cc", ["ltot"])

    if upto == "A2":
        _final_wait(nc, em)
        return nc
    pop_scope()
    pop_scope()

    em.collective(lambda e: e.collective_compute(
        "AllGather", ALU.bypass, replica_groups=RG,
        ins=[cc_in.opt()], outs=[cc_out.opt()]), ["cc_in", "ltot_cc"], "cc_out")

    state_b = sbp("state_run_b", [128, HEADS * HD])

    push_scope()
    rbB = sbp("rbB", [128, 8 * CHUNK])
    ebg = sbp("ebg", [128, 8 * CHUNK], BF16)
    cpb = sbp("cpb", [128, HEADS * CHUNK], BF16)
    scast = sbp("scast", [128, HEADS * HD], BF16)
    dcolsB = sbp("dcolsB", [128, 32])
    scld = sbp("scld", [128, HEADS * HD])
    yp = [sbp(f"yp{i}", [128, INTER]) for i in range(2)]
    gload = [sbp(f"gload{i}", [128, INTER], BF16) for i in range(2)]
    sumsq = sbp("sumsq", [128, 1])
    rstdB = sbp("rstdB", [128, 4])
    yfmg = sbp("yfmg", [128, 32 * 512], BF16)
    attn_sb = sbp("attn_sb", [128, 4 * D])
    wop = [sbp(f"wop{i}", [128, 512], BF16) for i in range(3)]
    bnb2 = sbp("bnb2", [128, 24])
    mv2 = sbp("mv2", [128, 2])
    c1 = sbp("c1", [128, 1])
    c2 = sbp("c2", [128, 1])
    c3 = sbp("c3", [128, 1])
    xres = sbp("xres", [128, D])
    x1o = sbp("x1o", [128, D])
    ybf = scast

    ltg = sbp("ltg", [32, 16])
    ltg2 = sbp("ltg2", [32, 16])
    cmk_sb = sbp("cmk_sb", [32, E])
    ngk_sb = sbp("ngk_sb", [32, E])
    ohk_sb = sbp("ohk_sb", [32, E])
    t8 = sbp("t8", [32, E])
    cum1 = sbp("cum1", [32, 1])
    wmat = sbp("wmat", [32, 8])
    wcols = sbp("wcols", [128, 256])
    Sg = [scld, yp[0]]
    Sgk = ["scld", "yp0"]
    Inp = [yp[1], attn_sb]
    Inpk = ["yp1", "attn_sb"]
    em.dma("sync", cmk_sb[:], cmaskB, "cmk_sb")
    em.dma("sync", ngk_sb[:], negB, "ngk_sb")
    em.dma("sync", ohk_sb[:], onehotB, "ohk_sb")
    em.op("v", lambda e: e.memset(ltg[:], 0.0), writes=["ltg"])
    em.op("v", lambda e: e.memset(ltg2[:], 0.0), writes=["ltg2"])
    em.dma("sync", ltg[:, 8:16].rearrange("h (x j) -> h x j", x=1),
           cc_out.rearrange("(j r) n -> n r j", r=129)[0:32, 128:129, :],
           "ltg", ["cc_out"], allow_slow_non_contiguous=True)
    src_, dst_, sk, dk = ltg, ltg2, "ltg", "ltg2"
    for s in (1, 2, 4):
        em.op("v", lambda e, src_=src_, dst_=dst_, s=s: e.tensor_tensor(
            dst_[:, 8:16], src_[:, 8:16], src_[:, 8 - s:16 - s], ALU.add),
            reads=[sk], writes=[dk])
        src_, dst_, sk, dk = dst_, src_, dk, sk
    cum, cumk = src_, sk
    em.op("v", lambda e, cum=cum: e.tensor_tensor(t8[:], cum[:, 8:16], ohk_sb[:], ALU.mult),
          reads=[cumk, "ohk_sb"], writes=["t8"])
    em.op("v", lambda e: e.tensor_reduce(cum1[:], t8[:], AX.X, ALU.add),
          reads=["t8"], writes=["cum1"])
    em.op("v", lambda e, cum=cum: e.tensor_scalar(wmat[:], cum[:, 8:16], cum1[:], -1.0,
          ALU.subtract, ALU.mult), reads=[cumk, "cum1"], writes=["wmat"])
    em.op("v", lambda e: e.tensor_tensor(wmat[:], wmat[:], cmk_sb[:], ALU.mult),
          reads=["wmat", "cmk_sb"], writes=["wmat"])
    em.op("v", lambda e: e.tensor_tensor(wmat[:], wmat[:], ngk_sb[:], ALU.add),
          reads=["wmat", "ngk_sb"], writes=["wmat"])
    em.op("s", lambda e: e.activation(wmat[:], wmat[:], AF.Exp, bias=0.0, scale=1.0),
          reads=["wmat"], writes=["wmat"])
    em.dma("sync", wflat_d.rearrange("x (j h) -> h x j", h=32), wmat.ap().rearrange("h (x j) -> h x j", x=1),
           "wflat_d", ["wmat"], allow_slow_non_contiguous=True)
    em.dma("sync", wcols[:], bcast_part(wflat_d, 128), "wcols", ["wflat_d"])
    for j in range(NCORES):
        sj = j % 2
        em.dma("sync", Sg[sj][:, 0:HEADS * HD], cc_out[j * 129:j * 129 + 128, :], Sgk[sj], ["cc_out"])
        cur, prv = Inpk[j % 2], Inpk[(j + 1) % 2]
        curt, prvt = Inp[j % 2], Inp[(j + 1) % 2]
        for h in range(HEADS):
            if j == 0:
                em.op("v", lambda e, h=h, curt=curt, sj=sj: e.tensor_scalar(
                    curt[:, h * 128:(h + 1) * 128], Sg[sj][:, h * 128:(h + 1) * 128],
                    wcols[:, h:h + 1], None, ALU.mult),
                    reads=[Sgk[sj], "wcols"], writes=[cur])
            else:
                em.op("v", lambda e, h=h, j=j, curt=curt, prvt=prvt, sj=sj: e.scalar_tensor_tensor(
                    curt[:, h * 128:(h + 1) * 128], Sg[sj][:, h * 128:(h + 1) * 128],
                    wcols[:, j * 32 + h:j * 32 + h + 1], prvt[:, h * 128:(h + 1) * 128],
                    ALU.mult, ALU.add),
                    reads=[Sgk[sj], "wcols", prv], writes=[cur])
    em.op("v", lambda e, curt=Inp[(NCORES - 1) % 2]: e.tensor_copy(state_b[:], curt[:, 0:HEADS * HD]),
          reads=[Inpk[(NCORES - 1) % 2]], writes=["state_run_b"])

    if upto == "IN":
        em.dma("sync", dbg_d[0:128, 0:HEADS * HD], state_b[:], "dbg_d", ["state_run_b"])
        _final_wait(nc, em)
        return nc


    def b_chunk(c):
        c0 = c * CHUNK
        ltg4 = (c % 2) * 2
        for g in range(4):
            em.dma("sync", rbB.ap().rearrange("p (h l) -> p h l", h=8),
                   bcast_prepend(acum_d[8 * g:8 * (g + 1), c0:c0 + CHUNK], 128), "rbB", [f"acum_d_{c}"])
            em.op("s", lambda e: e.activation(ebg[:], rbB[:], AF.Exp, bias=0.0, scale=1.0),
                  reads=["rbB"], writes=["ebg"])
            em.op("v", lambda e, g=g, c0=c0: e.tensor_tensor(
                cpb.ap().rearrange("p (h l) -> p h l", h=HEADS)[:, 8 * g:8 * (g + 1), :],
                rep_free(C_fm[:, c0:c0 + CHUNK], 8),
                ebg.ap().rearrange("p (h l) -> p h l", h=8), ALU.mult),
                reads=["ebg", "C_fm"], writes=["cpb"])
        em.op("v", lambda e: e.tensor_copy(scast[:], state_b[:]),
              reads=["state_run_b"], writes=["scast"])
        em.dma("sync", dcolsB[:], bcast_part(dvec_d[c:c + 1, :], 128), "dcolsB", [f"dvec_d_{c}"])
        em.dma("sync", scld[:], sstates_d[c * 128:(c + 1) * 128, :], "scld", [f"sst_{c}_{g4}" for g4 in range(8)])
        for h in range(HEADS):
            em.op("v", lambda e, h=h: e.scalar_tensor_tensor(
                state_b[:, h * 128:(h + 1) * 128], state_b[:, h * 128:(h + 1) * 128],
                dcolsB[:, h:h + 1], scld[:, h * 128:(h + 1) * 128], ALU.mult, ALU.add),
                reads=["state_run_b", "dcolsB", "scld"], writes=["state_run_b"])
        for lt in range(2):
            ypt, kyp = yp[lt], f"yp{lt}"
            glt, kgl = gload[lt], f"gload{lt}"
            em.dma("sync", ypt[:], ypart_d[c0 + lt * 128:c0 + (lt + 1) * 128, :], kyp, [f"ypart_{c}_{lt}"])
            for g4 in range(8):
                pp, ppk = next_p()
                for q in range(4):
                    h = g4 * 4 + q
                    em.mm(lambda e, pp=pp, q=q, h=h, lt=lt: e.matmul(
                        pp[:, q * 128:(q + 1) * 128],
                        cpb[:, h * CHUNK + lt * 128: h * CHUNK + lt * 128 + 128],
                        scast[:, h * 128:(h + 1) * 128],
                        start=(q == 0), stop=(q == 3)),
                        ppk, ["cpb", "scast"], last=(q == 3))
                em.op("v", lambda e, pp=pp, g4=g4, ypt=ypt: e.tensor_tensor(
                    ypt[:, g4 * 512:(g4 + 1) * 512], ypt[:, g4 * 512:(g4 + 1) * 512], pp[:],
                    ALU.add), reads=[ppk, kyp], writes=[kyp])
            em.dma("sync", glt[:], gate_s_d[c0 + lt * 128:c0 + (lt + 1) * 128, :], kgl,
                   [f"gates_{c * 2 + lt}_{p}" for p in range(8)])
            em.op("v", lambda e, ypt=ypt, glt=glt: e.tensor_tensor(ypt[:], ypt[:], glt[:], ALU.mult),
                  reads=[kyp, kgl], writes=[kyp])
            em.op("s", lambda e, ypt=ypt: e.activation(scld[:, 0:2048], ypt[:, 0:2048], AF.Square,
                  bias=0.0, scale=1.0, accum_out=sumsq[:]),
                  reads=[kyp], writes=["scld", "sumsq"])
            em.op("s", lambda e, ypt=ypt: e.activation(scld[:, 0:2048], ypt[:, 2048:4096], AF.Square,
                  bias=0.0, scale=1.0, accum_out=c1[:]),
                  reads=[kyp, "scld"], writes=["scld", "c1"])
            em.op("v", lambda e: e.tensor_tensor(sumsq[:], sumsq[:], c1[:], ALU.add),
                  reads=["sumsq", "c1"], writes=["sumsq"])
            em.op("v", lambda e: e.tensor_scalar(sumsq[:], sumsq[:], 1.0 / INTER, EPS,
                  ALU.mult, ALU.add), reads=["sumsq"], writes=["sumsq"])
            em.op("s", lambda e: e.activation(sumsq[:], sumsq[:], AF.Ln, bias=0.0, scale=1.0),
                  reads=["sumsq"], writes=["sumsq"])
            em.op("s", lambda e, lt=lt, ltg4=ltg4: e.activation(rstdB[:, ltg4 + lt:ltg4 + lt + 1],
                  sumsq[:], AF.Exp, bias=0.0, scale=-0.5),
                  reads=["sumsq"], writes=["rstdB"])
            em.op("v", lambda e, ypt=ypt: e.tensor_copy(ybf[:], ypt[:]), reads=[kyp, "scast"], writes=["scast"])
            for fh in range(4):
                pt, ptkey = next_pt()
                for ff in range(8):
                    f = fh * 8 + ff
                    em.mm(lambda e, pt=pt, ff=ff, f=f: e.transpose(
                        pt[:, ff * 128:(ff + 1) * 128], ybf[:, f * 128:(f + 1) * 128], eye_b[:]),
                        ptkey, ["scast", "eye_b"], last=(ff == 7))
                em.op("v", lambda e, pt=pt, fh=fh, lt=lt, ltg4=ltg4: e.tensor_copy(
                    yfmg.ap().rearrange("p (f l) -> p f l", l=512)[:, fh * 8:(fh + 1) * 8,
                                                                  (ltg4 + lt) * 128:(ltg4 + lt + 1) * 128],
                    pt.ap().rearrange("p (f l) -> p f l", f=8)),
                    reads=[ptkey], writes=["yfmg"])
        if c % 2 == 1:
            gtok0 = (c - 1) * CHUNK
            for dpiece in range(4):
                for f in range(32):
                    jw = f % 3
                    em.dma("sync", wop[jw][:], opWTb_d[f * 128:(f + 1) * 128,
                           dpiece * 512:(dpiece + 1) * 512], f"wop{jw}", [f"opwtb_{f}"])
                    for lt4 in range(4):
                        em.mm(lambda e, lt4=lt4, f=f, jw=jw, dpiece=dpiece: e.matmul(
                            P[lt4][:], yfmg[:, f * 512 + lt4 * 128: f * 512 + (lt4 + 1) * 128],
                            wop[jw][:], start=(f == 0), stop=(f == 31)),
                            pk[lt4], ["yfmg", f"wop{jw}"], last=(lt4 == 3))
                for lt4 in range(4):
                    em.op("v", lambda e, lt4=lt4, dpiece=dpiece: e.tensor_copy(
                        attn_sb[:, lt4 * D + dpiece * 512: lt4 * D + (dpiece + 1) * 512],
                        P[lt4][:]), reads=[pk[lt4]], writes=["attn_sb"])
            for lt4 in range(4):
                for s4 in range(4):
                    em.op("v", lambda e, lt4=lt4, s4=s4: e.bn_stats(
                        bnb2[:, s4 * 6:(s4 + 1) * 6],
                        attn_sb[:, lt4 * D + s4 * 512: lt4 * D + (s4 + 1) * 512]),
                        reads=["attn_sb"], writes=["bnb2"])
                em.op("v", lambda e: e.bn_aggr(mv2[:], bnb2[:]), reads=["bnb2"], writes=["mv2"])
                em.op("v", lambda e, lt4=lt4: e.tensor_tensor(c1[:], rstdB[:, lt4:lt4 + 1],
                      rstdB[:, lt4:lt4 + 1], ALU.mult), reads=["rstdB"], writes=["c1"])
                em.op("v", lambda e: e.tensor_scalar(c1[:], c1[:], mv2[:, 1:2], EPS,
                      ALU.mult, ALU.add), reads=["c1", "mv2"], writes=["c1"])
                em.op("s", lambda e: e.activation(c1[:], c1[:], AF.Ln, bias=0.0, scale=1.0),
                      reads=["c1"], writes=["c1"])
                em.op("s", lambda e: e.activation(c1[:], c1[:], AF.Exp, bias=0.0, scale=-0.5),
                      reads=["c1"], writes=["c1"])
                em.op("v", lambda e, lt4=lt4: e.tensor_tensor(c2[:], c1[:], rstdB[:, lt4:lt4 + 1],
                      ALU.mult), reads=["c1", "rstdB"], writes=["c2"])
                em.op("v", lambda e: e.tensor_scalar(c3[:], mv2[:, 0:1], c2[:], -1.0,
                      ALU.mult, ALU.mult), reads=["mv2", "c2"], writes=["c3"])
                tok0 = gtok0 + lt4 * 128
                em.dma("sync", xres[:], x_in[PAD + tok0:PAD + tok0 + 128, :], "xres")
                em.op("v", lambda e: e.tensor_scalar(xres[:], xres[:], c3[:], None, ALU.add),
                      reads=["xres", "c3"], writes=["xres"])
                em.op("v", lambda e, lt4=lt4: e.scalar_tensor_tensor(x1o[:],
                      attn_sb[:, lt4 * D:(lt4 + 1) * D], c2[:], xres[:], ALU.mult, ALU.add),
                      reads=["attn_sb", "c2", "xres"], writes=["x1o"])
                em.dma("sync", x1_d[tok0:tok0 + 128, :], x1o[:], f"x1_{tok0 // 128}", ["x1o"])
    for c in range(NCH):
        b_chunk(c)

    pop_scope()

    if upto == "B":
        _final_wait(nc, em)
        return nc
    pop_scope()

    push_scope()
    xt_fm = sbp("xt_fm", [128, NKD * TC], BF16)
    h_sb = sbp("h_sb", [128, NKI * TC], BF16)
    w1sb = [sbp(f"w1sb{i}", [128, NKD * 128], BF16) for i in range(2)]
    w3sb = [sbp(f"w3sb{i}", [128, NKD * 128], BF16) for i in range(2)]
    w2sb = [sbp(f"w2sb{i}", [128, NKI * 512], BF16) for i in range(2)]
    x1t = [sbp(f"x1t{i}", [128, D]) for i in range(2)]
    bnb3 = [sbp(f"bnb3{i}", [128, 24]) for i in range(2)]
    mv3 = [sbp(f"mv3{i}", [128, 2]) for i in range(2)]
    r3 = [sbp(f"r3{i}", [128, 1]) for i in range(2)]
    n3 = [sbp(f"n3{i}", [128, 1]) for i in range(2)]
    xth = [sbp(f"xth{i}", [128, D], BF16) for i in range(2)]
    xtl = [sbp(f"xtl{i}", [128, D], BF16) for i in range(2)]
    lofm = sbp("lofm", [128, NKD * 128], BF16)
    gw_sb = sbp("gw_sb", [128, NKD * E], BF16)
    gb_bc = sbp("gb_bc", [128, E])
    id_bc = sbp("id_bc", [128, E])
    lg = sbp("lg", [128, E])
    rcol = sbp("rcol", [128, 1])
    r2col = sbp("r2col", [128, 1])
    e8a = sbp("e8a", [128, E])
    e8b = sbp("e8b", [128, E])
    dw_sb = sbp("dw_sb", [128, 16 * E])
    stg = [sbp(f"stg{i}", [128, 512]) for i in range(4)]
    stgb = [sbp(f"stgb{i}", [128, 512]) for i in range(4)]
    outx = [sbp(f"outx{i}", [128, D]) for i in range(2)]

    em.dma("gp", gw_sb.ap().rearrange("p (k c) -> p k c", k=NKD),
           gateWT.rearrange("(k p) c -> p k c", p=128), "gw_sb")
    em.dma("sync", gb_bc[:], bcast_part(gatebB, 128), "gb_bc")
    em.dma("sync", id_bc[:], bcast_part(idxdB, 128), "id_bc")

    for tt in range(16):
        j = tt % 2
        kx, kb, km, kr, kn = f"x1t{j}", f"bnb3{j}", f"mv3{j}", f"r3{j}", f"n3{j}"
        em.dma("sync", x1t[j][:], x1_d[tt * 128:(tt + 1) * 128, :], kx, [f"x1_{tt}"])
        for s4 in range(4):
            em.op("v", lambda e, j=j, s4=s4: e.bn_stats(bnb3[j][:, s4 * 6:(s4 + 1) * 6],
                  x1t[j][:, s4 * 512:(s4 + 1) * 512]), reads=[kx], writes=[kb])
        em.op("v", lambda e, j=j: e.bn_aggr(mv3[j][:], bnb3[j][:]), reads=[kb], writes=[km])
        em.op("v", lambda e, j=j: e.tensor_scalar(r3[j][:], mv3[j][:, 1:2], EPS, None, ALU.add),
              reads=[km], writes=[kr])
        em.op("s", lambda e, j=j: e.activation(r3[j][:], r3[j][:], AF.Ln, bias=0.0, scale=1.0),
              reads=[kr], writes=[kr])
        em.op("s", lambda e, j=j: e.activation(r3[j][:], r3[j][:], AF.Exp, bias=0.0, scale=-0.5),
              reads=[kr], writes=[kr])
        em.op("v", lambda e, j=j: e.tensor_scalar(n3[j][:], r3[j][:], mv3[j][:, 0:1], -1.0,
              ALU.mult, ALU.mult), reads=[kr, km], writes=[kn])
        em.op("v", lambda e, j=j: e.tensor_scalar(xth[j][:], x1t[j][:], r3[j][:], n3[j][:],
              ALU.mult, ALU.add), reads=[kx, kr, kn], writes=[f"xth{j}"])
        em.op("v", lambda e, j=j: e.tensor_scalar(x1t[j][:], x1t[j][:], r3[j][:], n3[j][:],
              ALU.mult, ALU.add), reads=[kx, kr, kn], writes=[kx])
        em.op("v", lambda e, j=j: e.tensor_tensor(xtl[j][:], x1t[j][:], xth[j][:], ALU.subtract),
              reads=[kx, f"xth{j}"], writes=[f"xtl{j}"])
        for half in range(2):
            pt, ptkey = next_pt()
            for kk in range(8):
                k = half * 8 + kk
                em.mm(lambda e, j=j, k=k, kk=kk, pt=pt: e.transpose(
                    pt[:, kk * 128:(kk + 1) * 128], xth[j][:, k * 128:(k + 1) * 128], eye_b[:]),
                    ptkey, [f"xth{j}", "eye_b"], last=(kk == 7))
            em.op("v", lambda e, tt=tt, half=half, pt=pt: e.tensor_copy(
                xt_fm.ap().rearrange("p (k t) -> p k t", k=NKD)[:, half * 8:(half + 1) * 8,
                                                               tt * 128:(tt + 1) * 128],
                pt.ap().rearrange("p (k t) -> p k t", k=8)),
                reads=[ptkey], writes=["xt_fm"])
        for half in range(2):
            pt, ptkey = next_pt()
            for kk in range(8):
                k = half * 8 + kk
                em.mm(lambda e, j=j, k=k, kk=kk, pt=pt: e.transpose(
                    pt[:, kk * 128:(kk + 1) * 128], xtl[j][:, k * 128:(k + 1) * 128], eye_b[:]),
                    ptkey, [f"xtl{j}", "eye_b"], last=(kk == 7))
            em.op("v", lambda e, half=half, pt=pt: e.tensor_copy(
                lofm.ap().rearrange("p (k t) -> p k t", k=NKD)[:, half * 8:(half + 1) * 8, :],
                pt.ap().rearrange("p (k t) -> p k t", k=8)),
                reads=[ptkey], writes=["lofm"])
        pp, ppk = next_p()
        for k in range(NKD):
            em.mm(lambda e, pp=pp, k=k, tt=tt: e.matmul(pp[:, 0:E],
                  xt_fm[:, k * TC + tt * 128: k * TC + (tt + 1) * 128],
                  gw_sb[:, k * E:(k + 1) * E], start=(k == 0), stop=False),
                  ppk, ["xt_fm", "gw_sb"], last=False)
        for k in range(NKD):
            em.mm(lambda e, pp=pp, k=k: e.matmul(pp[:, 0:E],
                  lofm[:, k * 128:(k + 1) * 128],
                  gw_sb[:, k * E:(k + 1) * E], start=False, stop=(k == NKD - 1)),
                  ppk, ["lofm", "gw_sb"], last=(k == NKD - 1))
        em.op("v", lambda e, pp=pp: e.tensor_copy(lg[:], pp[:, 0:E]), reads=[ppk], writes=["lg"])
        em.op("v", lambda e: e.tensor_reduce(rcol[:], lg[:], AX.X, ALU.max),
              reads=["lg"], writes=["rcol"])
        em.op("v", lambda e: e.tensor_scalar(rcol[:], rcol[:], -1.0, None, ALU.mult),
              reads=["rcol"], writes=["rcol"])
        em.op("s", lambda e: e.activation(lg[:], lg[:], AF.Exp, bias=rcol[:], scale=1.0),
              reads=["lg", "rcol"], writes=["lg"])
        em.op("v", lambda e: e.tensor_reduce(rcol[:], lg[:], AX.X, ALU.add),
              reads=["lg"], writes=["rcol"])
        em.op("v", lambda e: e.reciprocal(rcol[:], rcol[:]), reads=["rcol"], writes=["rcol"])
        em.op("v", lambda e: e.tensor_scalar(lg[:], lg[:], rcol[:], None, ALU.mult),
              reads=["lg", "rcol"], writes=["lg"])
        em.op("v", lambda e: e.tensor_tensor(e8a[:], lg[:], gb_bc[:], ALU.add),
              reads=["lg", "gb_bc"], writes=["e8a"])
        em.op("v", lambda e: e.tensor_reduce(rcol[:], e8a[:], AX.X, ALU.max),
              reads=["e8a"], writes=["rcol"])
        em.op("v", lambda e: e.tensor_scalar(e8b[:], e8a[:], rcol[:], None, ALU.is_equal),
              reads=["e8a", "rcol"], writes=["e8b"])
        em.op("v", lambda e: e.tensor_tensor(e8b[:], e8b[:], id_bc[:], ALU.mult),
              reads=["e8b", "id_bc"], writes=["e8b"])
        em.op("v", lambda e: e.tensor_reduce(r2col[:], e8b[:], AX.X, ALU.max),
              reads=["e8b"], writes=["r2col"])
        em.op("v", lambda e: e.tensor_scalar(e8b[:], e8b[:], r2col[:], None, ALU.is_equal),
              reads=["e8b", "r2col"], writes=["e8b"])
        em.op("v", lambda e: e.scalar_tensor_tensor(e8a[:], e8b[:], -1e9, e8a[:],
              ALU.mult, ALU.add), reads=["e8b", "e8a"], writes=["e8a"])
        em.op("v", lambda e: e.tensor_reduce(rcol[:], e8a[:], AX.X, ALU.max),
              reads=["e8a"], writes=["rcol"])
        em.op("v", lambda e: e.tensor_scalar(e8a[:], e8a[:], rcol[:], None, ALU.is_equal),
              reads=["e8a", "rcol"], writes=["e8a"])
        em.op("v", lambda e: e.tensor_tensor(e8a[:], e8a[:], id_bc[:], ALU.mult),
              reads=["e8a", "id_bc"], writes=["e8a"])
        em.op("v", lambda e: e.tensor_reduce(r2col[:], e8a[:], AX.X, ALU.max),
              reads=["e8a"], writes=["r2col"])
        em.op("v", lambda e: e.tensor_scalar(e8a[:], e8a[:], r2col[:], None, ALU.is_equal),
              reads=["e8a", "r2col"], writes=["e8a"])
        em.op("v", lambda e: e.tensor_tensor(e8a[:], e8a[:], e8b[:], ALU.add),
              reads=["e8a", "e8b"], writes=["e8a"])
        em.op("v", lambda e, tt=tt: e.tensor_tensor(dw_sb[:, tt * E:(tt + 1) * E], lg[:],
              e8a[:], ALU.mult), reads=["lg", "e8a"], writes=["dw_sb"])

    for ei in range(E + 1):
        shared = (ei == 0)
        e_idx = ei - 1
        for ik in range(NKI):
            jj = ik % 2
            if shared:
                w1src = ws1t[:, ik * 128:(ik + 1) * 128]
                w3src = ws3t[:, ik * 128:(ik + 1) * 128]
            else:
                w1src = w1t[e_idx * D:(e_idx + 1) * D, ik * 128:(ik + 1) * 128]
                w3src = w3t[e_idx * D:(e_idx + 1) * D, ik * 128:(ik + 1) * 128]
            em.dma("gp", w1sb[jj].ap().rearrange("p (k c) -> p k c", k=NKD),
                   w1src.rearrange("(k p) c -> p k c", p=128), f"w1sb{jj}")
            em.dma("gp", w3sb[jj].ap().rearrange("p (k c) -> p k c", k=NKD),
                   w3src.rearrange("(k p) c -> p k c", p=128), f"w3sb{jj}")
            for piece in range(4):
                pa, pak = next_p()
                pb, pbk = next_p()
                for k in range(NKD):
                    em.mm(lambda e, pa=pa, k=k, jj=jj, piece=piece: e.matmul(
                        pa[:], w1sb[jj][:, k * 128:(k + 1) * 128],
                        xt_fm[:, k * TC + piece * 512: k * TC + (piece + 1) * 512],
                        start=(k == 0), stop=(k == NKD - 1)),
                        pak, [f"w1sb{jj}", "xt_fm"], last=(k == NKD - 1))
                for k in range(NKD):
                    em.mm(lambda e, pb=pb, k=k, jj=jj, piece=piece: e.matmul(
                        pb[:], w3sb[jj][:, k * 128:(k + 1) * 128],
                        xt_fm[:, k * TC + piece * 512: k * TC + (piece + 1) * 512],
                        start=(k == 0), stop=(k == NKD - 1)),
                        pbk, [f"w3sb{jj}", "xt_fm"], last=(k == NKD - 1))
                sj = piece % 4
                em.op("s", lambda e, sj=sj, pa=pa: e.activation(stg[sj][:], pa[:], AF.Silu,
                      bias=0.0, scale=1.0), reads=[pak], writes=[f"stg{sj}"])
                em.op("v", lambda e, sj=sj, pb=pb, ik=ik, piece=piece: e.tensor_tensor(
                    h_sb[:, ik * TC + piece * 512: ik * TC + (piece + 1) * 512],
                    stg[sj][:], pb[:], ALU.mult),
                    reads=[f"stg{sj}", pbk], writes=["h_sb"])
        for dpiece in range(4):
            jj = dpiece % 2
            w2src = (ws2t if shared else w2t[e_idx * I_FF:(e_idx + 1) * I_FF, :])
            em.dma("gp", w2sb[jj].ap().rearrange("p (k c) -> p k c", k=NKI),
                   w2src[:, dpiece * 512:(dpiece + 1) * 512].rearrange("(k p) c -> p k c", p=128),
                   f"w2sb{jj}")
            for tt in range(16):
                pp, ppk = next_p()
                for ik in range(NKI):
                    em.mm(lambda e, pp=pp, ik=ik, jj=jj, tt=tt: e.matmul(
                        pp[:], h_sb[:, ik * TC + tt * 128: ik * TC + (tt + 1) * 128],
                        w2sb[jj][:, ik * 512:(ik + 1) * 512],
                        start=(ik == 0), stop=(ik == NKI - 1)),
                        ppk, ["h_sb", f"w2sb{jj}"], last=(ik == NKI - 1))
                sj = tt % 4
                if shared:
                    em.op("v", lambda e, sj=sj, pp=pp: e.tensor_copy(stgb[sj][:], pp[:]),
                          reads=[ppk], writes=[f"stgb{sj}"])
                    em.dma("sync", y_d[tt * 128:(tt + 1) * 128, dpiece * 512:(dpiece + 1) * 512],
                           stgb[sj][:], f"y_d_{tt}_{dpiece}", [f"stgb{sj}"])
                else:
                    em.op("v", lambda e, sj=sj, pp=pp, e_idx=e_idx, tt=tt: e.tensor_scalar(
                        stgb[sj][:], pp[:], dw_sb[:, tt * E + e_idx: tt * E + e_idx + 1], None,
                        ALU.mult), reads=[ppk, "dw_sb"], writes=[f"stgb{sj}"])
                    em.dma("gp", y_d[tt * 128:(tt + 1) * 128, dpiece * 512:(dpiece + 1) * 512],
                           stgb[sj][:], f"y_d_{tt}_{dpiece}", [f"stgb{sj}"], accum_op=ALU.add)

    for tt in range(16):
        j = tt % 2
        kx, kb, km, kr, kn = f"x1t{j}", f"bnb3{j}", f"mv3{j}", f"r3{j}", f"n3{j}"
        em.dma("sync", outx[j][:], y_d[tt * 128:(tt + 1) * 128, :], f"outx{j}", [f"y_d_{tt}_{dp}" for dp in range(4)])
        for s4 in range(4):
            em.op("v", lambda e, j=j, s4=s4: e.bn_stats(bnb3[j][:, s4 * 6:(s4 + 1) * 6],
                  outx[j][:, s4 * 512:(s4 + 1) * 512]), reads=[f"outx{j}"], writes=[kb])
        em.op("v", lambda e, j=j: e.bn_aggr(mv3[j][:], bnb3[j][:]), reads=[kb], writes=[km])
        em.op("v", lambda e, j=j: e.tensor_scalar(r3[j][:], mv3[j][:, 1:2], EPS, None, ALU.add),
              reads=[km], writes=[kr])
        em.op("s", lambda e, j=j: e.activation(r3[j][:], r3[j][:], AF.Ln, bias=0.0, scale=1.0),
              reads=[kr], writes=[kr])
        em.op("s", lambda e, j=j: e.activation(r3[j][:], r3[j][:], AF.Exp, bias=0.0, scale=-0.5),
              reads=[kr], writes=[kr])
        em.op("v", lambda e, j=j: e.tensor_scalar(n3[j][:], r3[j][:], mv3[j][:, 0:1], -1.0,
              ALU.mult, ALU.mult), reads=[kr, km], writes=[kn])
        em.dma("sync", x1t[j][:], x1_d[tt * 128:(tt + 1) * 128, :], kx, [f"x1_{tt}"])
        em.op("v", lambda e, j=j: e.tensor_scalar(x1t[j][:], x1t[j][:], n3[j][:], None, ALU.add),
              reads=[kx, kn], writes=[kx])
        em.op("v", lambda e, j=j: e.scalar_tensor_tensor(outx[j][:], outx[j][:], r3[j][:],
              x1t[j][:], ALU.mult, ALU.add), reads=[f"outx{j}", kr, kx], writes=[f"outx{j}"])
        em.dma("sync", out_d[tt * 128:(tt + 1) * 128, :], outx[j][:], f"out_{tt}", [f"outx{j}"])

    pop_scope()
    _final_wait(nc, em)
    return nc


def _final_wait(nc, em):
    for q in ("sync", "gp"):
        for i, sem in enumerate(em.dma_pools[q]):
            u = em.dma_uses[q][i]
            if u:
                em._wait("sync", (sem, u * 16, f"d_{q}{i}"))
    for c in em.COMPUTE:
        if em.cnt[c]:
            em._wait("sync", (em.sem[c], em.cnt[c], f"c_{c}"))
    em.finalize()


def host_inputs(inputs):
    x = np.asarray(inputs["L0_attn_hidden_states"])[0]
    ipW = np.asarray(inputs["L0_attn_in_proj_w"])
    conv_w = np.asarray(inputs["L0_attn_conv_w"])
    conv_b = np.asarray(inputs["L0_attn_conv_b"])
    A_log = np.asarray(inputs["L0_attn_A_log"])
    dt_bias = np.asarray(inputs["L0_attn_dt_bias"])
    Dp = np.asarray(inputs["L0_attn_D"])
    norm_w = np.asarray(inputs["L0_attn_norm_w"])
    out_w = np.asarray(inputs["L0_attn_out_proj_w"])
    gate_w = np.asarray(inputs["L0_ffn_gate_w"])
    gate_b = np.asarray(inputs["L0_ffn_gate_b"])
    W1 = np.asarray(inputs["L0_ffn_W1"])
    W2 = np.asarray(inputs["L0_ffn_W2"])
    W3 = np.asarray(inputs["L0_ffn_W3"])
    Ws1 = np.asarray(inputs["L0_ffn_Ws1"])
    Ws2 = np.asarray(inputs["L0_ffn_Ws2"])
    Ws3 = np.asarray(inputs["L0_ffn_Ws3"])

    shared = {}
    shared["ipWT"] = np.ascontiguousarray(ipW.T)
    cd = np.zeros((NO_HBC, CONVK, 128, 128), np.float32)
    cw = conv_w[:, 0, :]
    ar = np.arange(128)
    for j in range(NO_HBC):
        for k in range(CONVK):
            cd[j, k, ar, ar] = cw[j * 128:(j + 1) * 128, k]
    shared["convdiag"] = cd.reshape(NO_HBC * CONVK * 128, 128)
    shared["convb"] = np.ascontiguousarray(conv_b.reshape(NO_HBC, 128))
    shared["alog"] = np.ascontiguousarray(A_log.reshape(32, 1))
    shared["dtbias"] = np.ascontiguousarray(dt_bias.reshape(32, 1))
    dd = np.zeros((HEADS, 128, 128), np.float32)
    ar128 = np.arange(128)
    dd[:, ar128, ar128] = Dp[:, None]
    shared["ddiag"] = dd.reshape(HEADS * 128, 128)
    shared["normcols"] = np.ascontiguousarray(norm_w.reshape(32, 128).T)
    shared["opWT"] = np.ascontiguousarray(out_w.T)
    shared["gateWT"] = np.ascontiguousarray(gate_w.T)
    shared["gatebB"] = np.ascontiguousarray(gate_b.reshape(1, E))
    shared["idxdB"] = np.ascontiguousarray((E - np.arange(E, dtype=np.float32)).reshape(1, E))
    shared["w1t"] = np.ascontiguousarray(W1.transpose(0, 2, 1)).reshape(E * D, I_FF)
    shared["w3t"] = np.ascontiguousarray(W3.transpose(0, 2, 1)).reshape(E * D, I_FF)
    shared["w2t"] = np.ascontiguousarray(W2.transpose(0, 2, 1)).reshape(E * I_FF, D)
    shared["ws1t"] = np.ascontiguousarray(Ws1.T)
    shared["ws3t"] = np.ascontiguousarray(Ws3.T)
    shared["ws2t"] = np.ascontiguousarray(Ws2.T)
    m01 = np.zeros((2, 128, CHUNK), np.float32)
    for st in range(2):
        for s in range(128):
            m01[st, s, st * 128 + s:] = 1.0
    shared["mask01"] = m01.reshape(2 * 128, CHUNK)
    shared["eye"] = np.eye(128, dtype=np.float32)
    shared["trilM_in"] = np.triu(np.ones((128, 128), np.float32))

    in_maps = []
    for c in range(NCORES):
        m = dict(shared)
        halo = np.zeros((3, D), np.float32) if c == 0 else x[c * TC - 3:c * TC]
        m["x"] = np.ascontiguousarray(
            np.concatenate([np.zeros((125, D), np.float32), halo, x[c * TC:(c + 1) * TC]], 0))
        cm = np.zeros((32, E), np.float32)
        nb = np.full((32, E), -30000.0, np.float32)
        oh = np.zeros((32, E), np.float32)
        cm[:, :c] = 1.0
        nb[:, :c] = 0.0
        if c > 0:
            oh[:, c - 1] = 1.0
        m["cmaskB"], m["negB"], m["onehotB"] = cm, nb, oh
        in_maps.append(m)
    return in_maps


def kernel(**inputs):
    in_maps = host_inputs(inputs)
    nc = build_nc("full")
    res = run_bass_kernel_spmd(nc, in_maps, core_ids=list(range(NCORES)))
    out = np.concatenate([r["out"] for r in res.results], axis=0)
    return out.reshape(1, SEQ, D).astype(np.float32)


# version 06
# speedup vs baseline: 1.2907x; 1.0886x over previous
import dataclasses
import numpy as np
import ml_dtypes
import concourse.bass as bass
import concourse.mybir as mybir
from concourse.bass_utils import run_bass_kernel_spmd

F32 = mybir.dt.float32
BF16 = mybir.dt.bfloat16
AF = mybir.ActivationFunctionType
ALU = mybir.AluOpType
AX = mybir.AxisListType
BF = ml_dtypes.bfloat16

NCORES = 8
SEQ = 16384
D = 2048
TC = SEQ // NCORES
PAD = 128
TF = TC + PAD
NKD = D // 128
INTER = 4096
HEADS = 32
HD = 128
STATE = 128
CONVK = 4
CONV_DIM = INTER + 2 * STATE
NPROJ = INTER + CONV_DIM + HEADS
NO_HBC = CONV_DIM // 128
NO_GATE = INTER // 128
CHUNK = 256
NCH = TC // CHUNK
EPS = 1e-5
E = 8
I_FF = 1024
NKI = I_FF // 128
RG = [[0, 1, 2, 3, 4, 5, 6, 7]]


def bcast_part(ap, n):
    new = [[0, n]] + [list(x) for x in ap.ap][1:]
    return dataclasses.replace(ap, ap=type(ap.ap)(new))


def bcast_prepend(ap, n):
    new = [[0, n]] + [list(x) for x in ap.ap]
    return dataclasses.replace(ap, ap=type(ap.ap)(new))


def rep_inner(ap, n):
    aps = [list(x) for x in ap.ap]
    new = aps + [[0, n]]
    return dataclasses.replace(ap, ap=type(ap.ap)(new))


def rep_free(ap, n):
    aps = [list(x) for x in ap.ap]
    new = [aps[0]] + [[0, n]] + aps[1:]
    return dataclasses.replace(ap, ap=type(ap.ap)(new))


class EM:
    COMPUTE = ("v", "s", "t", "gp")
    NDMA = 6

    def __init__(self, nc):
        self.nc = nc
        self.q = {k: [] for k in ("sync", "gp", "v", "s", "t")}
        self.sem = {k: nc.alloc_semaphore(f"sem_{k}") for k in self.COMPUTE}
        self.cnt = {k: 0 for k in self.COMPUTE}
        self.dma_pools = {q: [nc.alloc_semaphore(f"dma_{q}{i}") for i in range(self.NDMA)]
                          for q in ("sync", "gp")}
        self.dma_uses = {q: [0] * self.NDMA for q in ("sync", "gp")}
        self.dma_rr = {q: 0 for q in ("sync", "gp")}
        self.waited = {k: {} for k in self.q}
        self.writer = {}
        self.readers = {}
        self.pending_t = []
        self._segs = []

    def _wait(self, queue, rec):
        if rec is None:
            return
        sem, val, name = rec
        if val <= 0:
            return
        w = self.waited[queue]
        if w.get(name, -1) >= val:
            return
        w[name] = val
        self.q[queue].append(lambda e, s=sem, v=val: e.wait_ge(s, v))

    def _note_read(self, key, rec):
        d = self.readers.setdefault(key, {})
        old = d.get(rec[2])
        if old is None or old[1] < rec[1]:
            d[rec[2]] = rec

    def _pre(self, queue, reads, writes, pe_acc_key=None):
        for k in reads:
            self._wait(queue, self.writer.get(k))
        for k in writes:
            wrec = self.writer.get(k)
            if not (k == pe_acc_key and wrec is not None and wrec[2] == "c_t"):
                self._wait(queue, wrec)
            for rec in self.readers.get(k, {}).values():
                self._wait(queue, rec)

    def _finish(self, reads, writes, rec):
        for k in reads:
            self._note_read(k, rec)
        for k in writes:
            self.writer[k] = rec
            self.readers[k] = {}

    def op(self, queue, fn, reads=(), writes=()):
        assert queue in self.COMPUTE
        reads, writes = list(reads), list(writes)
        self._pre(queue, reads, writes)
        self.cnt[queue] += 1
        sem, val = self.sem[queue], self.cnt[queue]
        self.q[queue].append(lambda e, f=fn, s=sem, v=1: f(e).then_inc(s, v))
        self._finish(reads, writes, (sem, val, f"c_{queue}"))

    def mm(self, fn, out_key, in_keys, last=True):
        q = "t"
        in_keys = [k for k in in_keys if k is not None]
        self._pre(q, in_keys, [out_key], pe_acc_key=out_key)
        if last:
            self.cnt[q] += 1
            sem, val = self.sem[q], self.cnt[q]
            self.q[q].append(lambda e, f=fn, s=sem: f(e).then_inc(s, 1))
            rec = (sem, val, "c_t")
            self.pending_t.append((in_keys, [out_key]))
            for rds, wrs in self.pending_t:
                self._finish(rds, wrs, rec)
            self.pending_t = []
        else:
            self.q[q].append(fn)
            self.pending_t.append((in_keys, [out_key]))
            self.writer[out_key] = (self.sem[q], self.cnt[q] + 1, "c_t")

    def dma(self, queue, dst_ap, src_ap, wkey, rkeys=(), **kw):
        pool, uses = self.dma_pools[queue], self.dma_uses[queue]
        i = self.dma_rr[queue]
        self.dma_rr[queue] = (i + 1) % self.NDMA
        sem = pool[i]
        name = f"d_{queue}{i}"
        self._wait(queue, (sem, uses[i] * 16, name))
        rkeys = list(rkeys)
        self._pre(queue, rkeys, [wkey] if wkey is not None else [])
        uses[i] += 1
        val = uses[i] * 16
        self.q[queue].append(
            lambda e, d=dst_ap, s=src_ap, sm=sem, k=dict(kw): e.dma_start(d, s, **k).then_inc(sm, 16))
        self._finish(rkeys, [wkey] if wkey is not None else [], (sem, val, name))

    def collective(self, fn, rkeys, wkey):
        q = "gp"
        self._pre(q, list(rkeys), [wkey])
        self.cnt[q] += 1
        sem, val = self.sem[q], self.cnt[q]
        self.q[q].append(lambda e, f=fn, s=sem: f(e).then_inc(s, 1))
        self._finish(list(rkeys), [wkey], (sem, val, f"c_{q}"))

    def barrier(self):
        for q in ("sync", "gp"):
            for i, sem in enumerate(self.dma_pools[q]):
                u = self.dma_uses[q][i]
                if u:
                    self._wait("sync", (sem, u * 16, f"d_{q}{i}"))
        self._segs.append(self.q)
        self.q = {k: [] for k in self.q}
        for qq in self.waited:
            w = self.waited[qq]
            for c in self.COMPUTE:
                w[f"c_{c}"] = max(w.get(f"c_{c}", 0), self.cnt[c])
            for q in ("sync", "gp"):
                for i in range(self.NDMA):
                    w[f"d_{q}{i}"] = max(w.get(f"d_{q}{i}", 0), self.dma_uses[q][i] * 16)

    def finalize(self):
        self.barrier()
        for seg in self._segs:
            if all(len(v) == 0 for v in seg.values()):
                continue
            with self.nc.Block() as b:
                for qname, reg in (("sync", b.sync), ("gp", b.gpsimd), ("v", b.vector),
                                   ("s", b.scalar), ("t", b.tensor)):
                    if seg[qname]:
                        reg(lambda e, ops=tuple(seg[qname]): [f(e) for f in ops])


def build_nc(upto="full", dbg=()):
    nc = bass.Bass("TRN2", target_bir_lowering=False, debug=False)
    em = EM(nc)

    def dram(name, shape, dtype=F32):
        kind = "ExternalOutput" if name in dbg else "Internal"
        return nc.dram_tensor(name, shape, dtype, kind=kind).ap()

    def inp(name, shape, dtype=F32):
        return nc.dram_tensor(name, shape, dtype, kind="ExternalInput").ap()

    def sb(name, shape, dtype=F32):
        return nc.alloc_sbuf_tensor(name, shape, dtype)

    scopes = []

    def push_scope():
        scopes.append([])

    def sbp(name, shape, dtype=F32):
        g = nc.sbuf_tensor(name, shape, dtype)
        t = g.__enter__()
        scopes[-1].append(g)
        return t

    def pop_scope():
        em.barrier()
        for g in reversed(scopes.pop()):
            g.__exit__(None, None, None)

    x_in = inp("x", [TF, D])
    ipWT = inp("ipWT", [D, NPROJ])
    convdiag = inp("convdiag", [NO_HBC * CONVK * 128, 128])
    convb = inp("convb", [NO_HBC, 128])
    alog = inp("alog", [32, 1])
    dtbias = inp("dtbias", [32, 1])
    ddiag = inp("ddiag", [HEADS * 128, 128])
    normcols = inp("normcols", [128, 32])
    opWT = inp("opWT", [INTER, D])
    gateWT = inp("gateWT", [D, E])
    gatebB = inp("gatebB", [1, E])
    idxdB = inp("idxdB", [1, E])
    w1t = inp("w1t", [E * D, I_FF])
    w3t = inp("w3t", [E * D, I_FF])
    w2t = inp("w2t", [E * I_FF, D])
    ws1t = inp("ws1t", [D, I_FF])
    ws3t = inp("ws3t", [D, I_FF])
    ws2t = inp("ws2t", [I_FF, D])
    mask01 = inp("mask01", [2 * 128, CHUNK])
    cmaskB = inp("cmaskB", [32, E])
    negB = inp("negB", [32, E])
    onehotB = inp("onehotB", [32, E])
    eye_in = inp("eye", [128, 128])
    trilM_in = inp("trilM_in", [128, 128])

    out_d = nc.dram_tensor("out", [TC, D], F32, kind="ExternalOutput").ap()
    xs_d = dram("xs_d", [INTER, TC], BF16)
    u_d = dram("u_d", [128, NKD * TF], BF16)
    ypart_d = dram("ypart_d", [TC, INTER])
    acum_d = dram("acum_d", [32, TC])
    acumf_d = dram("acumf_d", [NCH, 32 * CHUNK])
    x1_d = dram("x1_d", [TC, D])
    gate_s_d = dram("gate_s_d", [TC, INTER], BF16)
    opWTb_d = dram("opWTb_d", [INTER, D], BF16)
    y_d = dram("y_d", [TC, D])
    sstates_d = dram("sstates_d", [NCH * 128, HEADS * HD])
    dvec_d = dram("dvec_d", [NCH, 32])
    cc_in = dram("cc_in", [129, HEADS * HD])
    cc_out = dram("cc_out", [NCORES * 129, HEADS * HD])
    wflat_d = dram("wflat_d", [1, NCORES * 32])
    dbg_d = dram("dbg_d", [512, max(TC, INTER)]) if "dbg_d" in dbg else None

    P = [nc.alloc_psum_tensor(f"P{i}", [128, 512], F32) for i in range(6)]
    PT = [nc.alloc_psum_tensor(f"PT{i}", [128, 1024], BF16) for i in range(2)]
    pk = [f"P{i}" for i in range(6)]
    ptk = [f"PT{i}" for i in range(2)]
    prr = [0]

    def next_p():
        i = prr[0] % 6
        prr[0] += 1
        return P[i], pk[i]

    ptrr = [0]

    def next_pt():
        i = ptrr[0] % 2
        ptrr[0] += 1
        return PT[i], ptk[i]

    eye_f = sb("eye_f", [128, 128], F32)
    trilM = sb("trilM", [128, 128], F32)
    onesM = sb("onesM", [128, 128], F32)
    eye_b = sb("eye_b", [128, 128], BF16)
    ones_f = sb("ones_f", [1, 128], F32)
    em.dma("sync", eye_f[:], eye_in, "eye_f")
    em.dma("sync", trilM[:], trilM_in, "trilM")
    em.op("v", lambda e: e.tensor_copy(eye_b[:], eye_f[:]), reads=["eye_f"], writes=["eye_b"])
    em.op("v", lambda e: e.memset(ones_f[:], 1.0), writes=["ones_f"])
    em.op("v", lambda e: e.memset(onesM[:], 1.0), writes=["onesM"])
    normc_sb = sb("normc_sb", [128, 32])
    em.dma("sync", normc_sb[:], normcols, "normc_sb")
    push_scope()
    C_fm = sbp("C_fm", [128, TC], BF16)
    push_scope()
    B_fm = sbp("B_fm", [128, TC], BF16)
    dtv = sbp("dtv", [32, TC])
    ad = sbp("ad", [32, TC])
    alog_s = sbp("alog_s", [32, 1])
    dtb_s = sbp("dtb_s", [32, 1])
    acol = sbp("acol", [32, 1])
    ltot = sbp("ltot", [32, 1])

    push_scope()
    u_fm = sbp("u_fm", [128, NKD * TF], BF16)
    NT = TF // 128
    dtraw = sbp("dtraw", [32, TC])
    spt = sbp("spt", [32, TC])
    xt = [sbp(f"xt{i}", [128, D]) for i in range(2)]
    bnb = [sbp(f"bnb{i}", [128, 24]) for i in range(2)]
    mv = [sbp(f"mv{i}", [128, 2]) for i in range(2)]
    rstd = [sbp(f"rstd{i}", [128, 1]) for i in range(2)]
    negmr = [sbp(f"negmr{i}", [128, 1]) for i in range(2)]
    ubf = [sbp(f"ubf{i}", [128, D], BF16) for i in range(2)]

    for i in range(NT):
        j = i % 2
        kx, kb, km, kr, kn, ku = (f"xt{j}", f"bnb{j}", f"mv{j}", f"rstd{j}", f"negmr{j}", f"ubf{j}")
        em.dma("sync", xt[j][:], x_in[i * 128:(i + 1) * 128, :], kx)
        for s4 in range(4):
            em.op("v", lambda e, j=j, s4=s4: e.bn_stats(bnb[j][:, s4 * 6:(s4 + 1) * 6],
                                                        xt[j][:, s4 * 512:(s4 + 1) * 512]),
                  reads=[kx], writes=[kb])
        em.op("v", lambda e, j=j: e.bn_aggr(mv[j][:], bnb[j][:]), reads=[kb], writes=[km])
        em.op("v", lambda e, j=j: e.tensor_scalar(rstd[j][:], mv[j][:, 1:2], EPS, None, ALU.add),
              reads=[km], writes=[kr])
        em.op("s", lambda e, j=j: e.activation(rstd[j][:], rstd[j][:], AF.Ln, bias=0.0, scale=1.0),
              reads=[kr], writes=[kr])
        em.op("s", lambda e, j=j: e.activation(rstd[j][:], rstd[j][:], AF.Exp, bias=0.0, scale=-0.5),
              reads=[kr], writes=[kr])
        em.op("v", lambda e, j=j: e.tensor_scalar(negmr[j][:], rstd[j][:], mv[j][:, 0:1], -1.0,
                                                  ALU.mult, ALU.mult),
              reads=[kr, km], writes=[kn])
        em.op("v", lambda e, j=j: e.tensor_scalar(ubf[j][:], xt[j][:], rstd[j][:], negmr[j][:],
                                                  ALU.mult, ALU.add),
              reads=[kx, kr, kn], writes=[ku])
        for half in range(2):
            pt, ptkey = next_pt()
            for kk in range(8):
                k = half * 8 + kk
                em.mm(lambda e, j=j, k=k, kk=kk, pt=pt: e.transpose(
                    pt[:, kk * 128:(kk + 1) * 128], ubf[j][:, k * 128:(k + 1) * 128], eye_b[:]),
                    ptkey, [ku, "eye_b"], last=(kk == 7))
            em.op("v", lambda e, i=i, half=half, pt=pt: e.tensor_copy(
                u_fm.ap().rearrange("p (k t) -> p k t", k=NKD)[:, half * 8:(half + 1) * 8,
                                                              i * 128:(i + 1) * 128],
                pt.ap().rearrange("p (k t) -> p k t", k=8)),
                reads=[ptkey], writes=["u_fm"])

    for f in range(32):
        jj = f % 2
        em.dma("sync", xt[jj][:], opWT[f * 128:(f + 1) * 128, :], f"xt{jj}")
        em.op("v", lambda e, jj=jj, f=f: e.tensor_scalar(ubf[jj][:], xt[jj][:],
              normc_sb[:, f:f + 1], None, ALU.mult),
              reads=[f"xt{jj}", "normc_sb"], writes=[f"ubf{jj}"])
        em.dma("sync", opWTb_d[f * 128:(f + 1) * 128, :], ubf[jj][:], f"opwtb_{f}", [f"ubf{jj}"])

    if upto == "A1ln":
        em.dma("sync", u_d, u_fm[:], "u_d", ["u_fm"])
        _final_wait(nc, em)
        return nc
    wsb = [sbp(f"wsb{i}", [128, NKD * 128], BF16) for i in range(2)]
    projbf = [sbp(f"projbf{i}", [128, TF], BF16) for i in range(2)]
    cdiag = [sbp(f"cdiag{i}", [128, CONVK * 128], BF16) for i in range(2)]
    cbcol = [sbp(f"cbcol{i}", [128, 1]) for i in range(2)]
    hbco = [sbp(f"hbco{i}", [128, TC], BF16) for i in range(2)]
    PIECES = [(0, 512), (512, 512), (1024, 512), (1536, 512), (2048, 128)]

    ipWT_r = ipWT.rearrange("(k p) c -> p k c", p=128)

    for oj in range(NO_HBC + 1):
        jj = oj % 2
        ocol = NO_GATE * 128 + oj * 128
        wk = f"wsb{jj}"
        ncols = 128 if oj < NO_HBC else 32
        em.dma("gp", wsb[jj].ap().rearrange("p (k c) -> p k c", k=NKD)[:, :, 0:ncols],
               ipWT_r[:, :, ocol:ocol + ncols], wk)
        pj = f"projbf{jj}"
        if oj < NO_HBC:
            pps = [next_p() for _ in range(5)]
            for k in range(NKD):
                for pi, (poff, plen) in enumerate(PIECES):
                    pp, ppk = pps[pi]
                    em.mm(lambda e, pp=pp, plen=plen, k=k, jj=jj, poff=poff: e.matmul(
                        pp[:, 0:plen], wsb[jj][:, k * 128:(k + 1) * 128],
                        u_fm[:, k * TF + poff: k * TF + poff + plen],
                        start=(k == 0), stop=(k == NKD - 1)),
                        ppk, [wk, "u_fm"], last=(k == NKD - 1 and pi == 4))
            for pi, (poff, plen) in enumerate(PIECES):
                pp, ppk = pps[pi]
                eng = "v" if pi % 2 == 0 else "s"
                if eng == "v":
                    em.op("v", lambda e, jj=jj, poff=poff, plen=plen, pp=pp: e.tensor_copy(
                        projbf[jj][:, poff:poff + plen], pp[:, 0:plen]),
                        reads=[ppk], writes=[pj])
                else:
                    em.op("s", lambda e, jj=jj, poff=poff, plen=plen, pp=pp: e.activation(
                        projbf[jj][:, poff:poff + plen], pp[:, 0:plen], AF.Copy, bias=0.0, scale=1.0),
                        reads=[ppk], writes=[pj])
            ck = f"cdiag{jj}"
            cbk = f"cbcol{jj}"
            em.dma("gp", cdiag[jj][:], convdiag[oj * CONVK * 128:(oj + 1) * CONVK * 128, :]
                   .rearrange("(k p) c -> p k c", p=128), ck)
            em.dma("sync", cbcol[jj][:], convb[oj:oj + 1, :].rearrange("x c -> c x"), cbk)
            hk = f"hbco{jj}"
            for piece in range(4):
                pp, ppk = next_p()
                for k in range(CONVK):
                    em.mm(lambda e, pp=pp, k=k, jj=jj, piece=piece: e.matmul(
                        pp[:], cdiag[jj][:, k * 128:(k + 1) * 128],
                        projbf[jj][:, 125 + k + piece * 512: 125 + k + piece * 512 + 512],
                        start=(k == 0), stop=(k == CONVK - 1)),
                        ppk, [ck, pj], last=(k == CONVK - 1))
                if oj < NO_GATE:
                    em.op("s", lambda e, jj=jj, piece=piece, pp=pp: e.activation(
                        hbco[jj][:, piece * 512:(piece + 1) * 512], pp[:], AF.Silu,
                        bias=cbcol[jj][:], scale=1.0), reads=[ppk, cbk], writes=[hk])
                else:
                    dstt = B_fm if oj == NO_GATE else C_fm
                    dk = "B_fm" if oj == NO_GATE else "C_fm"
                    em.op("s", lambda e, dstt=dstt, piece=piece, pp=pp, jj=jj: e.activation(
                        dstt[:, piece * 512:(piece + 1) * 512], pp[:], AF.Silu,
                        bias=cbcol[jj][:], scale=1.0), reads=[ppk, cbk], writes=[dk])
            if oj < NO_GATE:
                em.dma("sync", xs_d[oj * 128:(oj + 1) * 128, :], hbco[jj][:], f"xs_d_{oj}", [hk])
        else:
            for piece in range(4):
                pp, ppk = next_p()
                for k in range(NKD):
                    em.mm(lambda e, pp=pp, k=k, jj=jj, piece=piece: e.matmul(
                        pp[0:32, :], wsb[jj][:, k * 128: k * 128 + 32],
                        u_fm[:, k * TF + 128 + piece * 512: k * TF + 128 + piece * 512 + 512],
                        start=(k == 0), stop=(k == NKD - 1)),
                        ppk, [wk, "u_fm"], last=(k == NKD - 1))
                em.op("v", lambda e, piece=piece, pp=pp: e.tensor_copy(
                    dtraw[:, piece * 512:(piece + 1) * 512], pp[0:32, :]),
                    reads=[ppk], writes=["dtraw"])


    em.dma("sync", alog_s[:], alog, "alog_s")
    em.dma("sync", dtb_s[:], dtbias, "dtb_s")
    em.op("s", lambda e: e.activation(acol[:], alog_s[:], AF.Exp, bias=0.0, scale=1.0),
          reads=["alog_s"], writes=["acol"])
    em.op("v", lambda e: e.tensor_scalar(acol[:], acol[:], -1.0, None, ALU.mult),
          reads=["acol"], writes=["acol"])
    em.op("v", lambda e: e.tensor_scalar(dtv[:], dtraw[:], dtb_s[:], None, ALU.add),
          reads=["dtraw", "dtb_s"], writes=["dtv"])
    em.op("s", lambda e: e.activation(spt[:], dtv[:], AF.Abs, bias=0.0, scale=1.0),
          reads=["dtv"], writes=["spt"])
    em.op("s", lambda e: e.activation(spt[:], spt[:], AF.Exp, bias=0.0, scale=-1.0),
          reads=["spt"], writes=["spt"])
    em.op("v", lambda e: e.tensor_scalar(spt[:], spt[:], 1.0, None, ALU.add),
          reads=["spt"], writes=["spt"])
    em.op("s", lambda e: e.activation(spt[:], spt[:], AF.Ln, bias=0.0, scale=1.0),
          reads=["spt"], writes=["spt"])
    em.op("v", lambda e: e.tensor_scalar(dtv[:], dtv[:], 0.0, None, ALU.max),
          reads=["dtv"], writes=["dtv"])
    em.op("v", lambda e: e.tensor_tensor(dtv[:], dtv[:], spt[:], ALU.add),
          reads=["dtv", "spt"], writes=["dtv"])
    em.op("v", lambda e: e.tensor_scalar(dtv[:], dtv[:], 100.0, None, ALU.min),
          reads=["dtv"], writes=["dtv"])
    em.op("v", lambda e: e.tensor_scalar(ad[:], dtv[:], acol[:], None, ALU.mult),
          reads=["dtv", "acol"], writes=["ad"])

    if upto == "A1":
        em.dma("sync", dbg_d[0:32, 0:TC], dtv[:], "dbg_d", ["dtv"])
        em.dma("sync", dbg_d[32:64, 0:TC], ad[:], "dbg_d", ["ad"])
        em.dma("gp", dbg_d[64:192, 0:TC], B_fm[:], "dbg_d", ["B_fm"])
        em.dma("gp", dbg_d[192:320, 0:TC], C_fm[:], "dbg_d", ["C_fm"])
        _final_wait(nc, em)
        return nc

    wgbuf = [sbp(f"wgbuf{i}", [128, NKD * 512], BF16) for i in range(2)]
    gstg = [sbp(f"gstg{i}", [128, 512], BF16) for i in range(2)]
    for piece in range(8):
        jj = piece % 2
        wk = f"wgbuf{jj}"
        em.dma("gp", wgbuf[jj].ap().rearrange("p (k c) -> p k c", k=NKD),
               ipWT_r[:, :, piece * 512:(piece + 1) * 512], wk)
        for tt in range(16):
            pp, ppk = next_p()
            for k in range(NKD):
                em.mm(lambda e, pp=pp, k=k, jj=jj, tt=tt: e.matmul(
                    pp[:], u_fm[:, k * TF + 128 + tt * 128: k * TF + 128 + (tt + 1) * 128],
                    wgbuf[jj][:, k * 512:(k + 1) * 512],
                    start=(k == 0), stop=(k == NKD - 1)),
                    ppk, [wk, "u_fm"], last=(k == NKD - 1))
            sj = tt % 2
            em.op("s", lambda e, sj=sj, pp=pp: e.activation(gstg[sj][:], pp[:], AF.Silu,
                  bias=0.0, scale=1.0), reads=[ppk], writes=[f"gstg{sj}"])
            em.dma("sync", gate_s_d[tt * 128:(tt + 1) * 128, piece * 512:(piece + 1) * 512],
                   gstg[sj][:], f"gates_{tt}_{piece}", [f"gstg{sj}"])

    pop_scope()

    push_scope()
    mask_sb = sbp("mask_sb", [128, 2 * CHUNK])
    em.dma("sync", mask_sb.ap().rearrange("p (s c) -> p s c", s=2),
           mask01.rearrange("(s p) c -> p s c", p=128), "mask_sb")
    ddiag_sb = sbp("ddiag_sb", [128, HEADS * 128], BF16)
    em.dma("gp", ddiag_sb.ap().rearrange("p (h c) -> p h c", h=HEADS),
           ddiag.rearrange("(h p) c -> p h c", p=128), "ddiag_sb")
    adTb = [sbp(f"adT{i}", [128, 64]) for i in range(2)]
    em.op("v", lambda e: e.memset(ltot[:], 0.0), writes=["ltot"])
    acumC = [sbp(f"acumC{i}", [32, CHUNK]) for i in range(2)]
    dvecC = [sbp(f"dvecC{i}", [32, 1]) for i in range(2)]
    dsrow = [sbp(f"dsrow{i}", [32, CHUNK]) for i in range(2)]
    ddrow = [sbp(f"ddrow{i}", [32, CHUNK]) for i in range(2)]
    acumT = [[sbp(f"acumT{i}_{p}", [128, 32]) for i in range(2)] for p in range(2)]
    dtvT = [[sbp(f"dtvT{i}_{p}", [128, 32]) for i in range(2)] for p in range(2)]
    ddT = [[sbp(f"ddT{i}_{p}", [128, 32]) for i in range(2)] for p in range(2)]
    dcols = [sbp(f"dcols{i}", [128, 32]) for i in range(2)]
    rbg = [sbp(f"rbg{i}", [128, 8 * CHUNK]) for i in range(2)]
    dbg8 = [sbp(f"dbg8{i}", [128, 8 * CHUNK], BF16) for i in range(2)]
    lbg = [sbp(f"lbg{i}", [128, 8 * CHUNK], BF16) for i in range(2)]
    mbufS = [sbp(f"mbuf{i}", [128, HEADS * CHUNK], BF16) for i in range(2)]
    mbuf = [mbufS, mbufS]
    gm = [[sbp(f"gm{i}_{p}", [128, CHUNK], BF16) for i in range(2)] for p in range(2)]
    xsgS = sbp("xsg", [128, 8 * CHUNK], BF16)
    xsg = [xsgS, xsgS]
    xsTS = [sbp(f"xsT{i}", [128, HEADS * 128], BF16) for i in range(2)]
    xsT = [xsTS, xsTS]
    xdXS = [sbp(f"xdX{i}", [128, HEADS * 128], BF16) for i in range(2)]
    xdX = [xdXS, xdXS]
    btT = [sbp(f"btT{i}", [128, 2 * 128], BF16) for i in range(2)]
    state_run = sbp("state_run", [128, HEADS * 128])
    em.op("v", lambda e, t=state_run: e.memset(t[:], 0.0), writes=["state_run"])
    scsb = [sbp(f"scsb{i}", [128, 512]) for i in range(2)]
    ysbS = [sbp(f"ysb{i}", [128, INTER]) for i in range(2)]
    ysb = [ysbS, ysbS]

    def a2_chunk(c):
        c0 = c * CHUNK
        par = c % 2
        acum, dvec = acumC[par], dvecC[par]
        adT, kadT = adTb[par], f"adT{par}"
        dsr, ddr = dsrow[par], ddrow[par]
        acT, dtT, ddTT = acumT[par], dtvT[par], ddT[par]
        dcl, rbg_, dbg_, lbg_ = dcols[par], rbg[par], dbg8[par], lbg[par]
        mb, gmp, xsgp, xsTp, xdXp = mbuf[par], gm[par], xsg[par], xsT[par], xdX[par]
        btTp, scsbp, ysbp = btT[par], scsb[par], ysb[par]
        kac, kdv = f"acumC{par}", f"dvecC{par}"
        kaA, kaB = f"acA{par}", f"acB{par}"
        kdsr, kddr = f"dsrow{par}", f"ddrow{par}"
        kdcl, krbg, kdbg, klbg = f"dcols{par}", f"rbg{par}", f"dbg8{par}", f"lbg{par}"
        kxsg, kbtT, kscsb = "xsg", f"btT{par}", f"scsb{par}"
        adTp, adTk = next_p()
        for st in range(2):
            em.mm(lambda e, st=st, c0=c0, adTp=adTp: e.transpose(
                adTp[0:128, st * 32:(st + 1) * 32], ad[:, c0 + st * 128:c0 + (st + 1) * 128],
                eye_f[0:32, 0:32]), adTk, ["ad", "eye_f"], last=(st == 1))
        em.op("v", lambda e, adTp=adTp, adT=adT: e.tensor_copy(adT[:], adTp[0:128, 0:64]),
              reads=[adTk], writes=[kadT])
        acTp, acTk = next_p()
        em.mm(lambda e, acTp=acTp, adT=adT: e.matmul(acTp[:, 0:32], trilM[:], adT[:, 0:32],
              start=True, stop=True), acTk, [kadT, "trilM"], last=False)
        em.mm(lambda e, acTp=acTp, adT=adT: e.matmul(acTp[:, 32:64], onesM[:], adT[:, 0:32],
              start=True, stop=False), acTk, [kadT, "onesM"], last=False)
        em.mm(lambda e, acTp=acTp, adT=adT: e.matmul(acTp[:, 32:64], trilM[:], adT[:, 32:64],
              start=False, stop=True), acTk, [kadT, "trilM"], last=True)
        for st in range(2):
            em.op("v", lambda e, st=st, acTp=acTp, acT=acT: e.tensor_copy(
                acT[st][:], acTp[:, st * 32:(st + 1) * 32]), reads=[acTk], writes=[f"acumT{st}_{par}"])
        acRp, acRk = next_p()
        for st in range(2):
            em.mm(lambda e, st=st, acT=acT, acRp=acRp: e.transpose(
                acRp[0:32, st * 128:(st + 1) * 128], acT[st][:],
                eye_f[0:128, 0:128]), acRk, [f"acumT{st}_{par}", "eye_f"], last=(st == 1))
        em.op("v", lambda e, acRp=acRp, acum=acum: e.tensor_copy(acum[:], acRp[0:32, 0:256]),
              reads=[acRk], writes=[kac])
        em.op("v", lambda e, acum=acum: e.tensor_tensor(ltot[:], ltot[:], acum[:, 255:256], ALU.add),
              reads=["ltot", kac], writes=["ltot"])
        em.op("s", lambda e, acum=acum, dvec=dvec: e.activation(dvec[:], acum[:, 255:256], AF.Exp, bias=0.0, scale=1.0),
              reads=[kac], writes=[kdv])
        em.dma("sync", dvec_d[c:c + 1, :].rearrange("x h -> h x"), dvec[:], f"dvec_d_{c}", [kdv])
        em.dma("sync", acum_d[:, c0:c0 + CHUNK], acum[:], f"acum_d_{c}", [kac])
        em.dma("sync", acumf_d[c:c + 1, :].rearrange("x (h l) -> h x l", h=32),
               acum.ap().rearrange("h (x l) -> h x l", x=1), f"acumf_{c}", [kac])
        em.op("v", lambda e, acum=acum, dsr=dsr: e.tensor_scalar(dsr[:], acum[:], acum[:, 255:256], -1.0,
              ALU.subtract, ALU.mult), reads=[kac], writes=[kdsr])
        em.op("s", lambda e, dsr=dsr: e.activation(dsr[:], dsr[:], AF.Exp, bias=0.0, scale=1.0),
              reads=[kdsr], writes=[kdsr])
        em.op("v", lambda e, c0=c0, ddr=ddr, dsr=dsr: e.tensor_tensor(ddr[:], dtv[:, c0:c0 + CHUNK], dsr[:], ALU.mult),
              reads=["dtv", kdsr], writes=[kddr])
        for st in range(2):
            for (srcap, srck, dstt, dstk) in (
                    (dtv[:, c0 + st * 128:c0 + (st + 1) * 128], "dtv", dtT[st], f"dtvT{st}_{par}"),
                    (ddr[:, st * 128:(st + 1) * 128], kddr, ddTT[st], f"ddT{st}_{par}")):
                pp, ppk = next_p()
                em.mm(lambda e, pp=pp, srcap=srcap: e.transpose(pp[0:128, 0:32], srcap,
                      eye_f[0:32, 0:32]), ppk, [srck, "eye_f"], last=True)
                em.op("v", lambda e, pp=pp, dstt=dstt: e.tensor_copy(dstt[:], pp[0:128, 0:32]),
                      reads=[ppk], writes=[dstk])
        em.dma("sync", dcl[:], bcast_part(dvec_d[c:c + 1, :], 128), kdcl, [f"dvec_d_{c}"])
        for st in range(2):
            pp, ppk = next_p()
            em.mm(lambda e, st=st, pp=pp, c0=c0: e.matmul(pp[:, 0:CHUNK],
                  B_fm[:, c0 + st * 128:c0 + (st + 1) * 128], C_fm[:, c0:c0 + CHUNK],
                  start=True, stop=True), ppk, ["B_fm", "C_fm"], last=True)
            em.op("v", lambda e, st=st, pp=pp, gmp=gmp: e.tensor_tensor(gmp[st][:], pp[:, 0:CHUNK],
                  mask_sb[:, st * CHUNK:(st + 1) * CHUNK], ALU.mult),
                  reads=[ppk, "mask_sb"], writes=[f"gm{st}_{par}"])
        for g in range(4):
            em.dma("sync", rbg_[:], bcast_part(
                acumf_d[c:c + 1, g * 8 * CHUNK:(g + 1) * 8 * CHUNK], 128), krbg, [f"acumf_{c}"])
            for st in range(2):
                em.op("v", lambda e, st=st, g=g, dbgt=dbg_, rbgt=rbg_, acT=acT: e.tensor_tensor(
                    dbgt.ap().rearrange("p (h l) -> p h l", h=8),
                    rbgt.ap().rearrange("p (h l) -> p h l", h=8),
                    rep_inner(acT[st][:, 8 * g:8 * (g + 1)], CHUNK), ALU.subtract),
                    reads=[krbg, f"acumT{st}_{par}"], writes=[kdbg])
                em.op("s", lambda e, lbgt=lbg_, dbgt=dbg_: e.activation(lbgt[:], dbgt[:], AF.Exp, bias=0.0, scale=1.0),
                      reads=[kdbg], writes=[klbg])
                em.op("v", lambda e, lbgt=lbg_: e.tensor_scalar(lbgt[:], lbgt[:], 1.0, None, ALU.min),
                      reads=[klbg], writes=[klbg])
                em.op("v", lambda e, st=st, g=g, mb=mb, lbgt=lbg_, gmp=gmp: e.tensor_tensor(
                    mb[st].ap().rearrange("p (h l) -> p h l", h=HEADS)[:, 8 * g:8 * (g + 1), :],
                    lbgt.ap().rearrange("p (h l) -> p h l", h=8),
                    rep_free(gmp[st][:], 8), ALU.mult),
                    reads=[klbg, f"gm{st}_{par}"], writes=[f"mbuf{st}"])
        for g in range(4):
            em.dma("sync", xsgp.ap().rearrange("p (h t) -> p h t", h=8),
                   xs_d.rearrange("(h p) t -> p h t", p=128)[:, 8 * g:8 * (g + 1), c0:c0 + CHUNK],
                   kxsg, [f"xs_d_{o}" for o in range(8 * g, 8 * (g + 1))])
            for st in range(2):
                pt, ptkey = next_pt()
                for hh in range(8):
                    em.mm(lambda e, pt=pt, hh=hh, st=st, xsgp=xsgp: e.transpose(
                        pt[:, hh * 128:(hh + 1) * 128],
                        xsgp[:, hh * CHUNK + st * 128: hh * CHUNK + (st + 1) * 128], eye_b[:]),
                        ptkey, [kxsg, "eye_b"], last=(hh == 7))
                em.op("v", lambda e, st=st, g=g, pt=pt, xsTp=xsTp: e.tensor_copy(
                    xsTp[st][:, g * 1024:(g + 1) * 1024], pt[:]),
                    reads=[ptkey], writes=[f"xsT{st}"])
        for st in range(2):
            for g in range(4):
                em.op("v", lambda e, st=st, g=g, xdXp=xdXp, xsTp=xsTp, ddTT=ddTT: e.tensor_tensor(
                    xdXp[st].ap().rearrange("p (h q) -> p h q", q=128)[:, 8 * g:8 * (g + 1), :],
                    xsTp[st].ap().rearrange("p (h q) -> p h q", q=128)[:, 8 * g:8 * (g + 1), :],
                    rep_inner(ddTT[st][:, 8 * g:8 * (g + 1)], 128), ALU.mult),
                    reads=[f"xsT{st}", f"ddT{st}_{par}"], writes=[f"xdX{st}"])
        pt, ptkey = next_pt()
        for st in range(2):
            em.mm(lambda e, pt=pt, st=st, c0=c0: e.transpose(pt[:, st * 128:(st + 1) * 128],
                  B_fm[:, c0 + st * 128:c0 + (st + 1) * 128], eye_b[:]),
                  ptkey, ["B_fm", "eye_b"], last=(st == 1))
        em.op("v", lambda e, pt=pt, btTp=btTp: e.tensor_copy(btTp[:], pt[:, 0:256]),
              reads=[ptkey], writes=[kbtT])
        for g4 in range(8):
            pp, ppk = next_p()
            for q in range(4):
                h = g4 * 4 + q
                for st in range(2):
                    em.mm(lambda e, pp=pp, q=q, h=h, st=st, btTp=btTp, xdXp=xdXp: e.matmul(
                        pp[:, q * 128:(q + 1) * 128], btTp[:, st * 128:(st + 1) * 128],
                        xdXp[st][:, h * 128:(h + 1) * 128],
                        start=(q == 0 and st == 0), stop=(q == 3 and st == 1)),
                        ppk, [kbtT, f"xdX{st}"], last=(q == 3 and st == 1))
            em.op("v", lambda e, pp=pp, scsbp=scsbp: e.tensor_copy(scsbp[:], pp[:]),
                  reads=[ppk], writes=[kscsb])
            em.dma("sync", sstates_d[c * 128:(c + 1) * 128, g4 * 512:(g4 + 1) * 512],
                   scsbp[:], f"sst_{c}_{g4}", [kscsb])
            for q in range(4):
                h = g4 * 4 + q
                em.op("v", lambda e, h=h, q=q, t=state_run, dcl=dcl, scsbp=scsbp: e.scalar_tensor_tensor(
                    t[:, h * 128:(h + 1) * 128], t[:, h * 128:(h + 1) * 128],
                    dcl[:, h:h + 1], scsbp[:, q * 128:(q + 1) * 128], ALU.mult, ALU.add),
                    reads=["state_run", kdcl, kscsb], writes=["state_run"])
        for st in range(2):
            for g in range(4):
                em.op("v", lambda e, st=st, g=g, xdXp=xdXp, xsTp=xsTp, dtT=dtT: e.tensor_tensor(
                    xdXp[st].ap().rearrange("p (h q) -> p h q", q=128)[:, 8 * g:8 * (g + 1), :],
                    xsTp[st].ap().rearrange("p (h q) -> p h q", q=128)[:, 8 * g:8 * (g + 1), :],
                    rep_inner(dtT[st][:, 8 * g:8 * (g + 1)], 128), ALU.mult),
                    reads=[f"xsT{st}", f"dtvT{st}_{par}"], writes=[f"xdX{st}"])
        for lt in range(2):
            for g4 in range(8):
                pp, ppk = next_p()
                for q in range(4):
                    h = g4 * 4 + q
                    for st in range(2):
                        em.mm(lambda e, pp=pp, q=q, h=h, st=st, lt=lt, mb=mb, xdXp=xdXp: e.matmul(
                            pp[:, q * 128:(q + 1) * 128],
                            mb[st][:, h * CHUNK + lt * 128: h * CHUNK + lt * 128 + 128],
                            xdXp[st][:, h * 128:(h + 1) * 128],
                            start=(q == 0 and st == 0), stop=False),
                            ppk, [f"mbuf{st}", f"xdX{st}"], last=False)
                    em.mm(lambda e, pp=pp, q=q, h=h, lt=lt, xsTp=xsTp: e.matmul(
                        pp[:, q * 128:(q + 1) * 128], ddiag_sb[:, h * 128:(h + 1) * 128],
                        xsTp[lt][:, h * 128:(h + 1) * 128],
                        start=False, stop=(q == 3)),
                        ppk, ["ddiag_sb", f"xsT{lt}"], last=(q == 3))
                eng = "v" if g4 % 2 == 0 else "s"
                if eng == "v":
                    em.op("v", lambda e, pp=pp, lt=lt, g4=g4, ysbp=ysbp: e.tensor_copy(
                        ysbp[lt][:, g4 * 512:(g4 + 1) * 512], pp[:]),
                        reads=[ppk], writes=[f"ysb{lt}"])
                else:
                    em.op("s", lambda e, pp=pp, lt=lt, g4=g4, ysbp=ysbp: e.activation(
                        ysbp[lt][:, g4 * 512:(g4 + 1) * 512], pp[:], AF.Copy, bias=0.0, scale=1.0),
                        reads=[ppk], writes=[f"ysb{lt}"])
            em.dma("sync", ypart_d[c0 + lt * 128:c0 + (lt + 1) * 128, :], ysbp[lt][:],
                   f"ypart_{c}_{lt}", [f"ysb{lt}"])

    for c in range(NCH):
        a2_chunk(c)

    em.dma("sync", cc_in[0:128, :], state_run[:], "cc_in", ["state_run"])
    em.dma("sync", cc_in[128:129, 0:32].rearrange("x h -> h x"), ltot[:], "ltot_cc", ["ltot"])

    if upto == "A2":
        _final_wait(nc, em)
        return nc
    pop_scope()
    pop_scope()

    em.collective(lambda e: e.collective_compute(
        "AllGather", ALU.bypass, replica_groups=RG,
        ins=[cc_in.opt()], outs=[cc_out.opt()]), ["cc_in", "ltot_cc"], "cc_out")

    state_b = sbp("state_run_b", [128, HEADS * HD])

    push_scope()
    rbB = sbp("rbB", [128, 8 * CHUNK])
    ebg = sbp("ebg", [128, 8 * CHUNK], BF16)
    cpb = sbp("cpb", [128, HEADS * CHUNK], BF16)
    scast = sbp("scast", [128, HEADS * HD], BF16)
    dcolsB = sbp("dcolsB", [128, 32])
    scld = sbp("scld", [128, HEADS * HD])
    yp = [sbp(f"yp{i}", [128, INTER]) for i in range(2)]
    gload = [sbp(f"gload{i}", [128, INTER], BF16) for i in range(2)]
    sumsq = sbp("sumsq", [128, 1])
    rstdB = sbp("rstdB", [128, 4])
    yfmg = sbp("yfmg", [128, 32 * 512], BF16)
    attn_sb = sbp("attn_sb", [128, 4 * D])
    wop = [sbp(f"wop{i}", [128, 512], BF16) for i in range(3)]
    bnb2 = sbp("bnb2", [128, 24])
    mv2 = sbp("mv2", [128, 2])
    c1 = sbp("c1", [128, 1])
    c2 = sbp("c2", [128, 1])
    c3 = sbp("c3", [128, 1])
    xres = sbp("xres", [128, D])
    x1o = sbp("x1o", [128, D])
    ybf = scast

    ltg = sbp("ltg", [32, 16])
    ltg2 = sbp("ltg2", [32, 16])
    cmk_sb = sbp("cmk_sb", [32, E])
    ngk_sb = sbp("ngk_sb", [32, E])
    ohk_sb = sbp("ohk_sb", [32, E])
    t8 = sbp("t8", [32, E])
    cum1 = sbp("cum1", [32, 1])
    wmat = sbp("wmat", [32, 8])
    wcols = sbp("wcols", [128, 256])
    Sg = [scld, yp[0]]
    Sgk = ["scld", "yp0"]
    Inp = [yp[1], attn_sb]
    Inpk = ["yp1", "attn_sb"]
    em.dma("sync", cmk_sb[:], cmaskB, "cmk_sb")
    em.dma("sync", ngk_sb[:], negB, "ngk_sb")
    em.dma("sync", ohk_sb[:], onehotB, "ohk_sb")
    em.op("v", lambda e: e.memset(ltg[:], 0.0), writes=["ltg"])
    em.op("v", lambda e: e.memset(ltg2[:], 0.0), writes=["ltg2"])
    em.dma("sync", ltg[:, 8:16].rearrange("h (x j) -> h x j", x=1),
           cc_out.rearrange("(j r) n -> n r j", r=129)[0:32, 128:129, :],
           "ltg", ["cc_out"], allow_slow_non_contiguous=True)
    src_, dst_, sk, dk = ltg, ltg2, "ltg", "ltg2"
    for s in (1, 2, 4):
        em.op("v", lambda e, src_=src_, dst_=dst_, s=s: e.tensor_tensor(
            dst_[:, 8:16], src_[:, 8:16], src_[:, 8 - s:16 - s], ALU.add),
            reads=[sk], writes=[dk])
        src_, dst_, sk, dk = dst_, src_, dk, sk
    cum, cumk = src_, sk
    em.op("v", lambda e, cum=cum: e.tensor_tensor(t8[:], cum[:, 8:16], ohk_sb[:], ALU.mult),
          reads=[cumk, "ohk_sb"], writes=["t8"])
    em.op("v", lambda e: e.tensor_reduce(cum1[:], t8[:], AX.X, ALU.add),
          reads=["t8"], writes=["cum1"])
    em.op("v", lambda e, cum=cum: e.tensor_scalar(wmat[:], cum[:, 8:16], cum1[:], -1.0,
          ALU.subtract, ALU.mult), reads=[cumk, "cum1"], writes=["wmat"])
    em.op("v", lambda e: e.tensor_tensor(wmat[:], wmat[:], cmk_sb[:], ALU.mult),
          reads=["wmat", "cmk_sb"], writes=["wmat"])
    em.op("v", lambda e: e.tensor_tensor(wmat[:], wmat[:], ngk_sb[:], ALU.add),
          reads=["wmat", "ngk_sb"], writes=["wmat"])
    em.op("s", lambda e: e.activation(wmat[:], wmat[:], AF.Exp, bias=0.0, scale=1.0),
          reads=["wmat"], writes=["wmat"])
    em.dma("sync", wflat_d.rearrange("x (j h) -> h x j", h=32), wmat.ap().rearrange("h (x j) -> h x j", x=1),
           "wflat_d", ["wmat"], allow_slow_non_contiguous=True)
    em.dma("sync", wcols[:], bcast_part(wflat_d, 128), "wcols", ["wflat_d"])
    for j in range(NCORES):
        sj = j % 2
        em.dma("sync", Sg[sj][:, 0:HEADS * HD], cc_out[j * 129:j * 129 + 128, :], Sgk[sj], ["cc_out"])
        cur, prv = Inpk[j % 2], Inpk[(j + 1) % 2]
        curt, prvt = Inp[j % 2], Inp[(j + 1) % 2]
        for h in range(HEADS):
            if j == 0:
                em.op("v", lambda e, h=h, curt=curt, sj=sj: e.tensor_scalar(
                    curt[:, h * 128:(h + 1) * 128], Sg[sj][:, h * 128:(h + 1) * 128],
                    wcols[:, h:h + 1], None, ALU.mult),
                    reads=[Sgk[sj], "wcols"], writes=[cur])
            else:
                em.op("v", lambda e, h=h, j=j, curt=curt, prvt=prvt, sj=sj: e.scalar_tensor_tensor(
                    curt[:, h * 128:(h + 1) * 128], Sg[sj][:, h * 128:(h + 1) * 128],
                    wcols[:, j * 32 + h:j * 32 + h + 1], prvt[:, h * 128:(h + 1) * 128],
                    ALU.mult, ALU.add),
                    reads=[Sgk[sj], "wcols", prv], writes=[cur])
    em.op("v", lambda e, curt=Inp[(NCORES - 1) % 2]: e.tensor_copy(state_b[:], curt[:, 0:HEADS * HD]),
          reads=[Inpk[(NCORES - 1) % 2]], writes=["state_run_b"])

    if upto == "IN":
        em.dma("sync", dbg_d[0:128, 0:HEADS * HD], state_b[:], "dbg_d", ["state_run_b"])
        _final_wait(nc, em)
        return nc


    def b_chunk(c):
        c0 = c * CHUNK
        ltg4 = (c % 2) * 2
        for g in range(4):
            em.dma("sync", rbB[:], bcast_part(
                acumf_d[c:c + 1, g * 8 * CHUNK:(g + 1) * 8 * CHUNK], 128), "rbB", [f"acumf_{c}"])
            em.op("s", lambda e: e.activation(ebg[:], rbB[:], AF.Exp, bias=0.0, scale=1.0),
                  reads=["rbB"], writes=["ebg"])
            em.op("v", lambda e, g=g, c0=c0: e.tensor_tensor(
                cpb.ap().rearrange("p (h l) -> p h l", h=HEADS)[:, 8 * g:8 * (g + 1), :],
                rep_free(C_fm[:, c0:c0 + CHUNK], 8),
                ebg.ap().rearrange("p (h l) -> p h l", h=8), ALU.mult),
                reads=["ebg", "C_fm"], writes=["cpb"])
        em.op("v", lambda e: e.tensor_copy(scast[:], state_b[:]),
              reads=["state_run_b"], writes=["scast"])
        em.dma("sync", dcolsB[:], bcast_part(dvec_d[c:c + 1, :], 128), "dcolsB", [f"dvec_d_{c}"])
        em.dma("sync", scld[:], sstates_d[c * 128:(c + 1) * 128, :], "scld", [f"sst_{c}_{g4}" for g4 in range(8)])
        for h in range(HEADS):
            em.op("v", lambda e, h=h: e.scalar_tensor_tensor(
                state_b[:, h * 128:(h + 1) * 128], state_b[:, h * 128:(h + 1) * 128],
                dcolsB[:, h:h + 1], scld[:, h * 128:(h + 1) * 128], ALU.mult, ALU.add),
                reads=["state_run_b", "dcolsB", "scld"], writes=["state_run_b"])
        for lt in range(2):
            ypt, kyp = yp[lt], f"yp{lt}"
            glt, kgl = gload[lt], f"gload{lt}"
            em.dma("sync", ypt[:], ypart_d[c0 + lt * 128:c0 + (lt + 1) * 128, :], kyp, [f"ypart_{c}_{lt}"])
            for g4 in range(8):
                pp, ppk = next_p()
                for q in range(4):
                    h = g4 * 4 + q
                    em.mm(lambda e, pp=pp, q=q, h=h, lt=lt: e.matmul(
                        pp[:, q * 128:(q + 1) * 128],
                        cpb[:, h * CHUNK + lt * 128: h * CHUNK + lt * 128 + 128],
                        scast[:, h * 128:(h + 1) * 128],
                        start=(q == 0), stop=(q == 3)),
                        ppk, ["cpb", "scast"], last=(q == 3))
                em.op("v", lambda e, pp=pp, g4=g4, ypt=ypt: e.tensor_tensor(
                    ypt[:, g4 * 512:(g4 + 1) * 512], ypt[:, g4 * 512:(g4 + 1) * 512], pp[:],
                    ALU.add), reads=[ppk, kyp], writes=[kyp])
            em.dma("sync", glt[:], gate_s_d[c0 + lt * 128:c0 + (lt + 1) * 128, :], kgl,
                   [f"gates_{c * 2 + lt}_{p}" for p in range(8)])
            em.op("v", lambda e, ypt=ypt, glt=glt: e.tensor_tensor(ypt[:], ypt[:], glt[:], ALU.mult),
                  reads=[kyp, kgl], writes=[kyp])
            em.op("s", lambda e, ypt=ypt: e.activation(scld[:, 0:2048], ypt[:, 0:2048], AF.Square,
                  bias=0.0, scale=1.0, accum_out=sumsq[:]),
                  reads=[kyp], writes=["scld", "sumsq"])
            em.op("s", lambda e, ypt=ypt: e.activation(scld[:, 0:2048], ypt[:, 2048:4096], AF.Square,
                  bias=0.0, scale=1.0, accum_out=c1[:]),
                  reads=[kyp, "scld"], writes=["scld", "c1"])
            em.op("v", lambda e: e.tensor_tensor(sumsq[:], sumsq[:], c1[:], ALU.add),
                  reads=["sumsq", "c1"], writes=["sumsq"])
            em.op("v", lambda e: e.tensor_scalar(sumsq[:], sumsq[:], 1.0 / INTER, EPS,
                  ALU.mult, ALU.add), reads=["sumsq"], writes=["sumsq"])
            em.op("s", lambda e: e.activation(sumsq[:], sumsq[:], AF.Ln, bias=0.0, scale=1.0),
                  reads=["sumsq"], writes=["sumsq"])
            em.op("s", lambda e, lt=lt, ltg4=ltg4: e.activation(rstdB[:, ltg4 + lt:ltg4 + lt + 1],
                  sumsq[:], AF.Exp, bias=0.0, scale=-0.5),
                  reads=["sumsq"], writes=["rstdB"])
            em.op("v", lambda e, ypt=ypt: e.tensor_copy(ybf[:], ypt[:]), reads=[kyp, "scast"], writes=["scast"])
            for fh in range(4):
                pt, ptkey = next_pt()
                for ff in range(8):
                    f = fh * 8 + ff
                    em.mm(lambda e, pt=pt, ff=ff, f=f: e.transpose(
                        pt[:, ff * 128:(ff + 1) * 128], ybf[:, f * 128:(f + 1) * 128], eye_b[:]),
                        ptkey, ["scast", "eye_b"], last=(ff == 7))
                em.op("v", lambda e, pt=pt, fh=fh, lt=lt, ltg4=ltg4: e.tensor_copy(
                    yfmg.ap().rearrange("p (f l) -> p f l", l=512)[:, fh * 8:(fh + 1) * 8,
                                                                  (ltg4 + lt) * 128:(ltg4 + lt + 1) * 128],
                    pt.ap().rearrange("p (f l) -> p f l", f=8)),
                    reads=[ptkey], writes=["yfmg"])
        if c % 2 == 1:
            gtok0 = (c - 1) * CHUNK
            for dpiece in range(4):
                for f in range(32):
                    jw = f % 3
                    em.dma("sync", wop[jw][:], opWTb_d[f * 128:(f + 1) * 128,
                           dpiece * 512:(dpiece + 1) * 512], f"wop{jw}", [f"opwtb_{f}"])
                    for lt4 in range(4):
                        em.mm(lambda e, lt4=lt4, f=f, jw=jw, dpiece=dpiece: e.matmul(
                            P[lt4][:], yfmg[:, f * 512 + lt4 * 128: f * 512 + (lt4 + 1) * 128],
                            wop[jw][:], start=(f == 0), stop=(f == 31)),
                            pk[lt4], ["yfmg", f"wop{jw}"], last=(lt4 == 3))
                for lt4 in range(4):
                    em.op("v", lambda e, lt4=lt4, dpiece=dpiece: e.tensor_copy(
                        attn_sb[:, lt4 * D + dpiece * 512: lt4 * D + (dpiece + 1) * 512],
                        P[lt4][:]), reads=[pk[lt4]], writes=["attn_sb"])
            for lt4 in range(4):
                for s4 in range(4):
                    em.op("v", lambda e, lt4=lt4, s4=s4: e.bn_stats(
                        bnb2[:, s4 * 6:(s4 + 1) * 6],
                        attn_sb[:, lt4 * D + s4 * 512: lt4 * D + (s4 + 1) * 512]),
                        reads=["attn_sb"], writes=["bnb2"])
                em.op("v", lambda e: e.bn_aggr(mv2[:], bnb2[:]), reads=["bnb2"], writes=["mv2"])
                em.op("v", lambda e, lt4=lt4: e.tensor_tensor(c1[:], rstdB[:, lt4:lt4 + 1],
                      rstdB[:, lt4:lt4 + 1], ALU.mult), reads=["rstdB"], writes=["c1"])
                em.op("v", lambda e: e.tensor_scalar(c1[:], c1[:], mv2[:, 1:2], EPS,
                      ALU.mult, ALU.add), reads=["c1", "mv2"], writes=["c1"])
                em.op("s", lambda e: e.activation(c1[:], c1[:], AF.Ln, bias=0.0, scale=1.0),
                      reads=["c1"], writes=["c1"])
                em.op("s", lambda e: e.activation(c1[:], c1[:], AF.Exp, bias=0.0, scale=-0.5),
                      reads=["c1"], writes=["c1"])
                em.op("v", lambda e, lt4=lt4: e.tensor_tensor(c2[:], c1[:], rstdB[:, lt4:lt4 + 1],
                      ALU.mult), reads=["c1", "rstdB"], writes=["c2"])
                em.op("v", lambda e: e.tensor_scalar(c3[:], mv2[:, 0:1], c2[:], -1.0,
                      ALU.mult, ALU.mult), reads=["mv2", "c2"], writes=["c3"])
                tok0 = gtok0 + lt4 * 128
                em.dma("sync", xres[:], x_in[PAD + tok0:PAD + tok0 + 128, :], "xres")
                em.op("v", lambda e: e.tensor_scalar(xres[:], xres[:], c3[:], None, ALU.add),
                      reads=["xres", "c3"], writes=["xres"])
                em.op("v", lambda e, lt4=lt4: e.scalar_tensor_tensor(x1o[:],
                      attn_sb[:, lt4 * D:(lt4 + 1) * D], c2[:], xres[:], ALU.mult, ALU.add),
                      reads=["attn_sb", "c2", "xres"], writes=["x1o"])
                em.dma("sync", x1_d[tok0:tok0 + 128, :], x1o[:], f"x1_{tok0 // 128}", ["x1o"])
    for c in range(NCH):
        b_chunk(c)

    pop_scope()

    if upto == "B":
        _final_wait(nc, em)
        return nc
    pop_scope()

    push_scope()
    xt_fm = sbp("xt_fm", [128, NKD * TC], BF16)
    h_sb = sbp("h_sb", [128, NKI * TC], BF16)
    w1sb = [sbp(f"w1sb{i}", [128, NKD * 128], BF16) for i in range(2)]
    w3sb = [sbp(f"w3sb{i}", [128, NKD * 128], BF16) for i in range(2)]
    w2sb = [sbp(f"w2sb{i}", [128, NKI * 512], BF16) for i in range(2)]
    x1t = [sbp(f"x1t{i}", [128, D]) for i in range(2)]
    bnb3 = [sbp(f"bnb3{i}", [128, 24]) for i in range(2)]
    mv3 = [sbp(f"mv3{i}", [128, 2]) for i in range(2)]
    r3 = [sbp(f"r3{i}", [128, 1]) for i in range(2)]
    n3 = [sbp(f"n3{i}", [128, 1]) for i in range(2)]
    xth = [sbp(f"xth{i}", [128, D], BF16) for i in range(2)]
    xtl = [sbp(f"xtl{i}", [128, D], BF16) for i in range(2)]
    lofm = sbp("lofm", [128, NKD * 128], BF16)
    gw_sb = sbp("gw_sb", [128, NKD * E], BF16)
    gb_bc = sbp("gb_bc", [128, E])
    id_bc = sbp("id_bc", [128, E])
    lg = sbp("lg", [128, E])
    rcol = sbp("rcol", [128, 1])
    r2col = sbp("r2col", [128, 1])
    e8a = sbp("e8a", [128, E])
    e8b = sbp("e8b", [128, E])
    dw_sb = sbp("dw_sb", [128, 16 * E])
    stg = [sbp(f"stg{i}", [128, 512]) for i in range(4)]
    stgb = [sbp(f"stgb{i}", [128, 512]) for i in range(4)]
    outx = [sbp(f"outx{i}", [128, D]) for i in range(2)]

    em.dma("gp", gw_sb.ap().rearrange("p (k c) -> p k c", k=NKD),
           gateWT.rearrange("(k p) c -> p k c", p=128), "gw_sb")
    em.dma("sync", gb_bc[:], bcast_part(gatebB, 128), "gb_bc")
    em.dma("sync", id_bc[:], bcast_part(idxdB, 128), "id_bc")

    for tt in range(16):
        j = tt % 2
        kx, kb, km, kr, kn = f"x1t{j}", f"bnb3{j}", f"mv3{j}", f"r3{j}", f"n3{j}"
        em.dma("sync", x1t[j][:], x1_d[tt * 128:(tt + 1) * 128, :], kx, [f"x1_{tt}"])
        for s4 in range(4):
            em.op("v", lambda e, j=j, s4=s4: e.bn_stats(bnb3[j][:, s4 * 6:(s4 + 1) * 6],
                  x1t[j][:, s4 * 512:(s4 + 1) * 512]), reads=[kx], writes=[kb])
        em.op("v", lambda e, j=j: e.bn_aggr(mv3[j][:], bnb3[j][:]), reads=[kb], writes=[km])
        em.op("v", lambda e, j=j: e.tensor_scalar(r3[j][:], mv3[j][:, 1:2], EPS, None, ALU.add),
              reads=[km], writes=[kr])
        em.op("s", lambda e, j=j: e.activation(r3[j][:], r3[j][:], AF.Ln, bias=0.0, scale=1.0),
              reads=[kr], writes=[kr])
        em.op("s", lambda e, j=j: e.activation(r3[j][:], r3[j][:], AF.Exp, bias=0.0, scale=-0.5),
              reads=[kr], writes=[kr])
        em.op("v", lambda e, j=j: e.tensor_scalar(n3[j][:], r3[j][:], mv3[j][:, 0:1], -1.0,
              ALU.mult, ALU.mult), reads=[kr, km], writes=[kn])
        em.op("v", lambda e, j=j: e.tensor_scalar(xth[j][:], x1t[j][:], r3[j][:], n3[j][:],
              ALU.mult, ALU.add), reads=[kx, kr, kn], writes=[f"xth{j}"])
        em.op("v", lambda e, j=j: e.tensor_scalar(x1t[j][:], x1t[j][:], r3[j][:], n3[j][:],
              ALU.mult, ALU.add), reads=[kx, kr, kn], writes=[kx])
        em.op("v", lambda e, j=j: e.tensor_tensor(xtl[j][:], x1t[j][:], xth[j][:], ALU.subtract),
              reads=[kx, f"xth{j}"], writes=[f"xtl{j}"])
        for half in range(2):
            pt, ptkey = next_pt()
            for kk in range(8):
                k = half * 8 + kk
                em.mm(lambda e, j=j, k=k, kk=kk, pt=pt: e.transpose(
                    pt[:, kk * 128:(kk + 1) * 128], xth[j][:, k * 128:(k + 1) * 128], eye_b[:]),
                    ptkey, [f"xth{j}", "eye_b"], last=(kk == 7))
            em.op("v", lambda e, tt=tt, half=half, pt=pt: e.tensor_copy(
                xt_fm.ap().rearrange("p (k t) -> p k t", k=NKD)[:, half * 8:(half + 1) * 8,
                                                               tt * 128:(tt + 1) * 128],
                pt.ap().rearrange("p (k t) -> p k t", k=8)),
                reads=[ptkey], writes=["xt_fm"])
        for half in range(2):
            pt, ptkey = next_pt()
            for kk in range(8):
                k = half * 8 + kk
                em.mm(lambda e, j=j, k=k, kk=kk, pt=pt: e.transpose(
                    pt[:, kk * 128:(kk + 1) * 128], xtl[j][:, k * 128:(k + 1) * 128], eye_b[:]),
                    ptkey, [f"xtl{j}", "eye_b"], last=(kk == 7))
            em.op("v", lambda e, half=half, pt=pt: e.tensor_copy(
                lofm.ap().rearrange("p (k t) -> p k t", k=NKD)[:, half * 8:(half + 1) * 8, :],
                pt.ap().rearrange("p (k t) -> p k t", k=8)),
                reads=[ptkey], writes=["lofm"])
        pp, ppk = next_p()
        for k in range(NKD):
            em.mm(lambda e, pp=pp, k=k, tt=tt: e.matmul(pp[:, 0:E],
                  xt_fm[:, k * TC + tt * 128: k * TC + (tt + 1) * 128],
                  gw_sb[:, k * E:(k + 1) * E], start=(k == 0), stop=False),
                  ppk, ["xt_fm", "gw_sb"], last=False)
        for k in range(NKD):
            em.mm(lambda e, pp=pp, k=k: e.matmul(pp[:, 0:E],
                  lofm[:, k * 128:(k + 1) * 128],
                  gw_sb[:, k * E:(k + 1) * E], start=False, stop=(k == NKD - 1)),
                  ppk, ["lofm", "gw_sb"], last=(k == NKD - 1))
        em.op("v", lambda e, pp=pp: e.tensor_copy(lg[:], pp[:, 0:E]), reads=[ppk], writes=["lg"])
        em.op("v", lambda e: e.tensor_reduce(rcol[:], lg[:], AX.X, ALU.max),
              reads=["lg"], writes=["rcol"])
        em.op("v", lambda e: e.tensor_scalar(rcol[:], rcol[:], -1.0, None, ALU.mult),
              reads=["rcol"], writes=["rcol"])
        em.op("s", lambda e: e.activation(lg[:], lg[:], AF.Exp, bias=rcol[:], scale=1.0),
              reads=["lg", "rcol"], writes=["lg"])
        em.op("v", lambda e: e.tensor_reduce(rcol[:], lg[:], AX.X, ALU.add),
              reads=["lg"], writes=["rcol"])
        em.op("v", lambda e: e.reciprocal(rcol[:], rcol[:]), reads=["rcol"], writes=["rcol"])
        em.op("v", lambda e: e.tensor_scalar(lg[:], lg[:], rcol[:], None, ALU.mult),
              reads=["lg", "rcol"], writes=["lg"])
        em.op("v", lambda e: e.tensor_tensor(e8a[:], lg[:], gb_bc[:], ALU.add),
              reads=["lg", "gb_bc"], writes=["e8a"])
        em.op("v", lambda e: e.tensor_reduce(rcol[:], e8a[:], AX.X, ALU.max),
              reads=["e8a"], writes=["rcol"])
        em.op("v", lambda e: e.tensor_scalar(e8b[:], e8a[:], rcol[:], None, ALU.is_equal),
              reads=["e8a", "rcol"], writes=["e8b"])
        em.op("v", lambda e: e.tensor_tensor(e8b[:], e8b[:], id_bc[:], ALU.mult),
              reads=["e8b", "id_bc"], writes=["e8b"])
        em.op("v", lambda e: e.tensor_reduce(r2col[:], e8b[:], AX.X, ALU.max),
              reads=["e8b"], writes=["r2col"])
        em.op("v", lambda e: e.tensor_scalar(e8b[:], e8b[:], r2col[:], None, ALU.is_equal),
              reads=["e8b", "r2col"], writes=["e8b"])
        em.op("v", lambda e: e.scalar_tensor_tensor(e8a[:], e8b[:], -1e9, e8a[:],
              ALU.mult, ALU.add), reads=["e8b", "e8a"], writes=["e8a"])
        em.op("v", lambda e: e.tensor_reduce(rcol[:], e8a[:], AX.X, ALU.max),
              reads=["e8a"], writes=["rcol"])
        em.op("v", lambda e: e.tensor_scalar(e8a[:], e8a[:], rcol[:], None, ALU.is_equal),
              reads=["e8a", "rcol"], writes=["e8a"])
        em.op("v", lambda e: e.tensor_tensor(e8a[:], e8a[:], id_bc[:], ALU.mult),
              reads=["e8a", "id_bc"], writes=["e8a"])
        em.op("v", lambda e: e.tensor_reduce(r2col[:], e8a[:], AX.X, ALU.max),
              reads=["e8a"], writes=["r2col"])
        em.op("v", lambda e: e.tensor_scalar(e8a[:], e8a[:], r2col[:], None, ALU.is_equal),
              reads=["e8a", "r2col"], writes=["e8a"])
        em.op("v", lambda e: e.tensor_tensor(e8a[:], e8a[:], e8b[:], ALU.add),
              reads=["e8a", "e8b"], writes=["e8a"])
        em.op("v", lambda e, tt=tt: e.tensor_tensor(dw_sb[:, tt * E:(tt + 1) * E], lg[:],
              e8a[:], ALU.mult), reads=["lg", "e8a"], writes=["dw_sb"])

    for ei in range(E + 1):
        shared = (ei == 0)
        e_idx = ei - 1
        for ik in range(NKI):
            jj = ik % 2
            if shared:
                w1src = ws1t[:, ik * 128:(ik + 1) * 128]
                w3src = ws3t[:, ik * 128:(ik + 1) * 128]
            else:
                w1src = w1t[e_idx * D:(e_idx + 1) * D, ik * 128:(ik + 1) * 128]
                w3src = w3t[e_idx * D:(e_idx + 1) * D, ik * 128:(ik + 1) * 128]
            em.dma("gp", w1sb[jj].ap().rearrange("p (k c) -> p k c", k=NKD),
                   w1src.rearrange("(k p) c -> p k c", p=128), f"w1sb{jj}")
            em.dma("gp", w3sb[jj].ap().rearrange("p (k c) -> p k c", k=NKD),
                   w3src.rearrange("(k p) c -> p k c", p=128), f"w3sb{jj}")
            for piece in range(4):
                pa, pak = next_p()
                pb, pbk = next_p()
                for k in range(NKD):
                    em.mm(lambda e, pa=pa, k=k, jj=jj, piece=piece: e.matmul(
                        pa[:], w1sb[jj][:, k * 128:(k + 1) * 128],
                        xt_fm[:, k * TC + piece * 512: k * TC + (piece + 1) * 512],
                        start=(k == 0), stop=(k == NKD - 1)),
                        pak, [f"w1sb{jj}", "xt_fm"], last=(k == NKD - 1))
                for k in range(NKD):
                    em.mm(lambda e, pb=pb, k=k, jj=jj, piece=piece: e.matmul(
                        pb[:], w3sb[jj][:, k * 128:(k + 1) * 128],
                        xt_fm[:, k * TC + piece * 512: k * TC + (piece + 1) * 512],
                        start=(k == 0), stop=(k == NKD - 1)),
                        pbk, [f"w3sb{jj}", "xt_fm"], last=(k == NKD - 1))
                sj = piece % 4
                em.op("s", lambda e, sj=sj, pa=pa: e.activation(stg[sj][:], pa[:], AF.Silu,
                      bias=0.0, scale=1.0), reads=[pak], writes=[f"stg{sj}"])
                em.op("v", lambda e, sj=sj, pb=pb, ik=ik, piece=piece: e.tensor_tensor(
                    h_sb[:, ik * TC + piece * 512: ik * TC + (piece + 1) * 512],
                    stg[sj][:], pb[:], ALU.mult),
                    reads=[f"stg{sj}", pbk], writes=["h_sb"])
        for dpiece in range(4):
            jj = dpiece % 2
            w2src = (ws2t if shared else w2t[e_idx * I_FF:(e_idx + 1) * I_FF, :])
            em.dma("gp", w2sb[jj].ap().rearrange("p (k c) -> p k c", k=NKI),
                   w2src[:, dpiece * 512:(dpiece + 1) * 512].rearrange("(k p) c -> p k c", p=128),
                   f"w2sb{jj}")
            for tt in range(16):
                pp, ppk = next_p()
                for ik in range(NKI):
                    em.mm(lambda e, pp=pp, ik=ik, jj=jj, tt=tt: e.matmul(
                        pp[:], h_sb[:, ik * TC + tt * 128: ik * TC + (tt + 1) * 128],
                        w2sb[jj][:, ik * 512:(ik + 1) * 512],
                        start=(ik == 0), stop=(ik == NKI - 1)),
                        ppk, ["h_sb", f"w2sb{jj}"], last=(ik == NKI - 1))
                sj = tt % 4
                if shared:
                    em.op("v", lambda e, sj=sj, pp=pp: e.tensor_copy(stgb[sj][:], pp[:]),
                          reads=[ppk], writes=[f"stgb{sj}"])
                    em.dma("sync", y_d[tt * 128:(tt + 1) * 128, dpiece * 512:(dpiece + 1) * 512],
                           stgb[sj][:], f"y_d_{tt}_{dpiece}", [f"stgb{sj}"])
                else:
                    em.op("v", lambda e, sj=sj, pp=pp, e_idx=e_idx, tt=tt: e.tensor_scalar(
                        stgb[sj][:], pp[:], dw_sb[:, tt * E + e_idx: tt * E + e_idx + 1], None,
                        ALU.mult), reads=[ppk, "dw_sb"], writes=[f"stgb{sj}"])
                    em.dma("gp", y_d[tt * 128:(tt + 1) * 128, dpiece * 512:(dpiece + 1) * 512],
                           stgb[sj][:], f"y_d_{tt}_{dpiece}", [f"stgb{sj}"], accum_op=ALU.add)

    for tt in range(16):
        j = tt % 2
        kx, kb, km, kr, kn = f"x1t{j}", f"bnb3{j}", f"mv3{j}", f"r3{j}", f"n3{j}"
        em.dma("sync", outx[j][:], y_d[tt * 128:(tt + 1) * 128, :], f"outx{j}", [f"y_d_{tt}_{dp}" for dp in range(4)])
        for s4 in range(4):
            em.op("v", lambda e, j=j, s4=s4: e.bn_stats(bnb3[j][:, s4 * 6:(s4 + 1) * 6],
                  outx[j][:, s4 * 512:(s4 + 1) * 512]), reads=[f"outx{j}"], writes=[kb])
        em.op("v", lambda e, j=j: e.bn_aggr(mv3[j][:], bnb3[j][:]), reads=[kb], writes=[km])
        em.op("v", lambda e, j=j: e.tensor_scalar(r3[j][:], mv3[j][:, 1:2], EPS, None, ALU.add),
              reads=[km], writes=[kr])
        em.op("s", lambda e, j=j: e.activation(r3[j][:], r3[j][:], AF.Ln, bias=0.0, scale=1.0),
              reads=[kr], writes=[kr])
        em.op("s", lambda e, j=j: e.activation(r3[j][:], r3[j][:], AF.Exp, bias=0.0, scale=-0.5),
              reads=[kr], writes=[kr])
        em.op("v", lambda e, j=j: e.tensor_scalar(n3[j][:], r3[j][:], mv3[j][:, 0:1], -1.0,
              ALU.mult, ALU.mult), reads=[kr, km], writes=[kn])
        em.dma("sync", x1t[j][:], x1_d[tt * 128:(tt + 1) * 128, :], kx, [f"x1_{tt}"])
        em.op("v", lambda e, j=j: e.tensor_scalar(x1t[j][:], x1t[j][:], n3[j][:], None, ALU.add),
              reads=[kx, kn], writes=[kx])
        em.op("v", lambda e, j=j: e.scalar_tensor_tensor(outx[j][:], outx[j][:], r3[j][:],
              x1t[j][:], ALU.mult, ALU.add), reads=[f"outx{j}", kr, kx], writes=[f"outx{j}"])
        em.dma("sync", out_d[tt * 128:(tt + 1) * 128, :], outx[j][:], f"out_{tt}", [f"outx{j}"])

    pop_scope()
    _final_wait(nc, em)
    return nc


def _final_wait(nc, em):
    for q in ("sync", "gp"):
        for i, sem in enumerate(em.dma_pools[q]):
            u = em.dma_uses[q][i]
            if u:
                em._wait("sync", (sem, u * 16, f"d_{q}{i}"))
    for c in em.COMPUTE:
        if em.cnt[c]:
            em._wait("sync", (em.sem[c], em.cnt[c], f"c_{c}"))
    em.finalize()


def host_inputs(inputs):
    x = np.asarray(inputs["L0_attn_hidden_states"])[0]
    ipW = np.asarray(inputs["L0_attn_in_proj_w"])
    conv_w = np.asarray(inputs["L0_attn_conv_w"])
    conv_b = np.asarray(inputs["L0_attn_conv_b"])
    A_log = np.asarray(inputs["L0_attn_A_log"])
    dt_bias = np.asarray(inputs["L0_attn_dt_bias"])
    Dp = np.asarray(inputs["L0_attn_D"])
    norm_w = np.asarray(inputs["L0_attn_norm_w"])
    out_w = np.asarray(inputs["L0_attn_out_proj_w"])
    gate_w = np.asarray(inputs["L0_ffn_gate_w"])
    gate_b = np.asarray(inputs["L0_ffn_gate_b"])
    W1 = np.asarray(inputs["L0_ffn_W1"])
    W2 = np.asarray(inputs["L0_ffn_W2"])
    W3 = np.asarray(inputs["L0_ffn_W3"])
    Ws1 = np.asarray(inputs["L0_ffn_Ws1"])
    Ws2 = np.asarray(inputs["L0_ffn_Ws2"])
    Ws3 = np.asarray(inputs["L0_ffn_Ws3"])

    shared = {}
    shared["ipWT"] = np.ascontiguousarray(ipW.T)
    cd = np.zeros((NO_HBC, CONVK, 128, 128), np.float32)
    cw = conv_w[:, 0, :]
    ar = np.arange(128)
    for j in range(NO_HBC):
        for k in range(CONVK):
            cd[j, k, ar, ar] = cw[j * 128:(j + 1) * 128, k]
    shared["convdiag"] = cd.reshape(NO_HBC * CONVK * 128, 128)
    shared["convb"] = np.ascontiguousarray(conv_b.reshape(NO_HBC, 128))
    shared["alog"] = np.ascontiguousarray(A_log.reshape(32, 1))
    shared["dtbias"] = np.ascontiguousarray(dt_bias.reshape(32, 1))
    dd = np.zeros((HEADS, 128, 128), np.float32)
    ar128 = np.arange(128)
    dd[:, ar128, ar128] = Dp[:, None]
    shared["ddiag"] = dd.reshape(HEADS * 128, 128)
    shared["normcols"] = np.ascontiguousarray(norm_w.reshape(32, 128).T)
    shared["opWT"] = np.ascontiguousarray(out_w.T)
    shared["gateWT"] = np.ascontiguousarray(gate_w.T)
    shared["gatebB"] = np.ascontiguousarray(gate_b.reshape(1, E))
    shared["idxdB"] = np.ascontiguousarray((E - np.arange(E, dtype=np.float32)).reshape(1, E))
    shared["w1t"] = np.ascontiguousarray(W1.transpose(0, 2, 1)).reshape(E * D, I_FF)
    shared["w3t"] = np.ascontiguousarray(W3.transpose(0, 2, 1)).reshape(E * D, I_FF)
    shared["w2t"] = np.ascontiguousarray(W2.transpose(0, 2, 1)).reshape(E * I_FF, D)
    shared["ws1t"] = np.ascontiguousarray(Ws1.T)
    shared["ws3t"] = np.ascontiguousarray(Ws3.T)
    shared["ws2t"] = np.ascontiguousarray(Ws2.T)
    m01 = np.zeros((2, 128, CHUNK), np.float32)
    for st in range(2):
        for s in range(128):
            m01[st, s, st * 128 + s:] = 1.0
    shared["mask01"] = m01.reshape(2 * 128, CHUNK)
    shared["eye"] = np.eye(128, dtype=np.float32)
    shared["trilM_in"] = np.triu(np.ones((128, 128), np.float32))

    in_maps = []
    for c in range(NCORES):
        m = dict(shared)
        halo = np.zeros((3, D), np.float32) if c == 0 else x[c * TC - 3:c * TC]
        m["x"] = np.ascontiguousarray(
            np.concatenate([np.zeros((125, D), np.float32), halo, x[c * TC:(c + 1) * TC]], 0))
        cm = np.zeros((32, E), np.float32)
        nb = np.full((32, E), -30000.0, np.float32)
        oh = np.zeros((32, E), np.float32)
        cm[:, :c] = 1.0
        nb[:, :c] = 0.0
        if c > 0:
            oh[:, c - 1] = 1.0
        m["cmaskB"], m["negB"], m["onehotB"] = cm, nb, oh
        in_maps.append(m)
    return in_maps


def kernel(**inputs):
    in_maps = host_inputs(inputs)
    nc = build_nc("full")
    res = run_bass_kernel_spmd(nc, in_maps, core_ids=list(range(NCORES)))
    out = np.concatenate([r["out"] for r in res.results], axis=0)
    return out.reshape(1, SEQ, D).astype(np.float32)


# version 07
# speedup vs baseline: 1.3191x; 1.0220x over previous
import dataclasses
import numpy as np
import ml_dtypes
import concourse.bass as bass
import concourse.mybir as mybir
from concourse.bass_utils import run_bass_kernel_spmd

F32 = mybir.dt.float32
BF16 = mybir.dt.bfloat16
AF = mybir.ActivationFunctionType
ALU = mybir.AluOpType
AX = mybir.AxisListType
BF = ml_dtypes.bfloat16

NCORES = 8
SEQ = 16384
D = 2048
TC = SEQ // NCORES
PAD = 128
TF = TC + PAD
NKD = D // 128
INTER = 4096
HEADS = 32
HD = 128
STATE = 128
CONVK = 4
CONV_DIM = INTER + 2 * STATE
NPROJ = INTER + CONV_DIM + HEADS
NO_HBC = CONV_DIM // 128
NO_GATE = INTER // 128
CHUNK = 256
NCH = TC // CHUNK
EPS = 1e-5
E = 8
I_FF = 1024
NKI = I_FF // 128
RG = [[0, 1, 2, 3, 4, 5, 6, 7]]


def bcast_part(ap, n):
    new = [[0, n]] + [list(x) for x in ap.ap][1:]
    return dataclasses.replace(ap, ap=type(ap.ap)(new))


def bcast_prepend(ap, n):
    new = [[0, n]] + [list(x) for x in ap.ap]
    return dataclasses.replace(ap, ap=type(ap.ap)(new))


def rep_inner(ap, n):
    aps = [list(x) for x in ap.ap]
    new = aps + [[0, n]]
    return dataclasses.replace(ap, ap=type(ap.ap)(new))


def rep_free(ap, n):
    aps = [list(x) for x in ap.ap]
    new = [aps[0]] + [[0, n]] + aps[1:]
    return dataclasses.replace(ap, ap=type(ap.ap)(new))


class EM:
    COMPUTE = ("v", "s", "t", "gp")
    NDMA = 6

    def __init__(self, nc):
        self.nc = nc
        self.q = {k: [] for k in ("sync", "gp", "v", "s", "t")}
        self.sem = {k: nc.alloc_semaphore(f"sem_{k}") for k in self.COMPUTE}
        self.cnt = {k: 0 for k in self.COMPUTE}
        self.dma_pools = {q: [nc.alloc_semaphore(f"dma_{q}{i}") for i in range(self.NDMA)]
                          for q in ("sync", "gp")}
        self.dma_uses = {q: [0] * self.NDMA for q in ("sync", "gp")}
        self.dma_rr = {q: 0 for q in ("sync", "gp")}
        self.waited = {k: {} for k in self.q}
        self.writer = {}
        self.readers = {}
        self.pending_t = []
        self._segs = []

    def _wait(self, queue, rec):
        if rec is None:
            return
        sem, val, name = rec
        if val <= 0:
            return
        w = self.waited[queue]
        if w.get(name, -1) >= val:
            return
        w[name] = val
        self.q[queue].append(lambda e, s=sem, v=val: e.wait_ge(s, v))

    def _note_read(self, key, rec):
        d = self.readers.setdefault(key, {})
        old = d.get(rec[2])
        if old is None or old[1] < rec[1]:
            d[rec[2]] = rec

    def _pre(self, queue, reads, writes, pe_acc_key=None):
        for k in reads:
            self._wait(queue, self.writer.get(k))
        for k in writes:
            wrec = self.writer.get(k)
            if not (k == pe_acc_key and wrec is not None and wrec[2] == "c_t"):
                self._wait(queue, wrec)
            for rec in self.readers.get(k, {}).values():
                self._wait(queue, rec)

    def _finish(self, reads, writes, rec):
        for k in reads:
            self._note_read(k, rec)
        for k in writes:
            self.writer[k] = rec
            self.readers[k] = {}

    def op(self, queue, fn, reads=(), writes=()):
        assert queue in self.COMPUTE
        reads, writes = list(reads), list(writes)
        self._pre(queue, reads, writes)
        self.cnt[queue] += 1
        sem, val = self.sem[queue], self.cnt[queue]
        self.q[queue].append(lambda e, f=fn, s=sem, v=1: f(e).then_inc(s, v))
        self._finish(reads, writes, (sem, val, f"c_{queue}"))

    def mm(self, fn, out_key, in_keys, last=True):
        q = "t"
        in_keys = [k for k in in_keys if k is not None]
        self._pre(q, in_keys, [out_key], pe_acc_key=out_key)
        if last:
            self.cnt[q] += 1
            sem, val = self.sem[q], self.cnt[q]
            self.q[q].append(lambda e, f=fn, s=sem: f(e).then_inc(s, 1))
            rec = (sem, val, "c_t")
            self.pending_t.append((in_keys, [out_key]))
            for rds, wrs in self.pending_t:
                self._finish(rds, wrs, rec)
            self.pending_t = []
        else:
            self.q[q].append(fn)
            self.pending_t.append((in_keys, [out_key]))
            self.writer[out_key] = (self.sem[q], self.cnt[q] + 1, "c_t")

    def dma(self, queue, dst_ap, src_ap, wkey, rkeys=(), **kw):
        pool, uses = self.dma_pools[queue], self.dma_uses[queue]
        i = self.dma_rr[queue]
        self.dma_rr[queue] = (i + 1) % self.NDMA
        sem = pool[i]
        name = f"d_{queue}{i}"
        self._wait(queue, (sem, uses[i] * 16, name))
        rkeys = list(rkeys)
        self._pre(queue, rkeys, [wkey] if wkey is not None else [])
        uses[i] += 1
        val = uses[i] * 16
        self.q[queue].append(
            lambda e, d=dst_ap, s=src_ap, sm=sem, k=dict(kw): e.dma_start(d, s, **k).then_inc(sm, 16))
        self._finish(rkeys, [wkey] if wkey is not None else [], (sem, val, name))

    def collective(self, fn, rkeys, wkey):
        q = "gp"
        self._pre(q, list(rkeys), [wkey])
        self.cnt[q] += 1
        sem, val = self.sem[q], self.cnt[q]
        self.q[q].append(lambda e, f=fn, s=sem: f(e).then_inc(s, 1))
        self._finish(list(rkeys), [wkey], (sem, val, f"c_{q}"))

    def barrier(self):
        for q in ("sync", "gp"):
            for i, sem in enumerate(self.dma_pools[q]):
                u = self.dma_uses[q][i]
                if u:
                    self._wait("sync", (sem, u * 16, f"d_{q}{i}"))
        self._segs.append(self.q)
        self.q = {k: [] for k in self.q}
        for qq in self.waited:
            w = self.waited[qq]
            for c in self.COMPUTE:
                w[f"c_{c}"] = max(w.get(f"c_{c}", 0), self.cnt[c])
            for q in ("sync", "gp"):
                for i in range(self.NDMA):
                    w[f"d_{q}{i}"] = max(w.get(f"d_{q}{i}", 0), self.dma_uses[q][i] * 16)

    def finalize(self):
        self.barrier()
        for seg in self._segs:
            if all(len(v) == 0 for v in seg.values()):
                continue
            with self.nc.Block() as b:
                for qname, reg in (("sync", b.sync), ("gp", b.gpsimd), ("v", b.vector),
                                   ("s", b.scalar), ("t", b.tensor)):
                    if seg[qname]:
                        reg(lambda e, ops=tuple(seg[qname]): [f(e) for f in ops])


def build_nc(upto="full", dbg=()):
    nc = bass.Bass("TRN2", target_bir_lowering=False, debug=False)
    em = EM(nc)

    def dram(name, shape, dtype=F32):
        kind = "ExternalOutput" if name in dbg else "Internal"
        return nc.dram_tensor(name, shape, dtype, kind=kind).ap()

    def inp(name, shape, dtype=F32):
        return nc.dram_tensor(name, shape, dtype, kind="ExternalInput").ap()

    def sb(name, shape, dtype=F32):
        return nc.alloc_sbuf_tensor(name, shape, dtype)

    scopes = []

    def push_scope():
        scopes.append([])

    def sbp(name, shape, dtype=F32):
        g = nc.sbuf_tensor(name, shape, dtype)
        t = g.__enter__()
        scopes[-1].append(g)
        return t

    def pop_scope():
        em.barrier()
        for g in reversed(scopes.pop()):
            g.__exit__(None, None, None)

    x_in = inp("x", [TF, D])
    ipWT = inp("ipWT", [D, NPROJ])
    convdiag = inp("convdiag", [NO_HBC * CONVK * 128, 128])
    convb = inp("convb", [NO_HBC, 128])
    alog = inp("alog", [32, 1])
    dtbias = inp("dtbias", [32, 1])
    ddiag = inp("ddiag", [HEADS * 128, 128])
    normcols = inp("normcols", [128, 32])
    opWT = inp("opWT", [INTER, D])
    gateWT = inp("gateWT", [D, E])
    gatebB = inp("gatebB", [1, E])
    idxdB = inp("idxdB", [1, E])
    w1t = inp("w1t", [E * D, I_FF])
    w3t = inp("w3t", [E * D, I_FF])
    w2t = inp("w2t", [E * I_FF, D])
    ws1t = inp("ws1t", [D, I_FF])
    ws3t = inp("ws3t", [D, I_FF])
    ws2t = inp("ws2t", [I_FF, D])
    mask01 = inp("mask01", [2 * 128, CHUNK])
    cmaskB = inp("cmaskB", [32, E])
    negB = inp("negB", [32, E])
    onehotB = inp("onehotB", [32, E])
    eye_in = inp("eye", [128, 128])
    trilM_in = inp("trilM_in", [128, 128])

    out_d = nc.dram_tensor("out", [TC, D], F32, kind="ExternalOutput").ap()
    xs_d = dram("xs_d", [INTER, TC], BF16)
    u_d = dram("u_d", [128, NKD * TF], BF16)
    ypart_d = dram("ypart_d", [TC, INTER])
    acum_d = dram("acum_d", [32, TC])
    acumf_d = dram("acumf_d", [NCH, 32 * CHUNK])
    x1_d = dram("x1_d", [TC, D])
    gate_s_d = dram("gate_s_d", [TC, INTER], BF16)
    opWTb_d = dram("opWTb_d", [INTER, D], BF16)
    y_d = dram("y_d", [TC, D])
    sstates_d = dram("sstates_d", [NCH * 128, HEADS * HD])
    dvec_d = dram("dvec_d", [NCH, 32])
    cc_in = dram("cc_in", [129, HEADS * HD])
    cc_out = dram("cc_out", [NCORES * 129, HEADS * HD])
    wflat_d = dram("wflat_d", [1, NCORES * 32])
    dbg_d = dram("dbg_d", [512, max(TC, INTER)]) if "dbg_d" in dbg else None

    P = [nc.alloc_psum_tensor(f"P{i}", [128, 512], F32) for i in range(6)]
    PT = [nc.alloc_psum_tensor(f"PT{i}", [128, 1024], BF16) for i in range(2)]
    pk = [f"P{i}" for i in range(6)]
    ptk = [f"PT{i}" for i in range(2)]
    prr = [0]

    def next_p():
        i = prr[0] % 6
        prr[0] += 1
        return P[i], pk[i]

    ptrr = [0]

    def next_pt():
        i = ptrr[0] % 2
        ptrr[0] += 1
        return PT[i], ptk[i]

    eye_f = sb("eye_f", [128, 128], F32)
    trilM = sb("trilM", [128, 128], F32)
    onesM = sb("onesM", [128, 128], F32)
    eye_b = sb("eye_b", [128, 128], BF16)
    ones_f = sb("ones_f", [1, 128], F32)
    em.dma("sync", eye_f[:], eye_in, "eye_f")
    em.dma("sync", trilM[:], trilM_in, "trilM")
    em.op("v", lambda e: e.tensor_copy(eye_b[:], eye_f[:]), reads=["eye_f"], writes=["eye_b"])
    em.op("v", lambda e: e.memset(ones_f[:], 1.0), writes=["ones_f"])
    em.op("v", lambda e: e.memset(onesM[:], 1.0), writes=["onesM"])
    normc_sb = sb("normc_sb", [128, 32])
    em.dma("sync", normc_sb[:], normcols, "normc_sb")
    push_scope()
    C_fm = sbp("C_fm", [128, TC], BF16)
    push_scope()
    B_fm = sbp("B_fm", [128, TC], BF16)
    dtv = sbp("dtv", [32, TC])
    ad = sbp("ad", [32, TC])
    alog_s = sbp("alog_s", [32, 1])
    dtb_s = sbp("dtb_s", [32, 1])
    acol = sbp("acol", [32, 1])
    ltot = sbp("ltot", [32, 1])

    push_scope()
    u_fm = sbp("u_fm", [128, NKD * TF], BF16)
    NT = TF // 128
    dtraw = sbp("dtraw", [32, TC])
    spt = sbp("spt", [32, TC])
    xt = [sbp(f"xt{i}", [128, D]) for i in range(2)]
    bnb = [sbp(f"bnb{i}", [128, 24]) for i in range(2)]
    mv = [sbp(f"mv{i}", [128, 2]) for i in range(2)]
    rstd = [sbp(f"rstd{i}", [128, 1]) for i in range(2)]
    negmr = [sbp(f"negmr{i}", [128, 1]) for i in range(2)]
    ubf = [sbp(f"ubf{i}", [128, D], BF16) for i in range(2)]

    for i in range(NT):
        j = i % 2
        kx, kb, km, kr, kn, ku = (f"xt{j}", f"bnb{j}", f"mv{j}", f"rstd{j}", f"negmr{j}", f"ubf{j}")
        em.dma("sync", xt[j][:], x_in[i * 128:(i + 1) * 128, :], kx)
        for s4 in range(4):
            em.op("v", lambda e, j=j, s4=s4: e.bn_stats(bnb[j][:, s4 * 6:(s4 + 1) * 6],
                                                        xt[j][:, s4 * 512:(s4 + 1) * 512]),
                  reads=[kx], writes=[kb])
        em.op("v", lambda e, j=j: e.bn_aggr(mv[j][:], bnb[j][:]), reads=[kb], writes=[km])
        em.op("v", lambda e, j=j: e.tensor_scalar(rstd[j][:], mv[j][:, 1:2], EPS, None, ALU.add),
              reads=[km], writes=[kr])
        em.op("s", lambda e, j=j: e.activation(rstd[j][:], rstd[j][:], AF.Ln, bias=0.0, scale=1.0),
              reads=[kr], writes=[kr])
        em.op("s", lambda e, j=j: e.activation(rstd[j][:], rstd[j][:], AF.Exp, bias=0.0, scale=-0.5),
              reads=[kr], writes=[kr])
        em.op("v", lambda e, j=j: e.tensor_scalar(negmr[j][:], rstd[j][:], mv[j][:, 0:1], -1.0,
                                                  ALU.mult, ALU.mult),
              reads=[kr, km], writes=[kn])
        em.op("v", lambda e, j=j: e.tensor_scalar(ubf[j][:], xt[j][:], rstd[j][:], negmr[j][:],
                                                  ALU.mult, ALU.add),
              reads=[kx, kr, kn], writes=[ku])
        for half in range(2):
            pt, ptkey = next_pt()
            for kk in range(8):
                k = half * 8 + kk
                em.mm(lambda e, j=j, k=k, kk=kk, pt=pt: e.transpose(
                    pt[:, kk * 128:(kk + 1) * 128], ubf[j][:, k * 128:(k + 1) * 128], eye_b[:]),
                    ptkey, [ku, "eye_b"], last=(kk == 7))
            em.op("v", lambda e, i=i, half=half, pt=pt: e.tensor_copy(
                u_fm.ap().rearrange("p (k t) -> p k t", k=NKD)[:, half * 8:(half + 1) * 8,
                                                              i * 128:(i + 1) * 128],
                pt.ap().rearrange("p (k t) -> p k t", k=8)),
                reads=[ptkey], writes=["u_fm"])

    for f in range(32):
        jj = f % 2
        em.dma("sync", xt[jj][:], opWT[f * 128:(f + 1) * 128, :], f"xt{jj}")
        em.op("v", lambda e, jj=jj, f=f: e.tensor_scalar(ubf[jj][:], xt[jj][:],
              normc_sb[:, f:f + 1], None, ALU.mult),
              reads=[f"xt{jj}", "normc_sb"], writes=[f"ubf{jj}"])
        em.dma("sync", opWTb_d[f * 128:(f + 1) * 128, :], ubf[jj][:], f"opwtb_{f}", [f"ubf{jj}"])

    if upto == "A1ln":
        em.dma("sync", u_d, u_fm[:], "u_d", ["u_fm"])
        _final_wait(nc, em)
        return nc
    wsb = [sbp(f"wsb{i}", [128, NKD * 128], BF16) for i in range(2)]
    projbf = [sbp(f"projbf{i}", [128, TF], BF16) for i in range(2)]
    cdiag = [sbp(f"cdiag{i}", [128, CONVK * 128], BF16) for i in range(2)]
    cbcol = [sbp(f"cbcol{i}", [128, 1]) for i in range(2)]
    hbco = [sbp(f"hbco{i}", [128, TC], BF16) for i in range(2)]
    PIECES = [(0, 512), (512, 512), (1024, 512), (1536, 512), (2048, 128)]

    ipWT_r = ipWT.rearrange("(k p) c -> p k c", p=128)

    for oj in range(NO_HBC + 1):
        jj = oj % 2
        ocol = NO_GATE * 128 + oj * 128
        wk = f"wsb{jj}"
        ncols = 128 if oj < NO_HBC else 32
        em.dma("gp", wsb[jj].ap().rearrange("p (k c) -> p k c", k=NKD)[:, :, 0:ncols],
               ipWT_r[:, :, ocol:ocol + ncols], wk)
        pj = f"projbf{jj}"
        if oj < NO_HBC:
            pps = [next_p() for _ in range(5)]
            for k in range(NKD):
                for pi, (poff, plen) in enumerate(PIECES):
                    pp, ppk = pps[pi]
                    em.mm(lambda e, pp=pp, plen=plen, k=k, jj=jj, poff=poff: e.matmul(
                        pp[:, 0:plen], wsb[jj][:, k * 128:(k + 1) * 128],
                        u_fm[:, k * TF + poff: k * TF + poff + plen],
                        start=(k == 0), stop=(k == NKD - 1)),
                        ppk, [wk, "u_fm"], last=(k == NKD - 1 and pi == 4))
            for pi, (poff, plen) in enumerate(PIECES):
                pp, ppk = pps[pi]
                eng = "v" if pi % 2 == 0 else "s"
                if eng == "v":
                    em.op("v", lambda e, jj=jj, poff=poff, plen=plen, pp=pp: e.tensor_copy(
                        projbf[jj][:, poff:poff + plen], pp[:, 0:plen]),
                        reads=[ppk], writes=[pj])
                else:
                    em.op("s", lambda e, jj=jj, poff=poff, plen=plen, pp=pp: e.activation(
                        projbf[jj][:, poff:poff + plen], pp[:, 0:plen], AF.Copy, bias=0.0, scale=1.0),
                        reads=[ppk], writes=[pj])
            ck = f"cdiag{jj}"
            cbk = f"cbcol{jj}"
            em.dma("gp", cdiag[jj][:], convdiag[oj * CONVK * 128:(oj + 1) * CONVK * 128, :]
                   .rearrange("(k p) c -> p k c", p=128), ck)
            em.dma("sync", cbcol[jj][:], convb[oj:oj + 1, :].rearrange("x c -> c x"), cbk)
            hk = f"hbco{jj}"
            for piece in range(4):
                pp, ppk = next_p()
                for k in range(CONVK):
                    em.mm(lambda e, pp=pp, k=k, jj=jj, piece=piece: e.matmul(
                        pp[:], cdiag[jj][:, k * 128:(k + 1) * 128],
                        projbf[jj][:, 125 + k + piece * 512: 125 + k + piece * 512 + 512],
                        start=(k == 0), stop=(k == CONVK - 1)),
                        ppk, [ck, pj], last=(k == CONVK - 1))
                if oj < NO_GATE:
                    em.op("s", lambda e, jj=jj, piece=piece, pp=pp: e.activation(
                        hbco[jj][:, piece * 512:(piece + 1) * 512], pp[:], AF.Silu,
                        bias=cbcol[jj][:], scale=1.0), reads=[ppk, cbk], writes=[hk])
                else:
                    dstt = B_fm if oj == NO_GATE else C_fm
                    dk = "B_fm" if oj == NO_GATE else "C_fm"
                    em.op("s", lambda e, dstt=dstt, piece=piece, pp=pp, jj=jj: e.activation(
                        dstt[:, piece * 512:(piece + 1) * 512], pp[:], AF.Silu,
                        bias=cbcol[jj][:], scale=1.0), reads=[ppk, cbk], writes=[dk])
            if oj < NO_GATE:
                em.dma("sync", xs_d[oj * 128:(oj + 1) * 128, :], hbco[jj][:], f"xs_d_{oj}", [hk])
        else:
            for piece in range(4):
                pp, ppk = next_p()
                for k in range(NKD):
                    em.mm(lambda e, pp=pp, k=k, jj=jj, piece=piece: e.matmul(
                        pp[0:32, :], wsb[jj][:, k * 128: k * 128 + 32],
                        u_fm[:, k * TF + 128 + piece * 512: k * TF + 128 + piece * 512 + 512],
                        start=(k == 0), stop=(k == NKD - 1)),
                        ppk, [wk, "u_fm"], last=(k == NKD - 1))
                em.op("v", lambda e, piece=piece, pp=pp: e.tensor_copy(
                    dtraw[:, piece * 512:(piece + 1) * 512], pp[0:32, :]),
                    reads=[ppk], writes=["dtraw"])


    em.dma("sync", alog_s[:], alog, "alog_s")
    em.dma("sync", dtb_s[:], dtbias, "dtb_s")
    em.op("s", lambda e: e.activation(acol[:], alog_s[:], AF.Exp, bias=0.0, scale=1.0),
          reads=["alog_s"], writes=["acol"])
    em.op("v", lambda e: e.tensor_scalar(acol[:], acol[:], -1.0, None, ALU.mult),
          reads=["acol"], writes=["acol"])
    em.op("v", lambda e: e.tensor_scalar(dtv[:], dtraw[:], dtb_s[:], None, ALU.add),
          reads=["dtraw", "dtb_s"], writes=["dtv"])
    em.op("s", lambda e: e.activation(spt[:], dtv[:], AF.Abs, bias=0.0, scale=1.0),
          reads=["dtv"], writes=["spt"])
    em.op("s", lambda e: e.activation(spt[:], spt[:], AF.Exp, bias=0.0, scale=-1.0),
          reads=["spt"], writes=["spt"])
    em.op("v", lambda e: e.tensor_scalar(spt[:], spt[:], 1.0, None, ALU.add),
          reads=["spt"], writes=["spt"])
    em.op("s", lambda e: e.activation(spt[:], spt[:], AF.Ln, bias=0.0, scale=1.0),
          reads=["spt"], writes=["spt"])
    em.op("v", lambda e: e.tensor_scalar(dtv[:], dtv[:], 0.0, None, ALU.max),
          reads=["dtv"], writes=["dtv"])
    em.op("v", lambda e: e.tensor_tensor(dtv[:], dtv[:], spt[:], ALU.add),
          reads=["dtv", "spt"], writes=["dtv"])
    em.op("v", lambda e: e.tensor_scalar(dtv[:], dtv[:], 100.0, None, ALU.min),
          reads=["dtv"], writes=["dtv"])
    em.op("v", lambda e: e.tensor_scalar(ad[:], dtv[:], acol[:], None, ALU.mult),
          reads=["dtv", "acol"], writes=["ad"])

    if upto == "A1":
        em.dma("sync", dbg_d[0:32, 0:TC], dtv[:], "dbg_d", ["dtv"])
        em.dma("sync", dbg_d[32:64, 0:TC], ad[:], "dbg_d", ["ad"])
        em.dma("gp", dbg_d[64:192, 0:TC], B_fm[:], "dbg_d", ["B_fm"])
        em.dma("gp", dbg_d[192:320, 0:TC], C_fm[:], "dbg_d", ["C_fm"])
        _final_wait(nc, em)
        return nc

    wgbuf = [sbp(f"wgbuf{i}", [128, NKD * 512], BF16) for i in range(2)]
    gstg = [sbp(f"gstg{i}", [128, 512], BF16) for i in range(2)]
    for piece in range(8):
        jj = piece % 2
        wk = f"wgbuf{jj}"
        em.dma("gp", wgbuf[jj].ap().rearrange("p (k c) -> p k c", k=NKD),
               ipWT_r[:, :, piece * 512:(piece + 1) * 512], wk)
        for tt in range(16):
            pp, ppk = next_p()
            for k in range(NKD):
                em.mm(lambda e, pp=pp, k=k, jj=jj, tt=tt: e.matmul(
                    pp[:], u_fm[:, k * TF + 128 + tt * 128: k * TF + 128 + (tt + 1) * 128],
                    wgbuf[jj][:, k * 512:(k + 1) * 512],
                    start=(k == 0), stop=(k == NKD - 1)),
                    ppk, [wk, "u_fm"], last=(k == NKD - 1))
            sj = tt % 2
            em.op("s", lambda e, sj=sj, pp=pp: e.activation(gstg[sj][:], pp[:], AF.Silu,
                  bias=0.0, scale=1.0), reads=[ppk], writes=[f"gstg{sj}"])
            em.dma("sync", gate_s_d[tt * 128:(tt + 1) * 128, piece * 512:(piece + 1) * 512],
                   gstg[sj][:], f"gates_{tt}_{piece}", [f"gstg{sj}"])

    pop_scope()

    push_scope()
    mask_sb = sbp("mask_sb", [128, 2 * CHUNK])
    em.dma("sync", mask_sb.ap().rearrange("p (s c) -> p s c", s=2),
           mask01.rearrange("(s p) c -> p s c", p=128), "mask_sb")
    ddiag_sb = sbp("ddiag_sb", [128, HEADS * 128], BF16)
    em.dma("gp", ddiag_sb.ap().rearrange("p (h c) -> p h c", h=HEADS),
           ddiag.rearrange("(h p) c -> p h c", p=128), "ddiag_sb")
    adTb = [sbp(f"adT{i}", [128, 64]) for i in range(2)]
    em.op("v", lambda e: e.memset(ltot[:], 0.0), writes=["ltot"])
    acumC = [sbp(f"acumC{i}", [32, CHUNK]) for i in range(2)]
    dvecC = [sbp(f"dvecC{i}", [32, 1]) for i in range(2)]
    dsrow = [sbp(f"dsrow{i}", [32, CHUNK]) for i in range(2)]
    ddrow = [sbp(f"ddrow{i}", [32, CHUNK]) for i in range(2)]
    acumT = [[sbp(f"acumT{i}_{p}", [128, 32]) for i in range(2)] for p in range(2)]
    dtvT = [[sbp(f"dtvT{i}_{p}", [128, 32]) for i in range(2)] for p in range(2)]
    ddT = [[sbp(f"ddT{i}_{p}", [128, 32]) for i in range(2)] for p in range(2)]
    dcols = [sbp(f"dcols{i}", [128, 32]) for i in range(2)]
    rbg = [sbp(f"rbg{i}", [128, 8 * CHUNK]) for i in range(2)]
    dbg8 = [sbp(f"dbg8{i}", [128, 8 * CHUNK], BF16) for i in range(2)]
    lbg = [sbp(f"lbg{i}", [128, 8 * CHUNK], BF16) for i in range(2)]
    mbufS = [sbp(f"mbuf{i}", [128, HEADS * CHUNK], BF16) for i in range(2)]
    mbuf = [mbufS, mbufS]
    gm = [[sbp(f"gm{i}_{p}", [128, CHUNK], BF16) for i in range(2)] for p in range(2)]
    xsgS = sbp("xsg", [128, 8 * CHUNK], BF16)
    xsg = [xsgS, xsgS]
    xsTS = [sbp(f"xsT{i}", [128, HEADS * 128], BF16) for i in range(2)]
    xsT = [xsTS, xsTS]
    xdXS = [sbp(f"xdX{i}", [128, HEADS * 128], BF16) for i in range(2)]
    xdX = [xdXS, xdXS]
    btT = [sbp(f"btT{i}", [128, 2 * 128], BF16) for i in range(2)]
    state_run = sbp("state_run", [128, HEADS * 128])
    em.op("v", lambda e, t=state_run: e.memset(t[:], 0.0), writes=["state_run"])
    scsb = [sbp(f"scsb{i}", [128, 512]) for i in range(2)]
    ysbS = [sbp(f"ysb{i}", [128, INTER]) for i in range(2)]
    ysb = [ysbS, ysbS]

    def a2_chunk(c):
        c0 = c * CHUNK
        par = c % 2
        acum, dvec = acumC[par], dvecC[par]
        adT, kadT = adTb[par], f"adT{par}"
        dsr, ddr = dsrow[par], ddrow[par]
        acT, dtT, ddTT = acumT[par], dtvT[par], ddT[par]
        dcl, rbg_, dbg_, lbg_ = dcols[par], rbg[par], dbg8[par], lbg[par]
        mb, gmp, xsgp, xsTp, xdXp = mbuf[par], gm[par], xsg[par], xsT[par], xdX[par]
        btTp, scsbp, ysbp = btT[par], scsb[par], ysb[par]
        kac, kdv = f"acumC{par}", f"dvecC{par}"
        kaA, kaB = f"acA{par}", f"acB{par}"
        kdsr, kddr = f"dsrow{par}", f"ddrow{par}"
        kdcl, krbg, kdbg, klbg = f"dcols{par}", f"rbg{par}", f"dbg8{par}", f"lbg{par}"
        kxsg, kbtT, kscsb = "xsg", f"btT{par}", f"scsb{par}"
        adTp, adTk = next_p()
        for st in range(2):
            em.mm(lambda e, st=st, c0=c0, adTp=adTp: e.transpose(
                adTp[0:128, st * 32:(st + 1) * 32], ad[:, c0 + st * 128:c0 + (st + 1) * 128],
                eye_f[0:32, 0:32]), adTk, ["ad", "eye_f"], last=(st == 1))
        em.op("v", lambda e, adTp=adTp, adT=adT: e.tensor_copy(adT[:], adTp[0:128, 0:64]),
              reads=[adTk], writes=[kadT])
        acTp, acTk = next_p()
        em.mm(lambda e, acTp=acTp, adT=adT: e.matmul(acTp[:, 0:32], trilM[:], adT[:, 0:32],
              start=True, stop=True), acTk, [kadT, "trilM"], last=False)
        em.mm(lambda e, acTp=acTp, adT=adT: e.matmul(acTp[:, 32:64], onesM[:], adT[:, 0:32],
              start=True, stop=False), acTk, [kadT, "onesM"], last=False)
        em.mm(lambda e, acTp=acTp, adT=adT: e.matmul(acTp[:, 32:64], trilM[:], adT[:, 32:64],
              start=False, stop=True), acTk, [kadT, "trilM"], last=True)
        for st in range(2):
            em.op("v", lambda e, st=st, acTp=acTp, acT=acT: e.tensor_copy(
                acT[st][:], acTp[:, st * 32:(st + 1) * 32]), reads=[acTk], writes=[f"acumT{st}_{par}"])
        acRp, acRk = next_p()
        for st in range(2):
            em.mm(lambda e, st=st, acT=acT, acRp=acRp: e.transpose(
                acRp[0:32, st * 128:(st + 1) * 128], acT[st][:],
                eye_f[0:128, 0:128]), acRk, [f"acumT{st}_{par}", "eye_f"], last=(st == 1))
        em.op("v", lambda e, acRp=acRp, acum=acum: e.tensor_copy(acum[:], acRp[0:32, 0:256]),
              reads=[acRk], writes=[kac])
        em.op("v", lambda e, acum=acum: e.tensor_tensor(ltot[:], ltot[:], acum[:, 255:256], ALU.add),
              reads=["ltot", kac], writes=["ltot"])
        em.op("s", lambda e, acum=acum, dvec=dvec: e.activation(dvec[:], acum[:, 255:256], AF.Exp, bias=0.0, scale=1.0),
              reads=[kac], writes=[kdv])
        em.dma("sync", dvec_d[c:c + 1, :].rearrange("x h -> h x"), dvec[:], f"dvec_d_{c}", [kdv])
        em.dma("sync", acum_d[:, c0:c0 + CHUNK], acum[:], f"acum_d_{c}", [kac])
        em.dma("sync", acumf_d[c:c + 1, :].rearrange("x (h l) -> h x l", h=32),
               acum.ap().rearrange("h (x l) -> h x l", x=1), f"acumf_{c}", [kac])
        em.op("v", lambda e, acum=acum, dsr=dsr: e.tensor_scalar(dsr[:], acum[:], acum[:, 255:256], -1.0,
              ALU.subtract, ALU.mult), reads=[kac], writes=[kdsr])
        em.op("s", lambda e, dsr=dsr: e.activation(dsr[:], dsr[:], AF.Exp, bias=0.0, scale=1.0),
              reads=[kdsr], writes=[kdsr])
        em.op("v", lambda e, c0=c0, ddr=ddr, dsr=dsr: e.tensor_tensor(ddr[:], dtv[:, c0:c0 + CHUNK], dsr[:], ALU.mult),
              reads=["dtv", kdsr], writes=[kddr])
        for st in range(2):
            for (srcap, srck, dstt, dstk) in (
                    (dtv[:, c0 + st * 128:c0 + (st + 1) * 128], "dtv", dtT[st], f"dtvT{st}_{par}"),
                    (ddr[:, st * 128:(st + 1) * 128], kddr, ddTT[st], f"ddT{st}_{par}")):
                pp, ppk = next_p()
                em.mm(lambda e, pp=pp, srcap=srcap: e.transpose(pp[0:128, 0:32], srcap,
                      eye_f[0:32, 0:32]), ppk, [srck, "eye_f"], last=True)
                em.op("v", lambda e, pp=pp, dstt=dstt: e.tensor_copy(dstt[:], pp[0:128, 0:32]),
                      reads=[ppk], writes=[dstk])
        em.dma("sync", dcl[:], bcast_part(dvec_d[c:c + 1, :], 128), kdcl, [f"dvec_d_{c}"])
        for st in range(2):
            pp, ppk = next_p()
            em.mm(lambda e, st=st, pp=pp, c0=c0: e.matmul(pp[:, 0:CHUNK],
                  B_fm[:, c0 + st * 128:c0 + (st + 1) * 128], C_fm[:, c0:c0 + CHUNK],
                  start=True, stop=True), ppk, ["B_fm", "C_fm"], last=True)
            em.op("v", lambda e, st=st, pp=pp, gmp=gmp: e.tensor_tensor(gmp[st][:], pp[:, 0:CHUNK],
                  mask_sb[:, st * CHUNK:(st + 1) * CHUNK], ALU.mult),
                  reads=[ppk, "mask_sb"], writes=[f"gm{st}_{par}"])
        for g in range(4):
            em.dma("sync", rbg_[:], bcast_part(
                acumf_d[c:c + 1, g * 8 * CHUNK:(g + 1) * 8 * CHUNK], 128), krbg, [f"acumf_{c}"])
            for st in range(2):
                em.op("v", lambda e, st=st, g=g, dbgt=dbg_, rbgt=rbg_, acT=acT: e.tensor_tensor(
                    dbgt.ap().rearrange("p (h l) -> p h l", h=8),
                    rbgt.ap().rearrange("p (h l) -> p h l", h=8),
                    rep_inner(acT[st][:, 8 * g:8 * (g + 1)], CHUNK), ALU.subtract),
                    reads=[krbg, f"acumT{st}_{par}"], writes=[kdbg])
                em.op("s", lambda e, lbgt=lbg_, dbgt=dbg_: e.activation(lbgt[:], dbgt[:], AF.Exp, bias=0.0, scale=1.0),
                      reads=[kdbg], writes=[klbg])
                em.op("v", lambda e, lbgt=lbg_: e.tensor_scalar(lbgt[:], lbgt[:], 1.0, None, ALU.min),
                      reads=[klbg], writes=[klbg])
                em.op("v", lambda e, st=st, g=g, mb=mb, lbgt=lbg_, gmp=gmp: e.tensor_tensor(
                    mb[st].ap().rearrange("p (h l) -> p h l", h=HEADS)[:, 8 * g:8 * (g + 1), :],
                    lbgt.ap().rearrange("p (h l) -> p h l", h=8),
                    rep_free(gmp[st][:], 8), ALU.mult),
                    reads=[klbg, f"gm{st}_{par}"], writes=[f"mbuf{st}"])
        for g in range(4):
            em.dma("sync", xsgp.ap().rearrange("p (h t) -> p h t", h=8),
                   xs_d.rearrange("(h p) t -> p h t", p=128)[:, 8 * g:8 * (g + 1), c0:c0 + CHUNK],
                   kxsg, [f"xs_d_{o}" for o in range(8 * g, 8 * (g + 1))])
            for st in range(2):
                pt, ptkey = next_pt()
                for hh in range(8):
                    em.mm(lambda e, pt=pt, hh=hh, st=st, xsgp=xsgp: e.transpose(
                        pt[:, hh * 128:(hh + 1) * 128],
                        xsgp[:, hh * CHUNK + st * 128: hh * CHUNK + (st + 1) * 128], eye_b[:]),
                        ptkey, [kxsg, "eye_b"], last=(hh == 7))
                em.op("v", lambda e, st=st, g=g, pt=pt, xsTp=xsTp: e.tensor_copy(
                    xsTp[st][:, g * 1024:(g + 1) * 1024], pt[:]),
                    reads=[ptkey], writes=[f"xsT{st}"])
        for st in range(2):
            for g in range(4):
                em.op("v", lambda e, st=st, g=g, xdXp=xdXp, xsTp=xsTp, ddTT=ddTT: e.tensor_tensor(
                    xdXp[st].ap().rearrange("p (h q) -> p h q", q=128)[:, 8 * g:8 * (g + 1), :],
                    xsTp[st].ap().rearrange("p (h q) -> p h q", q=128)[:, 8 * g:8 * (g + 1), :],
                    rep_inner(ddTT[st][:, 8 * g:8 * (g + 1)], 128), ALU.mult),
                    reads=[f"xsT{st}", f"ddT{st}_{par}"], writes=[f"xdX{st}"])
        pt, ptkey = next_pt()
        for st in range(2):
            em.mm(lambda e, pt=pt, st=st, c0=c0: e.transpose(pt[:, st * 128:(st + 1) * 128],
                  B_fm[:, c0 + st * 128:c0 + (st + 1) * 128], eye_b[:]),
                  ptkey, ["B_fm", "eye_b"], last=(st == 1))
        em.op("v", lambda e, pt=pt, btTp=btTp: e.tensor_copy(btTp[:], pt[:, 0:256]),
              reads=[ptkey], writes=[kbtT])
        for g4 in range(8):
            pp, ppk = next_p()
            for q in range(4):
                h = g4 * 4 + q
                for st in range(2):
                    em.mm(lambda e, pp=pp, q=q, h=h, st=st, btTp=btTp, xdXp=xdXp: e.matmul(
                        pp[:, q * 128:(q + 1) * 128], btTp[:, st * 128:(st + 1) * 128],
                        xdXp[st][:, h * 128:(h + 1) * 128],
                        start=(q == 0 and st == 0), stop=(q == 3 and st == 1)),
                        ppk, [kbtT, f"xdX{st}"], last=(q == 3 and st == 1))
            em.op("v", lambda e, pp=pp, scsbp=scsbp: e.tensor_copy(scsbp[:], pp[:]),
                  reads=[ppk], writes=[kscsb])
            em.dma("sync", sstates_d[c * 128:(c + 1) * 128, g4 * 512:(g4 + 1) * 512],
                   scsbp[:], f"sst_{c}_{g4}", [kscsb])
            for q in range(4):
                h = g4 * 4 + q
                em.op("v", lambda e, h=h, q=q, t=state_run, dcl=dcl, scsbp=scsbp: e.scalar_tensor_tensor(
                    t[:, h * 128:(h + 1) * 128], t[:, h * 128:(h + 1) * 128],
                    dcl[:, h:h + 1], scsbp[:, q * 128:(q + 1) * 128], ALU.mult, ALU.add),
                    reads=["state_run", kdcl, kscsb], writes=["state_run"])
        for st in range(2):
            for g in range(4):
                em.op("v", lambda e, st=st, g=g, xdXp=xdXp, xsTp=xsTp, dtT=dtT: e.tensor_tensor(
                    xdXp[st].ap().rearrange("p (h q) -> p h q", q=128)[:, 8 * g:8 * (g + 1), :],
                    xsTp[st].ap().rearrange("p (h q) -> p h q", q=128)[:, 8 * g:8 * (g + 1), :],
                    rep_inner(dtT[st][:, 8 * g:8 * (g + 1)], 128), ALU.mult),
                    reads=[f"xsT{st}", f"dtvT{st}_{par}"], writes=[f"xdX{st}"])
        for lt in range(2):
            for g4 in range(8):
                pp, ppk = next_p()
                for q in range(4):
                    h = g4 * 4 + q
                    for st in range(2):
                        em.mm(lambda e, pp=pp, q=q, h=h, st=st, lt=lt, mb=mb, xdXp=xdXp: e.matmul(
                            pp[:, q * 128:(q + 1) * 128],
                            mb[st][:, h * CHUNK + lt * 128: h * CHUNK + lt * 128 + 128],
                            xdXp[st][:, h * 128:(h + 1) * 128],
                            start=(q == 0 and st == 0), stop=False),
                            ppk, [f"mbuf{st}", f"xdX{st}"], last=False)
                    em.mm(lambda e, pp=pp, q=q, h=h, lt=lt, xsTp=xsTp: e.matmul(
                        pp[:, q * 128:(q + 1) * 128], ddiag_sb[:, h * 128:(h + 1) * 128],
                        xsTp[lt][:, h * 128:(h + 1) * 128],
                        start=False, stop=(q == 3)),
                        ppk, ["ddiag_sb", f"xsT{lt}"], last=(q == 3))
                eng = "v" if g4 % 2 == 0 else "s"
                if eng == "v":
                    em.op("v", lambda e, pp=pp, lt=lt, g4=g4, ysbp=ysbp: e.tensor_copy(
                        ysbp[lt][:, g4 * 512:(g4 + 1) * 512], pp[:]),
                        reads=[ppk], writes=[f"ysb{lt}"])
                else:
                    em.op("s", lambda e, pp=pp, lt=lt, g4=g4, ysbp=ysbp: e.activation(
                        ysbp[lt][:, g4 * 512:(g4 + 1) * 512], pp[:], AF.Copy, bias=0.0, scale=1.0),
                        reads=[ppk], writes=[f"ysb{lt}"])
            em.dma("sync", ypart_d[c0 + lt * 128:c0 + (lt + 1) * 128, :], ysbp[lt][:],
                   f"ypart_{c}_{lt}", [f"ysb{lt}"])

    for c in range(NCH):
        a2_chunk(c)

    em.dma("sync", cc_in[0:128, :], state_run[:], "cc_in", ["state_run"])
    em.dma("sync", cc_in[128:129, 0:32].rearrange("x h -> h x"), ltot[:], "ltot_cc", ["ltot"])

    if upto == "A2":
        _final_wait(nc, em)
        return nc
    pop_scope()
    pop_scope()

    em.collective(lambda e: e.collective_compute(
        "AllGather", ALU.bypass, replica_groups=RG,
        ins=[cc_in.opt()], outs=[cc_out.opt()]), ["cc_in", "ltot_cc"], "cc_out")

    state_b = sbp("state_run_b", [128, HEADS * HD])

    push_scope()
    rbB = sbp("rbB", [128, 8 * CHUNK])
    ebg = sbp("ebg", [128, 8 * CHUNK], BF16)
    cpb = sbp("cpb", [128, HEADS * CHUNK], BF16)
    scast = sbp("scast", [128, HEADS * HD], BF16)
    dcolsB = sbp("dcolsB", [128, 32])
    scld = sbp("scld", [128, HEADS * HD])
    yp = [sbp(f"yp{i}", [128, INTER]) for i in range(2)]
    gload = [sbp(f"gload{i}", [128, INTER], BF16) for i in range(2)]
    sumsq = sbp("sumsq", [128, 1])
    rstdB = sbp("rstdB", [128, 4])
    yfmg = sbp("yfmg", [128, 32 * 512], BF16)
    attn_sb = sbp("attn_sb", [128, 4 * D])
    wop = [sbp(f"wop{i}", [128, 512], BF16) for i in range(3)]
    bnb2 = sbp("bnb2", [128, 24])
    mv2 = sbp("mv2", [128, 2])
    c1 = sbp("c1", [128, 1])
    c2 = sbp("c2", [128, 1])
    c3 = sbp("c3", [128, 1])
    xres = sbp("xres", [128, D])
    x1o = sbp("x1o", [128, D])
    ybf = scast

    ltg = sbp("ltg", [32, 16])
    ltg2 = sbp("ltg2", [32, 16])
    cmk_sb = sbp("cmk_sb", [32, E])
    ngk_sb = sbp("ngk_sb", [32, E])
    ohk_sb = sbp("ohk_sb", [32, E])
    t8 = sbp("t8", [32, E])
    cum1 = sbp("cum1", [32, 1])
    wmat = sbp("wmat", [32, 8])
    wcols = sbp("wcols", [128, 256])
    Sg = [scld, yp[0]]
    Sgk = ["scld", "yp0"]
    Inp = [yp[1], attn_sb]
    Inpk = ["yp1", "attn_sb"]
    em.dma("sync", cmk_sb[:], cmaskB, "cmk_sb")
    em.dma("sync", ngk_sb[:], negB, "ngk_sb")
    em.dma("sync", ohk_sb[:], onehotB, "ohk_sb")
    em.op("v", lambda e: e.memset(ltg[:], 0.0), writes=["ltg"])
    em.op("v", lambda e: e.memset(ltg2[:], 0.0), writes=["ltg2"])
    em.dma("sync", ltg[:, 8:16].rearrange("h (x j) -> h x j", x=1),
           cc_out.rearrange("(j r) n -> n r j", r=129)[0:32, 128:129, :],
           "ltg", ["cc_out"], allow_slow_non_contiguous=True)
    src_, dst_, sk, dk = ltg, ltg2, "ltg", "ltg2"
    for s in (1, 2, 4):
        em.op("v", lambda e, src_=src_, dst_=dst_, s=s: e.tensor_tensor(
            dst_[:, 8:16], src_[:, 8:16], src_[:, 8 - s:16 - s], ALU.add),
            reads=[sk], writes=[dk])
        src_, dst_, sk, dk = dst_, src_, dk, sk
    cum, cumk = src_, sk
    em.op("v", lambda e, cum=cum: e.tensor_tensor(t8[:], cum[:, 8:16], ohk_sb[:], ALU.mult),
          reads=[cumk, "ohk_sb"], writes=["t8"])
    em.op("v", lambda e: e.tensor_reduce(cum1[:], t8[:], AX.X, ALU.add),
          reads=["t8"], writes=["cum1"])
    em.op("v", lambda e, cum=cum: e.tensor_scalar(wmat[:], cum[:, 8:16], cum1[:], -1.0,
          ALU.subtract, ALU.mult), reads=[cumk, "cum1"], writes=["wmat"])
    em.op("v", lambda e: e.tensor_tensor(wmat[:], wmat[:], cmk_sb[:], ALU.mult),
          reads=["wmat", "cmk_sb"], writes=["wmat"])
    em.op("v", lambda e: e.tensor_tensor(wmat[:], wmat[:], ngk_sb[:], ALU.add),
          reads=["wmat", "ngk_sb"], writes=["wmat"])
    em.op("s", lambda e: e.activation(wmat[:], wmat[:], AF.Exp, bias=0.0, scale=1.0),
          reads=["wmat"], writes=["wmat"])
    em.dma("sync", wflat_d.rearrange("x (j h) -> h x j", h=32), wmat.ap().rearrange("h (x j) -> h x j", x=1),
           "wflat_d", ["wmat"], allow_slow_non_contiguous=True)
    em.dma("sync", wcols[:], bcast_part(wflat_d, 128), "wcols", ["wflat_d"])
    for j in range(NCORES):
        sj = j % 2
        em.dma("sync", Sg[sj][:, 0:HEADS * HD], cc_out[j * 129:j * 129 + 128, :], Sgk[sj], ["cc_out"])
        cur, prv = Inpk[j % 2], Inpk[(j + 1) % 2]
        curt, prvt = Inp[j % 2], Inp[(j + 1) % 2]
        for h in range(HEADS):
            if j == 0:
                em.op("v", lambda e, h=h, curt=curt, sj=sj: e.tensor_scalar(
                    curt[:, h * 128:(h + 1) * 128], Sg[sj][:, h * 128:(h + 1) * 128],
                    wcols[:, h:h + 1], None, ALU.mult),
                    reads=[Sgk[sj], "wcols"], writes=[cur])
            else:
                em.op("v", lambda e, h=h, j=j, curt=curt, prvt=prvt, sj=sj: e.scalar_tensor_tensor(
                    curt[:, h * 128:(h + 1) * 128], Sg[sj][:, h * 128:(h + 1) * 128],
                    wcols[:, j * 32 + h:j * 32 + h + 1], prvt[:, h * 128:(h + 1) * 128],
                    ALU.mult, ALU.add),
                    reads=[Sgk[sj], "wcols", prv], writes=[cur])
    em.op("v", lambda e, curt=Inp[(NCORES - 1) % 2]: e.tensor_copy(state_b[:], curt[:, 0:HEADS * HD]),
          reads=[Inpk[(NCORES - 1) % 2]], writes=["state_run_b"])

    if upto == "IN":
        em.dma("sync", dbg_d[0:128, 0:HEADS * HD], state_b[:], "dbg_d", ["state_run_b"])
        _final_wait(nc, em)
        return nc


    def b_chunk(c):
        c0 = c * CHUNK
        ltg4 = (c % 2) * 2
        for g in range(4):
            em.dma("sync", rbB[:], bcast_part(
                acumf_d[c:c + 1, g * 8 * CHUNK:(g + 1) * 8 * CHUNK], 128), "rbB", [f"acumf_{c}"])
            em.op("s", lambda e: e.activation(ebg[:], rbB[:], AF.Exp, bias=0.0, scale=1.0),
                  reads=["rbB"], writes=["ebg"])
            em.op("v", lambda e, g=g, c0=c0: e.tensor_tensor(
                cpb.ap().rearrange("p (h l) -> p h l", h=HEADS)[:, 8 * g:8 * (g + 1), :],
                rep_free(C_fm[:, c0:c0 + CHUNK], 8),
                ebg.ap().rearrange("p (h l) -> p h l", h=8), ALU.mult),
                reads=["ebg", "C_fm"], writes=["cpb"])
        em.op("v", lambda e: e.tensor_copy(scast[:], state_b[:]),
              reads=["state_run_b"], writes=["scast"])
        em.dma("sync", dcolsB[:], bcast_part(dvec_d[c:c + 1, :], 128), "dcolsB", [f"dvec_d_{c}"])
        em.dma("sync", scld[:], sstates_d[c * 128:(c + 1) * 128, :], "scld", [f"sst_{c}_{g4}" for g4 in range(8)])
        for h in range(HEADS):
            em.op("v", lambda e, h=h: e.scalar_tensor_tensor(
                state_b[:, h * 128:(h + 1) * 128], state_b[:, h * 128:(h + 1) * 128],
                dcolsB[:, h:h + 1], scld[:, h * 128:(h + 1) * 128], ALU.mult, ALU.add),
                reads=["state_run_b", "dcolsB", "scld"], writes=["state_run_b"])
        for lt in range(2):
            ypt, kyp = yp[lt], f"yp{lt}"
            glt, kgl = gload[lt], f"gload{lt}"
            em.dma("sync", ypt[:], ypart_d[c0 + lt * 128:c0 + (lt + 1) * 128, :], kyp, [f"ypart_{c}_{lt}"])
            for g4 in range(8):
                pp, ppk = next_p()
                for q in range(4):
                    h = g4 * 4 + q
                    em.mm(lambda e, pp=pp, q=q, h=h, lt=lt: e.matmul(
                        pp[:, q * 128:(q + 1) * 128],
                        cpb[:, h * CHUNK + lt * 128: h * CHUNK + lt * 128 + 128],
                        scast[:, h * 128:(h + 1) * 128],
                        start=(q == 0), stop=(q == 3)),
                        ppk, ["cpb", "scast"], last=(q == 3))
                em.op("v", lambda e, pp=pp, g4=g4, ypt=ypt: e.tensor_tensor(
                    ypt[:, g4 * 512:(g4 + 1) * 512], ypt[:, g4 * 512:(g4 + 1) * 512], pp[:],
                    ALU.add), reads=[ppk, kyp], writes=[kyp])
            em.dma("sync", glt[:], gate_s_d[c0 + lt * 128:c0 + (lt + 1) * 128, :], kgl,
                   [f"gates_{c * 2 + lt}_{p}" for p in range(8)])
            em.op("v", lambda e, ypt=ypt, glt=glt: e.tensor_tensor(ypt[:], ypt[:], glt[:], ALU.mult),
                  reads=[kyp, kgl], writes=[kyp])
            em.op("s", lambda e, ypt=ypt: e.activation(scld[:, 0:2048], ypt[:, 0:2048], AF.Square,
                  bias=0.0, scale=1.0, accum_out=sumsq[:]),
                  reads=[kyp], writes=["scld", "sumsq"])
            em.op("s", lambda e, ypt=ypt: e.activation(scld[:, 0:2048], ypt[:, 2048:4096], AF.Square,
                  bias=0.0, scale=1.0, accum_out=c1[:]),
                  reads=[kyp, "scld"], writes=["scld", "c1"])
            em.op("v", lambda e: e.tensor_tensor(sumsq[:], sumsq[:], c1[:], ALU.add),
                  reads=["sumsq", "c1"], writes=["sumsq"])
            em.op("v", lambda e: e.tensor_scalar(sumsq[:], sumsq[:], 1.0 / INTER, EPS,
                  ALU.mult, ALU.add), reads=["sumsq"], writes=["sumsq"])
            em.op("s", lambda e: e.activation(sumsq[:], sumsq[:], AF.Ln, bias=0.0, scale=1.0),
                  reads=["sumsq"], writes=["sumsq"])
            em.op("s", lambda e, lt=lt, ltg4=ltg4: e.activation(rstdB[:, ltg4 + lt:ltg4 + lt + 1],
                  sumsq[:], AF.Exp, bias=0.0, scale=-0.5),
                  reads=["sumsq"], writes=["rstdB"])
            em.op("v", lambda e, ypt=ypt: e.tensor_copy(ybf[:], ypt[:]), reads=[kyp, "scast"], writes=["scast"])
            for fh in range(4):
                pt, ptkey = next_pt()
                for ff in range(8):
                    f = fh * 8 + ff
                    em.mm(lambda e, pt=pt, ff=ff, f=f: e.transpose(
                        pt[:, ff * 128:(ff + 1) * 128], ybf[:, f * 128:(f + 1) * 128], eye_b[:]),
                        ptkey, ["scast", "eye_b"], last=(ff == 7))
                em.op("v", lambda e, pt=pt, fh=fh, lt=lt, ltg4=ltg4: e.tensor_copy(
                    yfmg.ap().rearrange("p (f l) -> p f l", l=512)[:, fh * 8:(fh + 1) * 8,
                                                                  (ltg4 + lt) * 128:(ltg4 + lt + 1) * 128],
                    pt.ap().rearrange("p (f l) -> p f l", f=8)),
                    reads=[ptkey], writes=["yfmg"])
        if c % 2 == 1:
            gtok0 = (c - 1) * CHUNK
            for dpiece in range(4):
                for f in range(32):
                    jw = f % 3
                    em.dma("sync", wop[jw][:], opWTb_d[f * 128:(f + 1) * 128,
                           dpiece * 512:(dpiece + 1) * 512], f"wop{jw}", [f"opwtb_{f}"])
                    for lt4 in range(4):
                        em.mm(lambda e, lt4=lt4, f=f, jw=jw, dpiece=dpiece: e.matmul(
                            P[lt4][:], yfmg[:, f * 512 + lt4 * 128: f * 512 + (lt4 + 1) * 128],
                            wop[jw][:], start=(f == 0), stop=(f == 31)),
                            pk[lt4], ["yfmg", f"wop{jw}"], last=(lt4 == 3))
                for lt4 in range(4):
                    em.op("v", lambda e, lt4=lt4, dpiece=dpiece: e.tensor_copy(
                        attn_sb[:, lt4 * D + dpiece * 512: lt4 * D + (dpiece + 1) * 512],
                        P[lt4][:]), reads=[pk[lt4]], writes=["attn_sb"])
            for lt4 in range(4):
                for s4 in range(4):
                    em.op("v", lambda e, lt4=lt4, s4=s4: e.bn_stats(
                        bnb2[:, s4 * 6:(s4 + 1) * 6],
                        attn_sb[:, lt4 * D + s4 * 512: lt4 * D + (s4 + 1) * 512]),
                        reads=["attn_sb"], writes=["bnb2"])
                em.op("v", lambda e: e.bn_aggr(mv2[:], bnb2[:]), reads=["bnb2"], writes=["mv2"])
                em.op("v", lambda e, lt4=lt4: e.tensor_tensor(c1[:], rstdB[:, lt4:lt4 + 1],
                      rstdB[:, lt4:lt4 + 1], ALU.mult), reads=["rstdB"], writes=["c1"])
                em.op("v", lambda e: e.tensor_scalar(c1[:], c1[:], mv2[:, 1:2], EPS,
                      ALU.mult, ALU.add), reads=["c1", "mv2"], writes=["c1"])
                em.op("s", lambda e: e.activation(c1[:], c1[:], AF.Ln, bias=0.0, scale=1.0),
                      reads=["c1"], writes=["c1"])
                em.op("s", lambda e: e.activation(c1[:], c1[:], AF.Exp, bias=0.0, scale=-0.5),
                      reads=["c1"], writes=["c1"])
                em.op("v", lambda e, lt4=lt4: e.tensor_tensor(c2[:], c1[:], rstdB[:, lt4:lt4 + 1],
                      ALU.mult), reads=["c1", "rstdB"], writes=["c2"])
                em.op("v", lambda e: e.tensor_scalar(c3[:], mv2[:, 0:1], c2[:], -1.0,
                      ALU.mult, ALU.mult), reads=["mv2", "c2"], writes=["c3"])
                tok0 = gtok0 + lt4 * 128
                em.dma("sync", xres[:], x_in[PAD + tok0:PAD + tok0 + 128, :], "xres")
                em.op("v", lambda e: e.tensor_scalar(xres[:], xres[:], c3[:], None, ALU.add),
                      reads=["xres", "c3"], writes=["xres"])
                em.op("v", lambda e, lt4=lt4: e.scalar_tensor_tensor(x1o[:],
                      attn_sb[:, lt4 * D:(lt4 + 1) * D], c2[:], xres[:], ALU.mult, ALU.add),
                      reads=["attn_sb", "c2", "xres"], writes=["x1o"])
                em.dma("sync", x1_d[tok0:tok0 + 128, :], x1o[:], f"x1_{tok0 // 128}", ["x1o"])
    for c in range(NCH):
        b_chunk(c)

    pop_scope()

    if upto == "B":
        _final_wait(nc, em)
        return nc
    pop_scope()

    push_scope()
    xt_fm = sbp("xt_fm", [128, NKD * TC], BF16)
    h_sb = sbp("h_sb", [128, NKI * TC], BF16)
    dw_sb = sbp("dw_sb", [128, 16 * E])
    stg = [sbp(f"stg{i}", [128, 512]) for i in range(4)]
    stgb = [sbp(f"stgb{i}", [128, 512]) for i in range(4)]
    push_scope()
    x1t = [sbp(f"x1t{i}", [128, D]) for i in range(2)]
    bnb3 = [sbp(f"bnb3{i}", [128, 24]) for i in range(2)]
    mv3 = [sbp(f"mv3{i}", [128, 2]) for i in range(2)]
    r3 = [sbp(f"r3{i}", [128, 1]) for i in range(2)]
    n3 = [sbp(f"n3{i}", [128, 1]) for i in range(2)]
    xth = [sbp(f"xth{i}", [128, D], BF16) for i in range(2)]
    xtl = [sbp(f"xtl{i}", [128, D], BF16) for i in range(2)]
    lofm = sbp("lofm", [128, NKD * 128], BF16)
    gw_sb = sbp("gw_sb", [128, NKD * E], BF16)
    gb_bc = sbp("gb_bc", [128, E])
    id_bc = sbp("id_bc", [128, E])
    lg = sbp("lg", [128, E])
    rcol = sbp("rcol", [128, 1])
    r2col = sbp("r2col", [128, 1])
    e8a = sbp("e8a", [128, E])
    e8b = sbp("e8b", [128, E])

    em.dma("gp", gw_sb.ap().rearrange("p (k c) -> p k c", k=NKD),
           gateWT.rearrange("(k p) c -> p k c", p=128), "gw_sb")
    em.dma("sync", gb_bc[:], bcast_part(gatebB, 128), "gb_bc")
    em.dma("sync", id_bc[:], bcast_part(idxdB, 128), "id_bc")

    for tt in range(16):
        j = tt % 2
        kx, kb, km, kr, kn = f"x1t{j}", f"bnb3{j}", f"mv3{j}", f"r3{j}", f"n3{j}"
        em.dma("sync", x1t[j][:], x1_d[tt * 128:(tt + 1) * 128, :], kx, [f"x1_{tt}"])
        for s4 in range(4):
            em.op("v", lambda e, j=j, s4=s4: e.bn_stats(bnb3[j][:, s4 * 6:(s4 + 1) * 6],
                  x1t[j][:, s4 * 512:(s4 + 1) * 512]), reads=[kx], writes=[kb])
        em.op("v", lambda e, j=j: e.bn_aggr(mv3[j][:], bnb3[j][:]), reads=[kb], writes=[km])
        em.op("v", lambda e, j=j: e.tensor_scalar(r3[j][:], mv3[j][:, 1:2], EPS, None, ALU.add),
              reads=[km], writes=[kr])
        em.op("s", lambda e, j=j: e.activation(r3[j][:], r3[j][:], AF.Ln, bias=0.0, scale=1.0),
              reads=[kr], writes=[kr])
        em.op("s", lambda e, j=j: e.activation(r3[j][:], r3[j][:], AF.Exp, bias=0.0, scale=-0.5),
              reads=[kr], writes=[kr])
        em.op("v", lambda e, j=j: e.tensor_scalar(n3[j][:], r3[j][:], mv3[j][:, 0:1], -1.0,
              ALU.mult, ALU.mult), reads=[kr, km], writes=[kn])
        em.op("v", lambda e, j=j: e.tensor_scalar(xth[j][:], x1t[j][:], r3[j][:], n3[j][:],
              ALU.mult, ALU.add), reads=[kx, kr, kn], writes=[f"xth{j}"])
        em.op("v", lambda e, j=j: e.tensor_scalar(x1t[j][:], x1t[j][:], r3[j][:], n3[j][:],
              ALU.mult, ALU.add), reads=[kx, kr, kn], writes=[kx])
        em.op("v", lambda e, j=j: e.tensor_tensor(xtl[j][:], x1t[j][:], xth[j][:], ALU.subtract),
              reads=[kx, f"xth{j}"], writes=[f"xtl{j}"])
        for half in range(2):
            pt, ptkey = next_pt()
            for kk in range(8):
                k = half * 8 + kk
                em.mm(lambda e, j=j, k=k, kk=kk, pt=pt: e.transpose(
                    pt[:, kk * 128:(kk + 1) * 128], xth[j][:, k * 128:(k + 1) * 128], eye_b[:]),
                    ptkey, [f"xth{j}", "eye_b"], last=(kk == 7))
            em.op("v", lambda e, tt=tt, half=half, pt=pt: e.tensor_copy(
                xt_fm.ap().rearrange("p (k t) -> p k t", k=NKD)[:, half * 8:(half + 1) * 8,
                                                               tt * 128:(tt + 1) * 128],
                pt.ap().rearrange("p (k t) -> p k t", k=8)),
                reads=[ptkey], writes=["xt_fm"])
        for half in range(2):
            pt, ptkey = next_pt()
            for kk in range(8):
                k = half * 8 + kk
                em.mm(lambda e, j=j, k=k, kk=kk, pt=pt: e.transpose(
                    pt[:, kk * 128:(kk + 1) * 128], xtl[j][:, k * 128:(k + 1) * 128], eye_b[:]),
                    ptkey, [f"xtl{j}", "eye_b"], last=(kk == 7))
            em.op("v", lambda e, half=half, pt=pt: e.tensor_copy(
                lofm.ap().rearrange("p (k t) -> p k t", k=NKD)[:, half * 8:(half + 1) * 8, :],
                pt.ap().rearrange("p (k t) -> p k t", k=8)),
                reads=[ptkey], writes=["lofm"])
        pp, ppk = next_p()
        for k in range(NKD):
            em.mm(lambda e, pp=pp, k=k, tt=tt: e.matmul(pp[:, 0:E],
                  xt_fm[:, k * TC + tt * 128: k * TC + (tt + 1) * 128],
                  gw_sb[:, k * E:(k + 1) * E], start=(k == 0), stop=False),
                  ppk, ["xt_fm", "gw_sb"], last=False)
        for k in range(NKD):
            em.mm(lambda e, pp=pp, k=k: e.matmul(pp[:, 0:E],
                  lofm[:, k * 128:(k + 1) * 128],
                  gw_sb[:, k * E:(k + 1) * E], start=False, stop=(k == NKD - 1)),
                  ppk, ["lofm", "gw_sb"], last=(k == NKD - 1))
        em.op("v", lambda e, pp=pp: e.tensor_copy(lg[:], pp[:, 0:E]), reads=[ppk], writes=["lg"])
        em.op("v", lambda e: e.tensor_reduce(rcol[:], lg[:], AX.X, ALU.max),
              reads=["lg"], writes=["rcol"])
        em.op("v", lambda e: e.tensor_scalar(rcol[:], rcol[:], -1.0, None, ALU.mult),
              reads=["rcol"], writes=["rcol"])
        em.op("s", lambda e: e.activation(lg[:], lg[:], AF.Exp, bias=rcol[:], scale=1.0),
              reads=["lg", "rcol"], writes=["lg"])
        em.op("v", lambda e: e.tensor_reduce(rcol[:], lg[:], AX.X, ALU.add),
              reads=["lg"], writes=["rcol"])
        em.op("v", lambda e: e.reciprocal(rcol[:], rcol[:]), reads=["rcol"], writes=["rcol"])
        em.op("v", lambda e: e.tensor_scalar(lg[:], lg[:], rcol[:], None, ALU.mult),
              reads=["lg", "rcol"], writes=["lg"])
        em.op("v", lambda e: e.tensor_tensor(e8a[:], lg[:], gb_bc[:], ALU.add),
              reads=["lg", "gb_bc"], writes=["e8a"])
        em.op("v", lambda e: e.tensor_reduce(rcol[:], e8a[:], AX.X, ALU.max),
              reads=["e8a"], writes=["rcol"])
        em.op("v", lambda e: e.tensor_scalar(e8b[:], e8a[:], rcol[:], None, ALU.is_equal),
              reads=["e8a", "rcol"], writes=["e8b"])
        em.op("v", lambda e: e.tensor_tensor(e8b[:], e8b[:], id_bc[:], ALU.mult),
              reads=["e8b", "id_bc"], writes=["e8b"])
        em.op("v", lambda e: e.tensor_reduce(r2col[:], e8b[:], AX.X, ALU.max),
              reads=["e8b"], writes=["r2col"])
        em.op("v", lambda e: e.tensor_scalar(e8b[:], e8b[:], r2col[:], None, ALU.is_equal),
              reads=["e8b", "r2col"], writes=["e8b"])
        em.op("v", lambda e: e.scalar_tensor_tensor(e8a[:], e8b[:], -1e9, e8a[:],
              ALU.mult, ALU.add), reads=["e8b", "e8a"], writes=["e8a"])
        em.op("v", lambda e: e.tensor_reduce(rcol[:], e8a[:], AX.X, ALU.max),
              reads=["e8a"], writes=["rcol"])
        em.op("v", lambda e: e.tensor_scalar(e8a[:], e8a[:], rcol[:], None, ALU.is_equal),
              reads=["e8a", "rcol"], writes=["e8a"])
        em.op("v", lambda e: e.tensor_tensor(e8a[:], e8a[:], id_bc[:], ALU.mult),
              reads=["e8a", "id_bc"], writes=["e8a"])
        em.op("v", lambda e: e.tensor_reduce(r2col[:], e8a[:], AX.X, ALU.max),
              reads=["e8a"], writes=["r2col"])
        em.op("v", lambda e: e.tensor_scalar(e8a[:], e8a[:], r2col[:], None, ALU.is_equal),
              reads=["e8a", "r2col"], writes=["e8a"])
        em.op("v", lambda e: e.tensor_tensor(e8a[:], e8a[:], e8b[:], ALU.add),
              reads=["e8a", "e8b"], writes=["e8a"])
        em.op("v", lambda e, tt=tt: e.tensor_tensor(dw_sb[:, tt * E:(tt + 1) * E], lg[:],
              e8a[:], ALU.mult), reads=["lg", "e8a"], writes=["dw_sb"])

    pop_scope()
    w1sb = [sbp(f"w1sb{i}", [128, NKD * 128], BF16) for i in range(2)]
    w3sb = [sbp(f"w3sb{i}", [128, NKD * 128], BF16) for i in range(2)]
    w2sb = sbp("w2sb", [128, NKI * D], BF16)

    for ei in range(E + 1):
        shared = (ei == 0)
        e_idx = ei - 1
        for ik in range(NKI):
            jj = ik % 2
            if shared:
                w1src = ws1t[:, ik * 128:(ik + 1) * 128]
                w3src = ws3t[:, ik * 128:(ik + 1) * 128]
            else:
                w1src = w1t[e_idx * D:(e_idx + 1) * D, ik * 128:(ik + 1) * 128]
                w3src = w3t[e_idx * D:(e_idx + 1) * D, ik * 128:(ik + 1) * 128]
            em.dma("gp", w1sb[jj].ap().rearrange("p (k c) -> p k c", k=NKD),
                   w1src.rearrange("(k p) c -> p k c", p=128), f"w1sb{jj}")
            em.dma("gp", w3sb[jj].ap().rearrange("p (k c) -> p k c", k=NKD),
                   w3src.rearrange("(k p) c -> p k c", p=128), f"w3sb{jj}")
            for ph in range(2):
                pa0, pak0 = next_p()
                pa1, pak1 = next_p()
                pb0, pbk0 = next_p()
                pb1, pbk1 = next_p()
                for k in range(NKD):
                    for pi, (pp, ppk) in enumerate(((pa0, pak0), (pa1, pak1))):
                        em.mm(lambda e, pp=pp, k=k, jj=jj, ph=ph, pi=pi: e.matmul(
                            pp[:], w1sb[jj][:, k * 128:(k + 1) * 128],
                            xt_fm[:, k * TC + (2 * ph + pi) * 512: k * TC + (2 * ph + pi + 1) * 512],
                            start=(k == 0), stop=(k == NKD - 1)),
                            ppk, [f"w1sb{jj}", "xt_fm"], last=(k == NKD - 1 and pi == 1))
                for k in range(NKD):
                    for pi, (pp, ppk) in enumerate(((pb0, pbk0), (pb1, pbk1))):
                        em.mm(lambda e, pp=pp, k=k, jj=jj, ph=ph, pi=pi: e.matmul(
                            pp[:], w3sb[jj][:, k * 128:(k + 1) * 128],
                            xt_fm[:, k * TC + (2 * ph + pi) * 512: k * TC + (2 * ph + pi + 1) * 512],
                            start=(k == 0), stop=(k == NKD - 1)),
                            ppk, [f"w3sb{jj}", "xt_fm"], last=(k == NKD - 1 and pi == 1))
                for pi, (pa, pak), (pb, pbk) in ((0, (pa0, pak0), (pb0, pbk0)),
                                                 (1, (pa1, pak1), (pb1, pbk1))):
                    piece = 2 * ph + pi
                    sj = piece % 4
                    em.op("s", lambda e, sj=sj, pa=pa: e.activation(stg[sj][:], pa[:], AF.Silu,
                          bias=0.0, scale=1.0), reads=[pak], writes=[f"stg{sj}"])
                    em.op("v", lambda e, sj=sj, pb=pb, ik=ik, piece=piece: e.tensor_tensor(
                        h_sb[:, ik * TC + piece * 512: ik * TC + (piece + 1) * 512],
                        stg[sj][:], pb[:], ALU.mult),
                        reads=[f"stg{sj}", pbk], writes=["h_sb"])
        w2src = (ws2t if shared else w2t[e_idx * I_FF:(e_idx + 1) * I_FF, :])
        em.dma("gp", w2sb.ap().rearrange("p (k c) -> p k c", k=NKI),
               w2src.rearrange("(k p) c -> p k c", p=128), "w2sb")
        for tt in range(16):
            pps = [next_p() for _ in range(4)]
            for ik in range(NKI):
                for dpiece in range(4):
                    pp, ppk = pps[dpiece]
                    em.mm(lambda e, pp=pp, ik=ik, tt=tt, dpiece=dpiece: e.matmul(
                        pp[:], h_sb[:, ik * TC + tt * 128: ik * TC + (tt + 1) * 128],
                        w2sb[:, ik * D + dpiece * 512: ik * D + (dpiece + 1) * 512],
                        start=(ik == 0), stop=(ik == NKI - 1)),
                        ppk, ["h_sb", "w2sb"], last=(ik == NKI - 1 and dpiece == 3))
            for dpiece in range(4):
                pp, ppk = pps[dpiece]
                sj = dpiece
                if shared:
                    em.op("v", lambda e, sj=sj, pp=pp: e.tensor_copy(stgb[sj][:], pp[:]),
                          reads=[ppk], writes=[f"stgb{sj}"])
                    em.dma("sync", y_d[tt * 128:(tt + 1) * 128, dpiece * 512:(dpiece + 1) * 512],
                           stgb[sj][:], f"y_d_{tt}_{dpiece}", [f"stgb{sj}"])
                else:
                    em.op("v", lambda e, sj=sj, pp=pp, e_idx=e_idx, tt=tt: e.tensor_scalar(
                        stgb[sj][:], pp[:], dw_sb[:, tt * E + e_idx: tt * E + e_idx + 1], None,
                        ALU.mult), reads=[ppk, "dw_sb"], writes=[f"stgb{sj}"])
                    em.dma("gp", y_d[tt * 128:(tt + 1) * 128, dpiece * 512:(dpiece + 1) * 512],
                           stgb[sj][:], f"y_d_{tt}_{dpiece}", [f"stgb{sj}"], accum_op=ALU.add)

    x1t4 = [sbp(f"x1t4_{i}", [128, D]) for i in range(2)]
    bnb34 = [sbp(f"bnb3_4{i}", [128, 24]) for i in range(2)]
    mv34 = [sbp(f"mv3_4{i}", [128, 2]) for i in range(2)]
    r34 = [sbp(f"r3_4{i}", [128, 1]) for i in range(2)]
    n34 = [sbp(f"n3_4{i}", [128, 1]) for i in range(2)]
    outx = [sbp(f"outx{i}", [128, D]) for i in range(2)]
    for tt in range(16):
        j = tt % 2
        kx, kb, km, kr, kn = f"x1t4_{j}", f"bnb3_4{j}", f"mv3_4{j}", f"r3_4{j}", f"n3_4{j}"
        em.dma("sync", outx[j][:], y_d[tt * 128:(tt + 1) * 128, :], f"outx{j}", [f"y_d_{tt}_{dp}" for dp in range(4)])
        for s4 in range(4):
            em.op("v", lambda e, j=j, s4=s4: e.bn_stats(bnb34[j][:, s4 * 6:(s4 + 1) * 6],
                  outx[j][:, s4 * 512:(s4 + 1) * 512]), reads=[f"outx{j}"], writes=[kb])
        em.op("v", lambda e, j=j: e.bn_aggr(mv34[j][:], bnb34[j][:]), reads=[kb], writes=[km])
        em.op("v", lambda e, j=j: e.tensor_scalar(r34[j][:], mv34[j][:, 1:2], EPS, None, ALU.add),
              reads=[km], writes=[kr])
        em.op("s", lambda e, j=j: e.activation(r34[j][:], r34[j][:], AF.Ln, bias=0.0, scale=1.0),
              reads=[kr], writes=[kr])
        em.op("s", lambda e, j=j: e.activation(r34[j][:], r34[j][:], AF.Exp, bias=0.0, scale=-0.5),
              reads=[kr], writes=[kr])
        em.op("v", lambda e, j=j: e.tensor_scalar(n34[j][:], r34[j][:], mv34[j][:, 0:1], -1.0,
              ALU.mult, ALU.mult), reads=[kr, km], writes=[kn])
        em.dma("sync", x1t4[j][:], x1_d[tt * 128:(tt + 1) * 128, :], kx, [f"x1_{tt}"])
        em.op("v", lambda e, j=j: e.tensor_scalar(x1t4[j][:], x1t4[j][:], n34[j][:], None, ALU.add),
              reads=[kx, kn], writes=[kx])
        em.op("v", lambda e, j=j: e.scalar_tensor_tensor(outx[j][:], outx[j][:], r34[j][:],
              x1t4[j][:], ALU.mult, ALU.add), reads=[f"outx{j}", kr, kx], writes=[f"outx{j}"])
        em.dma("sync", out_d[tt * 128:(tt + 1) * 128, :], outx[j][:], f"out_{tt}", [f"outx{j}"])

    pop_scope()
    _final_wait(nc, em)
    return nc


def _final_wait(nc, em):
    for q in ("sync", "gp"):
        for i, sem in enumerate(em.dma_pools[q]):
            u = em.dma_uses[q][i]
            if u:
                em._wait("sync", (sem, u * 16, f"d_{q}{i}"))
    for c in em.COMPUTE:
        if em.cnt[c]:
            em._wait("sync", (em.sem[c], em.cnt[c], f"c_{c}"))
    em.finalize()


def host_inputs(inputs):
    x = np.asarray(inputs["L0_attn_hidden_states"])[0]
    ipW = np.asarray(inputs["L0_attn_in_proj_w"])
    conv_w = np.asarray(inputs["L0_attn_conv_w"])
    conv_b = np.asarray(inputs["L0_attn_conv_b"])
    A_log = np.asarray(inputs["L0_attn_A_log"])
    dt_bias = np.asarray(inputs["L0_attn_dt_bias"])
    Dp = np.asarray(inputs["L0_attn_D"])
    norm_w = np.asarray(inputs["L0_attn_norm_w"])
    out_w = np.asarray(inputs["L0_attn_out_proj_w"])
    gate_w = np.asarray(inputs["L0_ffn_gate_w"])
    gate_b = np.asarray(inputs["L0_ffn_gate_b"])
    W1 = np.asarray(inputs["L0_ffn_W1"])
    W2 = np.asarray(inputs["L0_ffn_W2"])
    W3 = np.asarray(inputs["L0_ffn_W3"])
    Ws1 = np.asarray(inputs["L0_ffn_Ws1"])
    Ws2 = np.asarray(inputs["L0_ffn_Ws2"])
    Ws3 = np.asarray(inputs["L0_ffn_Ws3"])

    shared = {}
    shared["ipWT"] = np.ascontiguousarray(ipW.T)
    cd = np.zeros((NO_HBC, CONVK, 128, 128), np.float32)
    cw = conv_w[:, 0, :]
    ar = np.arange(128)
    for j in range(NO_HBC):
        for k in range(CONVK):
            cd[j, k, ar, ar] = cw[j * 128:(j + 1) * 128, k]
    shared["convdiag"] = cd.reshape(NO_HBC * CONVK * 128, 128)
    shared["convb"] = np.ascontiguousarray(conv_b.reshape(NO_HBC, 128))
    shared["alog"] = np.ascontiguousarray(A_log.reshape(32, 1))
    shared["dtbias"] = np.ascontiguousarray(dt_bias.reshape(32, 1))
    dd = np.zeros((HEADS, 128, 128), np.float32)
    ar128 = np.arange(128)
    dd[:, ar128, ar128] = Dp[:, None]
    shared["ddiag"] = dd.reshape(HEADS * 128, 128)
    shared["normcols"] = np.ascontiguousarray(norm_w.reshape(32, 128).T)
    shared["opWT"] = np.ascontiguousarray(out_w.T)
    shared["gateWT"] = np.ascontiguousarray(gate_w.T)
    shared["gatebB"] = np.ascontiguousarray(gate_b.reshape(1, E))
    shared["idxdB"] = np.ascontiguousarray((E - np.arange(E, dtype=np.float32)).reshape(1, E))
    shared["w1t"] = np.ascontiguousarray(W1.transpose(0, 2, 1)).reshape(E * D, I_FF)
    shared["w3t"] = np.ascontiguousarray(W3.transpose(0, 2, 1)).reshape(E * D, I_FF)
    shared["w2t"] = np.ascontiguousarray(W2.transpose(0, 2, 1)).reshape(E * I_FF, D)
    shared["ws1t"] = np.ascontiguousarray(Ws1.T)
    shared["ws3t"] = np.ascontiguousarray(Ws3.T)
    shared["ws2t"] = np.ascontiguousarray(Ws2.T)
    m01 = np.zeros((2, 128, CHUNK), np.float32)
    for st in range(2):
        for s in range(128):
            m01[st, s, st * 128 + s:] = 1.0
    shared["mask01"] = m01.reshape(2 * 128, CHUNK)
    shared["eye"] = np.eye(128, dtype=np.float32)
    shared["trilM_in"] = np.triu(np.ones((128, 128), np.float32))

    in_maps = []
    for c in range(NCORES):
        m = dict(shared)
        halo = np.zeros((3, D), np.float32) if c == 0 else x[c * TC - 3:c * TC]
        m["x"] = np.ascontiguousarray(
            np.concatenate([np.zeros((125, D), np.float32), halo, x[c * TC:(c + 1) * TC]], 0))
        cm = np.zeros((32, E), np.float32)
        nb = np.full((32, E), -30000.0, np.float32)
        oh = np.zeros((32, E), np.float32)
        cm[:, :c] = 1.0
        nb[:, :c] = 0.0
        if c > 0:
            oh[:, c - 1] = 1.0
        m["cmaskB"], m["negB"], m["onehotB"] = cm, nb, oh
        in_maps.append(m)
    return in_maps


def kernel(**inputs):
    in_maps = host_inputs(inputs)
    nc = build_nc("full")
    res = run_bass_kernel_spmd(nc, in_maps, core_ids=list(range(NCORES)))
    out = np.concatenate([r["out"] for r in res.results], axis=0)
    return out.reshape(1, SEQ, D).astype(np.float32)


# version 08 (final)
# speedup vs baseline: 1.3295x; 1.0079x over previous
import dataclasses
import numpy as np
import ml_dtypes
import concourse.bass as bass
import concourse.mybir as mybir
from concourse.bass_utils import run_bass_kernel_spmd

F32 = mybir.dt.float32
BF16 = mybir.dt.bfloat16
AF = mybir.ActivationFunctionType
ALU = mybir.AluOpType
AX = mybir.AxisListType
BF = ml_dtypes.bfloat16

NCORES = 8
SEQ = 16384
D = 2048
TC = SEQ // NCORES
PAD = 128
TF = TC + PAD
NKD = D // 128
INTER = 4096
HEADS = 32
HD = 128
STATE = 128
CONVK = 4
CONV_DIM = INTER + 2 * STATE
NPROJ = INTER + CONV_DIM + HEADS
NO_HBC = CONV_DIM // 128
NO_GATE = INTER // 128
CHUNK = 256
NCH = TC // CHUNK
EPS = 1e-5
E = 8
I_FF = 1024
NKI = I_FF // 128
RG = [[0, 1, 2, 3, 4, 5, 6, 7]]


def bcast_part(ap, n):
    new = [[0, n]] + [list(x) for x in ap.ap][1:]
    return dataclasses.replace(ap, ap=type(ap.ap)(new))


def bcast_prepend(ap, n):
    new = [[0, n]] + [list(x) for x in ap.ap]
    return dataclasses.replace(ap, ap=type(ap.ap)(new))


def rep_inner(ap, n):
    aps = [list(x) for x in ap.ap]
    new = aps + [[0, n]]
    return dataclasses.replace(ap, ap=type(ap.ap)(new))


def rep_free(ap, n):
    aps = [list(x) for x in ap.ap]
    new = [aps[0]] + [[0, n]] + aps[1:]
    return dataclasses.replace(ap, ap=type(ap.ap)(new))


class EM:
    COMPUTE = ("v", "s", "t", "gp")
    NDMA = 6

    def __init__(self, nc):
        self.nc = nc
        self.q = {k: [] for k in ("sync", "gp", "v", "s", "t")}
        self.sem = {k: nc.alloc_semaphore(f"sem_{k}") for k in self.COMPUTE}
        self.cnt = {k: 0 for k in self.COMPUTE}
        self.dma_pools = {q: [nc.alloc_semaphore(f"dma_{q}{i}") for i in range(self.NDMA)]
                          for q in ("sync", "gp")}
        self.ccsem = nc.alloc_semaphore("ccsem")
        self.cccnt = 0
        self.dma_uses = {q: [0] * self.NDMA for q in ("sync", "gp")}
        self.dma_rr = {q: 0 for q in ("sync", "gp")}
        self.waited = {k: {} for k in self.q}
        self.writer = {}
        self.readers = {}
        self.pending_t = []
        self._segs = []

    def _wait(self, queue, rec):
        if rec is None:
            return
        sem, val, name = rec
        if val <= 0:
            return
        w = self.waited[queue]
        if w.get(name, -1) >= val:
            return
        w[name] = val
        self.q[queue].append(lambda e, s=sem, v=val: e.wait_ge(s, v))

    def _note_read(self, key, rec):
        d = self.readers.setdefault(key, {})
        old = d.get(rec[2])
        if old is None or old[1] < rec[1]:
            d[rec[2]] = rec

    def _pre(self, queue, reads, writes, pe_acc_key=None):
        for k in reads:
            self._wait(queue, self.writer.get(k))
        for k in writes:
            wrec = self.writer.get(k)
            if not (k == pe_acc_key and wrec is not None and wrec[2] == "c_t"):
                self._wait(queue, wrec)
            for rec in self.readers.get(k, {}).values():
                self._wait(queue, rec)

    def _finish(self, reads, writes, rec):
        for k in reads:
            self._note_read(k, rec)
        for k in writes:
            self.writer[k] = rec
            self.readers[k] = {}

    def op(self, queue, fn, reads=(), writes=()):
        assert queue in self.COMPUTE
        reads, writes = list(reads), list(writes)
        self._pre(queue, reads, writes)
        self.cnt[queue] += 1
        sem, val = self.sem[queue], self.cnt[queue]
        self.q[queue].append(lambda e, f=fn, s=sem, v=1: f(e).then_inc(s, v))
        self._finish(reads, writes, (sem, val, f"c_{queue}"))

    def mm(self, fn, out_key, in_keys, last=True):
        q = "t"
        in_keys = [k for k in in_keys if k is not None]
        self._pre(q, in_keys, [out_key], pe_acc_key=out_key)
        if last:
            self.cnt[q] += 1
            sem, val = self.sem[q], self.cnt[q]
            self.q[q].append(lambda e, f=fn, s=sem: f(e).then_inc(s, 1))
            rec = (sem, val, "c_t")
            self.pending_t.append((in_keys, [out_key]))
            for rds, wrs in self.pending_t:
                self._finish(rds, wrs, rec)
            self.pending_t = []
        else:
            self.q[q].append(fn)
            self.pending_t.append((in_keys, [out_key]))
            self.writer[out_key] = (self.sem[q], self.cnt[q] + 1, "c_t")

    def dma(self, queue, dst_ap, src_ap, wkey, rkeys=(), **kw):
        pool, uses = self.dma_pools[queue], self.dma_uses[queue]
        i = self.dma_rr[queue]
        self.dma_rr[queue] = (i + 1) % self.NDMA
        sem = pool[i]
        name = f"d_{queue}{i}"
        self._wait(queue, (sem, uses[i] * 16, name))
        rkeys = list(rkeys)
        self._pre(queue, rkeys, [wkey] if wkey is not None else [])
        uses[i] += 1
        val = uses[i] * 16
        self.q[queue].append(
            lambda e, d=dst_ap, s=src_ap, sm=sem, k=dict(kw): e.dma_start(d, s, **k).then_inc(sm, 16))
        self._finish(rkeys, [wkey] if wkey is not None else [], (sem, val, name))

    def collective(self, fn, rkeys, wkey):
        q = "gp"
        self._pre(q, list(rkeys), [wkey])
        self.cccnt += 1
        sem, val = self.ccsem, self.cccnt
        self.q[q].append(lambda e, f=fn, s=sem: f(e).then_inc(s, 1))
        self._finish(list(rkeys), [wkey], (sem, val, "ccsem"))

    def barrier(self):
        for q in ("sync", "gp"):
            for i, sem in enumerate(self.dma_pools[q]):
                u = self.dma_uses[q][i]
                if u:
                    self._wait("sync", (sem, u * 16, f"d_{q}{i}"))
        self._segs.append(self.q)
        self.q = {k: [] for k in self.q}
        for qq in self.waited:
            w = self.waited[qq]
            for c in self.COMPUTE:
                w[f"c_{c}"] = max(w.get(f"c_{c}", 0), self.cnt[c])
            for q in ("sync", "gp"):
                for i in range(self.NDMA):
                    w[f"d_{q}{i}"] = max(w.get(f"d_{q}{i}", 0), self.dma_uses[q][i] * 16)


    def finalize(self):
        self.barrier()
        for seg in self._segs:
            if all(len(v) == 0 for v in seg.values()):
                continue
            with self.nc.Block() as b:
                for qname, reg in (("sync", b.sync), ("gp", b.gpsimd), ("v", b.vector),
                                   ("s", b.scalar), ("t", b.tensor)):
                    if seg[qname]:
                        reg(lambda e, ops=tuple(seg[qname]): [f(e) for f in ops])


def build_nc(upto="full", dbg=()):
    nc = bass.Bass("TRN2", target_bir_lowering=False, debug=False)
    em = EM(nc)

    def dram(name, shape, dtype=F32):
        kind = "ExternalOutput" if name in dbg else "Internal"
        return nc.dram_tensor(name, shape, dtype, kind=kind).ap()

    def inp(name, shape, dtype=F32):
        return nc.dram_tensor(name, shape, dtype, kind="ExternalInput").ap()

    def sb(name, shape, dtype=F32):
        return nc.alloc_sbuf_tensor(name, shape, dtype)

    scopes = []

    def push_scope():
        scopes.append([])

    def sbp(name, shape, dtype=F32):
        g = nc.sbuf_tensor(name, shape, dtype)
        t = g.__enter__()
        scopes[-1].append(g)
        return t

    def pop_scope():
        em.barrier()
        for g in reversed(scopes.pop()):
            g.__exit__(None, None, None)

    x_in = inp("x", [TF, D])
    ipWT = inp("ipWT", [D, NPROJ])
    convdiag = inp("convdiag", [NO_HBC * CONVK * 128, 128])
    convb = inp("convb", [NO_HBC, 128])
    alog = inp("alog", [32, 1])
    dtbias = inp("dtbias", [32, 1])
    ddiag = inp("ddiag", [HEADS * 128, 128])
    normcols = inp("normcols", [128, 32])
    opWT = inp("opWT", [INTER, D])
    gateWT = inp("gateWT", [D, E])
    gatebB = inp("gatebB", [1, E])
    idxdB = inp("idxdB", [1, E])
    w1t = inp("w1t", [E * D, I_FF])
    w3t = inp("w3t", [E * D, I_FF])
    w2t = inp("w2t", [E * I_FF, D])
    ws1t = inp("ws1t", [D, I_FF])
    ws3t = inp("ws3t", [D, I_FF])
    ws2t = inp("ws2t", [I_FF, D])
    mask01 = inp("mask01", [2 * 128, CHUNK])
    cmaskB = inp("cmaskB", [32, E])
    negB = inp("negB", [32, E])
    onehotB = inp("onehotB", [32, E])
    eye_in = inp("eye", [128, 128])
    trilM_in = inp("trilM_in", [128, 128])

    out_d = nc.dram_tensor("out", [TC, D], F32, kind="ExternalOutput").ap()
    xs_d = dram("xs_d", [INTER, TC], BF16)
    u_d = dram("u_d", [128, NKD * TF], BF16)
    ypart_d = dram("ypart_d", [TC, INTER])
    acum_d = dram("acum_d", [32, TC])
    acumf_d = dram("acumf_d", [NCH, 32 * CHUNK])
    x1_d = dram("x1_d", [TC, D])
    gate_s_d = dram("gate_s_d", [TC, INTER], BF16)
    opWTb_d = dram("opWTb_d", [INTER, D], BF16)
    y_d = dram("y_d", [TC, D])
    sstates_d = dram("sstates_d", [NCH * 128, HEADS * HD])
    dvec_d = dram("dvec_d", [NCH, 32])
    cc_in = dram("cc_in", [129, HEADS * HD])
    cc_out = dram("cc_out", [NCORES * 129, HEADS * HD])
    wflat_d = dram("wflat_d", [1, NCORES * 32])
    dbg_d = dram("dbg_d", [512, max(TC, INTER)]) if "dbg_d" in dbg else None

    P = [nc.alloc_psum_tensor(f"P{i}", [128, 512], F32) for i in range(6)]
    PT = [nc.alloc_psum_tensor(f"PT{i}", [128, 1024], BF16) for i in range(2)]
    pk = [f"P{i}" for i in range(6)]
    ptk = [f"PT{i}" for i in range(2)]
    prr = [0]

    def next_p():
        i = prr[0] % 6
        prr[0] += 1
        return P[i], pk[i]

    ptrr = [0]

    def next_pt():
        i = ptrr[0] % 2
        ptrr[0] += 1
        return PT[i], ptk[i]

    eye_f = sb("eye_f", [128, 128], F32)
    trilM = sb("trilM", [128, 128], F32)
    onesM = sb("onesM", [128, 128], F32)
    eye_b = sb("eye_b", [128, 128], BF16)
    ones_f = sb("ones_f", [1, 128], F32)
    em.dma("sync", eye_f[:], eye_in, "eye_f")
    em.dma("sync", trilM[:], trilM_in, "trilM")
    em.op("v", lambda e: e.tensor_copy(eye_b[:], eye_f[:]), reads=["eye_f"], writes=["eye_b"])
    em.op("v", lambda e: e.memset(ones_f[:], 1.0), writes=["ones_f"])
    em.op("v", lambda e: e.memset(onesM[:], 1.0), writes=["onesM"])
    normc_sb = sb("normc_sb", [128, 32])
    em.dma("sync", normc_sb[:], normcols, "normc_sb")
    push_scope()
    C_fm = sbp("C_fm", [128, TC], BF16)
    push_scope()
    B_fm = sbp("B_fm", [128, TC], BF16)
    dtv = sbp("dtv", [32, TC])
    ad = sbp("ad", [32, TC])
    alog_s = sbp("alog_s", [32, 1])
    dtb_s = sbp("dtb_s", [32, 1])
    acol = sbp("acol", [32, 1])
    ltot = sbp("ltot", [32, 1])

    push_scope()
    u_fm = sbp("u_fm", [128, NKD * TF], BF16)
    NT = TF // 128
    dtraw = sbp("dtraw", [32, TC])
    spt = sbp("spt", [32, TC])
    xt = [sbp(f"xt{i}", [128, D]) for i in range(2)]
    bnb = [sbp(f"bnb{i}", [128, 24]) for i in range(2)]
    mv = [sbp(f"mv{i}", [128, 2]) for i in range(2)]
    rstd = [sbp(f"rstd{i}", [128, 1]) for i in range(2)]
    negmr = [sbp(f"negmr{i}", [128, 1]) for i in range(2)]
    ubf = [sbp(f"ubf{i}", [128, D], BF16) for i in range(2)]

    for i in range(NT):
        j = i % 2
        kx, kb, km, kr, kn, ku = (f"xt{j}", f"bnb{j}", f"mv{j}", f"rstd{j}", f"negmr{j}", f"ubf{j}")
        em.dma("sync", xt[j][:], x_in[i * 128:(i + 1) * 128, :], kx)
        for s4 in range(4):
            em.op("v", lambda e, j=j, s4=s4: e.bn_stats(bnb[j][:, s4 * 6:(s4 + 1) * 6],
                                                        xt[j][:, s4 * 512:(s4 + 1) * 512]),
                  reads=[kx], writes=[kb])
        em.op("v", lambda e, j=j: e.bn_aggr(mv[j][:], bnb[j][:]), reads=[kb], writes=[km])
        em.op("v", lambda e, j=j: e.tensor_scalar(rstd[j][:], mv[j][:, 1:2], EPS, None, ALU.add),
              reads=[km], writes=[kr])
        em.op("s", lambda e, j=j: e.activation(rstd[j][:], rstd[j][:], AF.Ln, bias=0.0, scale=1.0),
              reads=[kr], writes=[kr])
        em.op("s", lambda e, j=j: e.activation(rstd[j][:], rstd[j][:], AF.Exp, bias=0.0, scale=-0.5),
              reads=[kr], writes=[kr])
        em.op("v", lambda e, j=j: e.tensor_scalar(negmr[j][:], rstd[j][:], mv[j][:, 0:1], -1.0,
                                                  ALU.mult, ALU.mult),
              reads=[kr, km], writes=[kn])
        em.op("v", lambda e, j=j: e.tensor_scalar(ubf[j][:], xt[j][:], rstd[j][:], negmr[j][:],
                                                  ALU.mult, ALU.add),
              reads=[kx, kr, kn], writes=[ku])
        for half in range(2):
            pt, ptkey = next_pt()
            for kk in range(8):
                k = half * 8 + kk
                em.mm(lambda e, j=j, k=k, kk=kk, pt=pt: e.transpose(
                    pt[:, kk * 128:(kk + 1) * 128], ubf[j][:, k * 128:(k + 1) * 128], eye_b[:]),
                    ptkey, [ku, "eye_b"], last=(kk == 7))
            em.op("v", lambda e, i=i, half=half, pt=pt: e.tensor_copy(
                u_fm.ap().rearrange("p (k t) -> p k t", k=NKD)[:, half * 8:(half + 1) * 8,
                                                              i * 128:(i + 1) * 128],
                pt.ap().rearrange("p (k t) -> p k t", k=8)),
                reads=[ptkey], writes=["u_fm"])

    for f in range(32):
        jj = f % 2
        em.dma("sync", xt[jj][:], opWT[f * 128:(f + 1) * 128, :], f"xt{jj}")
        em.op("v", lambda e, jj=jj, f=f: e.tensor_scalar(ubf[jj][:], xt[jj][:],
              normc_sb[:, f:f + 1], None, ALU.mult),
              reads=[f"xt{jj}", "normc_sb"], writes=[f"ubf{jj}"])
        em.dma("sync", opWTb_d[f * 128:(f + 1) * 128, :], ubf[jj][:], f"opwtb_{f}", [f"ubf{jj}"])

    if upto == "A1ln":
        em.dma("sync", u_d, u_fm[:], "u_d", ["u_fm"])
        _final_wait(nc, em)
        return nc
    wsb = [sbp(f"wsb{i}", [128, NKD * 128], BF16) for i in range(2)]
    projbf = [sbp(f"projbf{i}", [128, TF], BF16) for i in range(2)]
    cdiag = [sbp(f"cdiag{i}", [128, CONVK * 128], BF16) for i in range(2)]
    cbcol = [sbp(f"cbcol{i}", [128, 1]) for i in range(2)]
    hbco = [sbp(f"hbco{i}", [128, TC], BF16) for i in range(2)]
    PIECES = [(0, 512), (512, 512), (1024, 512), (1536, 512), (2048, 128)]

    ipWT_r = ipWT.rearrange("(k p) c -> p k c", p=128)

    for oj in range(NO_HBC + 1):
        jj = oj % 2
        ocol = NO_GATE * 128 + oj * 128
        wk = f"wsb{jj}"
        ncols = 128 if oj < NO_HBC else 32
        em.dma("gp", wsb[jj].ap().rearrange("p (k c) -> p k c", k=NKD)[:, :, 0:ncols],
               ipWT_r[:, :, ocol:ocol + ncols], wk)
        pj = f"projbf{jj}"
        if oj < NO_HBC:
            pps = [next_p() for _ in range(5)]
            for k in range(NKD):
                for pi, (poff, plen) in enumerate(PIECES):
                    pp, ppk = pps[pi]
                    em.mm(lambda e, pp=pp, plen=plen, k=k, jj=jj, poff=poff: e.matmul(
                        pp[:, 0:plen], wsb[jj][:, k * 128:(k + 1) * 128],
                        u_fm[:, k * TF + poff: k * TF + poff + plen],
                        start=(k == 0), stop=(k == NKD - 1)),
                        ppk, [wk, "u_fm"], last=(k == NKD - 1 and pi == 4))
            for pi, (poff, plen) in enumerate(PIECES):
                pp, ppk = pps[pi]
                eng = "v" if pi % 2 == 0 else "s"
                if eng == "v":
                    em.op("v", lambda e, jj=jj, poff=poff, plen=plen, pp=pp: e.tensor_copy(
                        projbf[jj][:, poff:poff + plen], pp[:, 0:plen]),
                        reads=[ppk], writes=[pj])
                else:
                    em.op("s", lambda e, jj=jj, poff=poff, plen=plen, pp=pp: e.activation(
                        projbf[jj][:, poff:poff + plen], pp[:, 0:plen], AF.Copy, bias=0.0, scale=1.0),
                        reads=[ppk], writes=[pj])
            ck = f"cdiag{jj}"
            cbk = f"cbcol{jj}"
            em.dma("gp", cdiag[jj][:], convdiag[oj * CONVK * 128:(oj + 1) * CONVK * 128, :]
                   .rearrange("(k p) c -> p k c", p=128), ck)
            em.dma("sync", cbcol[jj][:], convb[oj:oj + 1, :].rearrange("x c -> c x"), cbk)
            hk = f"hbco{jj}"
            for piece in range(4):
                pp, ppk = next_p()
                for k in range(CONVK):
                    em.mm(lambda e, pp=pp, k=k, jj=jj, piece=piece: e.matmul(
                        pp[:], cdiag[jj][:, k * 128:(k + 1) * 128],
                        projbf[jj][:, 125 + k + piece * 512: 125 + k + piece * 512 + 512],
                        start=(k == 0), stop=(k == CONVK - 1)),
                        ppk, [ck, pj], last=(k == CONVK - 1))
                if oj < NO_GATE:
                    em.op("s", lambda e, jj=jj, piece=piece, pp=pp: e.activation(
                        hbco[jj][:, piece * 512:(piece + 1) * 512], pp[:], AF.Silu,
                        bias=cbcol[jj][:], scale=1.0), reads=[ppk, cbk], writes=[hk])
                else:
                    dstt = B_fm if oj == NO_GATE else C_fm
                    dk = "B_fm" if oj == NO_GATE else "C_fm"
                    em.op("s", lambda e, dstt=dstt, piece=piece, pp=pp, jj=jj: e.activation(
                        dstt[:, piece * 512:(piece + 1) * 512], pp[:], AF.Silu,
                        bias=cbcol[jj][:], scale=1.0), reads=[ppk, cbk], writes=[dk])
            if oj < NO_GATE:
                em.dma("sync", xs_d[oj * 128:(oj + 1) * 128, :], hbco[jj][:], f"xs_d_{oj}", [hk])
        else:
            for piece in range(4):
                pp, ppk = next_p()
                for k in range(NKD):
                    em.mm(lambda e, pp=pp, k=k, jj=jj, piece=piece: e.matmul(
                        pp[0:32, :], wsb[jj][:, k * 128: k * 128 + 32],
                        u_fm[:, k * TF + 128 + piece * 512: k * TF + 128 + piece * 512 + 512],
                        start=(k == 0), stop=(k == NKD - 1)),
                        ppk, [wk, "u_fm"], last=(k == NKD - 1))
                em.op("v", lambda e, piece=piece, pp=pp: e.tensor_copy(
                    dtraw[:, piece * 512:(piece + 1) * 512], pp[0:32, :]),
                    reads=[ppk], writes=["dtraw"])


    em.dma("sync", alog_s[:], alog, "alog_s")
    em.dma("sync", dtb_s[:], dtbias, "dtb_s")
    em.op("s", lambda e: e.activation(acol[:], alog_s[:], AF.Exp, bias=0.0, scale=1.0),
          reads=["alog_s"], writes=["acol"])
    em.op("v", lambda e: e.tensor_scalar(acol[:], acol[:], -1.0, None, ALU.mult),
          reads=["acol"], writes=["acol"])
    em.op("v", lambda e: e.tensor_scalar(dtv[:], dtraw[:], dtb_s[:], None, ALU.add),
          reads=["dtraw", "dtb_s"], writes=["dtv"])
    em.op("s", lambda e: e.activation(spt[:], dtv[:], AF.Abs, bias=0.0, scale=1.0),
          reads=["dtv"], writes=["spt"])
    em.op("s", lambda e: e.activation(spt[:], spt[:], AF.Exp, bias=0.0, scale=-1.0),
          reads=["spt"], writes=["spt"])
    em.op("v", lambda e: e.tensor_scalar(spt[:], spt[:], 1.0, None, ALU.add),
          reads=["spt"], writes=["spt"])
    em.op("s", lambda e: e.activation(spt[:], spt[:], AF.Ln, bias=0.0, scale=1.0),
          reads=["spt"], writes=["spt"])
    em.op("v", lambda e: e.tensor_scalar(dtv[:], dtv[:], 0.0, None, ALU.max),
          reads=["dtv"], writes=["dtv"])
    em.op("v", lambda e: e.tensor_tensor(dtv[:], dtv[:], spt[:], ALU.add),
          reads=["dtv", "spt"], writes=["dtv"])
    em.op("v", lambda e: e.tensor_scalar(dtv[:], dtv[:], 100.0, None, ALU.min),
          reads=["dtv"], writes=["dtv"])
    em.op("v", lambda e: e.tensor_scalar(ad[:], dtv[:], acol[:], None, ALU.mult),
          reads=["dtv", "acol"], writes=["ad"])

    if upto == "A1":
        em.dma("sync", dbg_d[0:32, 0:TC], dtv[:], "dbg_d", ["dtv"])
        em.dma("sync", dbg_d[32:64, 0:TC], ad[:], "dbg_d", ["ad"])
        em.dma("gp", dbg_d[64:192, 0:TC], B_fm[:], "dbg_d", ["B_fm"])
        em.dma("gp", dbg_d[192:320, 0:TC], C_fm[:], "dbg_d", ["C_fm"])
        _final_wait(nc, em)
        return nc

    wgbuf = [sbp(f"wgbuf{i}", [128, NKD * 512], BF16) for i in range(2)]
    gstg = [sbp(f"gstg{i}", [128, 512], BF16) for i in range(2)]
    for piece in range(8):
        jj = piece % 2
        wk = f"wgbuf{jj}"
        em.dma("gp", wgbuf[jj].ap().rearrange("p (k c) -> p k c", k=NKD),
               ipWT_r[:, :, piece * 512:(piece + 1) * 512], wk)
        for tt in range(16):
            pp, ppk = next_p()
            for k in range(NKD):
                em.mm(lambda e, pp=pp, k=k, jj=jj, tt=tt: e.matmul(
                    pp[:], u_fm[:, k * TF + 128 + tt * 128: k * TF + 128 + (tt + 1) * 128],
                    wgbuf[jj][:, k * 512:(k + 1) * 512],
                    start=(k == 0), stop=(k == NKD - 1)),
                    ppk, [wk, "u_fm"], last=(k == NKD - 1))
            sj = tt % 2
            em.op("s", lambda e, sj=sj, pp=pp: e.activation(gstg[sj][:], pp[:], AF.Silu,
                  bias=0.0, scale=1.0), reads=[ppk], writes=[f"gstg{sj}"])
            em.dma("sync", gate_s_d[tt * 128:(tt + 1) * 128, piece * 512:(piece + 1) * 512],
                   gstg[sj][:], f"gates_{tt}_{piece}", [f"gstg{sj}"])

    pop_scope()

    push_scope()
    mask_sb = sbp("mask_sb", [128, 2 * CHUNK])
    em.dma("sync", mask_sb.ap().rearrange("p (s c) -> p s c", s=2),
           mask01.rearrange("(s p) c -> p s c", p=128), "mask_sb")
    ddiag_sb = sbp("ddiag_sb", [128, HEADS * 128], BF16)
    em.dma("gp", ddiag_sb.ap().rearrange("p (h c) -> p h c", h=HEADS),
           ddiag.rearrange("(h p) c -> p h c", p=128), "ddiag_sb")
    adTb = [sbp(f"adT{i}", [128, 64]) for i in range(2)]
    em.op("v", lambda e: e.memset(ltot[:], 0.0), writes=["ltot"])
    acumC = [sbp(f"acumC{i}", [32, CHUNK]) for i in range(2)]
    dvecC = [sbp(f"dvecC{i}", [32, 1]) for i in range(2)]
    dsrow = [sbp(f"dsrow{i}", [32, CHUNK]) for i in range(2)]
    ddrow = [sbp(f"ddrow{i}", [32, CHUNK]) for i in range(2)]
    acumT = [[sbp(f"acumT{i}_{p}", [128, 32]) for i in range(2)] for p in range(2)]
    dtvT = [[sbp(f"dtvT{i}_{p}", [128, 32]) for i in range(2)] for p in range(2)]
    ddT = [[sbp(f"ddT{i}_{p}", [128, 32]) for i in range(2)] for p in range(2)]
    dcols = [sbp(f"dcols{i}", [128, 32]) for i in range(2)]
    rbg = [sbp(f"rbg{i}", [128, 8 * CHUNK]) for i in range(2)]
    dbg8 = [sbp(f"dbg8{i}", [128, 8 * CHUNK], BF16) for i in range(2)]
    lbg = [sbp(f"lbg{i}", [128, 8 * CHUNK], BF16) for i in range(2)]
    mbufS = [sbp(f"mbuf{i}", [128, HEADS * CHUNK], BF16) for i in range(2)]
    mbuf = [mbufS, mbufS]
    gm = [[sbp(f"gm{i}_{p}", [128, CHUNK], BF16) for i in range(2)] for p in range(2)]
    xsgS = sbp("xsg", [128, 8 * CHUNK], BF16)
    xsg = [xsgS, xsgS]
    xsTS = [sbp(f"xsT{i}", [128, HEADS * 128], BF16) for i in range(2)]
    xsT = [xsTS, xsTS]
    xdXS = [sbp(f"xdX{i}", [128, HEADS * 128], BF16) for i in range(2)]
    xdX = [xdXS, xdXS]
    btT = [sbp(f"btT{i}", [128, 2 * 128], BF16) for i in range(2)]
    state_run = sbp("state_run", [128, HEADS * 128])
    em.op("v", lambda e, t=state_run: e.memset(t[:], 0.0), writes=["state_run"])
    scsb = [sbp(f"scsb{i}", [128, 512]) for i in range(2)]
    ysbS = [sbp(f"ysb{i}", [128, INTER]) for i in range(2)]
    ysb = [ysbS, ysbS]

    def a2_chunk(c):
        c0 = c * CHUNK
        par = c % 2
        acum, dvec = acumC[par], dvecC[par]
        adT, kadT = adTb[par], f"adT{par}"
        dsr, ddr = dsrow[par], ddrow[par]
        acT, dtT, ddTT = acumT[par], dtvT[par], ddT[par]
        dcl, rbg_, dbg_, lbg_ = dcols[par], rbg[par], dbg8[par], lbg[par]
        mb, gmp, xsgp, xsTp, xdXp = mbuf[par], gm[par], xsg[par], xsT[par], xdX[par]
        btTp, scsbp, ysbp = btT[par], scsb[par], ysb[par]
        kac, kdv = f"acumC{par}", f"dvecC{par}"
        kaA, kaB = f"acA{par}", f"acB{par}"
        kdsr, kddr = f"dsrow{par}", f"ddrow{par}"
        kdcl, krbg, kdbg, klbg = f"dcols{par}", f"rbg{par}", f"dbg8{par}", f"lbg{par}"
        kxsg, kbtT, kscsb = "xsg", f"btT{par}", f"scsb{par}"
        adTp, adTk = next_p()
        for st in range(2):
            em.mm(lambda e, st=st, c0=c0, adTp=adTp: e.transpose(
                adTp[0:128, st * 32:(st + 1) * 32], ad[:, c0 + st * 128:c0 + (st + 1) * 128],
                eye_f[0:32, 0:32]), adTk, ["ad", "eye_f"], last=(st == 1))
        em.op("v", lambda e, adTp=adTp, adT=adT: e.tensor_copy(adT[:], adTp[0:128, 0:64]),
              reads=[adTk], writes=[kadT])
        acTp, acTk = next_p()
        em.mm(lambda e, acTp=acTp, adT=adT: e.matmul(acTp[:, 0:32], trilM[:], adT[:, 0:32],
              start=True, stop=True), acTk, [kadT, "trilM"], last=False)
        em.mm(lambda e, acTp=acTp, adT=adT: e.matmul(acTp[:, 32:64], onesM[:], adT[:, 0:32],
              start=True, stop=False), acTk, [kadT, "onesM"], last=False)
        em.mm(lambda e, acTp=acTp, adT=adT: e.matmul(acTp[:, 32:64], trilM[:], adT[:, 32:64],
              start=False, stop=True), acTk, [kadT, "trilM"], last=True)
        for st in range(2):
            em.op("v", lambda e, st=st, acTp=acTp, acT=acT: e.tensor_copy(
                acT[st][:], acTp[:, st * 32:(st + 1) * 32]), reads=[acTk], writes=[f"acumT{st}_{par}"])
        acRp, acRk = next_p()
        for st in range(2):
            em.mm(lambda e, st=st, acT=acT, acRp=acRp: e.transpose(
                acRp[0:32, st * 128:(st + 1) * 128], acT[st][:],
                eye_f[0:128, 0:128]), acRk, [f"acumT{st}_{par}", "eye_f"], last=(st == 1))
        em.op("v", lambda e, acRp=acRp, acum=acum: e.tensor_copy(acum[:], acRp[0:32, 0:256]),
              reads=[acRk], writes=[kac])
        em.op("v", lambda e, acum=acum: e.tensor_tensor(ltot[:], ltot[:], acum[:, 255:256], ALU.add),
              reads=["ltot", kac], writes=["ltot"])
        em.op("s", lambda e, acum=acum, dvec=dvec: e.activation(dvec[:], acum[:, 255:256], AF.Exp, bias=0.0, scale=1.0),
              reads=[kac], writes=[kdv])
        em.dma("sync", dvec_d[c:c + 1, :].rearrange("x h -> h x"), dvec[:], f"dvec_d_{c}", [kdv])
        em.dma("sync", acum_d[:, c0:c0 + CHUNK], acum[:], f"acum_d_{c}", [kac])
        em.dma("sync", acumf_d[c:c + 1, :].rearrange("x (h l) -> h x l", h=32),
               acum.ap().rearrange("h (x l) -> h x l", x=1), f"acumf_{c}", [kac])
        em.op("v", lambda e, acum=acum, dsr=dsr: e.tensor_scalar(dsr[:], acum[:], acum[:, 255:256], -1.0,
              ALU.subtract, ALU.mult), reads=[kac], writes=[kdsr])
        em.op("s", lambda e, dsr=dsr: e.activation(dsr[:], dsr[:], AF.Exp, bias=0.0, scale=1.0),
              reads=[kdsr], writes=[kdsr])
        em.op("v", lambda e, c0=c0, ddr=ddr, dsr=dsr: e.tensor_tensor(ddr[:], dtv[:, c0:c0 + CHUNK], dsr[:], ALU.mult),
              reads=["dtv", kdsr], writes=[kddr])
        for st in range(2):
            for (srcap, srck, dstt, dstk) in (
                    (dtv[:, c0 + st * 128:c0 + (st + 1) * 128], "dtv", dtT[st], f"dtvT{st}_{par}"),
                    (ddr[:, st * 128:(st + 1) * 128], kddr, ddTT[st], f"ddT{st}_{par}")):
                pp, ppk = next_p()
                em.mm(lambda e, pp=pp, srcap=srcap: e.transpose(pp[0:128, 0:32], srcap,
                      eye_f[0:32, 0:32]), ppk, [srck, "eye_f"], last=True)
                em.op("v", lambda e, pp=pp, dstt=dstt: e.tensor_copy(dstt[:], pp[0:128, 0:32]),
                      reads=[ppk], writes=[dstk])
        em.dma("sync", dcl[:], bcast_part(dvec_d[c:c + 1, :], 128), kdcl, [f"dvec_d_{c}"])
        for st in range(2):
            pp, ppk = next_p()
            em.mm(lambda e, st=st, pp=pp, c0=c0: e.matmul(pp[:, 0:CHUNK],
                  B_fm[:, c0 + st * 128:c0 + (st + 1) * 128], C_fm[:, c0:c0 + CHUNK],
                  start=True, stop=True), ppk, ["B_fm", "C_fm"], last=True)
            em.op("v", lambda e, st=st, pp=pp, gmp=gmp: e.tensor_tensor(gmp[st][:], pp[:, 0:CHUNK],
                  mask_sb[:, st * CHUNK:(st + 1) * CHUNK], ALU.mult),
                  reads=[ppk, "mask_sb"], writes=[f"gm{st}_{par}"])
        for g in range(4):
            em.dma("sync", rbg_[:], bcast_part(
                acumf_d[c:c + 1, g * 8 * CHUNK:(g + 1) * 8 * CHUNK], 128), krbg, [f"acumf_{c}"])
            for st in range(2):
                em.op("v", lambda e, st=st, g=g, dbgt=dbg_, rbgt=rbg_, acT=acT: e.tensor_tensor(
                    dbgt.ap().rearrange("p (h l) -> p h l", h=8),
                    rbgt.ap().rearrange("p (h l) -> p h l", h=8),
                    rep_inner(acT[st][:, 8 * g:8 * (g + 1)], CHUNK), ALU.subtract),
                    reads=[krbg, f"acumT{st}_{par}"], writes=[kdbg])
                em.op("s", lambda e, lbgt=lbg_, dbgt=dbg_: e.activation(lbgt[:], dbgt[:], AF.Exp, bias=0.0, scale=1.0),
                      reads=[kdbg], writes=[klbg])
                em.op("v", lambda e, lbgt=lbg_: e.tensor_scalar(lbgt[:], lbgt[:], 1.0, None, ALU.min),
                      reads=[klbg], writes=[klbg])
                em.op("v", lambda e, st=st, g=g, mb=mb, lbgt=lbg_, gmp=gmp: e.tensor_tensor(
                    mb[st].ap().rearrange("p (h l) -> p h l", h=HEADS)[:, 8 * g:8 * (g + 1), :],
                    lbgt.ap().rearrange("p (h l) -> p h l", h=8),
                    rep_free(gmp[st][:], 8), ALU.mult),
                    reads=[klbg, f"gm{st}_{par}"], writes=[f"mbuf{st}"])
        for g in range(4):
            em.dma("sync", xsgp.ap().rearrange("p (h t) -> p h t", h=8),
                   xs_d.rearrange("(h p) t -> p h t", p=128)[:, 8 * g:8 * (g + 1), c0:c0 + CHUNK],
                   kxsg, [f"xs_d_{o}" for o in range(8 * g, 8 * (g + 1))])
            for st in range(2):
                pt, ptkey = next_pt()
                for hh in range(8):
                    em.mm(lambda e, pt=pt, hh=hh, st=st, xsgp=xsgp: e.transpose(
                        pt[:, hh * 128:(hh + 1) * 128],
                        xsgp[:, hh * CHUNK + st * 128: hh * CHUNK + (st + 1) * 128], eye_b[:]),
                        ptkey, [kxsg, "eye_b"], last=(hh == 7))
                em.op("v", lambda e, st=st, g=g, pt=pt, xsTp=xsTp: e.tensor_copy(
                    xsTp[st][:, g * 1024:(g + 1) * 1024], pt[:]),
                    reads=[ptkey], writes=[f"xsT{st}"])
        for st in range(2):
            for g in range(4):
                em.op("v", lambda e, st=st, g=g, xdXp=xdXp, xsTp=xsTp, ddTT=ddTT: e.tensor_tensor(
                    xdXp[st].ap().rearrange("p (h q) -> p h q", q=128)[:, 8 * g:8 * (g + 1), :],
                    xsTp[st].ap().rearrange("p (h q) -> p h q", q=128)[:, 8 * g:8 * (g + 1), :],
                    rep_inner(ddTT[st][:, 8 * g:8 * (g + 1)], 128), ALU.mult),
                    reads=[f"xsT{st}", f"ddT{st}_{par}"], writes=[f"xdX{st}"])
        pt, ptkey = next_pt()
        for st in range(2):
            em.mm(lambda e, pt=pt, st=st, c0=c0: e.transpose(pt[:, st * 128:(st + 1) * 128],
                  B_fm[:, c0 + st * 128:c0 + (st + 1) * 128], eye_b[:]),
                  ptkey, ["B_fm", "eye_b"], last=(st == 1))
        em.op("v", lambda e, pt=pt, btTp=btTp: e.tensor_copy(btTp[:], pt[:, 0:256]),
              reads=[ptkey], writes=[kbtT])
        for g4 in range(8):
            pp, ppk = next_p()
            for q in range(4):
                h = g4 * 4 + q
                for st in range(2):
                    em.mm(lambda e, pp=pp, q=q, h=h, st=st, btTp=btTp, xdXp=xdXp: e.matmul(
                        pp[:, q * 128:(q + 1) * 128], btTp[:, st * 128:(st + 1) * 128],
                        xdXp[st][:, h * 128:(h + 1) * 128],
                        start=(q == 0 and st == 0), stop=(q == 3 and st == 1)),
                        ppk, [kbtT, f"xdX{st}"], last=(q == 3 and st == 1))
            em.op("v", lambda e, pp=pp, scsbp=scsbp: e.tensor_copy(scsbp[:], pp[:]),
                  reads=[ppk], writes=[kscsb])
            em.dma("sync", sstates_d[c * 128:(c + 1) * 128, g4 * 512:(g4 + 1) * 512],
                   scsbp[:], f"sst_{c}_{g4}", [kscsb])
            for q in range(4):
                h = g4 * 4 + q
                em.op("v", lambda e, h=h, q=q, t=state_run, dcl=dcl, scsbp=scsbp: e.scalar_tensor_tensor(
                    t[:, h * 128:(h + 1) * 128], t[:, h * 128:(h + 1) * 128],
                    dcl[:, h:h + 1], scsbp[:, q * 128:(q + 1) * 128], ALU.mult, ALU.add),
                    reads=["state_run", kdcl, kscsb], writes=["state_run"])
        for st in range(2):
            for g in range(4):
                em.op("v", lambda e, st=st, g=g, xdXp=xdXp, xsTp=xsTp, dtT=dtT: e.tensor_tensor(
                    xdXp[st].ap().rearrange("p (h q) -> p h q", q=128)[:, 8 * g:8 * (g + 1), :],
                    xsTp[st].ap().rearrange("p (h q) -> p h q", q=128)[:, 8 * g:8 * (g + 1), :],
                    rep_inner(dtT[st][:, 8 * g:8 * (g + 1)], 128), ALU.mult),
                    reads=[f"xsT{st}", f"dtvT{st}_{par}"], writes=[f"xdX{st}"])
        for lt in range(2):
            for g4 in range(8):
                pp, ppk = next_p()
                for q in range(4):
                    h = g4 * 4 + q
                    for st in range(2):
                        em.mm(lambda e, pp=pp, q=q, h=h, st=st, lt=lt, mb=mb, xdXp=xdXp: e.matmul(
                            pp[:, q * 128:(q + 1) * 128],
                            mb[st][:, h * CHUNK + lt * 128: h * CHUNK + lt * 128 + 128],
                            xdXp[st][:, h * 128:(h + 1) * 128],
                            start=(q == 0 and st == 0), stop=False),
                            ppk, [f"mbuf{st}", f"xdX{st}"], last=False)
                    em.mm(lambda e, pp=pp, q=q, h=h, lt=lt, xsTp=xsTp: e.matmul(
                        pp[:, q * 128:(q + 1) * 128], ddiag_sb[:, h * 128:(h + 1) * 128],
                        xsTp[lt][:, h * 128:(h + 1) * 128],
                        start=False, stop=(q == 3)),
                        ppk, ["ddiag_sb", f"xsT{lt}"], last=(q == 3))
                eng = "v" if g4 % 2 == 0 else "s"
                if eng == "v":
                    em.op("v", lambda e, pp=pp, lt=lt, g4=g4, ysbp=ysbp: e.tensor_copy(
                        ysbp[lt][:, g4 * 512:(g4 + 1) * 512], pp[:]),
                        reads=[ppk], writes=[f"ysb{lt}"])
                else:
                    em.op("s", lambda e, pp=pp, lt=lt, g4=g4, ysbp=ysbp: e.activation(
                        ysbp[lt][:, g4 * 512:(g4 + 1) * 512], pp[:], AF.Copy, bias=0.0, scale=1.0),
                        reads=[ppk], writes=[f"ysb{lt}"])
            em.dma("sync", ypart_d[c0 + lt * 128:c0 + (lt + 1) * 128, :], ysbp[lt][:],
                   f"ypart_{c}_{lt}", [f"ysb{lt}"])

    for c in range(NCH):
        a2_chunk(c)

    em.dma("sync", cc_in[0:128, :], state_run[:], "cc_in", ["state_run"])
    em.dma("sync", cc_in[128:129, 0:32].rearrange("x h -> h x"), ltot[:], "ltot_cc", ["ltot"])

    em.collective(lambda e: e.collective_compute(
        "AllGather", ALU.bypass, replica_groups=RG,
        ins=[cc_in.opt()], outs=[cc_out.opt()]), ["cc_in", "ltot_cc"], "cc_out")

    if upto == "A2":
        _final_wait(nc, em)
        return nc
    pop_scope()
    pop_scope()

    state_b = sbp("state_run_b", [128, HEADS * HD])

    push_scope()
    rbB = sbp("rbB", [128, 8 * CHUNK])
    ebg = sbp("ebg", [128, 8 * CHUNK], BF16)
    cpb = sbp("cpb", [128, HEADS * CHUNK], BF16)
    scast = sbp("scast", [128, HEADS * HD], BF16)
    dcolsB = sbp("dcolsB", [128, 32])
    scld = sbp("scld", [128, HEADS * HD])
    yp = [sbp(f"yp{i}", [128, INTER]) for i in range(2)]
    gload = [sbp(f"gload{i}", [128, INTER], BF16) for i in range(2)]
    sumsq = sbp("sumsq", [128, 1])
    rstdB = sbp("rstdB", [128, 4])
    yfmg = sbp("yfmg", [128, 32 * 512], BF16)
    attn_sb = sbp("attn_sb", [128, 4 * D])
    wop = [sbp(f"wop{i}", [128, 512], BF16) for i in range(3)]
    bnb2 = sbp("bnb2", [128, 24])
    mv2 = sbp("mv2", [128, 2])
    c1 = sbp("c1", [128, 1])
    c2 = sbp("c2", [128, 1])
    c3 = sbp("c3", [128, 1])
    xres = sbp("xres", [128, D])
    x1o = sbp("x1o", [128, D])
    ybf = scast

    ltg = sbp("ltg", [32, 16])
    ltg2 = sbp("ltg2", [32, 16])
    cmk_sb = sbp("cmk_sb", [32, E])
    ngk_sb = sbp("ngk_sb", [32, E])
    ohk_sb = sbp("ohk_sb", [32, E])
    t8 = sbp("t8", [32, E])
    cum1 = sbp("cum1", [32, 1])
    wmat = sbp("wmat", [32, 8])
    wcols = sbp("wcols", [128, 256])
    Sg = [scld, yp[0]]
    Sgk = ["scld", "yp0"]
    Inp = [yp[1], attn_sb]
    Inpk = ["yp1", "attn_sb"]
    em.dma("sync", cmk_sb[:], cmaskB, "cmk_sb")
    em.dma("sync", ngk_sb[:], negB, "ngk_sb")
    em.dma("sync", ohk_sb[:], onehotB, "ohk_sb")
    em.op("v", lambda e: e.memset(ltg[:], 0.0), writes=["ltg"])
    em.op("v", lambda e: e.memset(ltg2[:], 0.0), writes=["ltg2"])
    em.dma("sync", ltg[:, 8:16].rearrange("h (x j) -> h x j", x=1),
           cc_out.rearrange("(j r) n -> n r j", r=129)[0:32, 128:129, :],
           "ltg", ["cc_out"], allow_slow_non_contiguous=True)
    src_, dst_, sk, dk = ltg, ltg2, "ltg", "ltg2"
    for s in (1, 2, 4):
        em.op("v", lambda e, src_=src_, dst_=dst_, s=s: e.tensor_tensor(
            dst_[:, 8:16], src_[:, 8:16], src_[:, 8 - s:16 - s], ALU.add),
            reads=[sk], writes=[dk])
        src_, dst_, sk, dk = dst_, src_, dk, sk
    cum, cumk = src_, sk
    em.op("v", lambda e, cum=cum: e.tensor_tensor(t8[:], cum[:, 8:16], ohk_sb[:], ALU.mult),
          reads=[cumk, "ohk_sb"], writes=["t8"])
    em.op("v", lambda e: e.tensor_reduce(cum1[:], t8[:], AX.X, ALU.add),
          reads=["t8"], writes=["cum1"])
    em.op("v", lambda e, cum=cum: e.tensor_scalar(wmat[:], cum[:, 8:16], cum1[:], -1.0,
          ALU.subtract, ALU.mult), reads=[cumk, "cum1"], writes=["wmat"])
    em.op("v", lambda e: e.tensor_tensor(wmat[:], wmat[:], cmk_sb[:], ALU.mult),
          reads=["wmat", "cmk_sb"], writes=["wmat"])
    em.op("v", lambda e: e.tensor_tensor(wmat[:], wmat[:], ngk_sb[:], ALU.add),
          reads=["wmat", "ngk_sb"], writes=["wmat"])
    em.op("s", lambda e: e.activation(wmat[:], wmat[:], AF.Exp, bias=0.0, scale=1.0),
          reads=["wmat"], writes=["wmat"])
    em.dma("sync", wflat_d.rearrange("x (j h) -> h x j", h=32), wmat.ap().rearrange("h (x j) -> h x j", x=1),
           "wflat_d", ["wmat"], allow_slow_non_contiguous=True)
    em.dma("sync", wcols[:], bcast_part(wflat_d, 128), "wcols", ["wflat_d"])
    for j in range(NCORES):
        sj = j % 2
        em.dma("sync", Sg[sj][:, 0:HEADS * HD], cc_out[j * 129:j * 129 + 128, :], Sgk[sj], ["cc_out"])
        cur, prv = Inpk[j % 2], Inpk[(j + 1) % 2]
        curt, prvt = Inp[j % 2], Inp[(j + 1) % 2]
        em.op("v", lambda e, j=j, curt=curt, sj=sj: e.tensor_tensor(
            curt.ap().rearrange("p (h q) -> p h q", q=128)[:, 0:HEADS, :],
            Sg[sj].ap().rearrange("p (h q) -> p h q", q=128)[:, 0:HEADS, :],
            rep_inner(wcols[:, j * 32:(j + 1) * 32], 128), ALU.mult),
            reads=[Sgk[sj], "wcols"], writes=[cur])
        if j > 0:
            em.op("v", lambda e, curt=curt, prvt=prvt: e.tensor_tensor(
                curt[:, 0:HEADS * HD], curt[:, 0:HEADS * HD], prvt[:, 0:HEADS * HD], ALU.add),
                reads=[cur, prv], writes=[cur])
    em.op("v", lambda e, curt=Inp[(NCORES - 1) % 2]: e.tensor_copy(state_b[:], curt[:, 0:HEADS * HD]),
          reads=[Inpk[(NCORES - 1) % 2]], writes=["state_run_b"])

    if upto == "IN":
        em.dma("sync", dbg_d[0:128, 0:HEADS * HD], state_b[:], "dbg_d", ["state_run_b"])
        _final_wait(nc, em)
        return nc


    def b_chunk(c):
        c0 = c * CHUNK
        ltg4 = (c % 2) * 2
        for g in range(4):
            em.dma("sync", rbB[:], bcast_part(
                acumf_d[c:c + 1, g * 8 * CHUNK:(g + 1) * 8 * CHUNK], 128), "rbB", [f"acumf_{c}"])
            em.op("s", lambda e: e.activation(ebg[:], rbB[:], AF.Exp, bias=0.0, scale=1.0),
                  reads=["rbB"], writes=["ebg"])
            em.op("v", lambda e, g=g, c0=c0: e.tensor_tensor(
                cpb.ap().rearrange("p (h l) -> p h l", h=HEADS)[:, 8 * g:8 * (g + 1), :],
                rep_free(C_fm[:, c0:c0 + CHUNK], 8),
                ebg.ap().rearrange("p (h l) -> p h l", h=8), ALU.mult),
                reads=["ebg", "C_fm"], writes=["cpb"])
        em.op("v", lambda e: e.tensor_copy(scast[:], state_b[:]),
              reads=["state_run_b"], writes=["scast"])
        em.dma("sync", dcolsB[:], bcast_part(dvec_d[c:c + 1, :], 128), "dcolsB", [f"dvec_d_{c}"])
        em.dma("sync", scld[:], sstates_d[c * 128:(c + 1) * 128, :], "scld", [f"sst_{c}_{g4}" for g4 in range(8)])
        for h in range(HEADS):
            em.op("v", lambda e, h=h: e.scalar_tensor_tensor(
                state_b[:, h * 128:(h + 1) * 128], state_b[:, h * 128:(h + 1) * 128],
                dcolsB[:, h:h + 1], scld[:, h * 128:(h + 1) * 128], ALU.mult, ALU.add),
                reads=["state_run_b", "dcolsB", "scld"], writes=["state_run_b"])
        for lt in range(2):
            ypt, kyp = yp[lt], f"yp{lt}"
            glt, kgl = gload[lt], f"gload{lt}"
            em.dma("sync", ypt[:], ypart_d[c0 + lt * 128:c0 + (lt + 1) * 128, :], kyp, [f"ypart_{c}_{lt}"])
            for g4 in range(8):
                pp, ppk = next_p()
                for q in range(4):
                    h = g4 * 4 + q
                    em.mm(lambda e, pp=pp, q=q, h=h, lt=lt: e.matmul(
                        pp[:, q * 128:(q + 1) * 128],
                        cpb[:, h * CHUNK + lt * 128: h * CHUNK + lt * 128 + 128],
                        scast[:, h * 128:(h + 1) * 128],
                        start=(q == 0), stop=(q == 3)),
                        ppk, ["cpb", "scast"], last=(q == 3))
                em.op("v", lambda e, pp=pp, g4=g4, ypt=ypt: e.tensor_tensor(
                    ypt[:, g4 * 512:(g4 + 1) * 512], ypt[:, g4 * 512:(g4 + 1) * 512], pp[:],
                    ALU.add), reads=[ppk, kyp], writes=[kyp])
            em.dma("sync", glt[:], gate_s_d[c0 + lt * 128:c0 + (lt + 1) * 128, :], kgl,
                   [f"gates_{c * 2 + lt}_{p}" for p in range(8)])
            em.op("v", lambda e, ypt=ypt, glt=glt: e.tensor_tensor(ypt[:], ypt[:], glt[:], ALU.mult),
                  reads=[kyp, kgl], writes=[kyp])
            em.op("s", lambda e, ypt=ypt: e.activation(
                  scld.ap().rearrange("p (a q) -> p a q", a=2), ypt[:], AF.Square,
                  bias=0.0, scale=1.0, accum_out=sumsq[:]),
                  reads=[kyp], writes=["scld", "sumsq"])
            em.op("v", lambda e: e.tensor_scalar(sumsq[:], sumsq[:], 1.0 / INTER, EPS,
                  ALU.mult, ALU.add), reads=["sumsq"], writes=["sumsq"])
            em.op("s", lambda e: e.activation(sumsq[:], sumsq[:], AF.Ln, bias=0.0, scale=1.0),
                  reads=["sumsq"], writes=["sumsq"])
            em.op("s", lambda e, lt=lt, ltg4=ltg4: e.activation(rstdB[:, ltg4 + lt:ltg4 + lt + 1],
                  sumsq[:], AF.Exp, bias=0.0, scale=-0.5),
                  reads=["sumsq"], writes=["rstdB"])
            em.op("v", lambda e, ypt=ypt: e.tensor_copy(ybf[:], ypt[:]), reads=[kyp, "scast"], writes=["scast"])
            for fh in range(4):
                pt, ptkey = next_pt()
                for ff in range(8):
                    f = fh * 8 + ff
                    em.mm(lambda e, pt=pt, ff=ff, f=f: e.transpose(
                        pt[:, ff * 128:(ff + 1) * 128], ybf[:, f * 128:(f + 1) * 128], eye_b[:]),
                        ptkey, ["scast", "eye_b"], last=(ff == 7))
                em.op("v", lambda e, pt=pt, fh=fh, lt=lt, ltg4=ltg4: e.tensor_copy(
                    yfmg.ap().rearrange("p (f l) -> p f l", l=512)[:, fh * 8:(fh + 1) * 8,
                                                                  (ltg4 + lt) * 128:(ltg4 + lt + 1) * 128],
                    pt.ap().rearrange("p (f l) -> p f l", f=8)),
                    reads=[ptkey], writes=["yfmg"])
        if c % 2 == 1:
            gtok0 = (c - 1) * CHUNK
            for dpiece in range(4):
                for f in range(32):
                    jw = f % 3
                    em.dma("sync", wop[jw][:], opWTb_d[f * 128:(f + 1) * 128,
                           dpiece * 512:(dpiece + 1) * 512], f"wop{jw}", [f"opwtb_{f}"])
                    for lt4 in range(4):
                        em.mm(lambda e, lt4=lt4, f=f, jw=jw, dpiece=dpiece: e.matmul(
                            P[lt4][:], yfmg[:, f * 512 + lt4 * 128: f * 512 + (lt4 + 1) * 128],
                            wop[jw][:], start=(f == 0), stop=(f == 31)),
                            pk[lt4], ["yfmg", f"wop{jw}"], last=(lt4 == 3))
                for lt4 in range(4):
                    em.op("v", lambda e, lt4=lt4, dpiece=dpiece: e.tensor_copy(
                        attn_sb[:, lt4 * D + dpiece * 512: lt4 * D + (dpiece + 1) * 512],
                        P[lt4][:]), reads=[pk[lt4]], writes=["attn_sb"])
            for lt4 in range(4):
                for s4 in range(4):
                    em.op("v", lambda e, lt4=lt4, s4=s4: e.bn_stats(
                        bnb2[:, s4 * 6:(s4 + 1) * 6],
                        attn_sb[:, lt4 * D + s4 * 512: lt4 * D + (s4 + 1) * 512]),
                        reads=["attn_sb"], writes=["bnb2"])
                em.op("v", lambda e: e.bn_aggr(mv2[:], bnb2[:]), reads=["bnb2"], writes=["mv2"])
                em.op("v", lambda e, lt4=lt4: e.tensor_tensor(c1[:], rstdB[:, lt4:lt4 + 1],
                      rstdB[:, lt4:lt4 + 1], ALU.mult), reads=["rstdB"], writes=["c1"])
                em.op("v", lambda e: e.tensor_scalar(c1[:], c1[:], mv2[:, 1:2], EPS,
                      ALU.mult, ALU.add), reads=["c1", "mv2"], writes=["c1"])
                em.op("s", lambda e: e.activation(c1[:], c1[:], AF.Ln, bias=0.0, scale=1.0),
                      reads=["c1"], writes=["c1"])
                em.op("s", lambda e: e.activation(c1[:], c1[:], AF.Exp, bias=0.0, scale=-0.5),
                      reads=["c1"], writes=["c1"])
                em.op("v", lambda e, lt4=lt4: e.tensor_tensor(c2[:], c1[:], rstdB[:, lt4:lt4 + 1],
                      ALU.mult), reads=["c1", "rstdB"], writes=["c2"])
                em.op("v", lambda e: e.tensor_scalar(c3[:], mv2[:, 0:1], c2[:], -1.0,
                      ALU.mult, ALU.mult), reads=["mv2", "c2"], writes=["c3"])
                tok0 = gtok0 + lt4 * 128
                em.dma("sync", xres[:], x_in[PAD + tok0:PAD + tok0 + 128, :], "xres")
                em.op("v", lambda e: e.tensor_scalar(xres[:], xres[:], c3[:], None, ALU.add),
                      reads=["xres", "c3"], writes=["xres"])
                em.op("v", lambda e, lt4=lt4: e.scalar_tensor_tensor(x1o[:],
                      attn_sb[:, lt4 * D:(lt4 + 1) * D], c2[:], xres[:], ALU.mult, ALU.add),
                      reads=["attn_sb", "c2", "xres"], writes=["x1o"])
                em.dma("sync", x1_d[tok0:tok0 + 128, :], x1o[:], f"x1_{tok0 // 128}", ["x1o"])
    for c in range(NCH):
        b_chunk(c)

    pop_scope()

    if upto == "B":
        _final_wait(nc, em)
        return nc
    pop_scope()

    push_scope()
    xt_fm = sbp("xt_fm", [128, NKD * TC], BF16)
    h_sb = sbp("h_sb", [128, NKI * TC], BF16)
    dw_sb = sbp("dw_sb", [128, 16 * E])
    stg = [sbp(f"stg{i}", [128, 512]) for i in range(4)]
    stgb = [sbp(f"stgb{i}", [128, 512]) for i in range(4)]
    push_scope()
    x1t = [sbp(f"x1t{i}", [128, D]) for i in range(2)]
    bnb3 = [sbp(f"bnb3{i}", [128, 24]) for i in range(2)]
    mv3 = [sbp(f"mv3{i}", [128, 2]) for i in range(2)]
    r3 = [sbp(f"r3{i}", [128, 1]) for i in range(2)]
    n3 = [sbp(f"n3{i}", [128, 1]) for i in range(2)]
    xth = [sbp(f"xth{i}", [128, D], BF16) for i in range(2)]
    xtl = [sbp(f"xtl{i}", [128, D], BF16) for i in range(2)]
    lofm = sbp("lofm", [128, NKD * 128], BF16)
    gw_sb = sbp("gw_sb", [128, NKD * E], BF16)
    gb_bc = sbp("gb_bc", [128, E])
    id_bc = sbp("id_bc", [128, E])
    lg = sbp("lg", [128, E])
    rcol = sbp("rcol", [128, 1])
    r2col = sbp("r2col", [128, 1])
    e8a = sbp("e8a", [128, E])
    e8b = sbp("e8b", [128, E])

    em.dma("gp", gw_sb.ap().rearrange("p (k c) -> p k c", k=NKD),
           gateWT.rearrange("(k p) c -> p k c", p=128), "gw_sb")
    em.dma("sync", gb_bc[:], bcast_part(gatebB, 128), "gb_bc")
    em.dma("sync", id_bc[:], bcast_part(idxdB, 128), "id_bc")

    for tt in range(16):
        j = tt % 2
        kx, kb, km, kr, kn = f"x1t{j}", f"bnb3{j}", f"mv3{j}", f"r3{j}", f"n3{j}"
        em.dma("sync", x1t[j][:], x1_d[tt * 128:(tt + 1) * 128, :], kx, [f"x1_{tt}"])
        for s4 in range(4):
            em.op("v", lambda e, j=j, s4=s4: e.bn_stats(bnb3[j][:, s4 * 6:(s4 + 1) * 6],
                  x1t[j][:, s4 * 512:(s4 + 1) * 512]), reads=[kx], writes=[kb])
        em.op("v", lambda e, j=j: e.bn_aggr(mv3[j][:], bnb3[j][:]), reads=[kb], writes=[km])
        em.op("v", lambda e, j=j: e.tensor_scalar(r3[j][:], mv3[j][:, 1:2], EPS, None, ALU.add),
              reads=[km], writes=[kr])
        em.op("s", lambda e, j=j: e.activation(r3[j][:], r3[j][:], AF.Ln, bias=0.0, scale=1.0),
              reads=[kr], writes=[kr])
        em.op("s", lambda e, j=j: e.activation(r3[j][:], r3[j][:], AF.Exp, bias=0.0, scale=-0.5),
              reads=[kr], writes=[kr])
        em.op("v", lambda e, j=j: e.tensor_scalar(n3[j][:], r3[j][:], mv3[j][:, 0:1], -1.0,
              ALU.mult, ALU.mult), reads=[kr, km], writes=[kn])
        em.op("v", lambda e, j=j: e.tensor_scalar(xth[j][:], x1t[j][:], r3[j][:], n3[j][:],
              ALU.mult, ALU.add), reads=[kx, kr, kn], writes=[f"xth{j}"])
        em.op("v", lambda e, j=j: e.tensor_scalar(x1t[j][:], x1t[j][:], r3[j][:], n3[j][:],
              ALU.mult, ALU.add), reads=[kx, kr, kn], writes=[kx])
        em.op("v", lambda e, j=j: e.tensor_tensor(xtl[j][:], x1t[j][:], xth[j][:], ALU.subtract),
              reads=[kx, f"xth{j}"], writes=[f"xtl{j}"])
        for half in range(2):
            pt, ptkey = next_pt()
            for kk in range(8):
                k = half * 8 + kk
                em.mm(lambda e, j=j, k=k, kk=kk, pt=pt: e.transpose(
                    pt[:, kk * 128:(kk + 1) * 128], xth[j][:, k * 128:(k + 1) * 128], eye_b[:]),
                    ptkey, [f"xth{j}", "eye_b"], last=(kk == 7))
            em.op("v", lambda e, tt=tt, half=half, pt=pt: e.tensor_copy(
                xt_fm.ap().rearrange("p (k t) -> p k t", k=NKD)[:, half * 8:(half + 1) * 8,
                                                               tt * 128:(tt + 1) * 128],
                pt.ap().rearrange("p (k t) -> p k t", k=8)),
                reads=[ptkey], writes=["xt_fm"])
        for half in range(2):
            pt, ptkey = next_pt()
            for kk in range(8):
                k = half * 8 + kk
                em.mm(lambda e, j=j, k=k, kk=kk, pt=pt: e.transpose(
                    pt[:, kk * 128:(kk + 1) * 128], xtl[j][:, k * 128:(k + 1) * 128], eye_b[:]),
                    ptkey, [f"xtl{j}", "eye_b"], last=(kk == 7))
            em.op("v", lambda e, half=half, pt=pt: e.tensor_copy(
                lofm.ap().rearrange("p (k t) -> p k t", k=NKD)[:, half * 8:(half + 1) * 8, :],
                pt.ap().rearrange("p (k t) -> p k t", k=8)),
                reads=[ptkey], writes=["lofm"])
        pp, ppk = next_p()
        for k in range(NKD):
            em.mm(lambda e, pp=pp, k=k, tt=tt: e.matmul(pp[:, 0:E],
                  xt_fm[:, k * TC + tt * 128: k * TC + (tt + 1) * 128],
                  gw_sb[:, k * E:(k + 1) * E], start=(k == 0), stop=False),
                  ppk, ["xt_fm", "gw_sb"], last=False)
        for k in range(NKD):
            em.mm(lambda e, pp=pp, k=k: e.matmul(pp[:, 0:E],
                  lofm[:, k * 128:(k + 1) * 128],
                  gw_sb[:, k * E:(k + 1) * E], start=False, stop=(k == NKD - 1)),
                  ppk, ["lofm", "gw_sb"], last=(k == NKD - 1))
        em.op("v", lambda e, pp=pp: e.tensor_copy(lg[:], pp[:, 0:E]), reads=[ppk], writes=["lg"])
        em.op("v", lambda e: e.tensor_reduce(rcol[:], lg[:], AX.X, ALU.max),
              reads=["lg"], writes=["rcol"])
        em.op("v", lambda e: e.tensor_scalar(rcol[:], rcol[:], -1.0, None, ALU.mult),
              reads=["rcol"], writes=["rcol"])
        em.op("s", lambda e: e.activation(lg[:], lg[:], AF.Exp, bias=rcol[:], scale=1.0),
              reads=["lg", "rcol"], writes=["lg"])
        em.op("v", lambda e: e.tensor_reduce(rcol[:], lg[:], AX.X, ALU.add),
              reads=["lg"], writes=["rcol"])
        em.op("v", lambda e: e.reciprocal(rcol[:], rcol[:]), reads=["rcol"], writes=["rcol"])
        em.op("v", lambda e: e.tensor_scalar(lg[:], lg[:], rcol[:], None, ALU.mult),
              reads=["lg", "rcol"], writes=["lg"])
        em.op("v", lambda e: e.tensor_tensor(e8a[:], lg[:], gb_bc[:], ALU.add),
              reads=["lg", "gb_bc"], writes=["e8a"])
        em.op("v", lambda e: e.tensor_reduce(rcol[:], e8a[:], AX.X, ALU.max),
              reads=["e8a"], writes=["rcol"])
        em.op("v", lambda e: e.tensor_scalar(e8b[:], e8a[:], rcol[:], None, ALU.is_equal),
              reads=["e8a", "rcol"], writes=["e8b"])
        em.op("v", lambda e: e.tensor_tensor(e8b[:], e8b[:], id_bc[:], ALU.mult),
              reads=["e8b", "id_bc"], writes=["e8b"])
        em.op("v", lambda e: e.tensor_reduce(r2col[:], e8b[:], AX.X, ALU.max),
              reads=["e8b"], writes=["r2col"])
        em.op("v", lambda e: e.tensor_scalar(e8b[:], e8b[:], r2col[:], None, ALU.is_equal),
              reads=["e8b", "r2col"], writes=["e8b"])
        em.op("v", lambda e: e.scalar_tensor_tensor(e8a[:], e8b[:], -1e9, e8a[:],
              ALU.mult, ALU.add), reads=["e8b", "e8a"], writes=["e8a"])
        em.op("v", lambda e: e.tensor_reduce(rcol[:], e8a[:], AX.X, ALU.max),
              reads=["e8a"], writes=["rcol"])
        em.op("v", lambda e: e.tensor_scalar(e8a[:], e8a[:], rcol[:], None, ALU.is_equal),
              reads=["e8a", "rcol"], writes=["e8a"])
        em.op("v", lambda e: e.tensor_tensor(e8a[:], e8a[:], id_bc[:], ALU.mult),
              reads=["e8a", "id_bc"], writes=["e8a"])
        em.op("v", lambda e: e.tensor_reduce(r2col[:], e8a[:], AX.X, ALU.max),
              reads=["e8a"], writes=["r2col"])
        em.op("v", lambda e: e.tensor_scalar(e8a[:], e8a[:], r2col[:], None, ALU.is_equal),
              reads=["e8a", "r2col"], writes=["e8a"])
        em.op("v", lambda e: e.tensor_tensor(e8a[:], e8a[:], e8b[:], ALU.add),
              reads=["e8a", "e8b"], writes=["e8a"])
        em.op("v", lambda e, tt=tt: e.tensor_tensor(dw_sb[:, tt * E:(tt + 1) * E], lg[:],
              e8a[:], ALU.mult), reads=["lg", "e8a"], writes=["dw_sb"])

    pop_scope()
    w1sb = [sbp(f"w1sb{i}", [128, NKD * 128], BF16) for i in range(2)]
    w3sb = [sbp(f"w3sb{i}", [128, NKD * 128], BF16) for i in range(2)]
    w2sb = sbp("w2sb", [128, NKI * D], BF16)

    for ei in range(E + 1):
        shared = (ei == 0)
        e_idx = ei - 1
        for ik in range(NKI):
            jj = ik % 2
            if shared:
                w1src = ws1t[:, ik * 128:(ik + 1) * 128]
                w3src = ws3t[:, ik * 128:(ik + 1) * 128]
            else:
                w1src = w1t[e_idx * D:(e_idx + 1) * D, ik * 128:(ik + 1) * 128]
                w3src = w3t[e_idx * D:(e_idx + 1) * D, ik * 128:(ik + 1) * 128]
            em.dma("gp", w1sb[jj].ap().rearrange("p (k c) -> p k c", k=NKD),
                   w1src.rearrange("(k p) c -> p k c", p=128), f"w1sb{jj}")
            em.dma("gp", w3sb[jj].ap().rearrange("p (k c) -> p k c", k=NKD),
                   w3src.rearrange("(k p) c -> p k c", p=128), f"w3sb{jj}")
            for ph in range(2):
                pa0, pak0 = next_p()
                pa1, pak1 = next_p()
                pb0, pbk0 = next_p()
                pb1, pbk1 = next_p()
                for k in range(NKD):
                    for pi, (pp, ppk) in enumerate(((pa0, pak0), (pa1, pak1))):
                        em.mm(lambda e, pp=pp, k=k, jj=jj, ph=ph, pi=pi: e.matmul(
                            pp[:], w1sb[jj][:, k * 128:(k + 1) * 128],
                            xt_fm[:, k * TC + (2 * ph + pi) * 512: k * TC + (2 * ph + pi + 1) * 512],
                            start=(k == 0), stop=(k == NKD - 1)),
                            ppk, [f"w1sb{jj}", "xt_fm"], last=(k == NKD - 1 and pi == 1))
                for k in range(NKD):
                    for pi, (pp, ppk) in enumerate(((pb0, pbk0), (pb1, pbk1))):
                        em.mm(lambda e, pp=pp, k=k, jj=jj, ph=ph, pi=pi: e.matmul(
                            pp[:], w3sb[jj][:, k * 128:(k + 1) * 128],
                            xt_fm[:, k * TC + (2 * ph + pi) * 512: k * TC + (2 * ph + pi + 1) * 512],
                            start=(k == 0), stop=(k == NKD - 1)),
                            ppk, [f"w3sb{jj}", "xt_fm"], last=(k == NKD - 1 and pi == 1))
                for pi, (pa, pak), (pb, pbk) in ((0, (pa0, pak0), (pb0, pbk0)),
                                                 (1, (pa1, pak1), (pb1, pbk1))):
                    piece = 2 * ph + pi
                    sj = piece % 4
                    em.op("s", lambda e, sj=sj, pa=pa: e.activation(stg[sj][:], pa[:], AF.Silu,
                          bias=0.0, scale=1.0), reads=[pak], writes=[f"stg{sj}"])
                    em.op("v", lambda e, sj=sj, pb=pb, ik=ik, piece=piece: e.tensor_tensor(
                        h_sb[:, ik * TC + piece * 512: ik * TC + (piece + 1) * 512],
                        stg[sj][:], pb[:], ALU.mult),
                        reads=[f"stg{sj}", pbk], writes=["h_sb"])
        w2src = (ws2t if shared else w2t[e_idx * I_FF:(e_idx + 1) * I_FF, :])
        em.dma("gp", w2sb.ap().rearrange("p (k c) -> p k c", k=NKI),
               w2src.rearrange("(k p) c -> p k c", p=128), "w2sb")
        for tt in range(16):
            pps = [next_p() for _ in range(4)]
            for ik in range(NKI):
                for dpiece in range(4):
                    pp, ppk = pps[dpiece]
                    em.mm(lambda e, pp=pp, ik=ik, tt=tt, dpiece=dpiece: e.matmul(
                        pp[:], h_sb[:, ik * TC + tt * 128: ik * TC + (tt + 1) * 128],
                        w2sb[:, ik * D + dpiece * 512: ik * D + (dpiece + 1) * 512],
                        start=(ik == 0), stop=(ik == NKI - 1)),
                        ppk, ["h_sb", "w2sb"], last=(ik == NKI - 1 and dpiece == 3))
            for dpiece in range(4):
                pp, ppk = pps[dpiece]
                sj = dpiece
                if shared:
                    em.op("v", lambda e, sj=sj, pp=pp: e.tensor_copy(stgb[sj][:], pp[:]),
                          reads=[ppk], writes=[f"stgb{sj}"])
                    em.dma("sync", y_d[tt * 128:(tt + 1) * 128, dpiece * 512:(dpiece + 1) * 512],
                           stgb[sj][:], f"y_d_{tt}_{dpiece}", [f"stgb{sj}"])
                else:
                    em.op("v", lambda e, sj=sj, pp=pp, e_idx=e_idx, tt=tt: e.tensor_scalar(
                        stgb[sj][:], pp[:], dw_sb[:, tt * E + e_idx: tt * E + e_idx + 1], None,
                        ALU.mult), reads=[ppk, "dw_sb"], writes=[f"stgb{sj}"])
                    em.dma("gp", y_d[tt * 128:(tt + 1) * 128, dpiece * 512:(dpiece + 1) * 512],
                           stgb[sj][:], f"y_d_{tt}_{dpiece}", [f"stgb{sj}"], accum_op=ALU.add)

    x1t4 = [sbp(f"x1t4_{i}", [128, D]) for i in range(2)]
    bnb34 = [sbp(f"bnb3_4{i}", [128, 24]) for i in range(2)]
    mv34 = [sbp(f"mv3_4{i}", [128, 2]) for i in range(2)]
    r34 = [sbp(f"r3_4{i}", [128, 1]) for i in range(2)]
    n34 = [sbp(f"n3_4{i}", [128, 1]) for i in range(2)]
    outx = [sbp(f"outx{i}", [128, D]) for i in range(2)]
    for tt in range(16):
        j = tt % 2
        kx, kb, km, kr, kn = f"x1t4_{j}", f"bnb3_4{j}", f"mv3_4{j}", f"r3_4{j}", f"n3_4{j}"
        em.dma("sync", outx[j][:], y_d[tt * 128:(tt + 1) * 128, :], f"outx{j}", [f"y_d_{tt}_{dp}" for dp in range(4)])
        for s4 in range(4):
            em.op("v", lambda e, j=j, s4=s4: e.bn_stats(bnb34[j][:, s4 * 6:(s4 + 1) * 6],
                  outx[j][:, s4 * 512:(s4 + 1) * 512]), reads=[f"outx{j}"], writes=[kb])
        em.op("v", lambda e, j=j: e.bn_aggr(mv34[j][:], bnb34[j][:]), reads=[kb], writes=[km])
        em.op("v", lambda e, j=j: e.tensor_scalar(r34[j][:], mv34[j][:, 1:2], EPS, None, ALU.add),
              reads=[km], writes=[kr])
        em.op("s", lambda e, j=j: e.activation(r34[j][:], r34[j][:], AF.Ln, bias=0.0, scale=1.0),
              reads=[kr], writes=[kr])
        em.op("s", lambda e, j=j: e.activation(r34[j][:], r34[j][:], AF.Exp, bias=0.0, scale=-0.5),
              reads=[kr], writes=[kr])
        em.op("v", lambda e, j=j: e.tensor_scalar(n34[j][:], r34[j][:], mv34[j][:, 0:1], -1.0,
              ALU.mult, ALU.mult), reads=[kr, km], writes=[kn])
        em.dma("sync", x1t4[j][:], x1_d[tt * 128:(tt + 1) * 128, :], kx, [f"x1_{tt}"])
        em.op("v", lambda e, j=j: e.tensor_scalar(x1t4[j][:], x1t4[j][:], n34[j][:], None, ALU.add),
              reads=[kx, kn], writes=[kx])
        em.op("v", lambda e, j=j: e.scalar_tensor_tensor(outx[j][:], outx[j][:], r34[j][:],
              x1t4[j][:], ALU.mult, ALU.add), reads=[f"outx{j}", kr, kx], writes=[f"outx{j}"])
        em.dma("sync", out_d[tt * 128:(tt + 1) * 128, :], outx[j][:], f"out_{tt}", [f"outx{j}"])

    pop_scope()
    _final_wait(nc, em)
    return nc


def _final_wait(nc, em):
    for q in ("sync", "gp"):
        for i, sem in enumerate(em.dma_pools[q]):
            u = em.dma_uses[q][i]
            if u:
                em._wait("sync", (sem, u * 16, f"d_{q}{i}"))
    for c in em.COMPUTE:
        if em.cnt[c]:
            em._wait("sync", (em.sem[c], em.cnt[c], f"c_{c}"))
    if em.cccnt:
        em._wait("sync", (em.ccsem, em.cccnt, "ccsem"))
    em.finalize()


def host_inputs(inputs):
    x = np.asarray(inputs["L0_attn_hidden_states"])[0]
    ipW = np.asarray(inputs["L0_attn_in_proj_w"])
    conv_w = np.asarray(inputs["L0_attn_conv_w"])
    conv_b = np.asarray(inputs["L0_attn_conv_b"])
    A_log = np.asarray(inputs["L0_attn_A_log"])
    dt_bias = np.asarray(inputs["L0_attn_dt_bias"])
    Dp = np.asarray(inputs["L0_attn_D"])
    norm_w = np.asarray(inputs["L0_attn_norm_w"])
    out_w = np.asarray(inputs["L0_attn_out_proj_w"])
    gate_w = np.asarray(inputs["L0_ffn_gate_w"])
    gate_b = np.asarray(inputs["L0_ffn_gate_b"])
    W1 = np.asarray(inputs["L0_ffn_W1"])
    W2 = np.asarray(inputs["L0_ffn_W2"])
    W3 = np.asarray(inputs["L0_ffn_W3"])
    Ws1 = np.asarray(inputs["L0_ffn_Ws1"])
    Ws2 = np.asarray(inputs["L0_ffn_Ws2"])
    Ws3 = np.asarray(inputs["L0_ffn_Ws3"])

    shared = {}
    shared["ipWT"] = np.ascontiguousarray(ipW.T)
    cd = np.zeros((NO_HBC, CONVK, 128, 128), np.float32)
    cw = conv_w[:, 0, :]
    ar = np.arange(128)
    for j in range(NO_HBC):
        for k in range(CONVK):
            cd[j, k, ar, ar] = cw[j * 128:(j + 1) * 128, k]
    shared["convdiag"] = cd.reshape(NO_HBC * CONVK * 128, 128)
    shared["convb"] = np.ascontiguousarray(conv_b.reshape(NO_HBC, 128))
    shared["alog"] = np.ascontiguousarray(A_log.reshape(32, 1))
    shared["dtbias"] = np.ascontiguousarray(dt_bias.reshape(32, 1))
    dd = np.zeros((HEADS, 128, 128), np.float32)
    ar128 = np.arange(128)
    dd[:, ar128, ar128] = Dp[:, None]
    shared["ddiag"] = dd.reshape(HEADS * 128, 128)
    shared["normcols"] = np.ascontiguousarray(norm_w.reshape(32, 128).T)
    shared["opWT"] = np.ascontiguousarray(out_w.T)
    shared["gateWT"] = np.ascontiguousarray(gate_w.T)
    shared["gatebB"] = np.ascontiguousarray(gate_b.reshape(1, E))
    shared["idxdB"] = np.ascontiguousarray((E - np.arange(E, dtype=np.float32)).reshape(1, E))
    shared["w1t"] = np.ascontiguousarray(W1.transpose(0, 2, 1)).reshape(E * D, I_FF)
    shared["w3t"] = np.ascontiguousarray(W3.transpose(0, 2, 1)).reshape(E * D, I_FF)
    shared["w2t"] = np.ascontiguousarray(W2.transpose(0, 2, 1)).reshape(E * I_FF, D)
    shared["ws1t"] = np.ascontiguousarray(Ws1.T)
    shared["ws3t"] = np.ascontiguousarray(Ws3.T)
    shared["ws2t"] = np.ascontiguousarray(Ws2.T)
    m01 = np.zeros((2, 128, CHUNK), np.float32)
    for st in range(2):
        for s in range(128):
            m01[st, s, st * 128 + s:] = 1.0
    shared["mask01"] = m01.reshape(2 * 128, CHUNK)
    shared["eye"] = np.eye(128, dtype=np.float32)
    shared["trilM_in"] = np.triu(np.ones((128, 128), np.float32))

    in_maps = []
    for c in range(NCORES):
        m = dict(shared)
        halo = np.zeros((3, D), np.float32) if c == 0 else x[c * TC - 3:c * TC]
        m["x"] = np.ascontiguousarray(
            np.concatenate([np.zeros((125, D), np.float32), halo, x[c * TC:(c + 1) * TC]], 0))
        cm = np.zeros((32, E), np.float32)
        nb = np.full((32, E), -30000.0, np.float32)
        oh = np.zeros((32, E), np.float32)
        cm[:, :c] = 1.0
        nb[:, :c] = 0.0
        if c > 0:
            oh[:, c - 1] = 1.0
        m["cmaskB"], m["negB"], m["onehotB"] = cm, nb, oh
        in_maps.append(m)
    return in_maps


def kernel(**inputs):
    in_maps = host_inputs(inputs)
    nc = build_nc("full")
    res = run_bass_kernel_spmd(nc, in_maps, core_ids=list(range(NCORES)))
    out = np.concatenate([r["out"] for r in res.results], axis=0)
    return out.reshape(1, SEQ, D).astype(np.float32)
